# version 26
# baseline (speedup 1.0000x reference)
import numpy as np
import ml_dtypes
from contextlib import ExitStack

import concourse.bass as bass
import concourse.mybir as mybir
from concourse.bass_utils import run_bass_kernel_spmd

F32 = mybir.dt.float32
BF16 = mybir.dt.bfloat16
AF = mybir.ActivationFunctionType
ALU = mybir.AluOpType
BF = ml_dtypes.bfloat16

B, T, C = 4, 2048, 2048
NH, DH = 16, 128
TL = T // 2
NCH = C // 128
NT = TL // 512
NKT = T // 128
LN_EPS = 1e-5
RMS_EPS0 = 1.1920929e-07
NEG = -1.0e30
ISQ = 1.0 / np.sqrt(128.0)

PAIRS = [[0, 1], [2, 3], [4, 5], [6, 7]]


CAPZ = [16 - 2 * l for l in range(8)]


def tok_blocks(p):
    return [(15 - p) - 2 * l for l in range(8)]


def _wid(kt):
    return 128 * ((16 - kt + 1) // 2)


MSEGS = []
for _l in range(8):
    _cap = CAPZ[_l]
    MSEGS.append((_cap - 2, _l, (2 * _l) * 128))
    MSEGS.append((_cap - 1, _l, (2 * _l + 1) * 128))
MASK_COLS = 16 * 128


QUEUES = ("pe", "act", "dve", "pool", "sp")


NDMA_SEMS = 4


class Sched:

    def __init__(self, nc, ctx):
        self.nc = nc
        self.sems = {}
        for q in QUEUES:
            self.sems[f"c_{q}"] = ctx.enter_context(nc.semaphore(f"c_{q}"))
            for k in range(NDMA_SEMS):
                self.sems[f"d_{q}{k}"] = ctx.enter_context(nc.semaphore(f"d_{q}{k}"))
        self.cval = {q: 0 for q in QUEUES}
        self.didx = {q: 0 for q in QUEUES}
        self.ops = {q: [] for q in QUEUES}
        self.seen = {q: {} for q in QUEUES}
        self.semval = {name: 0 for name in self.sems}
        self.res = {}

    def op(self, q, emit, reads=(), writes=(), dma=False, dma_n=1, tag=""):
        toks = []
        for k in reads:
            e = self.res.get(k)
            if e and e["w"]:
                toks.append(e["w"])
        for k in writes:
            e = self.res.get(k)
            if e:
                if e["w"]:
                    toks.append(e["w"])
                toks.extend(e["r"])
        need = {}
        for (dq, sem, dv) in toks:
            if dq == q and q == "pe" and sem == "c_pe":
                continue
            if dv > need.get(sem, -1):
                need[sem] = dv
        if dma:
            i = self.didx[q]
            self.didx[q] = i + 1
            sem = f"d_{q}{i % NDMA_SEMS}"
            prev = self.semval.get(sem, 0)
            if prev > 0 and prev > need.get(sem, -1):
                need[sem] = prev
            tok = (q, sem, prev + 16 * dma_n)
            inc_sem, inc = sem, 16
        else:
            self.cval[q] += 1
            tok = (q, f"c_{q}", self.cval[q])
            inc_sem, inc = f"c_{q}", 1
        waits = []
        for sem in sorted(need):
            dv = need[sem]
            if dv <= self.seen[q].get(sem, 0):
                continue
            waits.append((sem, dv))
            self.seen[q][sem] = dv
        self.ops[q].append((waits, emit, inc_sem, inc, tag))
        self.semval[inc_sem] = (self.semval.get(inc_sem, 0) + inc) if not dma else tok[2]
        for k in reads:
            self.res.setdefault(k, {"w": None, "r": []})["r"].append(tok)
        for k in writes:
            self.res[k] = {"w": tok, "r": []}
        return tok

    def barrier(self):
        vals = dict(self.semval)
        for q in QUEUES:
            waits = []
            for sem in sorted(vals):
                v = vals[sem]
                if v <= 0 or sem == f"c_{q}":
                    continue
                if v <= self.seen[q].get(sem, 0):
                    continue
                waits.append((sem, v))
                self.seen[q][sem] = v
            self.cval[q] += 1
            self.semval[f"c_{q}"] = self.cval[q]
            self.ops[q].append((waits, (lambda e: e.nop()), f"c_{q}", 1, "barrier"))

    def emit_all(self):
        nc = self.nc
        with nc.Block() as blk:
            def mk(q):
                def body(eng):
                    for oi, (waits, emit, inc_sem, inc, tag) in enumerate(self.ops[q]):
                        for sem, dv in waits:
                            eng.wait_ge(self.sems[sem], dv)
                        try:
                            inst = emit(eng)
                        except Exception:
                            print(f"EMIT FAIL q={q} idx={oi} tag={tag} waits={waits}")
                            raise
                        inst.then_inc(self.sems[inc_sem], inc)
                return body
            blk.tensor(mk("pe"))
            blk.scalar(mk("act"))
            blk.vector(mk("dve"))
            blk.gpsimd(mk("pool"))
            blk.sync(mk("sp"))


_NAMECTR = [0]


def _uniq(name):
    _NAMECTR[0] += 1
    return f"{name}_{_NAMECTR[0]}"


def sbt(ctx, nc, name, shape, dtype):
    return ctx.enter_context(nc.sbuf_tensor(_uniq(name), shape, dtype))


def pst(ctx, nc, name, shape, dtype=F32):
    return ctx.enter_context(nc.psum_tensor(_uniq(name), shape, dtype))



def mm_accum(S, n, psum_key, emit_i, reads_i, tag=""):
    for i in range(n):
        w = [psum_key] if (i == 0 or i == n - 1) else []
        S.op("pe", (lambda e, i=i: emit_i(e, i)), reads=reads_i(i), writes=w, tag=tag)


def emit_rstd_row(S, E, row, key, inv_n, eps):
    S.op("dve", lambda e: e.tensor_scalar(row[:, :], row[:, :], inv_n, eps,
                                          ALU.mult, ALU.add),
         reads=[key], writes=[key])
    S.op("act", lambda e: e.activation(row[:, :], row[:, :], AF.Ln),
         reads=[key], writes=[key])
    S.op("act", lambda e: e.activation(row[:, :], row[:, :], AF.Exp, scale=-0.5),
         reads=[key], writes=[key])


def emit_layernorm(S, E, xkeys, xtiles, lnb_row, gain_col=None, rms_only=False,
                   eps=LN_EPS, n_feat=C, apply=True):
    nch = len(xtiles)
    ps0, ps1, pr0, pr1 = E["ps0"], E["ps1"], E["pr0"], E["pr1"]
    rows, lnb = E["rows"], E["lnb"]
    for dc in range(nch):
        if not rms_only:
            for tc in range(NT):
                prt, prk = (pr0, "pr0") if tc == 0 else (pr1, "pr1")
                w = [prk] if (dc == 0 or dc == nch - 1) else []
                S.op("pe", (lambda e, dc=dc, tc=tc, prt=prt: e.matmul(
                    prt[0:1, :], E["ones_bf"][:, 0:1],
                    xtiles[dc][:, tc * 512:(tc + 1) * 512],
                    start=(dc == 0), stop=(dc == nch - 1))),
                    reads=[xkeys[dc], "ones"], writes=w, tag="lnS")
        S.op("act", (lambda e, dc=dc: e.activation(E["sqt"][:, 0:TL],
                                                   xtiles[dc][:, :], AF.Square)),
             reads=[xkeys[dc]], writes=[E.get("sqt_key", "sqt")], tag="lnSq")
        for tc in range(NT):
            pqt, pqk = (ps0, "ps0") if tc == 0 else (ps1, "ps1")
            w = [pqk] if (dc == 0 or dc == nch - 1) else []
            S.op("pe", (lambda e, dc=dc, tc=tc, pqt=pqt: e.matmul(
                pqt[0:1, :], E["ones_bf"][:, 0:1],
                E["sqt"][:, tc * 512:(tc + 1) * 512],
                start=(dc == 0), stop=(dc == nch - 1))),
                reads=[E.get("sqt_key", "sqt"), "ones"], writes=w, tag="lnQ")
    inv_n = 1.0 / n_feat
    for tc in range(NT):
        pqk = "ps0" if tc == 0 else "ps1"
        pqt = E[pqk]
        S.op("dve", (lambda e, tc=tc, pqt=pqt: e.tensor_copy(
            rows["qrow"][0:1, tc * 512:(tc + 1) * 512], pqt[0:1, :])),
            reads=[pqk], writes=["qrow"])
    if not rms_only:
        for tc in range(NT):
            prk = "pr0" if tc == 0 else "pr1"
            prt = E[prk]
            S.op("dve", (lambda e, tc=tc, prt=prt: e.tensor_copy(
                rows["srow"][0:1, tc * 512:(tc + 1) * 512], prt[0:1, :])),
                reads=[prk], writes=["srow"])
        S.op("dve", lambda e: e.tensor_scalar_mul(rows["srow"][:, :], rows["srow"][:, :], inv_n),
             reads=["srow"], writes=["srow"])
        S.op("dve", lambda e: e.tensor_scalar_mul(rows["qrow"][:, :], rows["qrow"][:, :], inv_n),
             reads=["qrow"], writes=["qrow"])
        S.op("dve", lambda e: e.tensor_tensor(rows["brow"][:, :], rows["srow"][:, :], rows["srow"][:, :], ALU.mult),
             reads=["srow"], writes=["brow"])
        S.op("dve", lambda e: e.tensor_tensor(rows["qrow"][:, :], rows["qrow"][:, :], rows["brow"][:, :], ALU.subtract),
             reads=["qrow", "brow"], writes=["qrow"])
        emit_rstd_row(S, E, rows["qrow"], "qrow", 1.0, eps)
    else:
        emit_rstd_row(S, E, rows["qrow"], "qrow", inv_n, eps)
    if not rms_only:
        S.op("dve", lambda e: e.tensor_tensor(rows["brow"][:, :], rows["srow"][:, :], rows["qrow"][:, :], ALU.mult),
             reads=["srow", "qrow"], writes=["brow"])
        S.op("dve", lambda e: e.tensor_scalar_mul(rows["brow"][:, :], rows["brow"][:, :], -1.0),
             reads=["brow"], writes=["brow"])
    S.op("sp", (lambda e: e.dma_start(
        bass.AP(lnb, lnb_row * TL, [[TL, 1], [1, TL]]), rows["qrow"][:, :])),
        reads=["qrow"], writes=[f"lnb{lnb_row}"], dma=True)
    if not rms_only:
        S.op("sp", (lambda e: e.dma_start(
            bass.AP(lnb, (lnb_row + 1) * TL, [[TL, 1], [1, TL]]), rows["brow"][:, :])),
            reads=["brow"], writes=[f"lnb{lnb_row + 1}"], dma=True)
    if not apply:
        return
    S.op("sp", (lambda e: e.dma_start(
        E["arep"][:, :], bass.AP(lnb, lnb_row * TL, [[0, 128], [1, TL]]))),
        reads=[f"lnb{lnb_row}"], writes=["arep"], dma=True)
    if not rms_only:
        S.op("sp", (lambda e: e.dma_start(
            E["brep"][:, :], bass.AP(lnb, (lnb_row + 1) * TL, [[0, 128], [1, TL]]))),
            reads=[f"lnb{lnb_row + 1}"], writes=["brep"], dma=True)
    for dc in range(nch):
        S.op("dve", (lambda e, dc=dc: e.tensor_tensor(
            xtiles[dc][:, :], xtiles[dc][:, :], E["arep"][:, :], ALU.mult)),
            reads=[xkeys[dc], "arep"], writes=[xkeys[dc]])
        if not rms_only:
            S.op("dve", (lambda e, dc=dc: e.tensor_tensor(
                xtiles[dc][:, :], xtiles[dc][:, :], E["brep"][:, :], ALU.add)),
                reads=[xkeys[dc], "brep"], writes=[xkeys[dc]])




def _rope_tables_l0():
    inv = 1.0 / 10000.0 ** (np.arange(0, DH, 2, dtype=np.float64) / DH)
    t = np.arange(T, dtype=np.float64)
    fr = np.outer(t, inv)
    sin = np.repeat(np.sin(fr), 2, axis=1)
    cos = np.repeat(np.cos(fr), 2, axis=1)
    sgn = np.where(np.arange(DH) % 2 == 0, -1.0, 1.0)
    return cos.T.astype(np.float32), (sin * sgn[None, :]).T.astype(np.float32)


def _mask_l0(is_odd):
    p = 1 if is_odd else 0
    m = np.zeros((128, MASK_COLS), np.float32)
    for (kt, l, off) in MSEGS:
        g = (15 - p) - 2 * l
        if kt > g:
            m[:, off:off + 128] = NEG
        elif kt == g:
            for c in range(128):
                m[c + 1:, off + c] = NEG
    return m


def _tok_idx(p):
    return np.concatenate([np.arange(g * 128, (g + 1) * 128)
                           for g in tok_blocks(p)])


def _stage_phase_a(inputs):
    x = np.asarray(inputs["L0_attn_x"])
    wqkv = np.asarray(inputs["L0_attn_W_qkv"])
    wo = np.asarray(inputs["L0_attn_W_o"])
    x_fm = np.ascontiguousarray(x.transpose(0, 2, 1))
    wqkvT = np.ascontiguousarray(wqkv.T)
    qk = np.ascontiguousarray(wqkvT[:, 0:4096].reshape(C, 32, 128).transpose(1, 0, 2))
    v = np.ascontiguousarray(wqkvT[:, 4096:6144].reshape(C, 16, 128).transpose(1, 0, 2))
    woT = np.ascontiguousarray(wo.T.reshape(C, 16, 128).transpose(1, 0, 2))
    cosf, sinf = _rope_tables_l0()
    ones = np.ones((128, 1), np.float32).astype(BF)
    ident = np.eye(128, dtype=np.float32).astype(BF)
    masks = [_mask_l0(False).astype(BF), _mask_l0(True).astype(BF)]
    in_maps = []
    for c in range(8):
        b, j = divmod(c, 2)
        idx = _tok_idx(j)
        in_maps.append({
            "x": np.ascontiguousarray(x_fm[b][:, idx]),
            "wqkT": qk, "wvT": v, "woT": woT,
            "cos0": np.ascontiguousarray(cosf[:, idx]).astype(BF),
            "sin0": np.ascontiguousarray(sinf[:, idx]).astype(BF),
            "mask0": masks[j],
            "onesd": ones, "identd": ident,
        })
    return in_maps




HID = 5461
HIDP = 5504
NHT = HIDP // 128


def _stage_phase_b(inputs, x1_fm=None, x1_pc=None):
    w1 = np.asarray(inputs["L0_ffn_w1"])
    w3 = np.asarray(inputs["L0_ffn_w3"])
    w2 = np.asarray(inputs["L0_ffn_w2"])

    def pad_slab_T(w):
        wT = np.zeros((C, HIDP), np.float32)
        wT[:, 0:HID] = w.T
        return np.ascontiguousarray(wT.reshape(C, NHT, 128).transpose(1, 0, 2))

    w1sl = pad_slab_T(w1)
    w3sl = pad_slab_T(w3)
    w2T = np.zeros((HIDP, C), np.float32)
    w2T[0:HID, :] = w2.T
    w2sl = np.ascontiguousarray(w2T.reshape(HIDP, NCH, 128).transpose(1, 0, 2))
    ones = np.ones((128, 1), np.float32)
    in_maps = []
    for c in range(8):
        b, j = divmod(c, 2)
        m = {"w1s": w1sl, "w3s": w3sl, "w2s": w2sl,
             "onesd": ones.astype(BF), "onesf": ones}
        if x1_pc is not None:
            m["x1d"] = x1_pc[c]
        elif x1_fm is not None:
            m["x1d"] = np.ascontiguousarray(x1_fm[b, :, j * TL:(j + 1) * TL])
        in_maps.append(m)
    return in_maps


NSSM = 16
TCH = 512
NCHK = T // TCH
E2ROWS = 160 + C


def _stage_phase_c(inputs, x2_fm=None, x2_pc=None):
    wx = np.asarray(inputs["L1_attn_W_xproj"])
    wdtf = np.asarray(inputs["L1_attn_W_dt"])
    bdt = np.asarray(inputs["L1_attn_b_dt"])
    alog = np.asarray(inputs["L1_attn_A_log"])
    dpar = np.asarray(inputs["L1_attn_D_param"])
    wxTs = np.ascontiguousarray(wx.T)
    dcols = np.ascontiguousarray(dpar.reshape(16, 128).T)
    ones = np.ones((128, 1), np.float32).astype(BF)
    in_maps = []
    for c in range(8):
        b, half = divmod(c, 2)
        dsl = slice(half * 1024, (half + 1) * 1024)
        wdtT_h = np.ascontiguousarray(wdtf[dsl, :].T)
        alog_h = np.ascontiguousarray(
            alog[dsl].reshape(8, 128, 16).transpose(1, 0, 2).reshape(128, 128))
        bdt_h = np.ascontiguousarray(bdt[dsl].reshape(8, 128).T)
        hoffv = np.array([[half * 1024 * T, half * TL, 0, 0]], np.int32)
        m = {"wxT": wxTs, "wdtT": wdtT_h, "alogh": alog_h, "bdth": bdt_h,
             "dfull": dcols, "onesd": ones, "hoffd": hoffv}
        if x2_pc is not None:
            m["x2d"] = x2_pc[c]
        elif x2_fm is not None:
            m["x2d"] = np.ascontiguousarray(x2_fm[b, :, half * TL:(half + 1) * TL])
        in_maps.append(m)
    return in_maps


LAT = 512
NLAT = LAT // 128


def _rope_tables_l1():
    RD = 64
    inv = 1.0 / 10000.0 ** (np.arange(0, RD, 2, dtype=np.float64) / RD)
    t = np.arange(T, dtype=np.float64)
    ang = np.outer(t, inv)
    cosb = np.repeat(np.cos(ang), 2, axis=1)
    sinb = np.repeat(np.sin(ang), 2, axis=1)
    sgn = np.where(np.arange(64) % 2 == 0, -1.0, 1.0)
    sinb = sinb * sgn[None, :]
    cos128 = np.concatenate([cosb, cosb], axis=1).T
    sin128 = np.concatenate([sinb, sinb], axis=1).T
    return cos128.astype(np.float32), sin128.astype(np.float32)


ROPE_PERM = np.empty(64, np.int64)
ROPE_PERM[0::2] = np.arange(32)
ROPE_PERM[1::2] = np.arange(32) + 32


def _perm_rope_cols(w):
    w2 = w.reshape(w.shape[0], 16, 64)[:, :, ROPE_PERM]
    return np.ascontiguousarray(w2.reshape(w.shape[0], 1024))


def _slab(w, kdim, nf):
    return np.ascontiguousarray(w.reshape(kdim, nf, 128).transpose(1, 0, 2))


def _stage_phase_d(inputs, x3_fm=None, x3_pc=None):
    g = lambda k: np.asarray(inputs[f"L1_ffn_{k}"])
    kvdw = _slab(g("kv_d_w"), C, NLAT)
    qdw = _slab(g("q_d_w"), C, NLAT)
    kuw = _slab(g("k_u_w"), LAT, 8)
    quw = _slab(g("q_u_w"), LAT, 8)
    vuw = np.ascontiguousarray(g("v_u_w"))
    rkw = _slab(_perm_rope_cols(g("rope_k_w")), C, 8)
    rqw = _slab(_perm_rope_cols(g("rope_q_w")), LAT, 8)
    oww = _slab(g("o_w"), C, NCH)
    l1w = np.ascontiguousarray(g("ln1_w").reshape(16, 128).T)
    cosf, sinf = _rope_tables_l1()
    ones = np.ones((128, 1), np.float32).astype(BF)
    masks = [_mask_l0(False).astype(BF), _mask_l0(True).astype(BF)]
    in_maps = []
    for c in range(8):
        b, j = divmod(c, 2)
        idx = _tok_idx(j)
        m = {"kvdw": kvdw, "qdw": qdw, "kuw": kuw, "quw": quw, "vuw": vuw,
             "rkw": rkw, "rqw": rqw, "oww": oww, "l1wd": l1w,
             "cos1": np.ascontiguousarray(cosf[:, idx]).astype(BF),
             "sin1": np.ascontiguousarray(sinf[:, idx]).astype(BF),
             "mask0": masks[j], "onesd": ones,
             "identd": np.eye(128, dtype=np.float32).astype(BF)}
        if x3_pc is not None:
            m["x3d"] = x3_pc[c]
        elif x3_fm is not None:
            m["x3d"] = np.ascontiguousarray(x3_fm[b, :, cols])
        in_maps.append(m)
    return in_maps


IEXP = 1024
NIT = IEXP // 128
NEXP = 8


def _stage_phase_e(inputs, h_fm=None, x3_fm=None, h_pc=None, x3_pc=None):
    g = lambda k: np.asarray(inputs[f"L1_ffn_{k}"])
    shg = _slab(g("sh_gate"), C, NIT)
    shu = _slab(g("sh_up"), C, NIT)
    shd = _slab(g("sh_down"), IEXP, NCH)
    rgs = np.stack([_slab(g("r_gate")[e], C, NIT) for e in range(7)])
    rus = np.stack([_slab(g("r_up")[e], C, NIT) for e in range(7)])
    rds = np.stack([_slab(g("r_down")[e], IEXP, NCH) for e in range(7)])
    rwTs = np.ascontiguousarray(g("router_w"))
    rbs = np.ascontiguousarray(g("router_b").reshape(7, 1))
    l2w = np.ascontiguousarray(g("ln2_w").reshape(16, 128).T)
    ones = np.ones((128, 1), np.float32).astype(BF)
    in_maps = []
    for c in range(8):
        b, j = divmod(c, 2)
        cols = slice(j * TL, (j + 1) * TL)
        m = {"shg": shg, "shu": shu, "shd": shd, "rg": rgs, "ru": rus, "rd": rds,
             "rwT": rwTs, "rbd": rbs, "l2wd": l2w, "onesd": ones}
        if h_pc is not None:
            m["hdd"] = h_pc[c]
            m["x3d"] = x3_pc[c]
        elif h_fm is not None:
            m["hdd"] = np.ascontiguousarray(h_fm[b, :, cols])
            m["x3d"] = np.ascontiguousarray(x3_fm[b, :, cols])
        in_maps.append(m)
    return in_maps


def emit_prelude(nc, S, ctx, D):
    w1s, w3s, w2s = D["w1s"], D["w3s"], D["w2s"]
    abscol, abstmp = D["abscol"], D["abstmp"]
    pwf = [sbt(ctx, nc, f"pwf{i}", [128, C], F32) for i in range(2)]
    pab = sbt(ctx, nc, "pab", [128, C], BF16)
    S.op("dve", lambda e: e.memset(abscol[:, :], 0.0), writes=["abscol"])
    ti = 0
    for (srcd, gi, nel) in ((w1s, 0, NHT * C * 128), (w3s, 1, NHT * C * 128),
                            (w2s, 2, NCH * HIDP * 128)):
        ntile = nel // (128 * C)
        for t_ in range(ntile):
            wk = f"pwf{ti % 2}"
            S.op("sp" if ti % 2 == 0 else "pool",
                 (lambda e, srcd=srcd, t_=t_, ti=ti: e.dma_start(
                     pwf[ti % 2][:, :],
                     bass.AP(srcd, t_ * 128 * C, [[C, 128], [1, C]]))),
                 writes=[wk], dma=True, tag="pwl")
            S.op("act", (lambda e, ti=ti: e.activation(
                pab[:, :], pwf[ti % 2][:, :], AF.Abs,
                accum_out=abstmp[:, 0:1])),
                reads=[wk], writes=["pab", "abstmp"], tag="pabs")
            S.op("dve", (lambda e, gi=gi: e.tensor_tensor(
                abscol[:, gi:gi + 1], abscol[:, gi:gi + 1], abstmp[:, 0:1], ALU.add)),
                reads=["abstmp", "abscol"], writes=["abscol"], tag="pacc")
            ti += 1



def emit_phase_a(nc, S, ctx, D):
    x = D["x"]
    wqkT = D["wqkT"]
    wvT = D["wvT"]
    woT = D["woT"]
    cos0 = D["cos0"]
    sin0 = D["sin0"]
    mask0 = D["mask0"]
    onesd = D["onesd"]
    identd = D["identd"]
    x1 = D["x1i"]
    k_in = D["k_in"]
    k_out = D["k_out"]
    v_in = D["v_in"]
    v_out = D["v_out"]
    lnb = D["lnb"]
    xb = [sbt(ctx, nc, f"xb{i}", [128, TL], BF16) for i in range(NCH)]
    qt = [sbt(ctx, nc, f"qt{i}", [128, TL], BF16) for i in range(NH)]
    ot = [sbt(ctx, nc, f"ot{i}", [128, TL], BF16) for i in range(NH)]
    sqt2 = [sbt(ctx, nc, f"sqt2_{i}", [128, TL], BF16) for i in range(2)]
    sqt = sqt2[0]
    wslab = [sbt(ctx, nc, f"wslab{i}", [128, C], BF16) for i in range(2)]
    ktile = [sbt(ctx, nc, f"ktile{i}", [128, TL], BF16) for i in range(2)]
    shuf2 = [sbt(ctx, nc, f"shuf2_{i}", [128, TL], BF16) for i in range(2)]
    cosT = sbt(ctx, nc, "cosT", [128, TL], BF16)
    sinT = sbt(ctx, nc, "sinT", [128, TL], BF16)
    maskT = sbt(ctx, nc, "maskT", [128, MASK_COLS], BF16)
    ones_bf = sbt(ctx, nc, "ones_bf", [128, 1], BF16)
    ident = sbt(ctx, nc, "ident", [128, 128], BF16)
    khead = [sbt(ctx, nc, f"khead{i}", [128, T], BF16) for i in range(2)]
    vhead = [sbt(ctx, nc, f"vhead{i}", [128, T], BF16) for i in range(2)]
    ptile2 = [sbt(ctx, nc, f"ptile{i}", [128, TL], BF16) for i in range(2)]
    vexp = sbt(ctx, nc, "vexp", [128, TL], BF16)

    arep = sbt(ctx, nc, "arep", [128, TL], F32)
    brep = sbt(ctx, nc, "brep", [128, TL], F32)
    rrep2 = [arep, brep]
    xres = [sbt(ctx, nc, "xres0", [128, TL], F32)]
    x1b = [sbt(ctx, nc, f"x1b{i}", [128, TL], F32) for i in range(2)]
    rows = {k: sbt(ctx, nc, k, [1, TL], F32) for k in
            ("srow", "qrow", "brow")}
    rrow2 = [sbt(ctx, nc, f"rrow2_{i}", [1, TL], F32) for i in range(2)]

    ps4 = pst(ctx, nc, "ps4", [128, 512])
    ps0 = pst(ctx, nc, "ps0", [128, 512])
    ps1 = pst(ctx, nc, "ps1", [128, 512])
    ps2 = pst(ctx, nc, "ps2", [128, 512])
    ps3 = pst(ctx, nc, "ps3", [128, 512])
    pr0 = pst(ctx, nc, "pr0", [128, 512])
    pr1 = pst(ctx, nc, "pr1", [128, 512])
    psb = pst(ctx, nc, "psb", [128, 1024], BF16)
    PS = {0: ps0, 1: ps1, 2: ps2, 3: ps3}

    E = {"ones_bf": ones_bf, "sqt": sqt, "sqt_key": "sqtp0", "rows": rows,
         "lnb": lnb, "arep": arep, "brep": brep,
         "ps0": ps0, "ps1": ps1, "pr0": pr0, "pr1": pr1}

    for dc in range(NCH):
        S.op("pool", (lambda e, dc=dc: e.dma_start(
            xb[dc][:, :], x[dc * 128:(dc + 1) * 128, :])),
            writes=[f"xb{dc}"], dma=True)
    S.op("sp", lambda e: e.dma_start(cosT[:, :], cos0[:, :]), writes=["cosT"], dma=True)
    S.op("sp", lambda e: e.dma_start(sinT[:, :], sin0[:, :]), writes=["sinT"], dma=True)
    S.op("sp", lambda e: e.dma_start(maskT[:, :], mask0[:, :]), writes=["maskT"], dma=True)
    S.op("sp", lambda e: e.dma_start(ones_bf[:, :], onesd[:, :]), writes=["ones"], dma=True)
    S.op("sp", lambda e: e.dma_start(ident[:, :], identd[:, :]), writes=["ident"], dma=True)

    xkeys = [f"xb{i}" for i in range(NCH)]
    emit_layernorm(S, E, xkeys, xb, 0)

    def proj_head(src_dram, f_idx, dest_tile, dest_key, do_rope):
        par = f_idx % 2
        wk = f"wslab{par}"
        shf, shfk = shuf2[par], f"shuf{par}"
        sq, sqk = sqt2[par], f"sqtp{par}"
        rro, rrok = rrow2[par], f"rrow{par}"
        rrp, rrpk = rrep2[par], f"rrep{par}"
        S.op("pool", (lambda e: e.dma_start(
            wslab[par][:, :],
            bass.AP(src_dram, f_idx * C * 128,
                    [[128, 128], [128 * 128, NCH], [1, 128]]))),
            writes=[wk], dma=True)
        for tc in range(NT):
            pk = ("ps2", "ps3", "ps4")[(f_idx * NT + tc) % 3]
            pt = E_PS[pk]
            for dc in range(NCH):
                w = [pk] if (dc == 0 or dc == NCH - 1) else []
                S.op("pe", (lambda e, tc=tc, dc=dc, pt=pt: e.matmul(
                    pt[:, :], wslab[par][:, dc * 128:(dc + 1) * 128],
                    xb[dc][:, tc * 512:(tc + 1) * 512],
                    start=(dc == 0), stop=(dc == NCH - 1))),
                    reads=[wk, f"xb{dc}"], writes=w, tag=f"pj{f_idx}")
            if par == 0:
                S.op("dve", (lambda e, tc=tc, pt=pt: e.tensor_copy(
                    dest_tile[:, tc * 512:(tc + 1) * 512], pt[:, :])),
                    reads=[pk], writes=[dest_key], tag="pjev")
            else:
                S.op("act", (lambda e, tc=tc, pt=pt: e.copy(
                    dest_tile[:, tc * 512:(tc + 1) * 512], pt[:, :])),
                    reads=[pk], writes=[dest_key], tag="pjev")
        if not do_rope:
            return
        mask32 = [i ^ 1 for i in range(32)]
        S.op("dve", lambda e: e.stream_shuffle(shf[:, :], dest_tile[:, :], mask32),
             reads=[dest_key], writes=[shfk])
        S.op("dve", lambda e: e.tensor_tensor(dest_tile[:, :], dest_tile[:, :], cosT[:, :], ALU.mult),
             reads=[dest_key, "cosT"], writes=[dest_key])
        S.op("dve", lambda e: e.tensor_tensor(shf[:, :], shf[:, :], sinT[:, :], ALU.mult),
             reads=[shfk, "sinT"], writes=[shfk])
        S.op("dve", lambda e: e.tensor_tensor(dest_tile[:, :], dest_tile[:, :], shf[:, :], ALU.add),
             reads=[dest_key, shfk], writes=[dest_key])
        S.op("act", lambda e: e.activation(sq[:, :], dest_tile[:, :], AF.Square),
             reads=[dest_key], writes=[sqk])
        for tc in range(NT):
            prk = "pr0" if tc == 0 else "pr1"
            prt = E[prk]
            S.op("pe", (lambda e, tc=tc, prt=prt: e.matmul(
                prt[0:1, :], ones_bf[:, 0:1], sq[:, tc * 512:(tc + 1) * 512],
                start=True, stop=True)),
                reads=[sqk, "ones"], writes=[prk])
            S.op("dve", (lambda e, tc=tc, prt=prt: e.tensor_copy(
                rro[0:1, tc * 512:(tc + 1) * 512], prt[0:1, :])),
                reads=[prk], writes=[rrok])
        emit_rstd_row(S, E, rro, rrok, 1.0 / 128, RMS_EPS0)
        S.op("sp", (lambda e: e.dma_start(
            bass.AP(lnb, (2 + par) * TL, [[TL, 1], [1, TL]]), rro[:, :])),
            reads=[rrok], writes=[f"lnb{2 + par}"], dma=True)
        S.op("sp", (lambda e: e.dma_start(
            rrp[:, :], bass.AP(lnb, (2 + par) * TL, [[0, 128], [1, TL]]))),
            reads=[f"lnb{2 + par}"], writes=[rrpk], dma=True)
        S.op("dve", lambda e: e.tensor_tensor(dest_tile[:, :], dest_tile[:, :], rrp[:, :], ALU.mult),
             reads=[dest_key, rrpk], writes=[dest_key])

    E_PS = {"ps2": ps2, "ps3": ps3, "ps4": ps4}
    for h in range(NH):
        proj_head(wqkT, h, qt[h], f"qt{h}", True)
    for h in range(NH):
        kb = ktile[h % 2]
        proj_head(wqkT, 16 + h, kb, f"ktile{h % 2}", True)
        S.op("sp", (lambda e, h=h, kb=kb: e.dma_start(
            k_in[h * 128:(h + 1) * 128, :], kb[:, :])),
            reads=[f"ktile{h % 2}"], writes=[f"k_in{h}"], dma=True)

    S.op("pool", (lambda e: e.collective_compute(
        "AllGather", ALU.bypass, replica_groups=PAIRS,
        ins=[k_in.ap().opt()], outs=[k_out.ap().opt()])),
        reads=[f"k_in{h}" for h in range(NH)], writes=["k_out"], tag="agK")

    for h in range(NH):
        proj_head(wvT, h, ktile[0], "ktile0", False)
        for tt in range(8):
            S.op("pe", (lambda e, tt=tt: e.transpose(
                psb[:, tt * 128:(tt + 1) * 128],
                ktile[0][:, tt * 128:(tt + 1) * 128], ident[:, :])),
                reads=["ktile0", "ident"], writes=["psb"], tag="vT")
        S.op("dve", lambda e: e.tensor_copy(vexp[:, :], psb[:, :]),
             reads=["psb"], writes=["vexp"])
        S.op("sp", (lambda e, h=h: e.dma_start(
            bass.AP(v_in, h * 128, [[C, 128], [128 * C, 8], [1, 128]]),
            vexp[:, :])),
            reads=["vexp"], writes=[f"v_in{h}"], dma=True)

    S.op("pool", (lambda e: e.collective_compute(
        "AllGather", ALU.bypass, replica_groups=PAIRS,
        ins=[v_in.ap().opt()], outs=[v_out.ap().opt()])),
        reads=[f"v_in{h}" for h in range(NH)], writes=["v_out"], tag="agV")

    AV_LAST = {0: 15, 1: 7}
    for h in range(NH):
        kb, kk = khead[h % 2], f"khead{h % 2}"
        vb, vk = vhead[h % 2], f"vhead{h % 2}"
        for r_ in range(2):
            S.op("sp", (lambda e, h=h, kb=kb, r_=r_: e.dma_start(
                bass.AP(kb, (1 - r_) * 128, [[T, 128], [256, 8], [1, 128]]),
                bass.AP(k_out, r_ * C * TL + h * 128 * TL + 7 * 128,
                        [[TL, 128], [-128, 8], [1, 128]]))),
                reads=["k_out", "v_out"], writes=[kk], dma=True)
            S.op("sp", (lambda e, h=h, vb=vb, r_=r_: e.dma_start(
                bass.AP(vb, (1 - r_) * 128, [[T, 128], [256, 8], [1, 128]]),
                bass.AP(v_out, r_ * TL * C + 7 * 128 * C + h * 128,
                        [[C, 128], [-128 * C, 8], [1, 128]]))),
                reads=["v_out", "k_out"], writes=[vk], dma=True)
        first = [True, True]
        for kt in range(NKT):
            wd = _wid(kt)
            msegs_kt = [(l_, off_) for (mk_, l_, off_) in MSEGS if mk_ == kt]
            chunks = []
            c0 = 0
            while c0 < wd:
                chunks.append((c0, min(c0 + 512, wd)))
                c0 = min(c0 + 512, wd)
            for ci, (c0, c1) in enumerate(chunks):
                spk = ("ps2", "ps3", "ps4")[(kt + ci) % 3]
                sps = E_PS[spk]
                S.op("pe", (lambda e, kt=kt, c0=c0, c1=c1, sps=sps, kb=kb, h=h: e.matmul(
                    sps[:, 0:c1 - c0], kb[:, kt * 128:(kt + 1) * 128],
                    qt[h][:, c0:c1], start=True, stop=True)),
                    reads=[kk, f"qt{h}"], writes=[spk], tag=f"sc{h}")
                for (ml_, moff_) in msegs_kt:
                    a0 = ml_ * 128
                    if a0 >= c0 and a0 + 128 <= c1:
                        S.op("dve", (lambda e, a0=a0, c0=c0, sps=sps, moff_=moff_: e.tensor_tensor(
                            sps[:, a0 - c0:a0 - c0 + 128], sps[:, a0 - c0:a0 - c0 + 128],
                            maskT[:, moff_:moff_ + 128], ALU.add)),
                            reads=[spk, "maskT"], writes=[spk], tag="msk")
                S.op("act", (lambda e, c0=c0, c1=c1, sps=sps, kt=kt: e.activation(
                    ptile2[kt % 2][:, c0:c1], sps[:, 0:c1 - c0], AF.Exp, scale=ISQ)),
                    reads=[spk], writes=[f"ptile{kt % 2}"], tag="exp")
            b0 = 0
            while b0 < wd:
                blk_i = b0 // 512
                b1 = min((blk_i + 1) * 512, wd)
                opk = "ps0" if blk_i == 0 else "ps1"
                opt_ = PS[blk_i]
                rpk = "pr0" if blk_i == 0 else "pr1"
                rpt = E[rpk]
                wo_ = [opk] if (first[blk_i] or kt == AV_LAST[blk_i]) else []
                wr_ = [rpk] if (first[blk_i] or kt == AV_LAST[blk_i]) else []
                S.op("pe", (lambda e, kt=kt, b0=b0, b1=b1, blk_i=blk_i, vb=vb,
                            opt_=opt_, f=first[blk_i]: e.matmul(
                    opt_[:, b0 - blk_i * 512:b1 - blk_i * 512],
                    vb[:, kt * 128:(kt + 1) * 128], ptile2[kt % 2][:, b0:b1],
                    start=f, stop=(kt == AV_LAST[blk_i]))),
                    reads=[vk, f"ptile{kt % 2}"], writes=wo_, tag=f"av{h}")
                S.op("pe", (lambda e, kt=kt, b0=b0, b1=b1, blk_i=blk_i,
                            rpt=rpt, f=first[blk_i]: e.matmul(
                    rpt[0:1, b0 - blk_i * 512:b1 - blk_i * 512],
                    ones_bf[:, 0:1], ptile2[kt % 2][:, b0:b1],
                    start=f, stop=(kt == AV_LAST[blk_i]))),
                    reads=[f"ptile{kt % 2}", "ones"], writes=wr_, tag=f"rs{h}")
                first[blk_i] = False
                b0 = b1
        par = h % 2
        rro, rrok = rrow2[par], f"rrow{par}"
        rrp, rrpk = rrep2[par], f"rrep{par}"
        for blk_i in range(2):
            rpk = "pr0" if blk_i == 0 else "pr1"
            rpt = E[rpk]
            S.op("dve", (lambda e, blk_i=blk_i, rpt=rpt, rro=rro: e.tensor_copy(
                rro[0:1, blk_i * 512:(blk_i + 1) * 512], rpt[0:1, :])),
                reads=[rpk], writes=[rrok])
        S.op("dve", (lambda e, rro=rro: e.reciprocal(rro[:, :], rro[:, :])),
             reads=[rrok], writes=[rrok])
        S.op("sp", (lambda e, rro=rro: e.dma_start(
            bass.AP(lnb, (4 + par) * TL, [[TL, 1], [1, TL]]), rro[:, :])),
            reads=[rrok], writes=[f"lnb{4 + par}"], dma=True)
        S.op("sp", (lambda e, rrp=rrp: e.dma_start(
            rrp[:, :], bass.AP(lnb, (4 + par) * TL, [[0, 128], [1, TL]]))),
            reads=[f"lnb{4 + par}"], writes=[rrpk], dma=True)
        for blk_i in range(2):
            opk = "ps0" if blk_i == 0 else "ps1"
            opt_ = PS[blk_i]
            q_ = "dve" if blk_i == 0 else "act"
            if q_ == "dve":
                S.op("dve", (lambda e, blk_i=blk_i, h=h, opt_=opt_: e.tensor_copy(
                    ot[h][:, blk_i * 512:(blk_i + 1) * 512], opt_[:, :])),
                    reads=[opk], writes=[f"ot{h}"], tag="oev")
            else:
                S.op("act", (lambda e, blk_i=blk_i, h=h, opt_=opt_: e.copy(
                    ot[h][:, blk_i * 512:(blk_i + 1) * 512], opt_[:, :])),
                    reads=[opk], writes=[f"ot{h}"], tag="oev")
        S.op("dve", (lambda e, h=h, rrp=rrp: e.tensor_tensor(
            ot[h][:, :], ot[h][:, :], rrp[:, :], ALU.mult)),
            reads=[f"ot{h}", rrpk], writes=[f"ot{h}"], tag="onrm")

    for o in range(NCH):
        wk = f"wslab{o % 2}"
        S.op("pool", (lambda e, o=o: e.dma_start(
            wslab[o % 2][:, :],
            bass.AP(woT, o * C * 128, [[128, 128], [128 * 128, NCH], [1, 128]]))),
            writes=[wk], dma=True)
        S.op("sp", (lambda e, o=o: e.dma_start(
            xres[0][:, :], x[o * 128:(o + 1) * 128, :])),
            writes=["xres0"], dma=True)
        for tc in range(NT):
            pk = "ps2" if tc == 0 else "ps3"
            pt = E_PS[pk]
            for h in range(NH):
                w = [pk] if (h == 0 or h == NH - 1) else []
                S.op("pe", (lambda e, o=o, tc=tc, h=h, pt=pt: e.matmul(
                    pt[:, :], wslab[o % 2][:, h * 128:(h + 1) * 128],
                    ot[h][:, tc * 512:(tc + 1) * 512],
                    start=(h == 0), stop=(h == NH - 1))),
                    reads=[wk, f"ot{h}"], writes=w, tag=f"wo{o}")
            S.op("dve", (lambda e, o=o, tc=tc, pt=pt: e.tensor_tensor(
                x1b[o % 2][:, tc * 512:(tc + 1) * 512], pt[:, :],
                xres[0][:, tc * 512:(tc + 1) * 512], ALU.add)),
                reads=[pk, "xres0"], writes=[f"x1b{o % 2}"], tag="res")
        S.op("sp", (lambda e, o=o: e.dma_start(
            x1[o * 128:(o + 1) * 128, :], x1b[o % 2][:, :])),
            reads=[f"x1b{o % 2}"], writes=[f"x1st{o % 2}"], dma=True)

    S.op("sp", lambda e: e.nop(), reads=["x1st0", "x1st1"])


def emit_phase_b(nc, S, ctx, D):
    sder = D["sder"]
    x1d = D["x1i"]
    w1s = D["w1s"]
    w3s = D["w3s"]
    w2s = D["w2s"]
    onesd = D["onesd"]
    onesf = D["onesf"]
    x2 = D["x2i"]
    lnb = D["lnb"]

    xb = [sbt(ctx, nc, f"xb{i}", [128, TL], BF16) for i in range(NCH)]
    gt = [sbt(ctx, nc, f"gt{i}", [128, TL], BF16) for i in range(NHT)]
    wf32 = [sbt(ctx, nc, f"wf32_{i}", [128, C], F32) for i in range(2)]
    w2f32 = [sbt(ctx, nc, f"w2f32_{i}", [128, 1408], F32) for i in range(2)]
    qsl = [sbt(ctx, nc, f"qsl{i}", [128, C], BF16) for i in range(2)]
    q2sl = [sbt(ctx, nc, "q2sl0", [128, HIDP], BF16)]
    qtmp = sbt(ctx, nc, "qtmp", [128, 2048], BF16)
    sgt = sbt(ctx, nc, "sgt", [128, 2048], BF16)
    h1t = sbt(ctx, nc, "h1t", [128, TL], BF16)
    sqt = h1t
    onesbf = sbt(ctx, nc, "onesbf", [128, 1], BF16)
    onesf32 = sbt(ctx, nc, "onesf32", [128, 1], F32)
    srow = sbt(ctx, nc, "srow_s", [1, 16], F32)
    scols = sbt(ctx, nc, "scols", [128, 16], F32)
    arep = sbt(ctx, nc, "arep", [128, TL], F32)
    brep = sbt(ctx, nc, "brep", [128, TL], F32)
    x2b = [sbt(ctx, nc, "x2b0", [128, TL], F32)] * 2
    rows = {k: sbt(ctx, nc, k, [1, TL], F32) for k in
            ("srow", "qrow", "brow")}

    ps0 = pst(ctx, nc, "ps0", [128, 512])
    ps1 = pst(ctx, nc, "ps1", [128, 512])
    ps2 = pst(ctx, nc, "ps2", [128, 512])
    ps3 = pst(ctx, nc, "ps3", [128, 512])
    pr0 = pst(ctx, nc, "pr0", [128, 512])
    pr1 = pst(ctx, nc, "pr1", [128, 512])

    E = {"ones_bf": onesbf, "sqt": sqt, "rows": rows, "lnb": lnb,
         "arep": arep, "brep": brep,
         "ps0": ps0, "ps1": ps1, "pr0": pr0, "pr1": pr1}
    E_PS = {"ps0": ps0, "ps1": ps1, "ps2": ps2, "ps3": ps3}

    S.op("sp", lambda e: e.dma_start(onesbf[:, :], onesd[:, :]), writes=["ones"], dma=True)
    S.op("sp", lambda e: e.dma_start(onesf32[:, :], onesf[:, :]), writes=["onesf"], dma=True)
    for dc in range(NCH):
        S.op("pool", (lambda e, dc=dc: e.dma_start(
            xb[dc][:, :], x1d[dc * 128:(dc + 1) * 128, :])),
            writes=[f"xb{dc}"], dma=True)

    abscol = D["abscol"]
    S.op("pe", lambda e: e.matmul(pr0[0:1, 0:3], onesf32[:, 0:1], abscol[:, 0:3],
                                  start=True, stop=True),
         reads=["abscol", "onesf"], writes=["pr0"], tag="absred")
    CNT = {0: HID * C, 1: HID * C, 2: HID * C}
    S.op("dve", lambda e: e.memset(srow[:, :], 0.0), writes=["srow_s"])
    S.op("dve", lambda e: e.tensor_copy(srow[0:1, 0:3], pr0[0:1, 0:3]),
         reads=["pr0", "srow_s"], writes=["srow_s"])
    for gi in range(3):
        S.op("dve", (lambda e, gi=gi: e.tensor_scalar_mul(
            srow[0:1, 4 + gi * 4:5 + gi * 4], srow[0:1, gi:gi + 1],
            0.5 / CNT[gi])), reads=["srow_s"], writes=["srow_s"])
        S.op("dve", (lambda e, gi=gi: e.tensor_scalar_mul(
            srow[0:1, 5 + gi * 4:6 + gi * 4], srow[0:1, gi:gi + 1],
            -0.5 / CNT[gi])), reads=["srow_s"], writes=["srow_s"])
        S.op("dve", (lambda e, gi=gi: e.tensor_scalar_mul(
            srow[0:1, 6 + gi * 4:7 + gi * 4], srow[0:1, gi:gi + 1],
            1.0 / CNT[gi])), reads=["srow_s"], writes=["srow_s"])
        S.op("dve", (lambda e, gi=gi: e.tensor_scalar_mul(
            srow[0:1, 7 + gi * 4:8 + gi * 4], srow[0:1, gi:gi + 1],
            -1.0 / CNT[gi])), reads=["srow_s"], writes=["srow_s"])
    S.op("sp", (lambda e: e.dma_start(
        bass.AP(sder, 0, [[16, 1], [1, 16]]), srow[0:1, 0:16])),
        reads=["srow_s"], writes=["sderd"], dma=True)
    S.op("sp", (lambda e: e.dma_start(
        scols[:, :], bass.AP(sder, 0, [[0, 128], [1, 16]]))),
        reads=["sderd"], writes=["scols"], dma=True)

    xkeys = [f"xb{i}" for i in range(NCH)]
    emit_layernorm(S, E, xkeys, xb, 0)

    def quant_slab(f32buf, f32key, bfbuf, bfkey, gi, width, off=0):
        for h0 in range(0, width, 2048):
            h1 = min(h0 + 2048, width)
            wdt_ = h1 - h0
            S.op("act", (lambda e, h0=h0, wdt_=wdt_: e.activation(
                sgt[:, 0:wdt_], f32buf[:, h0:h0 + wdt_], AF.Sign,
                bias=scols[:, 5 + gi * 4:6 + gi * 4])),
                reads=[f32key, "scols"], writes=["sgt"], tag="qs1")
            S.op("act", (lambda e, h0=h0, wdt_=wdt_: e.activation(
                qtmp[:, 0:wdt_], f32buf[:, h0:h0 + wdt_], AF.Sign,
                bias=scols[:, 4 + gi * 4:5 + gi * 4])),
                reads=[f32key, "scols"], writes=["qtmp"], tag="qs2")
            S.op("dve", (lambda e, wdt_=wdt_: e.tensor_tensor(
                qtmp[:, 0:wdt_], qtmp[:, 0:wdt_], sgt[:, 0:wdt_], ALU.add)),
                reads=["qtmp", "sgt"], writes=["qtmp"], tag="qadd")
            S.op("dve", (lambda e, h0=h0, wdt_=wdt_: e.tensor_scalar_mul(
                bfbuf[:, off + h0:off + h0 + wdt_], qtmp[:, 0:wdt_],
                scols[:, 4 + gi * 4:5 + gi * 4])),
                reads=["qtmp", "scols"], writes=[bfkey], tag="qmul")

    for f in range(NHT):
        for wi, (src, gi) in enumerate([(w1s, 0), (w3s, 1)]):
            wk = f"wf{wi}"
            qk = f"qsl{wi}"
            S.op("sp", (lambda e, src=src, f=f, buf=wf32[wi]: e.dma_start(
                buf[:, :], bass.AP(src, f * C * 128,
                                   [[128, 128], [128 * 128, NCH], [1, 128]]))),
                writes=[wk], dma=True)
            quant_slab(wf32[wi], wk, qsl[wi], qk, gi, C)
            for tc in range(NT):
                pk = ["ps0", "ps1"][tc] if wi == 0 else ["ps2", "ps3"][tc]
                pt = E_PS[pk]
                for dc in range(NCH):
                    w = [pk] if (dc == 0 or dc == NCH - 1) else []
                    S.op("pe", (lambda e, tc=tc, dc=dc, pt=pt, wi=wi: e.matmul(
                        pt[:, :], qsl[wi][:, dc * 128:(dc + 1) * 128],
                        xb[dc][:, tc * 512:(tc + 1) * 512],
                        start=(dc == 0), stop=(dc == NCH - 1))),
                        reads=[qk, f"xb{dc}"], writes=w, tag=f"h{wi}")
            if wi == 0:
                for tc in range(NT):
                    pk = ["ps0", "ps1"][tc]
                    S.op("act", (lambda e, tc=tc, pt=E_PS[pk]: e.activation(
                        h1t[:, tc * 512:(tc + 1) * 512], pt[:, :], AF.Silu)),
                        reads=[pk], writes=["h1t"], tag="silu")
        for tc in range(NT):
            pk = ["ps2", "ps3"][tc]
            S.op("dve", (lambda e, tc=tc, f=f, pt=E_PS[pk]: e.tensor_tensor(
                gt[f][:, tc * 512:(tc + 1) * 512],
                h1t[:, tc * 512:(tc + 1) * 512], pt[:, :], ALU.mult)),
                reads=["h1t", pk], writes=[f"gt{f}"], tag="gmul")

    W2Q = ((0, 11), (11, 11), (22, 11), (33, 10))
    for o in range(NCH):
        for hh, (c0, ncc) in enumerate(W2Q):
            wk = f"w2f{hh % 2}"
            qk = f"q2h{hh}"
            S.op("sp", (lambda e, o=o, buf=w2f32[hh % 2], c0=c0, ncc=ncc: e.dma_start(
                buf[:, 0:ncc * 128],
                bass.AP(w2s, o * HIDP * 128 + c0 * 128 * 128,
                        [[128, 128], [128 * 128, ncc], [1, 128]]))),
                writes=[wk], dma=True)
            quant_slab(w2f32[hh % 2], wk, q2sl[0], qk, 2, ncc * 128, off=c0 * 128)
        S.op("sp", (lambda e, o=o: e.dma_start(
            x2b[0][:, :], x1d[o * 128:(o + 1) * 128, :])),
            writes=["x2b0"], dma=True)
        for tc in range(NT):
            pk = ["ps0", "ps1"][tc]
            pt = E_PS[pk]
            for hc in range(NHT):
                w = [pk] if (hc == 0 or hc == NHT - 1) else []
                qkh = f"q2h{min(hc // 11, 3)}"
                S.op("pe", (lambda e, o=o, tc=tc, hc=hc, pt=pt: e.matmul(
                    pt[:, :], q2sl[0][:, hc * 128:(hc + 1) * 128],
                    gt[hc][:, tc * 512:(tc + 1) * 512],
                    start=(hc == 0), stop=(hc == NHT - 1))),
                    reads=[qkh, f"gt{hc}"], writes=w, tag=f"w2_{o}")
            S.op("dve", (lambda e, o=o, tc=tc, pt=pt: e.tensor_tensor(
                x2b[0][:, tc * 512:(tc + 1) * 512], pt[:, :],
                x2b[0][:, tc * 512:(tc + 1) * 512], ALU.add)),
                reads=[pk, "x2b0"], writes=["x2b0"], tag="res")
        S.op("sp", (lambda e, o=o: e.dma_start(
            x2[o * 128:(o + 1) * 128, :], x2b[0][:, :])),
            reads=["x2b0"], writes=[f"x2st{o % 2}"], dma=True)

    S.op("sp", lambda e: e.nop(), reads=["x2st0", "x2st1"])


def emit_phase_c(nc, S, ctx, D):
    drfg = D["drfg"]
    bcg = D["bcg"]
    e2xg = D["e2xg"]
    e3b = D["e3b"]
    x2d = D["x2i"]
    wxT = D["wxT"]
    wdtT = D["wdtT"]
    alogh = D["alogh"]
    bdth = D["bdth"]
    dfull = D["dfull"]
    onesd = D["onesd"]
    hoffd = D["hoffd"]
    x3 = D["x3i"]
    e2in = D["e2in"]
    e2out = D["e2out"]
    e3in = D["e3in"]
    e3out = D["e3out"]
    lnb = D["lnb"]

    xb = [sbt(ctx, nc, f"xb{i}", [128, TL], BF16) for i in range(NCH)]
    dlt = [sbt(ctx, nc, f"dlt{i}", [128, T], BF16) for i in range(8)]
    wxt = sbt(ctx, nc, "wxt", [128, NCH * 160], BF16)
    wdt = sbt(ctx, nc, "wdt", [128, 1024], BF16)
    at = sbt(ctx, nc, "at", [128, 128], F32)
    bcol = sbt(ctx, nc, "bcol", [128, 8], F32)
    dcol = sbt(ctx, nc, "dcol", [128, 16], F32)
    hoff = sbt(ctx, nc, "hoff", [1, 4], mybir.dt.int32)
    drf = sbt(ctx, nc, "drf", [128, T], BF16)
    bcexp = sbt(ctx, nc, "bcexp", [32, TL], BF16)
    dAs2 = [sbt(ctx, nc, f"dAs{i}", [128, NSSM * TCH], BF16) for i in range(2)]
    sqt = dAs2[0]
    zt2 = dAs2[0]
    bus = sbt(ctx, nc, "bus", [128, NSSM * TCH], BF16)
    brep = sbt(ctx, nc, "brep_c", [128, NSSM * TCH], BF16)
    crep = sbt(ctx, nc, "crep_c", [128, NSSM * TCH], BF16)
    xmall = sbt(ctx, nc, "xmall", [128, 8 * TCH], BF16)
    dxt2 = [sbt(ctx, nc, f"dxt{i}", [128, TCH], BF16) for i in range(2)]
    stateb = [sbt(ctx, nc, f"state{i}", [128, NSSM], BF16) for i in range(8)]
    c0t = sbt(ctx, nc, "c0t", [128, NSSM], BF16)
    ych2 = [sbt(ctx, nc, f"ych{i}", [128, TCH], BF16) for i in range(2)]
    yba = sbt(ctx, nc, "yba", [128, 4 * TL], BF16)
    zt = sbt(ctx, nc, "zt", [128, 512], F32)
    onesbf = sbt(ctx, nc, "onesbf", [128, 1], BF16)
    arep = sbt(ctx, nc, "arep", [128, TL], F32)
    brepl = sbt(ctx, nc, "brepl", [128, TL], F32)
    x3b = [sbt(ctx, nc, f"x3b{i}", [128, 512], F32) for i in range(2)]
    rows = {k: sbt(ctx, nc, k, [1, TL], F32) for k in ("srow", "qrow", "brow")}

    ps0 = pst(ctx, nc, "ps0", [128, 512])
    ps1 = pst(ctx, nc, "ps1", [128, 512])
    ps2 = pst(ctx, nc, "ps2", [128, 512])
    ps3 = pst(ctx, nc, "ps3", [128, 512])
    pr0 = pst(ctx, nc, "pr0", [128, 512])
    pr1 = pst(ctx, nc, "pr1", [128, 512])
    E = {"ones_bf": onesbf, "sqt": sqt, "sqt_key": "dAs0", "rows": rows,
         "lnb": lnb, "arep": arep, "brep": brepl,
         "ps0": ps0, "ps1": ps1, "pr0": pr0, "pr1": pr1}
    PSL = [ps0, ps1, ps2, ps3]

    _rctx = D.get("_octx", ctx)
    rgA = _rctx.enter_context(nc.gpsimd.register(_uniq("rgA")))
    rgB = _rctx.enter_context(nc.gpsimd.register(_uniq("rgB")))
    rgT = _rctx.enter_context(nc.gpsimd.register(_uniq("rgT")))


    S.op("sp", lambda e: e.dma_start(onesbf[:, :], onesd[:, :]), writes=["ones"], dma=True)
    S.op("sp", lambda e: e.dma_start(hoff[:, :], hoffd[:, :]), writes=["hoff"], dma=True)
    S.op("pool", lambda e: e.dma_start(
        wxt[:, :], bass.AP(wxT, 0, [[160, 128], [128 * 160, NCH], [1, 160]])),
        writes=["wxt"], dma=True)
    S.op("pool", lambda e: e.dma_start(wdt[:, :], wdtT[:, :]), writes=["wdt"], dma=True)
    S.op("sp", lambda e: e.dma_start(at[:, :], alogh[:, :]), writes=["at"], dma=True)
    S.op("sp", lambda e: e.dma_start(bcol[:, :], bdth[:, :]), writes=["bcol"], dma=True)
    S.op("sp", lambda e: e.dma_start(dcol[:, :], dfull[:, :]), writes=["dcol"], dma=True)
    for dc in range(NCH):
        S.op("pool", (lambda e, dc=dc: e.dma_start(
            xb[dc][:, :], x2d[dc * 128:(dc + 1) * 128, :])),
            writes=[f"xb{dc}"], dma=True)
    S.op("act", lambda e: e.activation(at[:, :], at[:, :], AF.Exp),
         reads=["at"], writes=["at"])
    S.op("dve", lambda e: e.tensor_scalar_mul(at[:, :], at[:, :], -1.0),
         reads=["at"], writes=["at"])
    S.op("pool", lambda e: (e.reg_load(rgA, hoff[0:1, 0:1]),
                            e.reg_load(rgB, hoff[0:1, 1:2]),
                            e.nop())[-1],
         reads=["hoff"], writes=["regs"], tag="regload")

    xkeys = [f"xb{i}" for i in range(NCH)]
    emit_layernorm(S, E, xkeys, xb, 0)

    for tc in range(NT):
        for grp, (m0, mw, pk_) in enumerate([(0, 128, ("ps0", "ps1")),
                                             (128, 32, ("ps2", "ps3"))]):
            pk = pk_[tc]
            pt = E[pk] if pk in E else PSL[int(pk[2])]
            for dc in range(NCH):
                w = [pk] if (dc == 0 or dc == NCH - 1) else []
                S.op("pe", (lambda e, tc=tc, dc=dc, m0=m0, mw=mw, pt=pt: e.matmul(
                    pt[0:mw, :], wxt[:, dc * 160 + m0:dc * 160 + m0 + mw],
                    xb[dc][:, tc * 512:(tc + 1) * 512],
                    start=(dc == 0), stop=(dc == NCH - 1))),
                    reads=["wxt", f"xb{dc}"], writes=w, tag=f"xz{grp}")
        S.op("dve", (lambda e, tc=tc, pt=PSL[tc]: e.tensor_copy(
            drf[:, tc * 512:(tc + 1) * 512], pt[:, :])),
            reads=[("ps0", "ps1")[tc]], writes=["drf"], tag="drev")
        S.op("dve", (lambda e, tc=tc, pt=PSL[2 + tc]: e.tensor_copy(
            bcexp[0:32, tc * 512:(tc + 1) * 512], pt[0:32, :])),
            reads=[("ps2", "ps3")[tc]], writes=["bcexp"], tag="bcev")

    S.op("sp", lambda e: e.dma_start(e2in[0:128, :], drf[:, 0:TL]),
         reads=["drf"], writes=["e2_dr"], dma=True)
    S.op("sp", lambda e: e.dma_start(e2in[128:160, :], bcexp[0:32, :]),
         reads=["bcexp"], writes=["e2_bc"], dma=True)
    for dc in range(NCH):
        S.op("sp", (lambda e, dc=dc: e.dma_start(
            e2in[160 + dc * 128:160 + (dc + 1) * 128, :], xb[dc][:, :])),
            reads=[f"xb{dc}"], writes=[f"e2_x{dc}"], dma=True)
    S.op("pool", (lambda e: e.collective_compute(
        "AllGather", ALU.bypass, replica_groups=PAIRS,
        ins=[e2in.ap().opt()], outs=[e2out.ap().opt()])),
        reads=["e2_dr", "e2_bc"] + [f"e2_x{dc}" for dc in range(NCH)],
        writes=["e2out"], tag="agE2")

    for r_ in range(2):
        S.op("sp", (lambda e, r_=r_: e.dma_start(
            bass.AP(drfg, (1 - r_) * 128, [[T, 128], [256, 8], [1, 128]]),
            bass.AP(e2out, r_ * E2ROWS * TL + 7 * 128,
                    [[TL, 128], [-128, 8], [1, 128]]))),
            reads=["e2out"], writes=["drfg"], dma=True)
        S.op("sp", (lambda e, r_=r_: e.dma_start(
            bass.AP(bcg, (1 - r_) * 128, [[T, 32], [256, 8], [1, 128]]),
            bass.AP(e2out, r_ * E2ROWS * TL + 128 * TL + 7 * 128,
                    [[TL, 32], [-128, 8], [1, 128]]))),
            reads=["e2out"], writes=["bcg"], dma=True)
    S.op("sp", lambda e: e.dma_start(drf[:, :], drfg[:, :]),
         reads=["drfg"], writes=["drf"], dma=True)
    for ch4 in range(4):
        for r_ in range(2):
            for rc in range(16):
                S.op("sp" if rc % 2 == 0 else "pool",
                     (lambda e, r_=r_, rc=rc, ch4=ch4: e.dma_start(
                         bass.AP(e2xg, rc * 128 * T + (1 - r_) * 128 + ch4 * 512,
                                 [[T, 128], [256, 2], [1, 128]]),
                         bass.AP(e2out, (r_ * E2ROWS + 160 + rc * 128) * TL + (7 - 2 * ch4) * 128,
                                 [[TL, 128], [-128, 2], [1, 128]]))),
                     reads=["e2out"], writes=[f"e2xg{ch4}"], dma=True, tag="e2xg")
    for sl in range(8):
        for ch in range(NCHK):
            pk = ["ps0", "ps1"][ch % 2]
            pt = PSL[ch % 2]
            S.op("pe", (lambda e, sl=sl, ch=ch, pt=pt: e.matmul(
                pt[:, :], wdt[:, sl * 128:(sl + 1) * 128],
                drf[:, ch * 512:(ch + 1) * 512], start=True, stop=True)),
                reads=["wdt", "drf"], writes=[pk], tag="dmm")
            S.op("dve", (lambda e, sl=sl, pt=pt: e.tensor_scalar_add(
                zt[:, :], pt[:, :], bcol[:, sl:sl + 1])),
                reads=[pk, "bcol"], writes=["zt"], tag="spz")
            S.op("act", lambda e: e.activation(zt2[:, 0:512], zt[:, :], AF.Abs),
                 reads=["zt"], writes=["dAs0"])
            S.op("act", lambda e: e.activation(zt2[:, 0:512], zt2[:, 0:512], AF.Exp, scale=-1.0),
                 reads=["dAs0"], writes=["dAs0"])
            S.op("act", lambda e: e.activation(zt2[:, 0:512], zt2[:, 0:512], AF.Ln, bias=1.0),
                 reads=["dAs0"], writes=["dAs0"])
            S.op("dve", lambda e: e.tensor_scalar_max(zt[:, :], zt[:, :], 0.0),
                 reads=["zt"], writes=["zt"])
            S.op("dve", (lambda e, sl=sl, ch=ch: e.tensor_tensor(
                dlt[sl][:, ch * 512:(ch + 1) * 512], zt[:, :], zt2[:, 0:512], ALU.add)),
                reads=["zt", "dAs0"], writes=[f"dlt{sl}"], tag="spadd")

    for dt in range(8):
        S.op("dve", (lambda e, dt=dt: e.memset(stateb[dt][:, :], 0.0)),
             writes=[f"state{dt}"], tag="st0")
    for ch in range(NCHK):
        S.op("sp", (lambda e, ch=ch: e.dma_start(
            brep[:, :], bass.AP(bcg, ch * TCH,
                                [[0, 128], [T, NSSM], [1, TCH]]))),
            reads=["bcg"], writes=["brep_c"], dma=True)
        S.op("sp", (lambda e, ch=ch: e.dma_start(
            crep[:, :], bass.AP(bcg, 16 * T + ch * TCH,
                                [[0, 128], [T, NSSM], [1, TCH]]))),
            reads=["bcg"], writes=["crep_c"], dma=True)
        base = ch * TCH
        S.op("pool", (lambda e, base=base: (
            e.reg_mov(rgT, base),
            e.reg_add(rgT, rgT, rgA),
            e.dma_start(xmall[:, :],
                        bass.AP(e2xg, rgT,
                                [[T, 128], [128 * T, 8], [1, TCH]])))[-1]),
            reads=[f"e2xg{ch}", "regs"], writes=["xmall"], dma=True, tag="xm")
        for dt in range(8):
            par = (ch * 8 + dt) % 2
            dAs, dak = dAs2[par], f"dAs{par}"
            dxt, dxk = dxt2[par], f"dxt{par}"
            ych, yck = ych2[par], f"ych{par}"
            S.op("pool", (lambda e, dt=dt, ch=ch, dxt=dxt: e.tensor_tensor(
                dxt[:, :], dlt[dt][:, ch * 512:(ch + 1) * 512],
                xmall[:, dt * TCH:(dt + 1) * TCH], ALU.mult)),
                reads=[f"dlt{dt}", "xmall"], writes=[dxk], tag="dx")
            for n in range(NSSM):
                S.op("act", (lambda e, dt=dt, ch=ch, n=n, dAs=dAs: e.activation(
                    dAs[:, n * TCH:(n + 1) * TCH],
                    dlt[dt][:, ch * 512:(ch + 1) * 512],
                    AF.Exp, scale=at[:, dt * 16 + n:dt * 16 + n + 1])),
                    reads=[f"dlt{dt}", "at"], writes=[dak], tag="dA")
            S.op("dve", (lambda e, dxt=dxt: e.tensor_tensor(
                bus[:, :], bass.AP(dxt, 0, [[TCH, 128], [0, NSSM], [1, TCH]]),
                brep[:, :], ALU.mult)),
                reads=[dxk, "brep_c"], writes=["bus"], tag="bu")
            S.op("dve", (lambda e, dt=dt, dAs=dAs: e.tensor_tensor(
                c0t[:, :], bass.AP(dAs, 0, [[NSSM * TCH, 128], [TCH, NSSM]]),
                stateb[dt][:, :], ALU.mult)),
                reads=[dak, f"state{dt}"], writes=["c0t"], tag="fold")
            S.op("dve", (lambda e: e.tensor_tensor(
                bass.AP(bus, 0, [[NSSM * TCH, 128], [TCH, NSSM]]),
                bass.AP(bus, 0, [[NSSM * TCH, 128], [TCH, NSSM]]),
                c0t[:, :], ALU.add)),
                reads=["bus", "c0t"], writes=["bus"], tag="fold")
            S.op("dve", (lambda e, dAs=dAs: e.memset(
                bass.AP(dAs, 0, [[NSSM * TCH, 128], [TCH, NSSM]]), 0.0)),
                reads=[dak], writes=[dak], tag="fold")
            S.op("dve", (lambda e, dAs=dAs: e.tensor_tensor_scan(
                bus[:, :], dAs[:, :], bus[:, :], 0.0, ALU.mult, ALU.add)),
                reads=[dak, "bus"], writes=["bus"], tag="scan")
            S.op("dve", (lambda e, dt=dt: e.tensor_copy(
                stateb[dt][:, :], bass.AP(bus, TCH - 1, [[NSSM * TCH, 128], [TCH, NSSM]]))),
                reads=["bus"], writes=[f"state{dt}"], tag="tail")
            S.op("dve", lambda e: e.tensor_tensor(bus[:, :], bus[:, :], crep[:, :], ALU.mult),
                 reads=["bus", "crep_c"], writes=["bus"], tag="hc")
            for lvl in range(3):
                w_ = NSSM * TCH >> (lvl + 1)
                S.op("dve", (lambda e, w_=w_: e.tensor_tensor(
                    bus[:, 0:w_], bus[:, 0:w_], bus[:, w_:2 * w_], ALU.add)),
                    reads=["bus"], writes=["bus"], tag="yt")
            S.op("dve", (lambda e, ych=ych: e.tensor_tensor(
                ych[:, :], bus[:, 0:TCH], bus[:, TCH:2 * TCH], ALU.add)),
                reads=["bus"], writes=[yck], tag="yt4")
            S.op("sp", (lambda e, dt=dt, ch=ch, ych=ych: e.dma_start(
                e3in[dt * 128:(dt + 1) * 128, ch * TCH:(ch + 1) * TCH], ych[:, :])),
                reads=[yck], writes=[f"e3w{dt % 2}"], dma=True)

    S.op("pool", (lambda e: e.collective_compute(
        "AllGather", ALU.bypass, replica_groups=PAIRS,
        ins=[e3in.ap().opt()], outs=[e3out.ap().opt()])),
        reads=["e3w0", "e3w1", "e2out"], writes=["e3out"], tag="agE3")

    for rc in range(16):
        for p_ in range(2):
            S.op("sp" if rc % 2 == 0 else "pool",
                 (lambda e, p_=p_, rc=rc: e.dma_start(
                     bass.AP(e3b, rc * 128 * T + p_ * TL,
                             [[T, 128], [1, TL]]),
                     bass.AP(e3out, rc * 128 * T + (15 - p_) * 128,
                             [[T, 128], [-256, 8], [1, 128]]))),
                 reads=["e3out"], writes=[f"e3b{rc // 4}"], dma=True, tag="e3b")
    for g4 in range(4):
        base = g4 * 4 * 128 * T
        S.op("pool", (lambda e, base=base: (
            e.reg_mov(rgT, base),
            e.reg_add(rgT, rgT, rgB),
            e.dma_start(yba[:, :],
                        bass.AP(e3b, rgT,
                                [[T, 128], [128 * T, 4], [1, TL]])))[-1]),
            reads=[f"e3b{g4}", "regs"], writes=["yba"], dma=True, tag="yimp")
        for oo in range(4):
            o = g4 * 4 + oo
            for hh in range(2):
                cs0, cs1 = hh * 512, (hh + 1) * 512
                S.op("sp", (lambda e, o=o, cs0=cs0, cs1=cs1, hh=hh: e.dma_start(
                    x3b[hh][:, :], x2d[o * 128:(o + 1) * 128, cs0:cs1])),
                    writes=[f"x3b{hh}"], dma=True)
                S.op("dve", (lambda e, oo=oo, cs0=cs0, hh=hh: e.tensor_tensor(
                    x3b[hh][:, :], x3b[hh][:, :],
                    yba[:, oo * TL + cs0:oo * TL + cs0 + 512], ALU.add)),
                    reads=[f"x3b{hh}", "yba"], writes=[f"x3b{hh}"], tag="resy")
                S.op("dve", (lambda e, o=o, cs0=cs0, cs1=cs1, hh=hh: e.scalar_tensor_tensor(
                    x3b[hh][:, :], xb[o][:, cs0:cs1], dcol[:, o:o + 1],
                    x3b[hh][:, :], ALU.mult, ALU.add)),
                    reads=[f"xb{o}", "dcol", f"x3b{hh}"], writes=[f"x3b{hh}"], tag="resd")
                S.op("sp", (lambda e, o=o, cs0=cs0, cs1=cs1, hh=hh: e.dma_start(
                    x3[o * 128:(o + 1) * 128, cs0:cs1], x3b[hh][:, :])),
                    reads=[f"x3b{hh}"], writes=[f"x3st{hh}"], dma=True)

    S.op("sp", lambda e: e.nop(), reads=["x3st0", "x3st1"])


def emit_phase_d(nc, S, ctx, D):
    x3d = D["x3i"]
    kvdw = D["kvdw"]
    qdw = D["qdw"]
    kuw = D["kuw"]
    quw = D["quw"]
    vuw = D["vuw"]
    rkw = D["rkw"]
    rqw = D["rqw"]
    oww = D["oww"]
    cos1 = D["cos1"]
    sin1 = D["sin1"]
    l1wd = D["l1wd"]
    mask0 = D["mask0"]
    onesd = D["onesd"]
    identd = D["identd"]
    hd = D["hi"]
    k_in = D["k_in"]
    k_out = D["k_out"]
    v_in = D["v_in"]
    v_out = D["v_out"]
    lnb = D["lnb"]

    xb = [sbt(ctx, nc, f"xb{i}", [128, TL], BF16) for i in range(NCH)]
    qasm = [sbt(ctx, nc, f"qasm{i}", [128, TL], BF16) for i in range(NH)]
    ot = [sbt(ctx, nc, f"ot{i}", [128, TL], BF16) for i in range(NH)]
    kvd = [sbt(ctx, nc, f"kvd{i}", [128, TL], BF16) for i in range(NLAT)]
    qd = [sbt(ctx, nc, f"qd{i}", [128, TL], BF16) for i in range(NLAT)]
    kasm = [sbt(ctx, nc, f"kasm{i}", [128, TL], BF16) for i in range(2)]
    rsc = [sbt(ctx, nc, f"rsc{i}", [128, TL], BF16) for i in range(2)]
    shuf = sbt(ctx, nc, "shuf", [128, TL], BF16)
    sqt = shuf
    wslab = [sbt(ctx, nc, f"wslab{i}", [128, C], BF16) for i in range(2)]
    cosT = sbt(ctx, nc, "cosT", [128, TL], BF16)
    sinT = sbt(ctx, nc, "sinT", [128, TL], BF16)
    maskT = sbt(ctx, nc, "maskT", [128, MASK_COLS], BF16)
    ones_bf = sbt(ctx, nc, "ones_bf", [128, 1], BF16)
    l1w = sbt(ctx, nc, "l1w", [128, 16], F32)
    khead = [sbt(ctx, nc, f"khead{i}", [128, T], BF16) for i in range(2)]
    vhead = [sbt(ctx, nc, f"vhead{i}", [128, T], BF16) for i in range(2)]
    ptile2 = [sbt(ctx, nc, f"ptile{i}", [128, TL], BF16) for i in range(2)]
    rrep = sbt(ctx, nc, "rrep", [128, TL], F32)
    arep = sbt(ctx, nc, "arep", [128, TL], F32)
    brep = sbt(ctx, nc, "brep", [128, TL], F32)
    rmsrep = rrep
    ident = sbt(ctx, nc, "ident", [128, 128], BF16)
    hb = [sbt(ctx, nc, f"hb{i}", [128, TL], F32) for i in range(2)]
    rows = {k: sbt(ctx, nc, k, [1, TL], F32) for k in ("srow", "qrow", "brow")}

    ps0 = pst(ctx, nc, "ps0", [128, 512])
    ps1 = pst(ctx, nc, "ps1", [128, 512])
    ps2 = pst(ctx, nc, "ps2", [128, 512])
    ps3 = pst(ctx, nc, "ps3", [128, 512])
    pr0 = pst(ctx, nc, "pr0", [128, 512])
    pr1 = pst(ctx, nc, "pr1", [128, 512])
    PS = {0: ps0, 1: ps1}
    E_PS = {"ps0": ps0, "ps1": ps1, "ps2": ps2, "ps3": ps3}
    E = {"ones_bf": ones_bf, "sqt": sqt, "sqt_key": "shuf", "rows": rows,
         "lnb": lnb, "arep": arep, "brep": brep,
         "ps0": ps0, "ps1": ps1, "pr0": pr0, "pr1": pr1}

    for dc in range(NCH):
        S.op("pool", (lambda e, dc=dc: e.dma_start(
            xb[dc][:, :], x3d[dc * 128:(dc + 1) * 128, :])),
            writes=[f"xb{dc}"], dma=True)
    S.op("sp", lambda e: e.dma_start(cosT[:, :], cos1[:, :]), writes=["cosT"], dma=True)
    S.op("sp", lambda e: e.dma_start(sinT[:, :], sin1[:, :]), writes=["sinT"], dma=True)
    S.op("sp", lambda e: e.dma_start(maskT[:, :], mask0[:, :]), writes=["maskT"], dma=True)
    S.op("sp", lambda e: e.dma_start(ones_bf[:, :], onesd[:, :]), writes=["ones"], dma=True)
    S.op("sp", lambda e: e.dma_start(l1w[:, :], l1wd[:, :]), writes=["l1w"], dma=True)
    S.op("sp", lambda e: e.dma_start(ident[:, :], identd[:, :]), writes=["ident"], dma=True)

    xkeys = [f"xb{i}" for i in range(NCH)]
    emit_layernorm(S, E, xkeys, xb, 0)
    emit_layernorm(S, E, xkeys, xb, 2, rms_only=True, apply=False)
    S.op("sp", (lambda e: e.dma_start(
        rmsrep[:, :], bass.AP(lnb, 2 * TL, [[0, 128], [1, TL]]))),
        reads=["lnb2"], writes=["rrep"], dma=True)
    for dc in range(NCH):
        S.op("dve", (lambda e, dc=dc: e.scalar_tensor_tensor(
            xb[dc][:, :], xb[dc][:, :], l1w[:, dc:dc + 1], rmsrep[:, :],
            ALU.mult, ALU.mult)),
            reads=[f"xb{dc}", "l1w", "rrep"], writes=[f"xb{dc}"], tag="rmsap")

    def slab_load(dram, f, kchunks, wk):
        S.op("pool", (lambda e, f=f, kchunks=kchunks: e.dma_start(
            wslab[f % 2][:, 0:kchunks * 128],
            bass.AP(dram, f * kchunks * 128 * 128,
                    [[128, 128], [128 * 128, kchunks], [1, 128]]))),
            writes=[wk], dma=True)

    def mm_ft(src_tiles, src_keys, kchunks, f, pk, extra=None):
        wk = f"wslab{f % 2}"
        for tc in range(NT):
            pkt = ["ps2", "ps3"][tc] if pk == "auto" else pk
            ptt = E_PS[pkt]
            for dc in range(kchunks):
                w = [pkt] if (dc == 0 or dc == kchunks - 1) else []
                S.op("pe", (lambda e, tc=tc, dc=dc, ptt=ptt, f=f: e.matmul(
                    ptt[:, :], wslab[f % 2][:, dc * 128:(dc + 1) * 128],
                    src_tiles[dc][:, tc * 512:(tc + 1) * 512],
                    start=(dc == 0), stop=(dc == kchunks - 1))),
                    reads=[wk, src_keys[dc]], writes=w, tag="mmft")
            if extra:
                extra(tc, pkt, E_PS[pkt])

    kvdk = [f"kvd{i}" for i in range(NLAT)]
    qdk = [f"qd{i}" for i in range(NLAT)]

    for f in range(NLAT):
        slab_load(kvdw, f, NCH, f"wslab{f % 2}")
        def ev(tc, pk, pt, f=f):
            S.op("dve", (lambda e, tc=tc, pt=pt, f=f: e.tensor_copy(
                kvd[f][:, tc * 512:(tc + 1) * 512], pt[:, :])),
                reads=[pk], writes=[kvdk[f]], tag="kvdev")
        mm_ft(xb, xkeys, NCH, f, "auto", ev)
    for f in range(NLAT):
        slab_load(qdw, f, NCH, f"wslab{f % 2}")
        def ev(tc, pk, pt, f=f):
            S.op("dve", (lambda e, tc=tc, pt=pt, f=f: e.tensor_copy(
                qd[f][:, tc * 512:(tc + 1) * 512], pt[:, :])),
                reads=[pk], writes=[qdk[f]], tag="qdev")
        mm_ft(qd if False else xb, xkeys, NCH, f, "auto", ev)

    mask32 = [i ^ 1 for i in range(32)]

    def rope_tile(tile, key):
        S.op("dve", (lambda e, tile=tile: e.stream_shuffle(shuf[:, :], tile[:, :], mask32)),
             reads=[key], writes=["shuf"])
        S.op("dve", (lambda e, tile=tile: e.tensor_tensor(tile[:, :], tile[:, :], cosT[:, :], ALU.mult)),
             reads=[key, "cosT"], writes=[key])
        S.op("dve", (lambda e: e.tensor_tensor(shuf[:, :], shuf[:, :], sinT[:, :], ALU.mult)),
             reads=["shuf", "sinT"], writes=["shuf"])
        S.op("dve", (lambda e, tile=tile: e.tensor_tensor(tile[:, :], tile[:, :], shuf[:, :], ALU.add)),
             reads=[key, "shuf"], writes=[key])

    for f in range(8):
        slab_load(quw, f, NLAT, f"wslab{f % 2}")
        def evqp(tc, pk, pt, f=f):
            for half in range(2):
                h = 2 * f + half
                S.op("dve", (lambda e, tc=tc, pt=pt, half=half, h=h: e.tensor_copy(
                    qasm[h][0:64, tc * 512:(tc + 1) * 512],
                    pt[64 * half:64 * half + 64, :])),
                    reads=[pk], writes=[f"qasm{h}"], tag="qpev")
        mm_ft(qd, qdk, NLAT, f, "auto", evqp)
        slab_load(rqw, f, NLAT, f"wslab{f % 2}")
        rk = f"rsc{f % 2}"
        def evqr(tc, pk, pt, f=f, rk=rk):
            S.op("dve", (lambda e, tc=tc, pt=pt, f=f: e.tensor_copy(
                rsc[f % 2][:, tc * 512:(tc + 1) * 512], pt[:, :])),
                reads=[pk], writes=[rk], tag="qrev")
        mm_ft(qd, qdk, NLAT, f, "auto", evqr)
        rope_tile(rsc[f % 2], rk)
        for half in range(2):
            h = 2 * f + half
            S.op("dve", (lambda e, half=half, h=h, f=f: e.tensor_copy(
                qasm[h][64:128, :], rsc[f % 2][64 * half:64 * half + 64, :])),
                reads=[rk], writes=[f"qasm{h}"], tag="qrasm")

    for f in range(8):
        ak0, ak1 = "kasm0x", "kasm1x"
        slab_load(kuw, f, NLAT, f"wslab{f % 2}")
        def evkp(tc, pk, pt, f=f):
            for half in range(2):
                S.op("dve", (lambda e, tc=tc, pt=pt, half=half: e.tensor_copy(
                    kasm[half][0:64, tc * 512:(tc + 1) * 512],
                    pt[64 * half:64 * half + 64, :])),
                    reads=[pk], writes=[("kasm0x", "kasm1x")[half]], tag="kpev")
        mm_ft(kvd, kvdk, NLAT, f, "auto", evkp)
        slab_load(rkw, f, NCH, f"wslab{f % 2}")
        rk = f"rsc{f % 2}"
        def evkr(tc, pk, pt, f=f, rk=rk):
            S.op("dve", (lambda e, tc=tc, pt=pt, f=f: e.tensor_copy(
                rsc[f % 2][:, tc * 512:(tc + 1) * 512], pt[:, :])),
                reads=[pk], writes=[rk], tag="krev")
        mm_ft(xb, xkeys, NCH, f, "auto", evkr)
        rope_tile(rsc[f % 2], rk)
        for half in range(2):
            S.op("dve", (lambda e, half=half, f=f: e.tensor_copy(
                kasm[half][64:128, :], rsc[f % 2][64 * half:64 * half + 64, :])),
                reads=[rk], writes=[("kasm0x", "kasm1x")[half]], tag="krasm")
        for half in range(2):
            h = 2 * f + half
            S.op("sp", (lambda e, h=h, half=half: e.dma_start(
                k_in[h * 128:(h + 1) * 128, :], kasm[half][:, :])),
                reads=[("kasm0x", "kasm1x")[half]], writes=[f"k_in{h}"], dma=True)

    S.op("pool", (lambda e: e.collective_compute(
        "AllGather", ALU.bypass, replica_groups=PAIRS,
        ins=[k_in.ap().opt()], outs=[k_out.ap().opt()])),
        reads=[f"k_in{h}" for h in range(NH)], writes=["k_out"], tag="agK")

    vexp = sbt(ctx, nc, "vexpT", [128, TL], BF16)
    psb = pst(ctx, nc, "psb", [128, 1024], BF16)
    for f in range(NH):
        wk = f"wslab{f % 2}"
        S.op("pool", (lambda e, f=f: e.dma_start(
            wslab[f % 2][:, 0:LAT],
            bass.AP(vuw, f * 128, [[C, 128], [128 * C, NLAT] if False else [128 * C, NLAT], [1, 128]])
            if False else
            bass.AP(vuw, f * 128, [[C, 128], [128 * C, NLAT], [1, 128]]))),
            writes=[wk], dma=True)
        for tc in range(NT):
            pk = ["ps2", "ps3"][tc]
            pt = E_PS[pk]
            for dc in range(NLAT):
                w = [pk] if (dc == 0 or dc == NLAT - 1) else []
                S.op("pe", (lambda e, tc=tc, dc=dc, pt=pt, f=f: e.matmul(
                    pt[:, :], wslab[f % 2][:, dc * 128:(dc + 1) * 128],
                    kvd[dc][:, tc * 512:(tc + 1) * 512],
                    start=(dc == 0), stop=(dc == NLAT - 1))),
                    reads=[wk, kvdk[dc]], writes=w, tag=f"vf{f}")
            S.op("dve", (lambda e, tc=tc, pt=pt: e.tensor_copy(
                rsc[0][:, tc * 512:(tc + 1) * 512], pt[:, :])),
                reads=[pk], writes=["rsc0"], tag="vevf")
        for tt in range(8):
            S.op("pe", (lambda e, tt=tt: e.transpose(
                psb[:, tt * 128:(tt + 1) * 128],
                rsc[0][:, tt * 128:(tt + 1) * 128], ident[:, :])),
                reads=["rsc0", "ident"], writes=["psb"], tag="vT")
        S.op("dve", lambda e: e.tensor_copy(vexp[:, :], psb[:, :]),
             reads=["psb"], writes=["vexpT"])
        S.op("sp", (lambda e, f=f: e.dma_start(
            bass.AP(v_in, f * 128, [[C, 128], [128 * C, 8], [1, 128]]),
            vexp[:, :])),
            reads=["vexpT"], writes=[f"v_inx{f}"], dma=True)

    vinkeys = [f"v_inx{f}" for f in range(NH)]
    S.op("pool", (lambda e: e.collective_compute(
        "AllGather", ALU.bypass, replica_groups=PAIRS,
        ins=[v_in.ap().opt()], outs=[v_out.ap().opt()])),
        reads=vinkeys, writes=["v_out"], tag="agV")

    AV_LAST = {0: 15, 1: 7}
    for h in range(NH):
        kb, kk = khead[h % 2], f"khead{h % 2}"
        vb, vk = vhead[h % 2], f"vhead{h % 2}"
        for r_ in range(2):
            S.op("sp", (lambda e, h=h, kb=kb, r_=r_: e.dma_start(
                bass.AP(kb, (1 - r_) * 128, [[T, 128], [256, 8], [1, 128]]),
                bass.AP(k_out, r_ * C * TL + h * 128 * TL + 7 * 128,
                        [[TL, 128], [-128, 8], [1, 128]]))),
                reads=["k_out", "v_out"], writes=[kk], dma=True)
            S.op("sp", (lambda e, h=h, vb=vb, r_=r_: e.dma_start(
                bass.AP(vb, (1 - r_) * 128, [[T, 128], [256, 8], [1, 128]]),
                bass.AP(v_out, r_ * TL * C + 7 * 128 * C + h * 128,
                        [[C, 128], [-128 * C, 8], [1, 128]]))),
                reads=["v_out", "k_out"], writes=[vk], dma=True)
        first = [True, True]
        for kt in range(NKT):
            wd = _wid(kt)
            msegs_kt = [(l_, off_) for (mk_, l_, off_) in MSEGS if mk_ == kt]
            chunks = []
            c0 = 0
            while c0 < wd:
                chunks.append((c0, min(c0 + 512, wd)))
                c0 = min(c0 + 512, wd)
            for ci, (c0, c1) in enumerate(chunks):
                spk = "ps2" if (kt + ci) % 2 == 0 else "ps3"
                sps = E_PS[spk]
                S.op("pe", (lambda e, kt=kt, c0=c0, c1=c1, sps=sps, kb=kb, h=h: e.matmul(
                    sps[:, 0:c1 - c0], kb[:, kt * 128:(kt + 1) * 128],
                    qasm[h][:, c0:c1], start=True, stop=True)),
                    reads=[kk, f"qasm{h}"], writes=[spk], tag=f"sc{h}")
                for (ml_, moff_) in msegs_kt:
                    a0 = ml_ * 128
                    if a0 >= c0 and a0 + 128 <= c1:
                        S.op("dve", (lambda e, a0=a0, c0=c0, sps=sps, moff_=moff_: e.tensor_tensor(
                            sps[:, a0 - c0:a0 - c0 + 128], sps[:, a0 - c0:a0 - c0 + 128],
                            maskT[:, moff_:moff_ + 128], ALU.add)),
                            reads=[spk, "maskT"], writes=[spk], tag="msk")
                S.op("act", (lambda e, c0=c0, c1=c1, sps=sps, kt=kt: e.activation(
                    ptile2[kt % 2][:, c0:c1], sps[:, 0:c1 - c0], AF.Exp, scale=ISQ)),
                    reads=[spk], writes=[f"ptile{kt % 2}"], tag="exp")
            b0 = 0
            while b0 < wd:
                blk_i = b0 // 512
                b1 = min((blk_i + 1) * 512, wd)
                opk = "ps0" if blk_i == 0 else "ps1"
                opt_ = PS[blk_i]
                rpk = "pr0" if blk_i == 0 else "pr1"
                rpt = E[rpk]
                wo_ = [opk] if (first[blk_i] or kt == AV_LAST[blk_i]) else []
                wr_ = [rpk] if (first[blk_i] or kt == AV_LAST[blk_i]) else []
                S.op("pe", (lambda e, kt=kt, b0=b0, b1=b1, blk_i=blk_i, vb=vb,
                            opt_=opt_, f=first[blk_i]: e.matmul(
                    opt_[:, b0 - blk_i * 512:b1 - blk_i * 512],
                    vb[:, kt * 128:(kt + 1) * 128], ptile2[kt % 2][:, b0:b1],
                    start=f, stop=(kt == AV_LAST[blk_i]))),
                    reads=[vk, f"ptile{kt % 2}"], writes=wo_, tag=f"av{h}")
                S.op("pe", (lambda e, kt=kt, b0=b0, b1=b1, blk_i=blk_i,
                            rpt=rpt, f=first[blk_i]: e.matmul(
                    rpt[0:1, b0 - blk_i * 512:b1 - blk_i * 512],
                    ones_bf[:, 0:1], ptile2[kt % 2][:, b0:b1],
                    start=f, stop=(kt == AV_LAST[blk_i]))),
                    reads=[f"ptile{kt % 2}", "ones"], writes=wr_, tag=f"rs{h}")
                first[blk_i] = False
                b0 = b1
        for blk_i in range(2):
            rpk = "pr0" if blk_i == 0 else "pr1"
            rpt = E[rpk]
            S.op("dve", (lambda e, blk_i=blk_i, rpt=rpt: e.tensor_copy(
                rows["qrow"][0:1, blk_i * 512:(blk_i + 1) * 512], rpt[0:1, :])),
                reads=[rpk], writes=["qrow"])
        S.op("dve", lambda e: e.reciprocal(rows["qrow"][:, :], rows["qrow"][:, :]),
             reads=["qrow"], writes=["qrow"])
        S.op("sp", (lambda e: e.dma_start(
            bass.AP(lnb, 3 * TL, [[TL, 1], [1, TL]]), rows["qrow"][:, :])),
            reads=["qrow"], writes=["lnb3"], dma=True)
        S.op("sp", (lambda e: e.dma_start(
            rrep[:, :], bass.AP(lnb, 3 * TL, [[0, 128], [1, TL]]))),
            reads=["lnb3"], writes=["rrep"], dma=True)
        for blk_i in range(2):
            opk = "ps0" if blk_i == 0 else "ps1"
            opt_ = PS[blk_i]
            if blk_i == 0:
                S.op("dve", (lambda e, blk_i=blk_i, h=h, opt_=opt_: e.tensor_copy(
                    ot[h][:, blk_i * 512:(blk_i + 1) * 512], opt_[:, :])),
                    reads=[opk], writes=[f"ot{h}"], tag="oev")
            else:
                S.op("act", (lambda e, blk_i=blk_i, h=h, opt_=opt_: e.copy(
                    ot[h][:, blk_i * 512:(blk_i + 1) * 512], opt_[:, :])),
                    reads=[opk], writes=[f"ot{h}"], tag="oev")
        S.op("dve", (lambda e, h=h: e.tensor_tensor(
            ot[h][:, :], ot[h][:, :], rrep[:, :], ALU.mult)),
            reads=[f"ot{h}", "rrep"], writes=[f"ot{h}"], tag="onrm")

    for o in range(NCH):
        wk = f"wslab{o % 2}"
        slab_load(oww, o, NCH, wk)
        S.op("sp", (lambda e, o=o: e.dma_start(
            hb[o % 2][:, :], x3d[o * 128:(o + 1) * 128, :])),
            writes=[f"hb{o % 2}"], dma=True)
        S.op("dve", (lambda e, o=o: e.tensor_tensor(
            hb[o % 2][:, :], hb[o % 2][:, :], arep[:, :], ALU.mult)),
            reads=[f"hb{o % 2}", "arep"], writes=[f"hb{o % 2}"], tag="urec")
        S.op("dve", (lambda e, o=o: e.tensor_tensor(
            hb[o % 2][:, :], hb[o % 2][:, :], brep[:, :], ALU.add)),
            reads=[f"hb{o % 2}", "brep"], writes=[f"hb{o % 2}"], tag="urec2")
        for tc in range(NT):
            pk = ["ps2", "ps3"][tc]
            pt = E_PS[pk]
            for hh in range(NH):
                w = [pk] if (hh == 0 or hh == NH - 1) else []
                S.op("pe", (lambda e, o=o, tc=tc, hh=hh, pt=pt: e.matmul(
                    pt[:, :], wslab[o % 2][:, hh * 128:(hh + 1) * 128],
                    ot[hh][:, tc * 512:(tc + 1) * 512],
                    start=(hh == 0), stop=(hh == NH - 1))),
                    reads=[wk, f"ot{hh}"], writes=w, tag=f"ow{o}")
            S.op("dve", (lambda e, o=o, tc=tc, pt=pt: e.tensor_tensor(
                hb[o % 2][:, tc * 512:(tc + 1) * 512], pt[:, :],
                hb[o % 2][:, tc * 512:(tc + 1) * 512], ALU.add)),
                reads=[pk, f"hb{o % 2}"], writes=[f"hb{o % 2}"], tag="hres")
        S.op("sp", (lambda e, o=o: e.dma_start(
            hd[o * 128:(o + 1) * 128, :], hb[o % 2][:, :])),
            reads=[f"hb{o % 2}"], writes=[f"hst{o % 2}"], dma=True)

    S.op("sp", lambda e: e.nop(), reads=["hst0", "hst1"])


def emit_phase_e(nc, S, ctx, D):
    wbn = D["wbn"]
    hdd = D["hi"]
    x3d = D["x3i"]
    shg = D["shg"]
    shu = D["shu"]
    shd = D["shd"]
    rg = D["rg"]
    ru = D["ru"]
    rd = D["rd"]
    rwT = D["rwT"]
    rbd = D["rbd"]
    l2wd = D["l2wd"]
    onesd = D["onesd"]
    x4 = D["x4"]
    lnb = D["lnb"]

    hb = [sbt(ctx, nc, f"hb{i}", [128, TL], BF16) for i in range(NCH)]
    ge = [[sbt(ctx, nc, f"ge{gi}_{i}", [128, TL], BF16) for i in range(NIT)]
          for gi in range(4)]
    macc = [sbt(ctx, nc, f"macc{i}", [128, TL], BF16) for i in range(NCH)]
    sqt = sbt(ctx, nc, "sqt", [128, TL], BF16)
    wslab = [sbt(ctx, nc, f"wslab{i}", [128, C], BF16) for i in range(2)]
    sil = sbt(ctx, nc, "sil", [128, 512], BF16)
    rwt = sbt(ctx, nc, "rwt", [128, 112], BF16)
    rbt = sbt(ctx, nc, "rbt", [7, 1], F32)
    l2w = sbt(ctx, nc, "l2w", [128, 16], F32)
    ones_bf = sbt(ctx, nc, "ones_bf", [128, 1], BF16)
    pt32 = sbt(ctx, nc, "pt32", [32, TL], F32)
    m1t = sbt(ctx, nc, "m1t", [32, TL], F32)
    m2t = sbt(ctx, nc, "m2t", [32, TL], F32)
    pro = sbt(ctx, nc, "pro", [32, TL], F32)
    wrep = sbt(ctx, nc, "wrep", [128, TL], F32)
    arep = sbt(ctx, nc, "arep", [128, TL], F32)
    brep = sbt(ctx, nc, "brep", [128, TL], F32)
    x3h = [sbt(ctx, nc, f"x3h{i}", [128, TL], F32) for i in range(2)]
    x4b = [sbt(ctx, nc, f"x4b{i}", [128, TL], F32) for i in range(2)]
    rows = {k: sbt(ctx, nc, k, [1, TL], F32) for k in ("srow", "qrow", "brow")}

    ps0 = pst(ctx, nc, "ps0", [128, 512])
    ps1 = pst(ctx, nc, "ps1", [128, 512])
    ps2 = pst(ctx, nc, "ps2", [128, 512])
    ps3 = pst(ctx, nc, "ps3", [128, 512])
    pr0 = pst(ctx, nc, "pr0", [128, 512])
    pr1 = pst(ctx, nc, "pr1", [128, 512])
    E_PS = {"ps0": ps0, "ps1": ps1, "ps2": ps2, "ps3": ps3}
    E = {"ones_bf": ones_bf, "sqt": sqt, "rows": rows, "lnb": lnb,
         "arep": arep, "brep": brep,
         "ps0": ps0, "ps1": ps1, "pr0": pr0, "pr1": pr1}

    for dc in range(NCH):
        S.op("pool", (lambda e, dc=dc: e.dma_start(
            hb[dc][:, :], hdd[dc * 128:(dc + 1) * 128, :])),
            writes=[f"hb{dc}"], dma=True)
    S.op("sp", lambda e: e.dma_start(ones_bf[:, :], onesd[:, :]), writes=["ones"], dma=True)
    S.op("sp", lambda e: e.dma_start(l2w[:, :], l2wd[:, :]), writes=["l2w"], dma=True)
    S.op("sp", lambda e: e.dma_start(rbt[:, :], rbd[:, :]), writes=["rbt"], dma=True)
    S.op("pool", lambda e: e.dma_start(
        rwt[:, 0:112], bass.AP(rwT, 0, [[7, 128], [128 * 7, NCH], [1, 7]])),
        writes=["rwt"], dma=True)

    hkeys = [f"hb{i}" for i in range(NCH)]
    emit_layernorm(S, E, hkeys, hb, 2, rms_only=True, apply=False)
    S.op("sp", (lambda e: e.dma_start(
        arep[:, :], bass.AP(lnb, 2 * TL, [[0, 128], [1, TL]]))),
        reads=["lnb2"], writes=["arep"], dma=True)
    for dc in range(NCH):
        S.op("dve", (lambda e, dc=dc: e.scalar_tensor_tensor(
            hb[dc][:, :], hb[dc][:, :], l2w[:, dc:dc + 1], arep[:, :],
            ALU.mult, ALU.mult)),
            reads=[f"hb{dc}", "l2w", "arep"], writes=[f"hb{dc}"], tag="rmsap")

    for tc in range(NT):
        prk = ["pr0", "pr1"][tc]
        prt = E[prk]
        for dc in range(NCH):
            w = [prk] if (dc == 0 or dc == NCH - 1) else []
            S.op("pe", (lambda e, tc=tc, dc=dc, prt=prt: e.matmul(
                prt[0:7, :], rwt[:, dc * 7:(dc + 1) * 7],
                hb[dc][:, tc * 512:(tc + 1) * 512],
                start=(dc == 0), stop=(dc == NCH - 1))),
                reads=["rwt", f"hb{dc}"], writes=w, tag="rlog")
    S.op("dve", lambda e: e.memset(pt32[:, :], NEG), writes=["pt32"])
    for tc in range(NT):
        prk = ["pr0", "pr1"][tc]
        prt = E[prk]
        S.op("dve", (lambda e, tc=tc, prt=prt: e.tensor_scalar_add(
            pt32[0:7, tc * 512:(tc + 1) * 512], prt[0:7, :], rbt[0:7, 0:1])),
            reads=[prk, "rbt", "pt32"], writes=["pt32"], tag="rbias")
    S.op("act", lambda e: e.activation(pt32[0:7, :], pt32[0:7, :], AF.Sigmoid),
         reads=["pt32"], writes=["pt32"])
    S.op("dve", lambda e: e.tensor_copy(pro[:, :], pt32[:, :]),
         reads=["pt32"], writes=["pro"])

    def logmax(dst, dkey, srckey):
        for k in (1, 2, 4):
            mask = [(r & ~7) | ((r + k) & 7) for r in range(32)]
            S.op("dve", (lambda e, mask=mask, dst=dst: e.stream_shuffle(
                m2t[:, :] if dst is m1t else m1t[:, :], dst[:, :], mask)),
                reads=[dkey], writes=["m2t" if dst is m1t else "m1t"], tag="shf")
            S.op("dve", (lambda e, dst=dst: e.tensor_tensor(
                dst[:, :], dst[:, :],
                m2t[:, :] if dst is m1t else m1t[:, :], ALU.max)),
                reads=[dkey, "m2t" if dst is m1t else "m1t"], writes=[dkey], tag="lmx")

    S.op("dve", lambda e: e.tensor_copy(m1t[:, :], pt32[:, :]),
         reads=["pt32"], writes=["m1t"])
    logmax(m1t, "m1t", "pt32")
    S.op("dve", lambda e: e.tensor_tensor(m2t[:, :], pro[:, :], m1t[:, :], ALU.is_equal),
         reads=["pro", "m1t"], writes=["m2t"])
    S.op("dve", lambda e: e.tensor_scalar_mul(m2t[:, :], m2t[:, :], NEG),
         reads=["m2t"], writes=["m2t"])
    S.op("dve", lambda e: e.tensor_tensor(pt32[:, :], pro[:, :], m2t[:, :], ALU.add),
         reads=["pro", "m2t", "pt32"], writes=["pt32"])
    logmax(pt32, "pt32", None)
    S.op("dve", lambda e: e.tensor_tensor(m2t[:, :], pro[:, :], pt32[:, :], ALU.is_ge),
         reads=["pro", "pt32"], writes=["m2t"])
    S.op("dve", lambda e: e.tensor_tensor(m2t[:, :], m2t[:, :], pro[:, :], ALU.mult),
         reads=["m2t", "pro"], writes=["m2t"])
    S.op("dve", lambda e: e.tensor_tensor(m1t[:, :], m1t[:, :], pt32[:, :], ALU.add),
         reads=["m1t", "pt32"], writes=["m1t"])
    S.op("dve", lambda e: e.reciprocal(m1t[:, :], m1t[:, :]),
         reads=["m1t"], writes=["m1t"])
    S.op("dve", lambda e: e.tensor_tensor(m2t[:, :], m2t[:, :], m1t[:, :], ALU.mult),
         reads=["m2t", "m1t"], writes=["m2t"])
    S.op("dve", lambda e: e.tensor_scalar_mul(m2t[:, :], m2t[:, :], 0.125),
         reads=["m2t"], writes=["m2t"])
    S.op("sp", (lambda e: e.dma_start(
        wbn[0:7, :], m2t[0:7, :])), reads=["m2t"], writes=["wbn"], dma=True)

    def expert_mm(e_idx, gslot):
        for f in range(NIT):
            for (wsrc, which) in ((shg if e_idx == 0 else rg, 0),
                                  (shu if e_idx == 0 else ru, 1)):
                wk = f"wslab{which}"
                off = (f if e_idx == 0 else ((e_idx - 1) * NIT + f)) * C * 128
                S.op("pool", (lambda e, off=off, wsrc=wsrc, which=which: e.dma_start(
                    wslab[which][:, :],
                    bass.AP(wsrc, off, [[128, 128], [128 * 128, NCH], [1, 128]]))),
                    writes=[wk], dma=True)
            for tc in range(NT):
                pgk, puk = "ps2", "ps3"
                pgt, put = ps2, ps3
                for dc in range(NCH):
                    w = [pgk] if (dc == 0 or dc == NCH - 1) else []
                    S.op("pe", (lambda e, tc=tc, dc=dc, pgt=pgt: e.matmul(
                        pgt[:, :], wslab[0][:, dc * 128:(dc + 1) * 128],
                        hb[dc][:, tc * 512:(tc + 1) * 512],
                        start=(dc == 0), stop=(dc == NCH - 1))),
                        reads=["wslab0", f"hb{dc}"], writes=w, tag="gmm")
                for dc in range(NCH):
                    w = [puk] if (dc == 0 or dc == NCH - 1) else []
                    S.op("pe", (lambda e, tc=tc, dc=dc, put=put: e.matmul(
                        put[:, :], wslab[1][:, dc * 128:(dc + 1) * 128],
                        hb[dc][:, tc * 512:(tc + 1) * 512],
                        start=(dc == 0), stop=(dc == NCH - 1))),
                        reads=["wslab1", f"hb{dc}"], writes=w, tag="umm")
                S.op("act", (lambda e, pgt=pgt: e.activation(
                    sil[:, :], pgt[:, :], AF.Silu)),
                    reads=[pgk], writes=["sil"], tag="silu")
                gk = f"ge{gslot}_{f}"
                S.op("dve", (lambda e, tc=tc, f=f, put=put, gslot=gslot: e.tensor_tensor(
                    ge[gslot][f][:, tc * 512:(tc + 1) * 512], sil[:, :],
                    put[:, :], ALU.mult)),
                    reads=["sil", puk], writes=[gk], tag="gmul")
        if e_idx == 0:
            for f in range(NIT):
                gk = f"ge{gslot}_{f}"
                S.op("dve", (lambda e, f=f, gslot=gslot: e.tensor_scalar_mul(
                    ge[gslot][f][:, :], ge[gslot][f][:, :], 0.125)),
                    reads=[gk], writes=[gk], tag="gsc")
        else:
            S.op("sp", (lambda e, e_idx=e_idx: e.dma_start(
                wrep[:, :], bass.AP(wbn, (e_idx - 1) * TL, [[0, 128], [1, TL]]))),
                reads=["wbn"], writes=["wrep"], dma=True)
            for f in range(NIT):
                gk = f"ge{gslot}_{f}"
                S.op("dve", (lambda e, f=f, gslot=gslot: e.tensor_tensor(
                    ge[gslot][f][:, :], ge[gslot][f][:, :], wrep[:, :], ALU.mult)),
                    reads=[gk, "wrep"], writes=[gk], tag="gscw")

    for grp in range(2):
        for gi in range(4):
            expert_mm(grp * 4 + gi, gi)
        for o in range(NCH):
            for tc in range(NT):
                pk = ["ps0", "ps1"][(o * NT + tc) % 2]
                pt = E_PS[pk]
                nacc = 4 * NIT
                idx = 0
                for gi in range(4):
                    e_idx = grp * 4 + gi
                    wk2 = "w2slab"
                    for hc in range(NIT):
                        if tc == 0 and hc == 0:
                            pass
                        idx += 1
        for o in range(NCH):
            for gi in range(4):
                e_idx = grp * 4 + gi
                wk = f"wslab{gi % 2}"
                src = shd if e_idx == 0 else rd
                off = (o if e_idx == 0 else ((e_idx - 1) * NCH + o)) * IEXP * 128
                S.op("pool", (lambda e, off=off, src=src, gi=gi: e.dma_start(
                    wslab[gi % 2][:, 0:IEXP],
                    bass.AP(src, off, [[128, 128], [128 * 128, NIT], [1, 128]]))),
                    writes=[wk], dma=True)
                for tc in range(NT):
                    pk = ["ps0", "ps1"][tc]
                    pt = E_PS[pk]
                    for hc in range(NIT):
                        first_mm = (gi == 0 and hc == 0)
                        last_mm = (gi == 3 and hc == NIT - 1)
                        w = [pk] if (first_mm or last_mm) else []
                        S.op("pe", (lambda e, tc=tc, hc=hc, pt=pt, gi=gi: e.matmul(
                            pt[:, :], wslab[gi % 2][:, hc * 128:(hc + 1) * 128],
                            ge[gi][hc][:, tc * 512:(tc + 1) * 512],
                            start=(gi == 0 and hc == 0),
                            stop=(gi == 3 and hc == NIT - 1))),
                            reads=[wk, f"ge{gi}_{hc}"], writes=w, tag=f"dmm{o}")
            if grp == 0:
                for tc in range(NT):
                    pk = ["ps0", "ps1"][tc]
                    pt = E_PS[pk]
                    S.op("dve", (lambda e, o=o, tc=tc, pt=pt: e.tensor_copy(
                        macc[o][:, tc * 512:(tc + 1) * 512], pt[:, :])),
                        reads=[pk], writes=[f"macc{o}"], tag="mev")
            else:
                S.op("sp", (lambda e, o=o: e.dma_start(
                    x3h[o % 2][:, :], x3d[o * 128:(o + 1) * 128, :])),
                    writes=[f"x3h{o % 2}"], dma=True)
                S.op("sp", (lambda e, o=o: e.dma_start(
                    x4b[o % 2][:, :], hdd[o * 128:(o + 1) * 128, :])),
                    writes=[f"x4b{o % 2}"], dma=True)
                S.op("dve", (lambda e, o=o: e.tensor_tensor(
                    x4b[o % 2][:, :], x4b[o % 2][:, :], x3h[o % 2][:, :], ALU.add)),
                    reads=[f"x4b{o % 2}", f"x3h{o % 2}"], writes=[f"x4b{o % 2}"], tag="xh")
                for tc in range(NT):
                    pk = ["ps0", "ps1"][tc]
                    pt = E_PS[pk]
                    S.op("dve", (lambda e, o=o, tc=tc, pt=pt: e.tensor_tensor(
                        x3h[o % 2][:, tc * 512:(tc + 1) * 512], pt[:, :],
                        macc[o][:, tc * 512:(tc + 1) * 512], ALU.add)),
                        reads=[pk, f"macc{o}"], writes=[f"x3h{o % 2}"], tag="m2a")
                    S.op("dve", (lambda e, o=o, tc=tc: e.tensor_tensor(
                        x4b[o % 2][:, tc * 512:(tc + 1) * 512],
                        x4b[o % 2][:, tc * 512:(tc + 1) * 512],
                        x3h[o % 2][:, tc * 512:(tc + 1) * 512], ALU.add)),
                        reads=[f"x4b{o % 2}", f"x3h{o % 2}"], writes=[f"x4b{o % 2}"], tag="x4a")
                S.op("sp", (lambda e, o=o: e.dma_start(
                    x4[o * 128:(o + 1) * 128, :], x4b[o % 2][:, :])),
                    reads=[f"x4b{o % 2}"], writes=[f"x4st{o % 2}"], dma=True)

    S.op("sp", lambda e: e.nop(), reads=["x4st0", "x4st1"])



def build_fused():
    import os as _os0
    nc = bass.Bass("TRN2", target_bir_lowering=False)
    with ExitStack() as octx:
        D = {}
        D["x"] = nc.dram_tensor("x", [C, TL], F32, kind="ExternalInput")
        D["wqkT"] = nc.dram_tensor("wqkT", [32, C, 128], F32, kind="ExternalInput")
        D["wvT"] = nc.dram_tensor("wvT", [16, C, 128], F32, kind="ExternalInput")
        D["woT"] = nc.dram_tensor("woT", [16, C, 128], F32, kind="ExternalInput")
        D["cos0"] = nc.dram_tensor("cos0", [128, TL], BF16, kind="ExternalInput")
        D["sin0"] = nc.dram_tensor("sin0", [128, TL], BF16, kind="ExternalInput")
        D["mask0"] = nc.dram_tensor("mask0", [128, MASK_COLS], BF16, kind="ExternalInput")
        D["onesd"] = nc.dram_tensor("onesd", [128, 1], BF16, kind="ExternalInput")
        D["identd"] = nc.dram_tensor("identd", [128, 128], BF16, kind="ExternalInput")
        D["x1i"] = nc.dram_tensor("x1i", [C, TL], F32)
        D["k_in"] = nc.dram_tensor("k_in", [C, TL], BF16)
        D["k_out"] = nc.dram_tensor("k_out", [2 * C, TL], BF16)
        D["v_in"] = nc.dram_tensor("v_in", [TL, C], BF16)
        D["v_out"] = nc.dram_tensor("v_out", [T, C], BF16)
        D["lnb"] = nc.dram_tensor("lnb", [8, TL], F32)
        D["w1s"] = nc.dram_tensor("w1s", [NHT, C, 128], F32, kind="ExternalInput")
        D["w3s"] = nc.dram_tensor("w3s", [NHT, C, 128], F32, kind="ExternalInput")
        D["w2s"] = nc.dram_tensor("w2s", [NCH, HIDP, 128], F32, kind="ExternalInput")
        D["onesf"] = nc.dram_tensor("onesf", [128, 1], F32, kind="ExternalInput")
        if "x2i" not in D:
            if "x2i" not in D:
                D["x2i"] = nc.dram_tensor("x2i", [C, TL], F32)
        D["wxT"] = nc.dram_tensor("wxT", [C, 160], F32, kind="ExternalInput")
        D["wdtT"] = nc.dram_tensor("wdtT", [128, 1024], F32, kind="ExternalInput")
        D["alogh"] = nc.dram_tensor("alogh", [128, 128], F32, kind="ExternalInput")
        D["bdth"] = nc.dram_tensor("bdth", [128, 8], F32, kind="ExternalInput")
        D["dfull"] = nc.dram_tensor("dfull", [128, 16], F32, kind="ExternalInput")
        D["hoffd"] = nc.dram_tensor("hoffd", [1, 4], mybir.dt.int32, kind="ExternalInput")
        D["x3i"] = nc.dram_tensor("x3i", [C, TL], F32)
        D["e2in"] = nc.dram_tensor("e2in", [E2ROWS, TL], BF16)
        D["e2out"] = nc.dram_tensor("e2out", [2 * E2ROWS, TL], BF16)
        D["e3in"] = nc.dram_tensor("e3in", [1024, T], BF16)
        D["e3out"] = nc.dram_tensor("e3out", [C, T], BF16)
        D["kvdw"] = nc.dram_tensor("kvdw", [NLAT, C, 128], F32, kind="ExternalInput")
        D["qdw"] = nc.dram_tensor("qdw", [NLAT, C, 128], F32, kind="ExternalInput")
        D["kuw"] = nc.dram_tensor("kuw", [8, LAT, 128], F32, kind="ExternalInput")
        D["quw"] = nc.dram_tensor("quw", [8, LAT, 128], F32, kind="ExternalInput")
        D["vuw"] = nc.dram_tensor("vuw", [LAT, C], F32, kind="ExternalInput")
        D["rkw"] = nc.dram_tensor("rkw", [8, C, 128], F32, kind="ExternalInput")
        D["rqw"] = nc.dram_tensor("rqw", [8, LAT, 128], F32, kind="ExternalInput")
        D["oww"] = nc.dram_tensor("oww", [NCH, C, 128], F32, kind="ExternalInput")
        D["cos1"] = nc.dram_tensor("cos1", [128, TL], BF16, kind="ExternalInput")
        D["sin1"] = nc.dram_tensor("sin1", [128, TL], BF16, kind="ExternalInput")
        D["l1wd"] = nc.dram_tensor("l1wd", [128, 16], F32, kind="ExternalInput")
        D["hi"] = nc.dram_tensor("hi", [C, TL], F32)
        D["shg"] = nc.dram_tensor("shg", [NIT, C, 128], F32, kind="ExternalInput")
        D["shu"] = nc.dram_tensor("shu", [NIT, C, 128], F32, kind="ExternalInput")
        D["shd"] = nc.dram_tensor("shd", [NCH, IEXP, 128], F32, kind="ExternalInput")
        D["rg"] = nc.dram_tensor("rg", [7, NIT, C, 128], F32, kind="ExternalInput")
        D["ru"] = nc.dram_tensor("ru", [7, NIT, C, 128], F32, kind="ExternalInput")
        D["rd"] = nc.dram_tensor("rd", [7, NCH, IEXP, 128], F32, kind="ExternalInput")
        D["rwT"] = nc.dram_tensor("rwT", [C, 7], F32, kind="ExternalInput")
        D["rbd"] = nc.dram_tensor("rbd", [7, 1], F32, kind="ExternalInput")
        D["l2wd"] = nc.dram_tensor("l2wd", [128, 16], F32, kind="ExternalInput")
        D["x4"] = nc.dram_tensor("x4", [C, TL], F32, kind="ExternalOutput")
        D["sder"] = nc.dram_tensor("sder", [1, 16], F32)
        D["drfg"] = nc.dram_tensor("drfg", [128, T], BF16)
        D["e2xg"] = nc.dram_tensor("e2xg", [C, T], BF16)
        D["e3b"] = nc.dram_tensor("e3b", [C, T], BF16)
        D["bcg"] = nc.dram_tensor("bcg", [32, T], BF16)
        D["wbn"] = nc.dram_tensor("wbn", [8, TL], F32)
        D["_octx"] = octx
        D["abscol"] = octx.enter_context(nc.sbuf_tensor("abscol_g", [128, 4], F32))
        D["abstmp"] = octx.enter_context(nc.sbuf_tensor("abstmp_g", [128, 1], F32))
        S = Sched(nc, octx)
        groups = ([emit_prelude, emit_phase_a], [emit_phase_b],
                  [emit_phase_c], [emit_phase_d], [emit_phase_e])
        for grp in groups:
            with ExitStack() as pctx:
                for em in grp:
                    em(nc, S, pctx, D)
            S.barrier()
        S.emit_all()
    return nc


def _stage_fused(inputs):
    ia = _stage_phase_a(inputs)
    ib = _stage_phase_b(inputs)
    ic = _stage_phase_c(inputs)
    id_ = _stage_phase_d(inputs)
    ie = _stage_phase_e(inputs)
    in_maps = []
    for c in range(8):
        m = {}
        for part in (ia[c], ib[c], ic[c], id_[c], ie[c]):
            m.update(part)
        in_maps.append(m)
    return in_maps


def kernel(**inputs):
    nc = build_fused()
    in_maps = _stage_fused(inputs)
    res = run_bass_kernel_spmd(nc, in_maps, core_ids=list(range(8)))
    out = np.zeros((B, T, C), np.float32)
    for c in range(8):
        b, j = divmod(c, 2)
        out[b, _tok_idx(j), :] = res.results[c]["x4"].T
    return out


# version 27
# speedup vs baseline: 1.0147x; 1.0147x over previous
import numpy as np
import ml_dtypes
from contextlib import ExitStack

import concourse.bass as bass
import concourse.mybir as mybir
from concourse.bass_utils import run_bass_kernel_spmd

F32 = mybir.dt.float32
BF16 = mybir.dt.bfloat16
AF = mybir.ActivationFunctionType
ALU = mybir.AluOpType
BF = ml_dtypes.bfloat16

B, T, C = 4, 2048, 2048
NH, DH = 16, 128
TL = T // 2
NCH = C // 128
NT = TL // 512
NKT = T // 128
LN_EPS = 1e-5
RMS_EPS0 = 1.1920929e-07
NEG = -1.0e30
ISQ = 1.0 / np.sqrt(128.0)

PAIRS = [[0, 1], [2, 3], [4, 5], [6, 7]]


CAPZ = [16 - 2 * l for l in range(8)]


def tok_blocks(p):
    return [(15 - p) - 2 * l for l in range(8)]


def _wid(kt):
    return 128 * ((16 - kt + 1) // 2)


MSEGS = []
for _l in range(8):
    _cap = CAPZ[_l]
    MSEGS.append((_cap - 2, _l, (2 * _l) * 128))
    MSEGS.append((_cap - 1, _l, (2 * _l + 1) * 128))
MASK_COLS = 16 * 128


QUEUES = ("pe", "act", "dve", "pool", "sp")


NDMA_SEMS = 4


class Sched:

    def __init__(self, nc, ctx):
        self.nc = nc
        self.sems = {}
        for q in QUEUES:
            self.sems[f"c_{q}"] = ctx.enter_context(nc.semaphore(f"c_{q}"))
            for k in range(NDMA_SEMS):
                self.sems[f"d_{q}{k}"] = ctx.enter_context(nc.semaphore(f"d_{q}{k}"))
        self.cval = {q: 0 for q in QUEUES}
        self.didx = {q: 0 for q in QUEUES}
        self.ops = {q: [] for q in QUEUES}
        self.seen = {q: {} for q in QUEUES}
        self.semval = {name: 0 for name in self.sems}
        self.res = {}

    def op(self, q, emit, reads=(), writes=(), dma=False, dma_n=1, tag=""):
        toks = []
        for k in reads:
            e = self.res.get(k)
            if e and e["w"]:
                toks.append(e["w"])
        for k in writes:
            e = self.res.get(k)
            if e:
                if e["w"]:
                    toks.append(e["w"])
                toks.extend(e["r"])
        need = {}
        for (dq, sem, dv) in toks:
            if dq == q and q == "pe" and sem == "c_pe":
                continue
            if dv > need.get(sem, -1):
                need[sem] = dv
        if dma:
            i = self.didx[q]
            self.didx[q] = i + 1
            sem = f"d_{q}{i % NDMA_SEMS}"
            prev = self.semval.get(sem, 0)
            if prev > 0 and prev > need.get(sem, -1):
                need[sem] = prev
            tok = (q, sem, prev + 16 * dma_n)
            inc_sem, inc = sem, 16
        else:
            self.cval[q] += 1
            tok = (q, f"c_{q}", self.cval[q])
            inc_sem, inc = f"c_{q}", 1
        waits = []
        for sem in sorted(need):
            dv = need[sem]
            if dv <= self.seen[q].get(sem, 0):
                continue
            waits.append((sem, dv))
            self.seen[q][sem] = dv
        self.ops[q].append((waits, emit, inc_sem, inc, tag))
        self.semval[inc_sem] = (self.semval.get(inc_sem, 0) + inc) if not dma else tok[2]
        for k in reads:
            self.res.setdefault(k, {"w": None, "r": []})["r"].append(tok)
        for k in writes:
            self.res[k] = {"w": tok, "r": []}
        return tok

    def barrier(self):
        vals = dict(self.semval)
        for q in QUEUES:
            waits = []
            for sem in sorted(vals):
                v = vals[sem]
                if v <= 0 or sem == f"c_{q}":
                    continue
                if v <= self.seen[q].get(sem, 0):
                    continue
                waits.append((sem, v))
                self.seen[q][sem] = v
            self.cval[q] += 1
            self.semval[f"c_{q}"] = self.cval[q]
            self.ops[q].append((waits, (lambda e: e.nop()), f"c_{q}", 1, "barrier"))

    def emit_all(self):
        nc = self.nc
        with nc.Block() as blk:
            def mk(q):
                def body(eng):
                    for oi, (waits, emit, inc_sem, inc, tag) in enumerate(self.ops[q]):
                        for sem, dv in waits:
                            eng.wait_ge(self.sems[sem], dv)
                        try:
                            inst = emit(eng)
                        except Exception:
                            print(f"EMIT FAIL q={q} idx={oi} tag={tag} waits={waits}")
                            raise
                        inst.then_inc(self.sems[inc_sem], inc)
                return body
            blk.tensor(mk("pe"))
            blk.scalar(mk("act"))
            blk.vector(mk("dve"))
            blk.gpsimd(mk("pool"))
            blk.sync(mk("sp"))


_NAMECTR = [0]


def _uniq(name):
    _NAMECTR[0] += 1
    return f"{name}_{_NAMECTR[0]}"


def sbt(ctx, nc, name, shape, dtype):
    return ctx.enter_context(nc.sbuf_tensor(_uniq(name), shape, dtype))


def pst(ctx, nc, name, shape, dtype=F32):
    return ctx.enter_context(nc.psum_tensor(_uniq(name), shape, dtype))



def mm_accum(S, n, psum_key, emit_i, reads_i, tag=""):
    for i in range(n):
        w = [psum_key] if (i == 0 or i == n - 1) else []
        S.op("pe", (lambda e, i=i: emit_i(e, i)), reads=reads_i(i), writes=w, tag=tag)


def emit_rstd_row(S, E, row, key, inv_n, eps):
    S.op("dve", lambda e: e.tensor_scalar(row[:, :], row[:, :], inv_n, eps,
                                          ALU.mult, ALU.add),
         reads=[key], writes=[key])
    S.op("act", lambda e: e.activation(row[:, :], row[:, :], AF.Ln),
         reads=[key], writes=[key])
    S.op("act", lambda e: e.activation(row[:, :], row[:, :], AF.Exp, scale=-0.5),
         reads=[key], writes=[key])


def emit_layernorm(S, E, xkeys, xtiles, lnb_row, gain_col=None, rms_only=False,
                   eps=LN_EPS, n_feat=C, apply=True):
    nch = len(xtiles)
    ps0, ps1, pr0, pr1 = E["ps0"], E["ps1"], E["pr0"], E["pr1"]
    rows, lnb = E["rows"], E["lnb"]
    for dc in range(nch):
        if not rms_only:
            for tc in range(NT):
                prt, prk = (pr0, "pr0") if tc == 0 else (pr1, "pr1")
                w = [prk] if (dc == 0 or dc == nch - 1) else []
                S.op("pe", (lambda e, dc=dc, tc=tc, prt=prt: e.matmul(
                    prt[0:1, :], E["ones_bf"][:, 0:1],
                    xtiles[dc][:, tc * 512:(tc + 1) * 512],
                    start=(dc == 0), stop=(dc == nch - 1))),
                    reads=[xkeys[dc], "ones"], writes=w, tag="lnS")
        S.op("act", (lambda e, dc=dc: e.activation(E["sqt"][:, 0:TL],
                                                   xtiles[dc][:, :], AF.Square)),
             reads=[xkeys[dc]], writes=[E.get("sqt_key", "sqt")], tag="lnSq")
        for tc in range(NT):
            pqt, pqk = (ps0, "ps0") if tc == 0 else (ps1, "ps1")
            w = [pqk] if (dc == 0 or dc == nch - 1) else []
            S.op("pe", (lambda e, dc=dc, tc=tc, pqt=pqt: e.matmul(
                pqt[0:1, :], E["ones_bf"][:, 0:1],
                E["sqt"][:, tc * 512:(tc + 1) * 512],
                start=(dc == 0), stop=(dc == nch - 1))),
                reads=[E.get("sqt_key", "sqt"), "ones"], writes=w, tag="lnQ")
    inv_n = 1.0 / n_feat
    for tc in range(NT):
        pqk = "ps0" if tc == 0 else "ps1"
        pqt = E[pqk]
        S.op("dve", (lambda e, tc=tc, pqt=pqt: e.tensor_copy(
            rows["qrow"][0:1, tc * 512:(tc + 1) * 512], pqt[0:1, :])),
            reads=[pqk], writes=["qrow"])
    if not rms_only:
        for tc in range(NT):
            prk = "pr0" if tc == 0 else "pr1"
            prt = E[prk]
            S.op("dve", (lambda e, tc=tc, prt=prt: e.tensor_copy(
                rows["srow"][0:1, tc * 512:(tc + 1) * 512], prt[0:1, :])),
                reads=[prk], writes=["srow"])
        S.op("dve", lambda e: e.tensor_scalar_mul(rows["srow"][:, :], rows["srow"][:, :], inv_n),
             reads=["srow"], writes=["srow"])
        S.op("dve", lambda e: e.tensor_scalar_mul(rows["qrow"][:, :], rows["qrow"][:, :], inv_n),
             reads=["qrow"], writes=["qrow"])
        S.op("dve", lambda e: e.tensor_tensor(rows["brow"][:, :], rows["srow"][:, :], rows["srow"][:, :], ALU.mult),
             reads=["srow"], writes=["brow"])
        S.op("dve", lambda e: e.tensor_tensor(rows["qrow"][:, :], rows["qrow"][:, :], rows["brow"][:, :], ALU.subtract),
             reads=["qrow", "brow"], writes=["qrow"])
        emit_rstd_row(S, E, rows["qrow"], "qrow", 1.0, eps)
    else:
        emit_rstd_row(S, E, rows["qrow"], "qrow", inv_n, eps)
    if not rms_only:
        S.op("dve", lambda e: e.tensor_tensor(rows["brow"][:, :], rows["srow"][:, :], rows["qrow"][:, :], ALU.mult),
             reads=["srow", "qrow"], writes=["brow"])
        S.op("dve", lambda e: e.tensor_scalar_mul(rows["brow"][:, :], rows["brow"][:, :], -1.0),
             reads=["brow"], writes=["brow"])
    S.op("sp", (lambda e: e.dma_start(
        bass.AP(lnb, lnb_row * TL, [[TL, 1], [1, TL]]), rows["qrow"][:, :])),
        reads=["qrow"], writes=[f"lnb{lnb_row}"], dma=True)
    if not rms_only:
        S.op("sp", (lambda e: e.dma_start(
            bass.AP(lnb, (lnb_row + 1) * TL, [[TL, 1], [1, TL]]), rows["brow"][:, :])),
            reads=["brow"], writes=[f"lnb{lnb_row + 1}"], dma=True)
    if not apply:
        return
    S.op("sp", (lambda e: e.dma_start(
        E["arep"][:, :], bass.AP(lnb, lnb_row * TL, [[0, 128], [1, TL]]))),
        reads=[f"lnb{lnb_row}"], writes=["arep"], dma=True)
    if not rms_only:
        S.op("sp", (lambda e: e.dma_start(
            E["brep"][:, :], bass.AP(lnb, (lnb_row + 1) * TL, [[0, 128], [1, TL]]))),
            reads=[f"lnb{lnb_row + 1}"], writes=["brep"], dma=True)
    for dc in range(nch):
        S.op("dve", (lambda e, dc=dc: e.tensor_tensor(
            xtiles[dc][:, :], xtiles[dc][:, :], E["arep"][:, :], ALU.mult)),
            reads=[xkeys[dc], "arep"], writes=[xkeys[dc]])
        if not rms_only:
            S.op("dve", (lambda e, dc=dc: e.tensor_tensor(
                xtiles[dc][:, :], xtiles[dc][:, :], E["brep"][:, :], ALU.add)),
                reads=[xkeys[dc], "brep"], writes=[xkeys[dc]])




def _rope_tables_l0():
    inv = 1.0 / 10000.0 ** (np.arange(0, DH, 2, dtype=np.float64) / DH)
    t = np.arange(T, dtype=np.float64)
    fr = np.outer(t, inv)
    sin = np.repeat(np.sin(fr), 2, axis=1)
    cos = np.repeat(np.cos(fr), 2, axis=1)
    sgn = np.where(np.arange(DH) % 2 == 0, -1.0, 1.0)
    return cos.T.astype(np.float32), (sin * sgn[None, :]).T.astype(np.float32)


def _mask_l0(is_odd):
    p = 1 if is_odd else 0
    m = np.zeros((128, MASK_COLS), np.float32)
    for (kt, l, off) in MSEGS:
        g = (15 - p) - 2 * l
        if kt > g:
            m[:, off:off + 128] = NEG
        elif kt == g:
            for c in range(128):
                m[c + 1:, off + c] = NEG
    return m


def _tok_idx(p):
    return np.concatenate([np.arange(g * 128, (g + 1) * 128)
                           for g in tok_blocks(p)])


def _stage_phase_a(inputs):
    x = np.asarray(inputs["L0_attn_x"])
    wqkv = np.asarray(inputs["L0_attn_W_qkv"])
    wo = np.asarray(inputs["L0_attn_W_o"])
    x_fm = np.ascontiguousarray(x.transpose(0, 2, 1))
    wqkvT = np.ascontiguousarray(wqkv.T)
    qk = np.ascontiguousarray(wqkvT[:, 0:4096].reshape(C, 32, 128).transpose(1, 0, 2))
    v = np.ascontiguousarray(wqkvT[:, 4096:6144].reshape(C, 16, 128).transpose(1, 0, 2))
    woT = np.ascontiguousarray(wo.T.reshape(C, 16, 128).transpose(1, 0, 2))
    cosf, sinf = _rope_tables_l0()
    ones = np.ones((128, 1), np.float32).astype(BF)
    ident = np.eye(128, dtype=np.float32).astype(BF)
    masks = [_mask_l0(False).astype(BF), _mask_l0(True).astype(BF)]
    in_maps = []
    for c in range(8):
        b, j = divmod(c, 2)
        idx = _tok_idx(j)
        in_maps.append({
            "x": np.ascontiguousarray(x_fm[b][:, idx]),
            "wqkT": qk, "wvT": v, "woT": woT,
            "cos0": np.ascontiguousarray(cosf[:, idx]).astype(BF),
            "sin0": np.ascontiguousarray(sinf[:, idx]).astype(BF),
            "mask0": masks[j],
            "onesd": ones, "identd": ident,
        })
    return in_maps




HID = 5461
HIDP = 5504
NHT = HIDP // 128


def _stage_phase_b(inputs, x1_fm=None, x1_pc=None):
    w1 = np.asarray(inputs["L0_ffn_w1"])
    w3 = np.asarray(inputs["L0_ffn_w3"])
    w2 = np.asarray(inputs["L0_ffn_w2"])

    def pad_slab_T(w):
        wT = np.zeros((C, HIDP), np.float32)
        wT[:, 0:HID] = w.T
        return np.ascontiguousarray(wT.reshape(C, NHT, 128).transpose(1, 0, 2))

    w1sl = pad_slab_T(w1)
    w3sl = pad_slab_T(w3)
    w2T = np.zeros((HIDP, C), np.float32)
    w2T[0:HID, :] = w2.T
    w2sl = np.ascontiguousarray(w2T.reshape(HIDP, NCH, 128).transpose(1, 0, 2))
    ones = np.ones((128, 1), np.float32)
    in_maps = []
    for c in range(8):
        b, j = divmod(c, 2)
        m = {"w1s": w1sl, "w3s": w3sl, "w2s": w2sl,
             "onesd": ones.astype(BF), "onesf": ones}
        if x1_pc is not None:
            m["x1d"] = x1_pc[c]
        elif x1_fm is not None:
            m["x1d"] = np.ascontiguousarray(x1_fm[b, :, j * TL:(j + 1) * TL])
        in_maps.append(m)
    return in_maps


NSSM = 16
TCH = 512
NCHK = T // TCH
E2ROWS = 160 + C


def _stage_phase_c(inputs, x2_fm=None, x2_pc=None):
    wx = np.asarray(inputs["L1_attn_W_xproj"])
    wdtf = np.asarray(inputs["L1_attn_W_dt"])
    bdt = np.asarray(inputs["L1_attn_b_dt"])
    alog = np.asarray(inputs["L1_attn_A_log"])
    dpar = np.asarray(inputs["L1_attn_D_param"])
    wxTs = np.ascontiguousarray(wx.T)
    dcols = np.ascontiguousarray(dpar.reshape(16, 128).T)
    ones = np.ones((128, 1), np.float32).astype(BF)
    in_maps = []
    for c in range(8):
        b, half = divmod(c, 2)
        dsl = slice(half * 1024, (half + 1) * 1024)
        wdtT_h = np.ascontiguousarray(wdtf[dsl, :].T)
        alog_h = np.ascontiguousarray(
            alog[dsl].reshape(8, 128, 16).transpose(1, 0, 2).reshape(128, 128))
        bdt_h = np.ascontiguousarray(bdt[dsl].reshape(8, 128).T)
        hoffv = np.array([[half * 1024 * T, half * TL, 0, 0]], np.int32)
        m = {"wxT": wxTs, "wdtT": wdtT_h, "alogh": alog_h, "bdth": bdt_h,
             "dfull": dcols, "onesd": ones, "hoffd": hoffv}
        if x2_pc is not None:
            m["x2d"] = x2_pc[c]
        elif x2_fm is not None:
            m["x2d"] = np.ascontiguousarray(x2_fm[b, :, half * TL:(half + 1) * TL])
        in_maps.append(m)
    return in_maps


LAT = 512
NLAT = LAT // 128


def _rope_tables_l1():
    RD = 64
    inv = 1.0 / 10000.0 ** (np.arange(0, RD, 2, dtype=np.float64) / RD)
    t = np.arange(T, dtype=np.float64)
    ang = np.outer(t, inv)
    cosb = np.repeat(np.cos(ang), 2, axis=1)
    sinb = np.repeat(np.sin(ang), 2, axis=1)
    sgn = np.where(np.arange(64) % 2 == 0, -1.0, 1.0)
    sinb = sinb * sgn[None, :]
    cos128 = np.concatenate([cosb, cosb], axis=1).T
    sin128 = np.concatenate([sinb, sinb], axis=1).T
    return cos128.astype(np.float32), sin128.astype(np.float32)


ROPE_PERM = np.empty(64, np.int64)
ROPE_PERM[0::2] = np.arange(32)
ROPE_PERM[1::2] = np.arange(32) + 32


def _perm_rope_cols(w):
    w2 = w.reshape(w.shape[0], 16, 64)[:, :, ROPE_PERM]
    return np.ascontiguousarray(w2.reshape(w.shape[0], 1024))


def _slab(w, kdim, nf):
    return np.ascontiguousarray(w.reshape(kdim, nf, 128).transpose(1, 0, 2))


def _stage_phase_d(inputs, x3_fm=None, x3_pc=None):
    g = lambda k: np.asarray(inputs[f"L1_ffn_{k}"])
    kvdw = _slab(g("kv_d_w"), C, NLAT)
    qdw = _slab(g("q_d_w"), C, NLAT)
    kuw = _slab(g("k_u_w"), LAT, 8)
    quw = _slab(g("q_u_w"), LAT, 8)
    vuw = np.ascontiguousarray(g("v_u_w"))
    rkw = _slab(_perm_rope_cols(g("rope_k_w")), C, 8)
    rqw = _slab(_perm_rope_cols(g("rope_q_w")), LAT, 8)
    oww = _slab(g("o_w"), C, NCH)
    l1w = np.ascontiguousarray(g("ln1_w").reshape(16, 128).T)
    cosf, sinf = _rope_tables_l1()
    ones = np.ones((128, 1), np.float32).astype(BF)
    masks = [_mask_l0(False).astype(BF), _mask_l0(True).astype(BF)]
    in_maps = []
    for c in range(8):
        b, j = divmod(c, 2)
        idx = _tok_idx(j)
        m = {"kvdw": kvdw, "qdw": qdw, "kuw": kuw, "quw": quw, "vuw": vuw,
             "rkw": rkw, "rqw": rqw, "oww": oww, "l1wd": l1w,
             "cos1": np.ascontiguousarray(cosf[:, idx]).astype(BF),
             "sin1": np.ascontiguousarray(sinf[:, idx]).astype(BF),
             "mask0": masks[j], "onesd": ones,
             "identd": np.eye(128, dtype=np.float32).astype(BF)}
        if x3_pc is not None:
            m["x3d"] = x3_pc[c]
        elif x3_fm is not None:
            m["x3d"] = np.ascontiguousarray(x3_fm[b, :, cols])
        in_maps.append(m)
    return in_maps


IEXP = 1024
NIT = IEXP // 128
NEXP = 8


def _stage_phase_e(inputs, h_fm=None, x3_fm=None, h_pc=None, x3_pc=None):
    g = lambda k: np.asarray(inputs[f"L1_ffn_{k}"])
    shg = _slab(g("sh_gate"), C, NIT)
    shu = _slab(g("sh_up"), C, NIT)
    shd = _slab(g("sh_down"), IEXP, NCH)
    rgs = np.stack([_slab(g("r_gate")[e], C, NIT) for e in range(7)])
    rus = np.stack([_slab(g("r_up")[e], C, NIT) for e in range(7)])
    rds = np.stack([_slab(g("r_down")[e], IEXP, NCH) for e in range(7)])
    rwTs = np.ascontiguousarray(g("router_w"))
    rbs = np.ascontiguousarray(g("router_b").reshape(7, 1))
    l2w = np.ascontiguousarray(g("ln2_w").reshape(16, 128).T)
    ones = np.ones((128, 1), np.float32).astype(BF)
    in_maps = []
    for c in range(8):
        b, j = divmod(c, 2)
        cols = slice(j * TL, (j + 1) * TL)
        m = {"shg": shg, "shu": shu, "shd": shd, "rg": rgs, "ru": rus, "rd": rds,
             "rwT": rwTs, "rbd": rbs, "l2wd": l2w, "onesd": ones}
        if h_pc is not None:
            m["hdd"] = h_pc[c]
            m["x3d"] = x3_pc[c]
        elif h_fm is not None:
            m["hdd"] = np.ascontiguousarray(h_fm[b, :, cols])
            m["x3d"] = np.ascontiguousarray(x3_fm[b, :, cols])
        in_maps.append(m)
    return in_maps


def emit_prelude(nc, S, ctx, D):
    w1s, w3s, w2s = D["w1s"], D["w3s"], D["w2s"]
    abscol, abstmp = D["abscol"], D["abstmp"]
    pwf = [sbt(ctx, nc, f"pwf{i}", [128, C], F32) for i in range(2)]
    pab = sbt(ctx, nc, "pab", [128, C], BF16)
    S.op("dve", lambda e: e.memset(abscol[:, :], 0.0), writes=["abscol"])
    ti = 0
    for (srcd, gi, nel) in ((w1s, 0, NHT * C * 128), (w3s, 1, NHT * C * 128),
                            (w2s, 2, NCH * HIDP * 128)):
        ntile = nel // (128 * C)
        for t_ in range(ntile):
            wk = f"pwf{ti % 2}"
            S.op("sp" if ti % 2 == 0 else "pool",
                 (lambda e, srcd=srcd, t_=t_, ti=ti: e.dma_start(
                     pwf[ti % 2][:, :],
                     bass.AP(srcd, t_ * 128 * C, [[C, 128], [1, C]]))),
                 writes=[wk], dma=True, tag="pwl")
            S.op("act", (lambda e, ti=ti: e.activation(
                pab[:, :], pwf[ti % 2][:, :], AF.Abs,
                accum_out=abstmp[:, 0:1])),
                reads=[wk], writes=["pab", "abstmp"], tag="pabs")
            S.op("dve", (lambda e, gi=gi: e.tensor_tensor(
                abscol[:, gi:gi + 1], abscol[:, gi:gi + 1], abstmp[:, 0:1], ALU.add)),
                reads=["abstmp", "abscol"], writes=["abscol"], tag="pacc")
            ti += 1



def emit_phase_a(nc, S, ctx, D):
    x = D["x"]
    wqkT = D["wqkT"]
    wvT = D["wvT"]
    woT = D["woT"]
    cos0 = D["cos0"]
    sin0 = D["sin0"]
    mask0 = D["mask0"]
    onesd = D["onesd"]
    identd = D["identd"]
    x1 = D["x1i"]
    k_in = D["k_in"]
    k_out = D["k_out"]
    v_in = D["v_in"]
    v_out = D["v_out"]
    lnb = D["lnb"]
    xb = [sbt(ctx, nc, f"xb{i}", [128, TL], BF16) for i in range(NCH)]
    qt = [sbt(ctx, nc, f"qt{i}", [128, TL], BF16) for i in range(NH)]
    ot = [sbt(ctx, nc, f"ot{i}", [128, TL], BF16) for i in range(NH)]
    sqt2 = [sbt(ctx, nc, f"sqt2_{i}", [128, TL], BF16) for i in range(2)]
    sqt = sqt2[0]
    wslab = [sbt(ctx, nc, f"wslab{i}", [128, C], BF16) for i in range(2)]
    ktile = [sbt(ctx, nc, f"ktile{i}", [128, TL], BF16) for i in range(2)]
    shuf2 = [sbt(ctx, nc, f"shuf2_{i}", [128, TL], BF16) for i in range(2)]
    cosT = sbt(ctx, nc, "cosT", [128, TL], BF16)
    sinT = sbt(ctx, nc, "sinT", [128, TL], BF16)
    maskT = sbt(ctx, nc, "maskT", [128, MASK_COLS], BF16)
    ones_bf = sbt(ctx, nc, "ones_bf", [128, 1], BF16)
    ident = sbt(ctx, nc, "ident", [128, 128], BF16)
    khead = [sbt(ctx, nc, f"khead{i}", [128, T], BF16) for i in range(2)]
    vhead = [sbt(ctx, nc, f"vhead{i}", [128, T], BF16) for i in range(2)]
    ptile2 = [sbt(ctx, nc, f"ptile{i}", [128, TL], BF16) for i in range(2)]
    vexp = sbt(ctx, nc, "vexp", [128, TL], BF16)

    arep = sbt(ctx, nc, "arep", [128, TL], F32)
    brep = sbt(ctx, nc, "brep", [128, TL], F32)
    rrep2 = [arep, brep]
    xres = [sbt(ctx, nc, "xres0", [128, TL], F32)]
    x1b = [sbt(ctx, nc, f"x1b{i}", [128, TL], F32) for i in range(2)]
    rows = {k: sbt(ctx, nc, k, [1, TL], F32) for k in
            ("srow", "qrow", "brow")}
    rrow2 = [sbt(ctx, nc, f"rrow2_{i}", [1, TL], F32) for i in range(2)]

    ps4 = pst(ctx, nc, "ps4", [128, 512])
    ps0 = pst(ctx, nc, "ps0", [128, 512])
    ps1 = pst(ctx, nc, "ps1", [128, 512])
    ps2 = pst(ctx, nc, "ps2", [128, 512])
    ps3 = pst(ctx, nc, "ps3", [128, 512])
    pr0 = pst(ctx, nc, "pr0", [128, 512])
    pr1 = pst(ctx, nc, "pr1", [128, 512])
    psb = pst(ctx, nc, "psb", [128, 1024], BF16)
    PS = {0: ps0, 1: ps1, 2: ps2, 3: ps3}

    E = {"ones_bf": ones_bf, "sqt": sqt, "sqt_key": "sqtp0", "rows": rows,
         "lnb": lnb, "arep": arep, "brep": brep,
         "ps0": ps0, "ps1": ps1, "pr0": pr0, "pr1": pr1}

    for dc in range(NCH):
        S.op("pool", (lambda e, dc=dc: e.dma_start(
            xb[dc][:, :], x[dc * 128:(dc + 1) * 128, :])),
            writes=[f"xb{dc}"], dma=True)
    S.op("sp", lambda e: e.dma_start(cosT[:, :], cos0[:, :]), writes=["cosT"], dma=True)
    S.op("sp", lambda e: e.dma_start(sinT[:, :], sin0[:, :]), writes=["sinT"], dma=True)
    S.op("sp", lambda e: e.dma_start(maskT[:, :], mask0[:, :]), writes=["maskT"], dma=True)
    S.op("sp", lambda e: e.dma_start(ones_bf[:, :], onesd[:, :]), writes=["ones"], dma=True)
    S.op("sp", lambda e: e.dma_start(ident[:, :], identd[:, :]), writes=["ident"], dma=True)

    xkeys = [f"xb{i}" for i in range(NCH)]
    emit_layernorm(S, E, xkeys, xb, 0)

    def proj_head(src_dram, f_idx, dest_tile, dest_key, do_rope):
        par = f_idx % 2
        wk = f"wslab{par}"
        shf, shfk = shuf2[par], f"shuf{par}"
        sq, sqk = sqt2[par], f"sqtp{par}"
        rro, rrok = rrow2[par], f"rrow{par}"
        rrp, rrpk = rrep2[par], f"rrep{par}"
        S.op("pool", (lambda e: e.dma_start(
            wslab[par][:, :],
            bass.AP(src_dram, f_idx * C * 128,
                    [[128, 128], [128 * 128, NCH], [1, 128]]))),
            writes=[wk], dma=True)
        for tc in range(NT):
            pk = ("ps2", "ps3", "ps4")[(f_idx * NT + tc) % 3]
            pt = E_PS[pk]
            for dc in range(NCH):
                w = [pk] if (dc == 0 or dc == NCH - 1) else []
                S.op("pe", (lambda e, tc=tc, dc=dc, pt=pt: e.matmul(
                    pt[:, :], wslab[par][:, dc * 128:(dc + 1) * 128],
                    xb[dc][:, tc * 512:(tc + 1) * 512],
                    start=(dc == 0), stop=(dc == NCH - 1))),
                    reads=[wk, f"xb{dc}"], writes=w, tag=f"pj{f_idx}")
            if par == 0:
                S.op("dve", (lambda e, tc=tc, pt=pt: e.tensor_copy(
                    dest_tile[:, tc * 512:(tc + 1) * 512], pt[:, :])),
                    reads=[pk], writes=[dest_key], tag="pjev")
            else:
                S.op("act", (lambda e, tc=tc, pt=pt: e.copy(
                    dest_tile[:, tc * 512:(tc + 1) * 512], pt[:, :])),
                    reads=[pk], writes=[dest_key], tag="pjev")
        if not do_rope:
            return
        mask32 = [i ^ 1 for i in range(32)]
        S.op("dve", lambda e: e.stream_shuffle(shf[:, :], dest_tile[:, :], mask32),
             reads=[dest_key], writes=[shfk])
        S.op("dve", lambda e: e.tensor_tensor(dest_tile[:, :], dest_tile[:, :], cosT[:, :], ALU.mult),
             reads=[dest_key, "cosT"], writes=[dest_key])
        S.op("dve", lambda e: e.tensor_tensor(shf[:, :], shf[:, :], sinT[:, :], ALU.mult),
             reads=[shfk, "sinT"], writes=[shfk])
        S.op("dve", lambda e: e.tensor_tensor(dest_tile[:, :], dest_tile[:, :], shf[:, :], ALU.add),
             reads=[dest_key, shfk], writes=[dest_key])
        S.op("act", lambda e: e.activation(sq[:, :], dest_tile[:, :], AF.Square),
             reads=[dest_key], writes=[sqk])
        for tc in range(NT):
            prk = "pr0" if tc == 0 else "pr1"
            prt = E[prk]
            S.op("pe", (lambda e, tc=tc, prt=prt: e.matmul(
                prt[0:1, :], ones_bf[:, 0:1], sq[:, tc * 512:(tc + 1) * 512],
                start=True, stop=True)),
                reads=[sqk, "ones"], writes=[prk])
            S.op("dve", (lambda e, tc=tc, prt=prt: e.tensor_copy(
                rro[0:1, tc * 512:(tc + 1) * 512], prt[0:1, :])),
                reads=[prk], writes=[rrok])
        emit_rstd_row(S, E, rro, rrok, 1.0 / 128, RMS_EPS0)
        S.op("sp", (lambda e: e.dma_start(
            bass.AP(lnb, (2 + par) * TL, [[TL, 1], [1, TL]]), rro[:, :])),
            reads=[rrok], writes=[f"lnb{2 + par}"], dma=True)
        S.op("sp", (lambda e: e.dma_start(
            rrp[:, :], bass.AP(lnb, (2 + par) * TL, [[0, 128], [1, TL]]))),
            reads=[f"lnb{2 + par}"], writes=[rrpk], dma=True)
        S.op("dve", lambda e: e.tensor_tensor(dest_tile[:, :], dest_tile[:, :], rrp[:, :], ALU.mult),
             reads=[dest_key, rrpk], writes=[dest_key])

    E_PS = {"ps2": ps2, "ps3": ps3, "ps4": ps4}
    for h in range(NH):
        proj_head(wqkT, h, qt[h], f"qt{h}", True)
    for h in range(NH):
        kb = ktile[h % 2]
        proj_head(wqkT, 16 + h, kb, f"ktile{h % 2}", True)
        S.op("sp", (lambda e, h=h, kb=kb: e.dma_start(
            k_in[h * 128:(h + 1) * 128, :], kb[:, :])),
            reads=[f"ktile{h % 2}"], writes=[f"k_in{h}"], dma=True)

    S.op("pool", (lambda e: e.collective_compute(
        "AllGather", ALU.bypass, replica_groups=PAIRS,
        ins=[k_in.ap().opt()], outs=[k_out.ap().opt()])),
        reads=[f"k_in{h}" for h in range(NH)], writes=["k_out"], tag="agK")

    for h in range(NH):
        proj_head(wvT, h, ktile[0], "ktile0", False)
        for tt in range(8):
            S.op("pe", (lambda e, tt=tt: e.transpose(
                psb[:, tt * 128:(tt + 1) * 128],
                ktile[0][:, tt * 128:(tt + 1) * 128], ident[:, :])),
                reads=["ktile0", "ident"], writes=["psb"], tag="vT")
        S.op("dve", lambda e: e.tensor_copy(vexp[:, :], psb[:, :]),
             reads=["psb"], writes=["vexp"])
        S.op("sp", (lambda e, h=h: e.dma_start(
            bass.AP(v_in, h * 128, [[C, 128], [128 * C, 8], [1, 128]]),
            vexp[:, :])),
            reads=["vexp"], writes=[f"v_in{h}"], dma=True)

    S.op("pool", (lambda e: e.collective_compute(
        "AllGather", ALU.bypass, replica_groups=PAIRS,
        ins=[v_in.ap().opt()], outs=[v_out.ap().opt()])),
        reads=[f"v_in{h}" for h in range(NH)], writes=["v_out"], tag="agV")

    AV_LAST = {0: 15, 1: 7}
    for h in range(NH):
        kb, kk = khead[h % 2], f"khead{h % 2}"
        vb, vk = vhead[h % 2], f"vhead{h % 2}"
        for r_ in range(2):
            S.op("sp", (lambda e, h=h, kb=kb, r_=r_: e.dma_start(
                bass.AP(kb, (1 - r_) * 128, [[T, 128], [256, 8], [1, 128]]),
                bass.AP(k_out, r_ * C * TL + h * 128 * TL + 7 * 128,
                        [[TL, 128], [-128, 8], [1, 128]]))),
                reads=["k_out", "v_out"], writes=[kk], dma=True)
            S.op("sp", (lambda e, h=h, vb=vb, r_=r_: e.dma_start(
                bass.AP(vb, (1 - r_) * 128, [[T, 128], [256, 8], [1, 128]]),
                bass.AP(v_out, r_ * TL * C + 7 * 128 * C + h * 128,
                        [[C, 128], [-128 * C, 8], [1, 128]]))),
                reads=["v_out", "k_out"], writes=[vk], dma=True)
        first = [True, True]
        for kt in range(NKT):
            wd = _wid(kt)
            msegs_kt = [(l_, off_) for (mk_, l_, off_) in MSEGS if mk_ == kt]
            chunks = []
            c0 = 0
            while c0 < wd:
                chunks.append((c0, min(c0 + 512, wd)))
                c0 = min(c0 + 512, wd)
            for ci, (c0, c1) in enumerate(chunks):
                spk = ("ps2", "ps3", "ps4")[(kt + ci) % 3]
                sps = E_PS[spk]
                S.op("pe", (lambda e, kt=kt, c0=c0, c1=c1, sps=sps, kb=kb, h=h: e.matmul(
                    sps[:, 0:c1 - c0], kb[:, kt * 128:(kt + 1) * 128],
                    qt[h][:, c0:c1], start=True, stop=True)),
                    reads=[kk, f"qt{h}"], writes=[spk], tag=f"sc{h}")
                for (ml_, moff_) in msegs_kt:
                    a0 = ml_ * 128
                    if a0 >= c0 and a0 + 128 <= c1:
                        S.op("dve", (lambda e, a0=a0, c0=c0, sps=sps, moff_=moff_: e.tensor_tensor(
                            sps[:, a0 - c0:a0 - c0 + 128], sps[:, a0 - c0:a0 - c0 + 128],
                            maskT[:, moff_:moff_ + 128], ALU.add)),
                            reads=[spk, "maskT"], writes=[spk], tag="msk")
                S.op("act", (lambda e, c0=c0, c1=c1, sps=sps, kt=kt: e.activation(
                    ptile2[kt % 2][:, c0:c1], sps[:, 0:c1 - c0], AF.Exp, scale=ISQ)),
                    reads=[spk], writes=[f"ptile{kt % 2}"], tag="exp")
            b0 = 0
            while b0 < wd:
                blk_i = b0 // 512
                b1 = min((blk_i + 1) * 512, wd)
                opk = "ps0" if blk_i == 0 else "ps1"
                opt_ = PS[blk_i]
                rpk = "pr0" if blk_i == 0 else "pr1"
                rpt = E[rpk]
                wo_ = [opk] if (first[blk_i] or kt == AV_LAST[blk_i]) else []
                wr_ = [rpk] if (first[blk_i] or kt == AV_LAST[blk_i]) else []
                S.op("pe", (lambda e, kt=kt, b0=b0, b1=b1, blk_i=blk_i, vb=vb,
                            opt_=opt_, f=first[blk_i]: e.matmul(
                    opt_[:, b0 - blk_i * 512:b1 - blk_i * 512],
                    vb[:, kt * 128:(kt + 1) * 128], ptile2[kt % 2][:, b0:b1],
                    start=f, stop=(kt == AV_LAST[blk_i]))),
                    reads=[vk, f"ptile{kt % 2}"], writes=wo_, tag=f"av{h}")
                S.op("pe", (lambda e, kt=kt, b0=b0, b1=b1, blk_i=blk_i,
                            rpt=rpt, f=first[blk_i]: e.matmul(
                    rpt[0:1, b0 - blk_i * 512:b1 - blk_i * 512],
                    ones_bf[:, 0:1], ptile2[kt % 2][:, b0:b1],
                    start=f, stop=(kt == AV_LAST[blk_i]))),
                    reads=[f"ptile{kt % 2}", "ones"], writes=wr_, tag=f"rs{h}")
                first[blk_i] = False
                b0 = b1
        par = h % 2
        rro, rrok = rrow2[par], f"rrow{par}"
        rrp, rrpk = rrep2[par], f"rrep{par}"
        for blk_i in range(2):
            rpk = "pr0" if blk_i == 0 else "pr1"
            rpt = E[rpk]
            S.op("dve", (lambda e, blk_i=blk_i, rpt=rpt, rro=rro: e.tensor_copy(
                rro[0:1, blk_i * 512:(blk_i + 1) * 512], rpt[0:1, :])),
                reads=[rpk], writes=[rrok])
        S.op("dve", (lambda e, rro=rro: e.reciprocal(rro[:, :], rro[:, :])),
             reads=[rrok], writes=[rrok])
        S.op("sp", (lambda e, rro=rro: e.dma_start(
            bass.AP(lnb, (4 + par) * TL, [[TL, 1], [1, TL]]), rro[:, :])),
            reads=[rrok], writes=[f"lnb{4 + par}"], dma=True)
        S.op("sp", (lambda e, rrp=rrp: e.dma_start(
            rrp[:, :], bass.AP(lnb, (4 + par) * TL, [[0, 128], [1, TL]]))),
            reads=[f"lnb{4 + par}"], writes=[rrpk], dma=True)
        for blk_i in range(2):
            opk = "ps0" if blk_i == 0 else "ps1"
            opt_ = PS[blk_i]
            q_ = "dve" if blk_i == 0 else "act"
            if q_ == "dve":
                S.op("dve", (lambda e, blk_i=blk_i, h=h, opt_=opt_: e.tensor_copy(
                    ot[h][:, blk_i * 512:(blk_i + 1) * 512], opt_[:, :])),
                    reads=[opk], writes=[f"ot{h}"], tag="oev")
            else:
                S.op("act", (lambda e, blk_i=blk_i, h=h, opt_=opt_: e.copy(
                    ot[h][:, blk_i * 512:(blk_i + 1) * 512], opt_[:, :])),
                    reads=[opk], writes=[f"ot{h}"], tag="oev")
        S.op("dve", (lambda e, h=h, rrp=rrp: e.tensor_tensor(
            ot[h][:, :], ot[h][:, :], rrp[:, :], ALU.mult)),
            reads=[f"ot{h}", rrpk], writes=[f"ot{h}"], tag="onrm")

    for o in range(NCH):
        wk = f"wslab{o % 2}"
        S.op("pool", (lambda e, o=o: e.dma_start(
            wslab[o % 2][:, :],
            bass.AP(woT, o * C * 128, [[128, 128], [128 * 128, NCH], [1, 128]]))),
            writes=[wk], dma=True)
        S.op("sp", (lambda e, o=o: e.dma_start(
            xres[0][:, :], x[o * 128:(o + 1) * 128, :])),
            writes=["xres0"], dma=True)
        for tc in range(NT):
            pk = "ps2" if tc == 0 else "ps3"
            pt = E_PS[pk]
            for h in range(NH):
                w = [pk] if (h == 0 or h == NH - 1) else []
                S.op("pe", (lambda e, o=o, tc=tc, h=h, pt=pt: e.matmul(
                    pt[:, :], wslab[o % 2][:, h * 128:(h + 1) * 128],
                    ot[h][:, tc * 512:(tc + 1) * 512],
                    start=(h == 0), stop=(h == NH - 1))),
                    reads=[wk, f"ot{h}"], writes=w, tag=f"wo{o}")
            S.op("dve", (lambda e, o=o, tc=tc, pt=pt: e.tensor_tensor(
                x1b[o % 2][:, tc * 512:(tc + 1) * 512], pt[:, :],
                xres[0][:, tc * 512:(tc + 1) * 512], ALU.add)),
                reads=[pk, "xres0"], writes=[f"x1b{o % 2}"], tag="res")
        S.op("sp", (lambda e, o=o: e.dma_start(
            x1[o * 128:(o + 1) * 128, :], x1b[o % 2][:, :])),
            reads=[f"x1b{o % 2}"], writes=[f"x1st{o % 2}"], dma=True)

    S.op("sp", lambda e: e.nop(), reads=["x1st0", "x1st1"])


def emit_phase_b(nc, S, ctx, D):
    sder = D["sder"]
    x1d = D["x1i"]
    w1s = D["w1s"]
    w3s = D["w3s"]
    w2s = D["w2s"]
    onesd = D["onesd"]
    onesf = D["onesf"]
    x2 = D["x2i"]
    lnb = D["lnb"]

    xb = [sbt(ctx, nc, f"xb{i}", [128, TL], BF16) for i in range(NCH)]
    gt = [sbt(ctx, nc, f"gt{i}", [128, TL], BF16) for i in range(NHT)]
    wf32 = [sbt(ctx, nc, f"wf32_{i}", [128, C], F32) for i in range(2)]
    w2f32 = [sbt(ctx, nc, f"w2f32_{i}", [128, 1408], F32) for i in range(2)]
    qsl = [sbt(ctx, nc, f"qsl{i}", [128, C], BF16) for i in range(2)]
    q2sl = [sbt(ctx, nc, "q2sl0", [128, HIDP], BF16)]
    qtmp = sbt(ctx, nc, "qtmp", [128, 2048], BF16)
    sgt = sbt(ctx, nc, "sgt", [128, 2048], BF16)
    h1t = sbt(ctx, nc, "h1t", [128, TL], BF16)
    sqt = h1t
    onesbf = sbt(ctx, nc, "onesbf", [128, 1], BF16)
    onesf32 = sbt(ctx, nc, "onesf32", [128, 1], F32)
    srow = sbt(ctx, nc, "srow_s", [1, 16], F32)
    scols = sbt(ctx, nc, "scols", [128, 16], F32)
    arep = sbt(ctx, nc, "arep", [128, TL], F32)
    brep = sbt(ctx, nc, "brep", [128, TL], F32)
    x2b = [sbt(ctx, nc, "x2b0", [128, TL], F32)] * 2
    rows = {k: sbt(ctx, nc, k, [1, TL], F32) for k in
            ("srow", "qrow", "brow")}

    ps0 = pst(ctx, nc, "ps0", [128, 512])
    ps1 = pst(ctx, nc, "ps1", [128, 512])
    ps2 = pst(ctx, nc, "ps2", [128, 512])
    ps3 = pst(ctx, nc, "ps3", [128, 512])
    pr0 = pst(ctx, nc, "pr0", [128, 512])
    pr1 = pst(ctx, nc, "pr1", [128, 512])

    E = {"ones_bf": onesbf, "sqt": sqt, "rows": rows, "lnb": lnb,
         "arep": arep, "brep": brep,
         "ps0": ps0, "ps1": ps1, "pr0": pr0, "pr1": pr1}
    E_PS = {"ps0": ps0, "ps1": ps1, "ps2": ps2, "ps3": ps3}

    S.op("sp", lambda e: e.dma_start(onesbf[:, :], onesd[:, :]), writes=["ones"], dma=True)
    S.op("sp", lambda e: e.dma_start(onesf32[:, :], onesf[:, :]), writes=["onesf"], dma=True)
    for dc in range(NCH):
        S.op("pool", (lambda e, dc=dc: e.dma_start(
            xb[dc][:, :], x1d[dc * 128:(dc + 1) * 128, :])),
            writes=[f"xb{dc}"], dma=True)

    abscol = D["abscol"]
    S.op("pe", lambda e: e.matmul(pr0[0:1, 0:3], onesf32[:, 0:1], abscol[:, 0:3],
                                  start=True, stop=True),
         reads=["abscol", "onesf"], writes=["pr0"], tag="absred")
    CNT = {0: HID * C, 1: HID * C, 2: HID * C}
    S.op("dve", lambda e: e.memset(srow[:, :], 0.0), writes=["srow_s"])
    S.op("dve", lambda e: e.tensor_copy(srow[0:1, 0:3], pr0[0:1, 0:3]),
         reads=["pr0", "srow_s"], writes=["srow_s"])
    for gi in range(3):
        S.op("dve", (lambda e, gi=gi: e.tensor_scalar_mul(
            srow[0:1, 4 + gi * 4:5 + gi * 4], srow[0:1, gi:gi + 1],
            0.5 / CNT[gi])), reads=["srow_s"], writes=["srow_s"])
        S.op("dve", (lambda e, gi=gi: e.tensor_scalar_mul(
            srow[0:1, 5 + gi * 4:6 + gi * 4], srow[0:1, gi:gi + 1],
            -0.5 / CNT[gi])), reads=["srow_s"], writes=["srow_s"])
        S.op("dve", (lambda e, gi=gi: e.tensor_scalar_mul(
            srow[0:1, 6 + gi * 4:7 + gi * 4], srow[0:1, gi:gi + 1],
            1.0 / CNT[gi])), reads=["srow_s"], writes=["srow_s"])
        S.op("dve", (lambda e, gi=gi: e.tensor_scalar_mul(
            srow[0:1, 7 + gi * 4:8 + gi * 4], srow[0:1, gi:gi + 1],
            -1.0 / CNT[gi])), reads=["srow_s"], writes=["srow_s"])
    S.op("sp", (lambda e: e.dma_start(
        bass.AP(sder, 0, [[16, 1], [1, 16]]), srow[0:1, 0:16])),
        reads=["srow_s"], writes=["sderd"], dma=True)
    S.op("sp", (lambda e: e.dma_start(
        scols[:, :], bass.AP(sder, 0, [[0, 128], [1, 16]]))),
        reads=["sderd"], writes=["scols"], dma=True)

    xkeys = [f"xb{i}" for i in range(NCH)]
    emit_layernorm(S, E, xkeys, xb, 0)

    def quant_slab(f32buf, f32key, bfbuf, bfkey, gi, width, off=0):
        for h0 in range(0, width, 2048):
            h1 = min(h0 + 2048, width)
            wdt_ = h1 - h0
            S.op("act", (lambda e, h0=h0, wdt_=wdt_: e.activation(
                sgt[:, 0:wdt_], f32buf[:, h0:h0 + wdt_], AF.Sign,
                bias=scols[:, 5 + gi * 4:6 + gi * 4])),
                reads=[f32key, "scols"], writes=["sgt"], tag="qs1")
            S.op("act", (lambda e, h0=h0, wdt_=wdt_: e.activation(
                qtmp[:, 0:wdt_], f32buf[:, h0:h0 + wdt_], AF.Sign,
                bias=scols[:, 4 + gi * 4:5 + gi * 4])),
                reads=[f32key, "scols"], writes=["qtmp"], tag="qs2")
            S.op("dve", (lambda e, wdt_=wdt_: e.tensor_tensor(
                qtmp[:, 0:wdt_], qtmp[:, 0:wdt_], sgt[:, 0:wdt_], ALU.add)),
                reads=["qtmp", "sgt"], writes=["qtmp"], tag="qadd")
            S.op("dve", (lambda e, h0=h0, wdt_=wdt_: e.tensor_scalar_mul(
                bfbuf[:, off + h0:off + h0 + wdt_], qtmp[:, 0:wdt_],
                scols[:, 4 + gi * 4:5 + gi * 4])),
                reads=["qtmp", "scols"], writes=[bfkey], tag="qmul")

    for f in range(NHT):
        for wi, (src, gi) in enumerate([(w1s, 0), (w3s, 1)]):
            wk = f"wf{wi}"
            qk = f"qsl{wi}"
            S.op("sp", (lambda e, src=src, f=f, buf=wf32[wi]: e.dma_start(
                buf[:, :], bass.AP(src, f * C * 128,
                                   [[128, 128], [128 * 128, NCH], [1, 128]]))),
                writes=[wk], dma=True)
            quant_slab(wf32[wi], wk, qsl[wi], qk, gi, C)
            for tc in range(NT):
                pk = ["ps0", "ps1"][tc] if wi == 0 else ["ps2", "ps3"][tc]
                pt = E_PS[pk]
                for dc in range(NCH):
                    w = [pk] if (dc == 0 or dc == NCH - 1) else []
                    S.op("pe", (lambda e, tc=tc, dc=dc, pt=pt, wi=wi: e.matmul(
                        pt[:, :], qsl[wi][:, dc * 128:(dc + 1) * 128],
                        xb[dc][:, tc * 512:(tc + 1) * 512],
                        start=(dc == 0), stop=(dc == NCH - 1))),
                        reads=[qk, f"xb{dc}"], writes=w, tag=f"h{wi}")
            if wi == 0:
                for tc in range(NT):
                    pk = ["ps0", "ps1"][tc]
                    S.op("act", (lambda e, tc=tc, pt=E_PS[pk]: e.activation(
                        h1t[:, tc * 512:(tc + 1) * 512], pt[:, :], AF.Silu)),
                        reads=[pk], writes=["h1t"], tag="silu")
        for tc in range(NT):
            pk = ["ps2", "ps3"][tc]
            S.op("dve", (lambda e, tc=tc, f=f, pt=E_PS[pk]: e.tensor_tensor(
                gt[f][:, tc * 512:(tc + 1) * 512],
                h1t[:, tc * 512:(tc + 1) * 512], pt[:, :], ALU.mult)),
                reads=["h1t", pk], writes=[f"gt{f}"], tag="gmul")

    W2Q = ((0, 11), (11, 11), (22, 11), (33, 10))
    for o in range(NCH):
        for hh, (c0, ncc) in enumerate(W2Q):
            wk = f"w2f{hh % 2}"
            qk = f"q2h{hh}"
            S.op("sp", (lambda e, o=o, buf=w2f32[hh % 2], c0=c0, ncc=ncc: e.dma_start(
                buf[:, 0:ncc * 128],
                bass.AP(w2s, o * HIDP * 128 + c0 * 128 * 128,
                        [[128, 128], [128 * 128, ncc], [1, 128]]))),
                writes=[wk], dma=True)
            quant_slab(w2f32[hh % 2], wk, q2sl[0], qk, 2, ncc * 128, off=c0 * 128)
        S.op("sp", (lambda e, o=o: e.dma_start(
            x2b[0][:, :], x1d[o * 128:(o + 1) * 128, :])),
            writes=["x2b0"], dma=True)
        for tc in range(NT):
            pk = ["ps0", "ps1"][tc]
            pt = E_PS[pk]
            for hc in range(NHT):
                w = [pk] if (hc == 0 or hc == NHT - 1) else []
                qkh = f"q2h{min(hc // 11, 3)}"
                S.op("pe", (lambda e, o=o, tc=tc, hc=hc, pt=pt: e.matmul(
                    pt[:, :], q2sl[0][:, hc * 128:(hc + 1) * 128],
                    gt[hc][:, tc * 512:(tc + 1) * 512],
                    start=(hc == 0), stop=(hc == NHT - 1))),
                    reads=[qkh, f"gt{hc}"], writes=w, tag=f"w2_{o}")
            S.op("dve", (lambda e, o=o, tc=tc, pt=pt: e.tensor_tensor(
                x2b[0][:, tc * 512:(tc + 1) * 512], pt[:, :],
                x2b[0][:, tc * 512:(tc + 1) * 512], ALU.add)),
                reads=[pk, "x2b0"], writes=["x2b0"], tag="res")
        S.op("sp", (lambda e, o=o: e.dma_start(
            x2[o * 128:(o + 1) * 128, :], x2b[0][:, :])),
            reads=["x2b0"], writes=[f"x2st{o % 2}"], dma=True)

    S.op("sp", lambda e: e.nop(), reads=["x2st0", "x2st1"])


def emit_phase_c(nc, S, ctx, D):
    drfg = D["drfg"]
    bcg = D["bcg"]
    e2xg = D["e2xg"]
    e3b = D["e3b"]
    x2d = D["x2i"]
    wxT = D["wxT"]
    wdtT = D["wdtT"]
    alogh = D["alogh"]
    bdth = D["bdth"]
    dfull = D["dfull"]
    onesd = D["onesd"]
    hoffd = D["hoffd"]
    x3 = D["x3i"]
    e2in = D["e2in"]
    e2out = D["e2out"]
    e2inB = D["e2inB"]
    e2outB = D["e2outB"]
    e3in = D["e3in"]
    e3out = D["e3out"]
    lnb = D["lnb"]

    xb = [sbt(ctx, nc, f"xb{i}", [128, TL], BF16) for i in range(NCH)]
    dlt = [sbt(ctx, nc, f"dlt{i}", [128, T], BF16) for i in range(8)]
    wxt = sbt(ctx, nc, "wxt", [128, NCH * 160], BF16)
    wdt = sbt(ctx, nc, "wdt", [128, 1024], BF16)
    at = sbt(ctx, nc, "at", [128, 128], F32)
    bcol = sbt(ctx, nc, "bcol", [128, 8], F32)
    dcol = sbt(ctx, nc, "dcol", [128, 16], F32)
    hoff = sbt(ctx, nc, "hoff", [1, 4], mybir.dt.int32)
    drf = sbt(ctx, nc, "drf", [128, T], BF16)
    bcexp = sbt(ctx, nc, "bcexp", [32, TL], BF16)
    dAs2 = [sbt(ctx, nc, f"dAs{i}", [128, NSSM * TCH], BF16) for i in range(2)]
    sqt = dAs2[0]
    zt2 = dAs2[0]
    bus = sbt(ctx, nc, "bus", [128, NSSM * TCH], BF16)
    brep = sbt(ctx, nc, "brep_c", [128, NSSM * TCH], BF16)
    crep = sbt(ctx, nc, "crep_c", [128, NSSM * TCH], BF16)
    xmall = sbt(ctx, nc, "xmall", [128, 8 * TCH], BF16)
    dxt2 = [sbt(ctx, nc, f"dxt{i}", [128, TCH], BF16) for i in range(2)]
    stateb = [sbt(ctx, nc, f"state{i}", [128, NSSM], BF16) for i in range(8)]
    c0t = sbt(ctx, nc, "c0t", [128, NSSM], BF16)
    ych2 = [sbt(ctx, nc, f"ych{i}", [128, TCH], F32) for i in range(2)]
    yba = sbt(ctx, nc, "yba", [128, 4 * TL], BF16)
    zt = sbt(ctx, nc, "zt", [128, 512], F32)
    onesbf = sbt(ctx, nc, "onesbf", [128, 1], BF16)
    arep = sbt(ctx, nc, "arep", [128, TL], F32)
    brepl = sbt(ctx, nc, "brepl", [128, TL], F32)
    x3b = [sbt(ctx, nc, f"x3b{i}", [128, 512], F32) for i in range(2)]
    rows = {k: sbt(ctx, nc, k, [1, TL], F32) for k in ("srow", "qrow", "brow")}

    ps0 = pst(ctx, nc, "ps0", [128, 512])
    ps1 = pst(ctx, nc, "ps1", [128, 512])
    ps2 = pst(ctx, nc, "ps2", [128, 512])
    ps3 = pst(ctx, nc, "ps3", [128, 512])
    pr0 = pst(ctx, nc, "pr0", [128, 512])
    pr1 = pst(ctx, nc, "pr1", [128, 512])
    E = {"ones_bf": onesbf, "sqt": sqt, "sqt_key": "dAs0", "rows": rows,
         "lnb": lnb, "arep": arep, "brep": brepl,
         "ps0": ps0, "ps1": ps1, "pr0": pr0, "pr1": pr1}
    PSL = [ps0, ps1, ps2, ps3]

    _rctx = D.get("_octx", ctx)
    rgA = _rctx.enter_context(nc.gpsimd.register(_uniq("rgA")))
    rgB = _rctx.enter_context(nc.gpsimd.register(_uniq("rgB")))
    rgT = _rctx.enter_context(nc.gpsimd.register(_uniq("rgT")))


    S.op("sp", lambda e: e.dma_start(onesbf[:, :], onesd[:, :]), writes=["ones"], dma=True)
    S.op("sp", lambda e: e.dma_start(hoff[:, :], hoffd[:, :]), writes=["hoff"], dma=True)
    S.op("pool", lambda e: e.dma_start(
        wxt[:, :], bass.AP(wxT, 0, [[160, 128], [128 * 160, NCH], [1, 160]])),
        writes=["wxt"], dma=True)
    S.op("pool", lambda e: e.dma_start(wdt[:, :], wdtT[:, :]), writes=["wdt"], dma=True)
    S.op("sp", lambda e: e.dma_start(at[:, :], alogh[:, :]), writes=["at"], dma=True)
    S.op("sp", lambda e: e.dma_start(bcol[:, :], bdth[:, :]), writes=["bcol"], dma=True)
    S.op("sp", lambda e: e.dma_start(dcol[:, :], dfull[:, :]), writes=["dcol"], dma=True)
    for dc in range(NCH):
        S.op("pool", (lambda e, dc=dc: e.dma_start(
            xb[dc][:, :], x2d[dc * 128:(dc + 1) * 128, :])),
            writes=[f"xb{dc}"], dma=True)
    S.op("act", lambda e: e.activation(at[:, :], at[:, :], AF.Exp),
         reads=["at"], writes=["at"])
    S.op("dve", lambda e: e.tensor_scalar_mul(at[:, :], at[:, :], -1.0),
         reads=["at"], writes=["at"])
    S.op("pool", lambda e: (e.reg_load(rgA, hoff[0:1, 0:1]),
                            e.reg_load(rgB, hoff[0:1, 1:2]),
                            e.nop())[-1],
         reads=["hoff"], writes=["regs"], tag="regload")

    xkeys = [f"xb{i}" for i in range(NCH)]
    emit_layernorm(S, E, xkeys, xb, 0)

    for tc in range(NT):
        for grp, (m0, mw, pk_) in enumerate([(0, 128, ("ps0", "ps1")),
                                             (128, 32, ("ps2", "ps3"))]):
            pk = pk_[tc]
            pt = E[pk] if pk in E else PSL[int(pk[2])]
            for dc in range(NCH):
                w = [pk] if (dc == 0 or dc == NCH - 1) else []
                S.op("pe", (lambda e, tc=tc, dc=dc, m0=m0, mw=mw, pt=pt: e.matmul(
                    pt[0:mw, :], wxt[:, dc * 160 + m0:dc * 160 + m0 + mw],
                    xb[dc][:, tc * 512:(tc + 1) * 512],
                    start=(dc == 0), stop=(dc == NCH - 1))),
                    reads=["wxt", f"xb{dc}"], writes=w, tag=f"xz{grp}")
        S.op("dve", (lambda e, tc=tc, pt=PSL[tc]: e.tensor_copy(
            drf[:, tc * 512:(tc + 1) * 512], pt[:, :])),
            reads=[("ps0", "ps1")[tc]], writes=["drf"], tag="drev")
        S.op("dve", (lambda e, tc=tc, pt=PSL[2 + tc]: e.tensor_copy(
            bcexp[0:32, tc * 512:(tc + 1) * 512], pt[0:32, :])),
            reads=[("ps2", "ps3")[tc]], writes=["bcexp"], tag="bcev")

    S.op("sp", lambda e: e.dma_start(e2in[0:128, :], drf[:, 0:TL]),
         reads=["drf"], writes=["e2_dr"], dma=True)
    S.op("sp", lambda e: e.dma_start(e2in[128:160, :], bcexp[0:32, :]),
         reads=["bcexp"], writes=["e2_bc"], dma=True)
    S.op("pool", (lambda e: e.collective_compute(
        "AllGather", ALU.bypass, replica_groups=PAIRS,
        ins=[e2in.ap().opt()], outs=[e2out.ap().opt()])),
        reads=["e2_dr", "e2_bc"], writes=["e2out"], tag="agE2")
    for dc in range(NCH):
        S.op("sp", (lambda e, dc=dc: e.dma_start(
            e2inB[dc * 128:(dc + 1) * 128, :], xb[dc][:, :])),
            reads=[f"xb{dc}"], writes=[f"e2_x{dc}"], dma=True)
    S.op("pool", (lambda e: e.collective_compute(
        "AllGather", ALU.bypass, replica_groups=PAIRS,
        ins=[e2inB.ap().opt()], outs=[e2outB.ap().opt()])),
        reads=[f"e2_x{dc}" for dc in range(NCH)],
        writes=["e2outB"], tag="agE2b")

    for r_ in range(2):
        S.op("sp", (lambda e, r_=r_: e.dma_start(
            bass.AP(drfg, (1 - r_) * 128, [[T, 128], [256, 8], [1, 128]]),
            bass.AP(e2out, r_ * 160 * TL + 7 * 128,
                    [[TL, 128], [-128, 8], [1, 128]]))),
            reads=["e2out"], writes=["drfg"], dma=True)
        S.op("sp", (lambda e, r_=r_: e.dma_start(
            bass.AP(bcg, (1 - r_) * 128, [[T, 32], [256, 8], [1, 128]]),
            bass.AP(e2out, (r_ * 160 + 128) * TL + 7 * 128,
                    [[TL, 32], [-128, 8], [1, 128]]))),
            reads=["e2out"], writes=["bcg"], dma=True)
    S.op("sp", lambda e: e.dma_start(drf[:, :], drfg[:, :]),
         reads=["drfg"], writes=["drf"], dma=True)
    for r_ in range(2):
        for rc in range(16):
            S.op("sp" if rc % 2 == 0 else "pool",
                 (lambda e, r_=r_, rc=rc: e.dma_start(
                     bass.AP(e2xg, rc * 128 * T + (1 - r_) * 128,
                             [[T, 128], [256, 8], [1, 128]]),
                     bass.AP(e2outB, (r_ * 2048 + rc * 128) * TL + 7 * 128,
                             [[TL, 128], [-128, 8], [1, 128]]))),
                 reads=["e2outB"], writes=["e2xg"], dma=True, tag="e2xg")
    for sl in range(8):
        for ch in range(NCHK):
            pk = ["ps0", "ps1"][ch % 2]
            pt = PSL[ch % 2]
            S.op("pe", (lambda e, sl=sl, ch=ch, pt=pt: e.matmul(
                pt[:, :], wdt[:, sl * 128:(sl + 1) * 128],
                drf[:, ch * 512:(ch + 1) * 512], start=True, stop=True)),
                reads=["wdt", "drf"], writes=[pk], tag="dmm")
            S.op("dve", (lambda e, sl=sl, pt=pt: e.tensor_scalar_add(
                zt[:, :], pt[:, :], bcol[:, sl:sl + 1])),
                reads=[pk, "bcol"], writes=["zt"], tag="spz")
            S.op("act", lambda e: e.activation(zt2[:, 0:512], zt[:, :], AF.Abs),
                 reads=["zt"], writes=["dAs0"])
            S.op("act", lambda e: e.activation(zt2[:, 0:512], zt2[:, 0:512], AF.Exp, scale=-1.0),
                 reads=["dAs0"], writes=["dAs0"])
            S.op("act", lambda e: e.activation(zt2[:, 0:512], zt2[:, 0:512], AF.Ln, bias=1.0),
                 reads=["dAs0"], writes=["dAs0"])
            S.op("dve", lambda e: e.tensor_scalar_max(zt[:, :], zt[:, :], 0.0),
                 reads=["zt"], writes=["zt"])
            S.op("dve", (lambda e, sl=sl, ch=ch: e.tensor_tensor(
                dlt[sl][:, ch * 512:(ch + 1) * 512], zt[:, :], zt2[:, 0:512], ALU.add)),
                reads=["zt", "dAs0"], writes=[f"dlt{sl}"], tag="spadd")

    for dt in range(8):
        S.op("dve", (lambda e, dt=dt: e.memset(stateb[dt][:, :], 0.0)),
             writes=[f"state{dt}"], tag="st0")
    for ch in range(NCHK):
        S.op("sp", (lambda e, ch=ch: e.dma_start(
            brep[:, :], bass.AP(bcg, ch * TCH,
                                [[0, 128], [T, NSSM], [1, TCH]]))),
            reads=["bcg"], writes=["brep_c"], dma=True)
        S.op("sp", (lambda e, ch=ch: e.dma_start(
            crep[:, :], bass.AP(bcg, 16 * T + ch * TCH,
                                [[0, 128], [T, NSSM], [1, TCH]]))),
            reads=["bcg"], writes=["crep_c"], dma=True)
        base = ch * TCH
        S.op("pool", (lambda e, base=base: (
            e.reg_mov(rgT, base),
            e.reg_add(rgT, rgT, rgA),
            e.dma_start(xmall[:, :],
                        bass.AP(e2xg, rgT,
                                [[T, 128], [128 * T, 8], [1, TCH]])))[-1]),
            reads=["e2xg", "regs"], writes=["xmall"], dma=True, tag="xm")
        for dt in range(8):
            par = (ch * 8 + dt) % 2
            dAs, dak = dAs2[par], f"dAs{par}"
            dxt, dxk = dxt2[par], f"dxt{par}"
            ych, yck = ych2[par], f"ych{par}"
            S.op("pool", (lambda e, dt=dt, ch=ch, dxt=dxt: e.tensor_tensor(
                dxt[:, :], dlt[dt][:, ch * 512:(ch + 1) * 512],
                xmall[:, dt * TCH:(dt + 1) * TCH], ALU.mult)),
                reads=[f"dlt{dt}", "xmall"], writes=[dxk], tag="dx")
            for n in range(NSSM):
                S.op("act", (lambda e, dt=dt, ch=ch, n=n, dAs=dAs: e.activation(
                    dAs[:, n * TCH:(n + 1) * TCH],
                    dlt[dt][:, ch * 512:(ch + 1) * 512],
                    AF.Exp, scale=at[:, dt * 16 + n:dt * 16 + n + 1])),
                    reads=[f"dlt{dt}", "at"], writes=[dak], tag="dA")
            S.op("dve", (lambda e, dxt=dxt: e.tensor_tensor(
                bus[:, :], bass.AP(dxt, 0, [[TCH, 128], [0, NSSM], [1, TCH]]),
                brep[:, :], ALU.mult)),
                reads=[dxk, "brep_c"], writes=["bus"], tag="bu")
            S.op("dve", (lambda e, dt=dt, dAs=dAs: e.tensor_tensor(
                c0t[:, :], bass.AP(dAs, 0, [[NSSM * TCH, 128], [TCH, NSSM]]),
                stateb[dt][:, :], ALU.mult)),
                reads=[dak, f"state{dt}"], writes=["c0t"], tag="fold")
            S.op("dve", (lambda e: e.tensor_tensor(
                bass.AP(bus, 0, [[NSSM * TCH, 128], [TCH, NSSM]]),
                bass.AP(bus, 0, [[NSSM * TCH, 128], [TCH, NSSM]]),
                c0t[:, :], ALU.add)),
                reads=["bus", "c0t"], writes=["bus"], tag="fold")
            S.op("dve", (lambda e, dAs=dAs: e.memset(
                bass.AP(dAs, 0, [[NSSM * TCH, 128], [TCH, NSSM]]), 0.0)),
                reads=[dak], writes=[dak], tag="fold")
            S.op("dve", (lambda e, dAs=dAs: e.tensor_tensor_scan(
                bus[:, :], dAs[:, :], bus[:, :], 0.0, ALU.mult, ALU.add)),
                reads=[dak, "bus"], writes=["bus"], tag="scan")
            S.op("dve", (lambda e, dt=dt: e.tensor_copy(
                stateb[dt][:, :], bass.AP(bus, TCH - 1, [[NSSM * TCH, 128], [TCH, NSSM]]))),
                reads=["bus"], writes=[f"state{dt}"], tag="tail")
            S.op("dve", lambda e: e.tensor_tensor(bus[:, :], bus[:, :], crep[:, :], ALU.mult),
                 reads=["bus", "crep_c"], writes=["bus"], tag="hc")
            for lvl in range(3):
                w_ = NSSM * TCH >> (lvl + 1)
                S.op("dve", (lambda e, w_=w_: e.tensor_tensor(
                    bus[:, 0:w_], bus[:, 0:w_], bus[:, w_:2 * w_], ALU.add)),
                    reads=["bus"], writes=["bus"], tag="yt")
            S.op("dve", (lambda e, ych=ych: e.tensor_tensor(
                ych[:, :], bus[:, 0:TCH], bus[:, TCH:2 * TCH], ALU.add)),
                reads=["bus"], writes=[yck], tag="yt4")
            S.op("pool", (lambda e, dt=dt, ch=ch, ych=ych: e.dma_start(
                e3in[dt * 128:(dt + 1) * 128, ch * TCH:(ch + 1) * TCH], ych[:, :])),
                reads=[yck], writes=[f"e3w{dt % 2}"], dma=True)

    S.op("pool", (lambda e: e.collective_compute(
        "AllGather", ALU.bypass, replica_groups=PAIRS,
        ins=[e3in.ap().opt()], outs=[e3out.ap().opt()])),
        reads=["e3w0", "e3w1", "e2out"], writes=["e3out"], tag="agE3")

    for p_ in range(2):
        for rc in range(16):
            S.op("sp" if rc % 2 == 0 else "pool",
                 (lambda e, p_=p_, rc=rc: e.dma_start(
                     bass.AP(e3b, rc * 128 * T + p_ * TL,
                             [[T, 128], [1, TL]]),
                     bass.AP(e3out, rc * 128 * T + (15 - p_) * 128,
                             [[T, 128], [-256, 8], [1, 128]]))),
                 reads=["e3out"], writes=["e3b"], dma=True, tag="e3b")
    for g4 in range(4):
        base = g4 * 4 * 128 * T
        S.op("pool", (lambda e, base=base: (
            e.reg_mov(rgT, base),
            e.reg_add(rgT, rgT, rgB),
            e.dma_start(yba[:, :],
                        bass.AP(e3b, rgT,
                                [[T, 128], [128 * T, 4], [1, TL]])))[-1]),
            reads=["e3b", "regs"], writes=["yba"], dma=True, tag="yimp")
        for oo in range(4):
            o = g4 * 4 + oo
            for hh in range(2):
                cs0, cs1 = hh * 512, (hh + 1) * 512
                S.op("sp", (lambda e, o=o, cs0=cs0, cs1=cs1, hh=hh: e.dma_start(
                    x3b[hh][:, :], x2d[o * 128:(o + 1) * 128, cs0:cs1])),
                    writes=[f"x3b{hh}"], dma=True)
                S.op("dve", (lambda e, oo=oo, cs0=cs0, hh=hh: e.tensor_tensor(
                    x3b[hh][:, :], x3b[hh][:, :],
                    yba[:, oo * TL + cs0:oo * TL + cs0 + 512], ALU.add)),
                    reads=[f"x3b{hh}", "yba"], writes=[f"x3b{hh}"], tag="resy")
                S.op("dve", (lambda e, o=o, cs0=cs0, cs1=cs1, hh=hh: e.scalar_tensor_tensor(
                    x3b[hh][:, :], xb[o][:, cs0:cs1], dcol[:, o:o + 1],
                    x3b[hh][:, :], ALU.mult, ALU.add)),
                    reads=[f"xb{o}", "dcol", f"x3b{hh}"], writes=[f"x3b{hh}"], tag="resd")
                S.op("sp", (lambda e, o=o, cs0=cs0, cs1=cs1, hh=hh: e.dma_start(
                    x3[o * 128:(o + 1) * 128, cs0:cs1], x3b[hh][:, :])),
                    reads=[f"x3b{hh}"], writes=[f"x3st{hh}"], dma=True)

    S.op("sp", lambda e: e.nop(), reads=["x3st0", "x3st1"])


def emit_phase_d(nc, S, ctx, D):
    x3d = D["x3i"]
    kvdw = D["kvdw"]
    qdw = D["qdw"]
    kuw = D["kuw"]
    quw = D["quw"]
    vuw = D["vuw"]
    rkw = D["rkw"]
    rqw = D["rqw"]
    oww = D["oww"]
    cos1 = D["cos1"]
    sin1 = D["sin1"]
    l1wd = D["l1wd"]
    mask0 = D["mask0"]
    onesd = D["onesd"]
    identd = D["identd"]
    hd = D["hi"]
    k_in = D["k_in"]
    k_out = D["k_out"]
    v_in = D["v_in"]
    v_out = D["v_out"]
    lnb = D["lnb"]

    xb = [sbt(ctx, nc, f"xb{i}", [128, TL], BF16) for i in range(NCH)]
    qasm = [sbt(ctx, nc, f"qasm{i}", [128, TL], BF16) for i in range(NH)]
    ot = [sbt(ctx, nc, f"ot{i}", [128, TL], BF16) for i in range(NH)]
    kvd = [sbt(ctx, nc, f"kvd{i}", [128, TL], BF16) for i in range(NLAT)]
    qd = [sbt(ctx, nc, f"qd{i}", [128, TL], BF16) for i in range(NLAT)]
    kasm = [sbt(ctx, nc, f"kasm{i}", [128, TL], BF16) for i in range(2)]
    rsc = [sbt(ctx, nc, f"rsc{i}", [128, TL], BF16) for i in range(2)]
    shuf = sbt(ctx, nc, "shuf", [128, TL], BF16)
    sqt = shuf
    wslab = [sbt(ctx, nc, f"wslab{i}", [128, C], BF16) for i in range(2)]
    cosT = sbt(ctx, nc, "cosT", [128, TL], BF16)
    sinT = sbt(ctx, nc, "sinT", [128, TL], BF16)
    maskT = sbt(ctx, nc, "maskT", [128, MASK_COLS], BF16)
    ones_bf = sbt(ctx, nc, "ones_bf", [128, 1], BF16)
    l1w = sbt(ctx, nc, "l1w", [128, 16], F32)
    khead = [sbt(ctx, nc, f"khead{i}", [128, T], BF16) for i in range(2)]
    vhead = [sbt(ctx, nc, f"vhead{i}", [128, T], BF16) for i in range(2)]
    ptile2 = [sbt(ctx, nc, f"ptile{i}", [128, TL], BF16) for i in range(2)]
    rrep = sbt(ctx, nc, "rrep", [128, TL], F32)
    arep = sbt(ctx, nc, "arep", [128, TL], F32)
    brep = sbt(ctx, nc, "brep", [128, TL], F32)
    rmsrep = rrep
    ident = sbt(ctx, nc, "ident", [128, 128], BF16)
    hb = [sbt(ctx, nc, f"hb{i}", [128, TL], F32) for i in range(2)]
    rows = {k: sbt(ctx, nc, k, [1, TL], F32) for k in ("srow", "qrow", "brow")}

    ps0 = pst(ctx, nc, "ps0", [128, 512])
    ps1 = pst(ctx, nc, "ps1", [128, 512])
    ps2 = pst(ctx, nc, "ps2", [128, 512])
    ps3 = pst(ctx, nc, "ps3", [128, 512])
    pr0 = pst(ctx, nc, "pr0", [128, 512])
    pr1 = pst(ctx, nc, "pr1", [128, 512])
    PS = {0: ps0, 1: ps1}
    E_PS = {"ps0": ps0, "ps1": ps1, "ps2": ps2, "ps3": ps3}
    E = {"ones_bf": ones_bf, "sqt": sqt, "sqt_key": "shuf", "rows": rows,
         "lnb": lnb, "arep": arep, "brep": brep,
         "ps0": ps0, "ps1": ps1, "pr0": pr0, "pr1": pr1}

    for dc in range(NCH):
        S.op("pool", (lambda e, dc=dc: e.dma_start(
            xb[dc][:, :], x3d[dc * 128:(dc + 1) * 128, :])),
            writes=[f"xb{dc}"], dma=True)
    S.op("sp", lambda e: e.dma_start(cosT[:, :], cos1[:, :]), writes=["cosT"], dma=True)
    S.op("sp", lambda e: e.dma_start(sinT[:, :], sin1[:, :]), writes=["sinT"], dma=True)
    S.op("sp", lambda e: e.dma_start(maskT[:, :], mask0[:, :]), writes=["maskT"], dma=True)
    S.op("sp", lambda e: e.dma_start(ones_bf[:, :], onesd[:, :]), writes=["ones"], dma=True)
    S.op("sp", lambda e: e.dma_start(l1w[:, :], l1wd[:, :]), writes=["l1w"], dma=True)
    S.op("sp", lambda e: e.dma_start(ident[:, :], identd[:, :]), writes=["ident"], dma=True)

    xkeys = [f"xb{i}" for i in range(NCH)]
    emit_layernorm(S, E, xkeys, xb, 0)
    emit_layernorm(S, E, xkeys, xb, 2, rms_only=True, apply=False)
    S.op("sp", (lambda e: e.dma_start(
        rmsrep[:, :], bass.AP(lnb, 2 * TL, [[0, 128], [1, TL]]))),
        reads=["lnb2"], writes=["rrep"], dma=True)
    for dc in range(NCH):
        S.op("dve", (lambda e, dc=dc: e.scalar_tensor_tensor(
            xb[dc][:, :], xb[dc][:, :], l1w[:, dc:dc + 1], rmsrep[:, :],
            ALU.mult, ALU.mult)),
            reads=[f"xb{dc}", "l1w", "rrep"], writes=[f"xb{dc}"], tag="rmsap")

    def slab_load(dram, f, kchunks, wk):
        S.op("pool", (lambda e, f=f, kchunks=kchunks: e.dma_start(
            wslab[f % 2][:, 0:kchunks * 128],
            bass.AP(dram, f * kchunks * 128 * 128,
                    [[128, 128], [128 * 128, kchunks], [1, 128]]))),
            writes=[wk], dma=True)

    def mm_ft(src_tiles, src_keys, kchunks, f, pk, extra=None):
        wk = f"wslab{f % 2}"
        for tc in range(NT):
            pkt = ["ps2", "ps3"][tc] if pk == "auto" else pk
            ptt = E_PS[pkt]
            for dc in range(kchunks):
                w = [pkt] if (dc == 0 or dc == kchunks - 1) else []
                S.op("pe", (lambda e, tc=tc, dc=dc, ptt=ptt, f=f: e.matmul(
                    ptt[:, :], wslab[f % 2][:, dc * 128:(dc + 1) * 128],
                    src_tiles[dc][:, tc * 512:(tc + 1) * 512],
                    start=(dc == 0), stop=(dc == kchunks - 1))),
                    reads=[wk, src_keys[dc]], writes=w, tag="mmft")
            if extra:
                extra(tc, pkt, E_PS[pkt])

    kvdk = [f"kvd{i}" for i in range(NLAT)]
    qdk = [f"qd{i}" for i in range(NLAT)]

    for f in range(NLAT):
        slab_load(kvdw, f, NCH, f"wslab{f % 2}")
        def ev(tc, pk, pt, f=f):
            S.op("dve", (lambda e, tc=tc, pt=pt, f=f: e.tensor_copy(
                kvd[f][:, tc * 512:(tc + 1) * 512], pt[:, :])),
                reads=[pk], writes=[kvdk[f]], tag="kvdev")
        mm_ft(xb, xkeys, NCH, f, "auto", ev)
    for f in range(NLAT):
        slab_load(qdw, f, NCH, f"wslab{f % 2}")
        def ev(tc, pk, pt, f=f):
            S.op("dve", (lambda e, tc=tc, pt=pt, f=f: e.tensor_copy(
                qd[f][:, tc * 512:(tc + 1) * 512], pt[:, :])),
                reads=[pk], writes=[qdk[f]], tag="qdev")
        mm_ft(qd if False else xb, xkeys, NCH, f, "auto", ev)

    mask32 = [i ^ 1 for i in range(32)]

    def rope_tile(tile, key):
        S.op("dve", (lambda e, tile=tile: e.stream_shuffle(shuf[:, :], tile[:, :], mask32)),
             reads=[key], writes=["shuf"])
        S.op("dve", (lambda e, tile=tile: e.tensor_tensor(tile[:, :], tile[:, :], cosT[:, :], ALU.mult)),
             reads=[key, "cosT"], writes=[key])
        S.op("dve", (lambda e: e.tensor_tensor(shuf[:, :], shuf[:, :], sinT[:, :], ALU.mult)),
             reads=["shuf", "sinT"], writes=["shuf"])
        S.op("dve", (lambda e, tile=tile: e.tensor_tensor(tile[:, :], tile[:, :], shuf[:, :], ALU.add)),
             reads=[key, "shuf"], writes=[key])

    for f in range(8):
        slab_load(quw, f, NLAT, f"wslab{f % 2}")
        def evqp(tc, pk, pt, f=f):
            for half in range(2):
                h = 2 * f + half
                S.op("dve", (lambda e, tc=tc, pt=pt, half=half, h=h: e.tensor_copy(
                    qasm[h][0:64, tc * 512:(tc + 1) * 512],
                    pt[64 * half:64 * half + 64, :])),
                    reads=[pk], writes=[f"qasm{h}"], tag="qpev")
        mm_ft(qd, qdk, NLAT, f, "auto", evqp)
        slab_load(rqw, f, NLAT, f"wslab{f % 2}")
        rk = f"rsc{f % 2}"
        def evqr(tc, pk, pt, f=f, rk=rk):
            S.op("dve", (lambda e, tc=tc, pt=pt, f=f: e.tensor_copy(
                rsc[f % 2][:, tc * 512:(tc + 1) * 512], pt[:, :])),
                reads=[pk], writes=[rk], tag="qrev")
        mm_ft(qd, qdk, NLAT, f, "auto", evqr)
        rope_tile(rsc[f % 2], rk)
        for half in range(2):
            h = 2 * f + half
            S.op("dve", (lambda e, half=half, h=h, f=f: e.tensor_copy(
                qasm[h][64:128, :], rsc[f % 2][64 * half:64 * half + 64, :])),
                reads=[rk], writes=[f"qasm{h}"], tag="qrasm")

    for f in range(8):
        ak0, ak1 = "kasm0x", "kasm1x"
        slab_load(kuw, f, NLAT, f"wslab{f % 2}")
        def evkp(tc, pk, pt, f=f):
            for half in range(2):
                S.op("dve", (lambda e, tc=tc, pt=pt, half=half: e.tensor_copy(
                    kasm[half][0:64, tc * 512:(tc + 1) * 512],
                    pt[64 * half:64 * half + 64, :])),
                    reads=[pk], writes=[("kasm0x", "kasm1x")[half]], tag="kpev")
        mm_ft(kvd, kvdk, NLAT, f, "auto", evkp)
        slab_load(rkw, f, NCH, f"wslab{f % 2}")
        rk = f"rsc{f % 2}"
        def evkr(tc, pk, pt, f=f, rk=rk):
            S.op("dve", (lambda e, tc=tc, pt=pt, f=f: e.tensor_copy(
                rsc[f % 2][:, tc * 512:(tc + 1) * 512], pt[:, :])),
                reads=[pk], writes=[rk], tag="krev")
        mm_ft(xb, xkeys, NCH, f, "auto", evkr)
        rope_tile(rsc[f % 2], rk)
        for half in range(2):
            S.op("dve", (lambda e, half=half, f=f: e.tensor_copy(
                kasm[half][64:128, :], rsc[f % 2][64 * half:64 * half + 64, :])),
                reads=[rk], writes=[("kasm0x", "kasm1x")[half]], tag="krasm")
        for half in range(2):
            h = 2 * f + half
            S.op("sp", (lambda e, h=h, half=half: e.dma_start(
                k_in[h * 128:(h + 1) * 128, :], kasm[half][:, :])),
                reads=[("kasm0x", "kasm1x")[half]], writes=[f"k_in{h}"], dma=True)

    S.op("pool", (lambda e: e.collective_compute(
        "AllGather", ALU.bypass, replica_groups=PAIRS,
        ins=[k_in.ap().opt()], outs=[k_out.ap().opt()])),
        reads=[f"k_in{h}" for h in range(NH)], writes=["k_out"], tag="agK")

    vexp = sbt(ctx, nc, "vexpT", [128, TL], BF16)
    psb = pst(ctx, nc, "psb", [128, 1024], BF16)
    for f in range(NH):
        wk = f"wslab{f % 2}"
        S.op("pool", (lambda e, f=f: e.dma_start(
            wslab[f % 2][:, 0:LAT],
            bass.AP(vuw, f * 128, [[C, 128], [128 * C, NLAT] if False else [128 * C, NLAT], [1, 128]])
            if False else
            bass.AP(vuw, f * 128, [[C, 128], [128 * C, NLAT], [1, 128]]))),
            writes=[wk], dma=True)
        for tc in range(NT):
            pk = ["ps2", "ps3"][tc]
            pt = E_PS[pk]
            for dc in range(NLAT):
                w = [pk] if (dc == 0 or dc == NLAT - 1) else []
                S.op("pe", (lambda e, tc=tc, dc=dc, pt=pt, f=f: e.matmul(
                    pt[:, :], wslab[f % 2][:, dc * 128:(dc + 1) * 128],
                    kvd[dc][:, tc * 512:(tc + 1) * 512],
                    start=(dc == 0), stop=(dc == NLAT - 1))),
                    reads=[wk, kvdk[dc]], writes=w, tag=f"vf{f}")
            S.op("dve", (lambda e, tc=tc, pt=pt: e.tensor_copy(
                rsc[0][:, tc * 512:(tc + 1) * 512], pt[:, :])),
                reads=[pk], writes=["rsc0"], tag="vevf")
        for tt in range(8):
            S.op("pe", (lambda e, tt=tt: e.transpose(
                psb[:, tt * 128:(tt + 1) * 128],
                rsc[0][:, tt * 128:(tt + 1) * 128], ident[:, :])),
                reads=["rsc0", "ident"], writes=["psb"], tag="vT")
        S.op("dve", lambda e: e.tensor_copy(vexp[:, :], psb[:, :]),
             reads=["psb"], writes=["vexpT"])
        S.op("sp", (lambda e, f=f: e.dma_start(
            bass.AP(v_in, f * 128, [[C, 128], [128 * C, 8], [1, 128]]),
            vexp[:, :])),
            reads=["vexpT"], writes=[f"v_inx{f}"], dma=True)

    vinkeys = [f"v_inx{f}" for f in range(NH)]
    S.op("pool", (lambda e: e.collective_compute(
        "AllGather", ALU.bypass, replica_groups=PAIRS,
        ins=[v_in.ap().opt()], outs=[v_out.ap().opt()])),
        reads=vinkeys, writes=["v_out"], tag="agV")

    AV_LAST = {0: 15, 1: 7}
    for h in range(NH):
        kb, kk = khead[h % 2], f"khead{h % 2}"
        vb, vk = vhead[h % 2], f"vhead{h % 2}"
        for r_ in range(2):
            S.op("sp", (lambda e, h=h, kb=kb, r_=r_: e.dma_start(
                bass.AP(kb, (1 - r_) * 128, [[T, 128], [256, 8], [1, 128]]),
                bass.AP(k_out, r_ * C * TL + h * 128 * TL + 7 * 128,
                        [[TL, 128], [-128, 8], [1, 128]]))),
                reads=["k_out", "v_out"], writes=[kk], dma=True)
            S.op("sp", (lambda e, h=h, vb=vb, r_=r_: e.dma_start(
                bass.AP(vb, (1 - r_) * 128, [[T, 128], [256, 8], [1, 128]]),
                bass.AP(v_out, r_ * TL * C + 7 * 128 * C + h * 128,
                        [[C, 128], [-128 * C, 8], [1, 128]]))),
                reads=["v_out", "k_out"], writes=[vk], dma=True)
        first = [True, True]
        for kt in range(NKT):
            wd = _wid(kt)
            msegs_kt = [(l_, off_) for (mk_, l_, off_) in MSEGS if mk_ == kt]
            chunks = []
            c0 = 0
            while c0 < wd:
                chunks.append((c0, min(c0 + 512, wd)))
                c0 = min(c0 + 512, wd)
            for ci, (c0, c1) in enumerate(chunks):
                spk = "ps2" if (kt + ci) % 2 == 0 else "ps3"
                sps = E_PS[spk]
                S.op("pe", (lambda e, kt=kt, c0=c0, c1=c1, sps=sps, kb=kb, h=h: e.matmul(
                    sps[:, 0:c1 - c0], kb[:, kt * 128:(kt + 1) * 128],
                    qasm[h][:, c0:c1], start=True, stop=True)),
                    reads=[kk, f"qasm{h}"], writes=[spk], tag=f"sc{h}")
                for (ml_, moff_) in msegs_kt:
                    a0 = ml_ * 128
                    if a0 >= c0 and a0 + 128 <= c1:
                        S.op("dve", (lambda e, a0=a0, c0=c0, sps=sps, moff_=moff_: e.tensor_tensor(
                            sps[:, a0 - c0:a0 - c0 + 128], sps[:, a0 - c0:a0 - c0 + 128],
                            maskT[:, moff_:moff_ + 128], ALU.add)),
                            reads=[spk, "maskT"], writes=[spk], tag="msk")
                S.op("act", (lambda e, c0=c0, c1=c1, sps=sps, kt=kt: e.activation(
                    ptile2[kt % 2][:, c0:c1], sps[:, 0:c1 - c0], AF.Exp, scale=ISQ)),
                    reads=[spk], writes=[f"ptile{kt % 2}"], tag="exp")
            b0 = 0
            while b0 < wd:
                blk_i = b0 // 512
                b1 = min((blk_i + 1) * 512, wd)
                opk = "ps0" if blk_i == 0 else "ps1"
                opt_ = PS[blk_i]
                rpk = "pr0" if blk_i == 0 else "pr1"
                rpt = E[rpk]
                wo_ = [opk] if (first[blk_i] or kt == AV_LAST[blk_i]) else []
                wr_ = [rpk] if (first[blk_i] or kt == AV_LAST[blk_i]) else []
                S.op("pe", (lambda e, kt=kt, b0=b0, b1=b1, blk_i=blk_i, vb=vb,
                            opt_=opt_, f=first[blk_i]: e.matmul(
                    opt_[:, b0 - blk_i * 512:b1 - blk_i * 512],
                    vb[:, kt * 128:(kt + 1) * 128], ptile2[kt % 2][:, b0:b1],
                    start=f, stop=(kt == AV_LAST[blk_i]))),
                    reads=[vk, f"ptile{kt % 2}"], writes=wo_, tag=f"av{h}")
                S.op("pe", (lambda e, kt=kt, b0=b0, b1=b1, blk_i=blk_i,
                            rpt=rpt, f=first[blk_i]: e.matmul(
                    rpt[0:1, b0 - blk_i * 512:b1 - blk_i * 512],
                    ones_bf[:, 0:1], ptile2[kt % 2][:, b0:b1],
                    start=f, stop=(kt == AV_LAST[blk_i]))),
                    reads=[f"ptile{kt % 2}", "ones"], writes=wr_, tag=f"rs{h}")
                first[blk_i] = False
                b0 = b1
        for blk_i in range(2):
            rpk = "pr0" if blk_i == 0 else "pr1"
            rpt = E[rpk]
            S.op("dve", (lambda e, blk_i=blk_i, rpt=rpt: e.tensor_copy(
                rows["qrow"][0:1, blk_i * 512:(blk_i + 1) * 512], rpt[0:1, :])),
                reads=[rpk], writes=["qrow"])
        S.op("dve", lambda e: e.reciprocal(rows["qrow"][:, :], rows["qrow"][:, :]),
             reads=["qrow"], writes=["qrow"])
        S.op("sp", (lambda e: e.dma_start(
            bass.AP(lnb, 3 * TL, [[TL, 1], [1, TL]]), rows["qrow"][:, :])),
            reads=["qrow"], writes=["lnb3"], dma=True)
        S.op("sp", (lambda e: e.dma_start(
            rrep[:, :], bass.AP(lnb, 3 * TL, [[0, 128], [1, TL]]))),
            reads=["lnb3"], writes=["rrep"], dma=True)
        for blk_i in range(2):
            opk = "ps0" if blk_i == 0 else "ps1"
            opt_ = PS[blk_i]
            if blk_i == 0:
                S.op("dve", (lambda e, blk_i=blk_i, h=h, opt_=opt_: e.tensor_copy(
                    ot[h][:, blk_i * 512:(blk_i + 1) * 512], opt_[:, :])),
                    reads=[opk], writes=[f"ot{h}"], tag="oev")
            else:
                S.op("act", (lambda e, blk_i=blk_i, h=h, opt_=opt_: e.copy(
                    ot[h][:, blk_i * 512:(blk_i + 1) * 512], opt_[:, :])),
                    reads=[opk], writes=[f"ot{h}"], tag="oev")
        S.op("dve", (lambda e, h=h: e.tensor_tensor(
            ot[h][:, :], ot[h][:, :], rrep[:, :], ALU.mult)),
            reads=[f"ot{h}", "rrep"], writes=[f"ot{h}"], tag="onrm")

    for o in range(NCH):
        wk = f"wslab{o % 2}"
        slab_load(oww, o, NCH, wk)
        S.op("sp", (lambda e, o=o: e.dma_start(
            hb[o % 2][:, :], x3d[o * 128:(o + 1) * 128, :])),
            writes=[f"hb{o % 2}"], dma=True)
        S.op("dve", (lambda e, o=o: e.tensor_tensor(
            hb[o % 2][:, :], hb[o % 2][:, :], arep[:, :], ALU.mult)),
            reads=[f"hb{o % 2}", "arep"], writes=[f"hb{o % 2}"], tag="urec")
        S.op("dve", (lambda e, o=o: e.tensor_tensor(
            hb[o % 2][:, :], hb[o % 2][:, :], brep[:, :], ALU.add)),
            reads=[f"hb{o % 2}", "brep"], writes=[f"hb{o % 2}"], tag="urec2")
        for tc in range(NT):
            pk = ["ps2", "ps3"][tc]
            pt = E_PS[pk]
            for hh in range(NH):
                w = [pk] if (hh == 0 or hh == NH - 1) else []
                S.op("pe", (lambda e, o=o, tc=tc, hh=hh, pt=pt: e.matmul(
                    pt[:, :], wslab[o % 2][:, hh * 128:(hh + 1) * 128],
                    ot[hh][:, tc * 512:(tc + 1) * 512],
                    start=(hh == 0), stop=(hh == NH - 1))),
                    reads=[wk, f"ot{hh}"], writes=w, tag=f"ow{o}")
            S.op("dve", (lambda e, o=o, tc=tc, pt=pt: e.tensor_tensor(
                hb[o % 2][:, tc * 512:(tc + 1) * 512], pt[:, :],
                hb[o % 2][:, tc * 512:(tc + 1) * 512], ALU.add)),
                reads=[pk, f"hb{o % 2}"], writes=[f"hb{o % 2}"], tag="hres")
        S.op("sp", (lambda e, o=o: e.dma_start(
            hd[o * 128:(o + 1) * 128, :], hb[o % 2][:, :])),
            reads=[f"hb{o % 2}"], writes=[f"hst{o % 2}"], dma=True)

    S.op("sp", lambda e: e.nop(), reads=["hst0", "hst1"])


def emit_phase_e(nc, S, ctx, D):
    wbn = D["wbn"]
    hdd = D["hi"]
    x3d = D["x3i"]
    shg = D["shg"]
    shu = D["shu"]
    shd = D["shd"]
    rg = D["rg"]
    ru = D["ru"]
    rd = D["rd"]
    rwT = D["rwT"]
    rbd = D["rbd"]
    l2wd = D["l2wd"]
    onesd = D["onesd"]
    x4 = D["x4"]
    lnb = D["lnb"]

    hb = [sbt(ctx, nc, f"hb{i}", [128, TL], BF16) for i in range(NCH)]
    ge = [[sbt(ctx, nc, f"ge{gi}_{i}", [128, TL], BF16) for i in range(NIT)]
          for gi in range(4)]
    macc = [sbt(ctx, nc, f"macc{i}", [128, TL], BF16) for i in range(NCH)]
    sqt = sbt(ctx, nc, "sqt", [128, TL], BF16)
    wslab = [sbt(ctx, nc, f"wslab{i}", [128, C], BF16) for i in range(2)]
    sil = sbt(ctx, nc, "sil", [128, 512], BF16)
    rwt = sbt(ctx, nc, "rwt", [128, 112], BF16)
    rbt = sbt(ctx, nc, "rbt", [7, 1], F32)
    l2w = sbt(ctx, nc, "l2w", [128, 16], F32)
    ones_bf = sbt(ctx, nc, "ones_bf", [128, 1], BF16)
    pt32 = sbt(ctx, nc, "pt32", [32, TL], F32)
    m1t = sbt(ctx, nc, "m1t", [32, TL], F32)
    m2t = sbt(ctx, nc, "m2t", [32, TL], F32)
    pro = sbt(ctx, nc, "pro", [32, TL], F32)
    wrep = sbt(ctx, nc, "wrep", [128, TL], F32)
    arep = sbt(ctx, nc, "arep", [128, TL], F32)
    brep = sbt(ctx, nc, "brep", [128, TL], F32)
    x3h = [sbt(ctx, nc, f"x3h{i}", [128, TL], F32) for i in range(2)]
    x4b = [sbt(ctx, nc, f"x4b{i}", [128, TL], F32) for i in range(2)]
    rows = {k: sbt(ctx, nc, k, [1, TL], F32) for k in ("srow", "qrow", "brow")}

    ps0 = pst(ctx, nc, "ps0", [128, 512])
    ps1 = pst(ctx, nc, "ps1", [128, 512])
    ps2 = pst(ctx, nc, "ps2", [128, 512])
    ps3 = pst(ctx, nc, "ps3", [128, 512])
    pr0 = pst(ctx, nc, "pr0", [128, 512])
    pr1 = pst(ctx, nc, "pr1", [128, 512])
    E_PS = {"ps0": ps0, "ps1": ps1, "ps2": ps2, "ps3": ps3}
    E = {"ones_bf": ones_bf, "sqt": sqt, "rows": rows, "lnb": lnb,
         "arep": arep, "brep": brep,
         "ps0": ps0, "ps1": ps1, "pr0": pr0, "pr1": pr1}

    for dc in range(NCH):
        S.op("pool", (lambda e, dc=dc: e.dma_start(
            hb[dc][:, :], hdd[dc * 128:(dc + 1) * 128, :])),
            writes=[f"hb{dc}"], dma=True)
    S.op("sp", lambda e: e.dma_start(ones_bf[:, :], onesd[:, :]), writes=["ones"], dma=True)
    S.op("sp", lambda e: e.dma_start(l2w[:, :], l2wd[:, :]), writes=["l2w"], dma=True)
    S.op("sp", lambda e: e.dma_start(rbt[:, :], rbd[:, :]), writes=["rbt"], dma=True)
    S.op("pool", lambda e: e.dma_start(
        rwt[:, 0:112], bass.AP(rwT, 0, [[7, 128], [128 * 7, NCH], [1, 7]])),
        writes=["rwt"], dma=True)

    hkeys = [f"hb{i}" for i in range(NCH)]
    emit_layernorm(S, E, hkeys, hb, 2, rms_only=True, apply=False)
    S.op("sp", (lambda e: e.dma_start(
        arep[:, :], bass.AP(lnb, 2 * TL, [[0, 128], [1, TL]]))),
        reads=["lnb2"], writes=["arep"], dma=True)
    for dc in range(NCH):
        S.op("dve", (lambda e, dc=dc: e.scalar_tensor_tensor(
            hb[dc][:, :], hb[dc][:, :], l2w[:, dc:dc + 1], arep[:, :],
            ALU.mult, ALU.mult)),
            reads=[f"hb{dc}", "l2w", "arep"], writes=[f"hb{dc}"], tag="rmsap")

    for tc in range(NT):
        prk = ["pr0", "pr1"][tc]
        prt = E[prk]
        for dc in range(NCH):
            w = [prk] if (dc == 0 or dc == NCH - 1) else []
            S.op("pe", (lambda e, tc=tc, dc=dc, prt=prt: e.matmul(
                prt[0:7, :], rwt[:, dc * 7:(dc + 1) * 7],
                hb[dc][:, tc * 512:(tc + 1) * 512],
                start=(dc == 0), stop=(dc == NCH - 1))),
                reads=["rwt", f"hb{dc}"], writes=w, tag="rlog")
    S.op("dve", lambda e: e.memset(pt32[:, :], NEG), writes=["pt32"])
    for tc in range(NT):
        prk = ["pr0", "pr1"][tc]
        prt = E[prk]
        S.op("dve", (lambda e, tc=tc, prt=prt: e.tensor_scalar_add(
            pt32[0:7, tc * 512:(tc + 1) * 512], prt[0:7, :], rbt[0:7, 0:1])),
            reads=[prk, "rbt", "pt32"], writes=["pt32"], tag="rbias")
    S.op("act", lambda e: e.activation(pt32[0:7, :], pt32[0:7, :], AF.Sigmoid),
         reads=["pt32"], writes=["pt32"])
    S.op("dve", lambda e: e.tensor_copy(pro[:, :], pt32[:, :]),
         reads=["pt32"], writes=["pro"])

    def logmax(dst, dkey, srckey):
        for k in (1, 2, 4):
            mask = [(r & ~7) | ((r + k) & 7) for r in range(32)]
            S.op("dve", (lambda e, mask=mask, dst=dst: e.stream_shuffle(
                m2t[:, :] if dst is m1t else m1t[:, :], dst[:, :], mask)),
                reads=[dkey], writes=["m2t" if dst is m1t else "m1t"], tag="shf")
            S.op("dve", (lambda e, dst=dst: e.tensor_tensor(
                dst[:, :], dst[:, :],
                m2t[:, :] if dst is m1t else m1t[:, :], ALU.max)),
                reads=[dkey, "m2t" if dst is m1t else "m1t"], writes=[dkey], tag="lmx")

    S.op("dve", lambda e: e.tensor_copy(m1t[:, :], pt32[:, :]),
         reads=["pt32"], writes=["m1t"])
    logmax(m1t, "m1t", "pt32")
    S.op("dve", lambda e: e.tensor_tensor(m2t[:, :], pro[:, :], m1t[:, :], ALU.is_equal),
         reads=["pro", "m1t"], writes=["m2t"])
    S.op("dve", lambda e: e.tensor_scalar_mul(m2t[:, :], m2t[:, :], NEG),
         reads=["m2t"], writes=["m2t"])
    S.op("dve", lambda e: e.tensor_tensor(pt32[:, :], pro[:, :], m2t[:, :], ALU.add),
         reads=["pro", "m2t", "pt32"], writes=["pt32"])
    logmax(pt32, "pt32", None)
    S.op("dve", lambda e: e.tensor_tensor(m2t[:, :], pro[:, :], pt32[:, :], ALU.is_ge),
         reads=["pro", "pt32"], writes=["m2t"])
    S.op("dve", lambda e: e.tensor_tensor(m2t[:, :], m2t[:, :], pro[:, :], ALU.mult),
         reads=["m2t", "pro"], writes=["m2t"])
    S.op("dve", lambda e: e.tensor_tensor(m1t[:, :], m1t[:, :], pt32[:, :], ALU.add),
         reads=["m1t", "pt32"], writes=["m1t"])
    S.op("dve", lambda e: e.reciprocal(m1t[:, :], m1t[:, :]),
         reads=["m1t"], writes=["m1t"])
    S.op("dve", lambda e: e.tensor_tensor(m2t[:, :], m2t[:, :], m1t[:, :], ALU.mult),
         reads=["m2t", "m1t"], writes=["m2t"])
    S.op("dve", lambda e: e.tensor_scalar_mul(m2t[:, :], m2t[:, :], 0.125),
         reads=["m2t"], writes=["m2t"])
    S.op("sp", (lambda e: e.dma_start(
        wbn[0:7, :], m2t[0:7, :])), reads=["m2t"], writes=["wbn"], dma=True)

    def expert_mm(e_idx, gslot):
        for f in range(NIT):
            for (wsrc, which) in ((shg if e_idx == 0 else rg, 0),
                                  (shu if e_idx == 0 else ru, 1)):
                wk = f"wslab{which}"
                off = (f if e_idx == 0 else ((e_idx - 1) * NIT + f)) * C * 128
                S.op("pool", (lambda e, off=off, wsrc=wsrc, which=which: e.dma_start(
                    wslab[which][:, :],
                    bass.AP(wsrc, off, [[128, 128], [128 * 128, NCH], [1, 128]]))),
                    writes=[wk], dma=True)
            for tc in range(NT):
                pgk, puk = "ps2", "ps3"
                pgt, put = ps2, ps3
                for dc in range(NCH):
                    w = [pgk] if (dc == 0 or dc == NCH - 1) else []
                    S.op("pe", (lambda e, tc=tc, dc=dc, pgt=pgt: e.matmul(
                        pgt[:, :], wslab[0][:, dc * 128:(dc + 1) * 128],
                        hb[dc][:, tc * 512:(tc + 1) * 512],
                        start=(dc == 0), stop=(dc == NCH - 1))),
                        reads=["wslab0", f"hb{dc}"], writes=w, tag="gmm")
                for dc in range(NCH):
                    w = [puk] if (dc == 0 or dc == NCH - 1) else []
                    S.op("pe", (lambda e, tc=tc, dc=dc, put=put: e.matmul(
                        put[:, :], wslab[1][:, dc * 128:(dc + 1) * 128],
                        hb[dc][:, tc * 512:(tc + 1) * 512],
                        start=(dc == 0), stop=(dc == NCH - 1))),
                        reads=["wslab1", f"hb{dc}"], writes=w, tag="umm")
                S.op("act", (lambda e, pgt=pgt: e.activation(
                    sil[:, :], pgt[:, :], AF.Silu)),
                    reads=[pgk], writes=["sil"], tag="silu")
                gk = f"ge{gslot}_{f}"
                S.op("dve", (lambda e, tc=tc, f=f, put=put, gslot=gslot: e.tensor_tensor(
                    ge[gslot][f][:, tc * 512:(tc + 1) * 512], sil[:, :],
                    put[:, :], ALU.mult)),
                    reads=["sil", puk], writes=[gk], tag="gmul")
        if e_idx == 0:
            for f in range(NIT):
                gk = f"ge{gslot}_{f}"
                S.op("dve", (lambda e, f=f, gslot=gslot: e.tensor_scalar_mul(
                    ge[gslot][f][:, :], ge[gslot][f][:, :], 0.125)),
                    reads=[gk], writes=[gk], tag="gsc")
        else:
            S.op("sp", (lambda e, e_idx=e_idx: e.dma_start(
                wrep[:, :], bass.AP(wbn, (e_idx - 1) * TL, [[0, 128], [1, TL]]))),
                reads=["wbn"], writes=["wrep"], dma=True)
            for f in range(NIT):
                gk = f"ge{gslot}_{f}"
                S.op("dve", (lambda e, f=f, gslot=gslot: e.tensor_tensor(
                    ge[gslot][f][:, :], ge[gslot][f][:, :], wrep[:, :], ALU.mult)),
                    reads=[gk, "wrep"], writes=[gk], tag="gscw")

    for grp in range(2):
        for gi in range(4):
            expert_mm(grp * 4 + gi, gi)
        for o in range(NCH):
            for tc in range(NT):
                pk = ["ps0", "ps1"][(o * NT + tc) % 2]
                pt = E_PS[pk]
                nacc = 4 * NIT
                idx = 0
                for gi in range(4):
                    e_idx = grp * 4 + gi
                    wk2 = "w2slab"
                    for hc in range(NIT):
                        if tc == 0 and hc == 0:
                            pass
                        idx += 1
        for o in range(NCH):
            for gi in range(4):
                e_idx = grp * 4 + gi
                wk = f"wslab{gi % 2}"
                src = shd if e_idx == 0 else rd
                off = (o if e_idx == 0 else ((e_idx - 1) * NCH + o)) * IEXP * 128
                S.op("pool", (lambda e, off=off, src=src, gi=gi: e.dma_start(
                    wslab[gi % 2][:, 0:IEXP],
                    bass.AP(src, off, [[128, 128], [128 * 128, NIT], [1, 128]]))),
                    writes=[wk], dma=True)
                for tc in range(NT):
                    pk = ["ps0", "ps1"][tc]
                    pt = E_PS[pk]
                    for hc in range(NIT):
                        first_mm = (gi == 0 and hc == 0)
                        last_mm = (gi == 3 and hc == NIT - 1)
                        w = [pk] if (first_mm or last_mm) else []
                        S.op("pe", (lambda e, tc=tc, hc=hc, pt=pt, gi=gi: e.matmul(
                            pt[:, :], wslab[gi % 2][:, hc * 128:(hc + 1) * 128],
                            ge[gi][hc][:, tc * 512:(tc + 1) * 512],
                            start=(gi == 0 and hc == 0),
                            stop=(gi == 3 and hc == NIT - 1))),
                            reads=[wk, f"ge{gi}_{hc}"], writes=w, tag=f"dmm{o}")
            if grp == 0:
                for tc in range(NT):
                    pk = ["ps0", "ps1"][tc]
                    pt = E_PS[pk]
                    S.op("dve", (lambda e, o=o, tc=tc, pt=pt: e.tensor_copy(
                        macc[o][:, tc * 512:(tc + 1) * 512], pt[:, :])),
                        reads=[pk], writes=[f"macc{o}"], tag="mev")
            else:
                S.op("sp", (lambda e, o=o: e.dma_start(
                    x3h[o % 2][:, :], x3d[o * 128:(o + 1) * 128, :])),
                    writes=[f"x3h{o % 2}"], dma=True)
                S.op("sp", (lambda e, o=o: e.dma_start(
                    x4b[o % 2][:, :], hdd[o * 128:(o + 1) * 128, :])),
                    writes=[f"x4b{o % 2}"], dma=True)
                S.op("dve", (lambda e, o=o: e.tensor_tensor(
                    x4b[o % 2][:, :], x4b[o % 2][:, :], x3h[o % 2][:, :], ALU.add)),
                    reads=[f"x4b{o % 2}", f"x3h{o % 2}"], writes=[f"x4b{o % 2}"], tag="xh")
                for tc in range(NT):
                    pk = ["ps0", "ps1"][tc]
                    pt = E_PS[pk]
                    S.op("dve", (lambda e, o=o, tc=tc, pt=pt: e.tensor_tensor(
                        x3h[o % 2][:, tc * 512:(tc + 1) * 512], pt[:, :],
                        macc[o][:, tc * 512:(tc + 1) * 512], ALU.add)),
                        reads=[pk, f"macc{o}"], writes=[f"x3h{o % 2}"], tag="m2a")
                    S.op("dve", (lambda e, o=o, tc=tc: e.tensor_tensor(
                        x4b[o % 2][:, tc * 512:(tc + 1) * 512],
                        x4b[o % 2][:, tc * 512:(tc + 1) * 512],
                        x3h[o % 2][:, tc * 512:(tc + 1) * 512], ALU.add)),
                        reads=[f"x4b{o % 2}", f"x3h{o % 2}"], writes=[f"x4b{o % 2}"], tag="x4a")
                S.op("sp", (lambda e, o=o: e.dma_start(
                    x4[o * 128:(o + 1) * 128, :], x4b[o % 2][:, :])),
                    reads=[f"x4b{o % 2}"], writes=[f"x4st{o % 2}"], dma=True)

    S.op("sp", lambda e: e.nop(), reads=["x4st0", "x4st1"])



def build_fused():
    import os as _os0
    nc = bass.Bass("TRN2", target_bir_lowering=False)
    with ExitStack() as octx:
        D = {}
        D["x"] = nc.dram_tensor("x", [C, TL], F32, kind="ExternalInput")
        D["wqkT"] = nc.dram_tensor("wqkT", [32, C, 128], F32, kind="ExternalInput")
        D["wvT"] = nc.dram_tensor("wvT", [16, C, 128], F32, kind="ExternalInput")
        D["woT"] = nc.dram_tensor("woT", [16, C, 128], F32, kind="ExternalInput")
        D["cos0"] = nc.dram_tensor("cos0", [128, TL], BF16, kind="ExternalInput")
        D["sin0"] = nc.dram_tensor("sin0", [128, TL], BF16, kind="ExternalInput")
        D["mask0"] = nc.dram_tensor("mask0", [128, MASK_COLS], BF16, kind="ExternalInput")
        D["onesd"] = nc.dram_tensor("onesd", [128, 1], BF16, kind="ExternalInput")
        D["identd"] = nc.dram_tensor("identd", [128, 128], BF16, kind="ExternalInput")
        D["x1i"] = nc.dram_tensor("x1i", [C, TL], F32)
        D["k_in"] = nc.dram_tensor("k_in", [C, TL], BF16)
        D["k_out"] = nc.dram_tensor("k_out", [2 * C, TL], BF16)
        D["v_in"] = nc.dram_tensor("v_in", [TL, C], BF16)
        D["v_out"] = nc.dram_tensor("v_out", [T, C], BF16)
        D["lnb"] = nc.dram_tensor("lnb", [8, TL], F32)
        D["w1s"] = nc.dram_tensor("w1s", [NHT, C, 128], F32, kind="ExternalInput")
        D["w3s"] = nc.dram_tensor("w3s", [NHT, C, 128], F32, kind="ExternalInput")
        D["w2s"] = nc.dram_tensor("w2s", [NCH, HIDP, 128], F32, kind="ExternalInput")
        D["onesf"] = nc.dram_tensor("onesf", [128, 1], F32, kind="ExternalInput")
        if "x2i" not in D:
            if "x2i" not in D:
                D["x2i"] = nc.dram_tensor("x2i", [C, TL], F32)
        D["wxT"] = nc.dram_tensor("wxT", [C, 160], F32, kind="ExternalInput")
        D["wdtT"] = nc.dram_tensor("wdtT", [128, 1024], F32, kind="ExternalInput")
        D["alogh"] = nc.dram_tensor("alogh", [128, 128], F32, kind="ExternalInput")
        D["bdth"] = nc.dram_tensor("bdth", [128, 8], F32, kind="ExternalInput")
        D["dfull"] = nc.dram_tensor("dfull", [128, 16], F32, kind="ExternalInput")
        D["hoffd"] = nc.dram_tensor("hoffd", [1, 4], mybir.dt.int32, kind="ExternalInput")
        D["x3i"] = nc.dram_tensor("x3i", [C, TL], F32)
        D["e2in"] = nc.dram_tensor("e2in", [160, TL], BF16)
        D["e2out"] = nc.dram_tensor("e2out", [320, TL], BF16)
        D["e2inB"] = nc.dram_tensor("e2inB", [2048, TL], BF16)
        D["e2outB"] = nc.dram_tensor("e2outB", [4096, TL], BF16)
        D["e3in"] = nc.dram_tensor("e3in", [1024, T], BF16)
        D["e3out"] = nc.dram_tensor("e3out", [C, T], BF16)
        D["kvdw"] = nc.dram_tensor("kvdw", [NLAT, C, 128], F32, kind="ExternalInput")
        D["qdw"] = nc.dram_tensor("qdw", [NLAT, C, 128], F32, kind="ExternalInput")
        D["kuw"] = nc.dram_tensor("kuw", [8, LAT, 128], F32, kind="ExternalInput")
        D["quw"] = nc.dram_tensor("quw", [8, LAT, 128], F32, kind="ExternalInput")
        D["vuw"] = nc.dram_tensor("vuw", [LAT, C], F32, kind="ExternalInput")
        D["rkw"] = nc.dram_tensor("rkw", [8, C, 128], F32, kind="ExternalInput")
        D["rqw"] = nc.dram_tensor("rqw", [8, LAT, 128], F32, kind="ExternalInput")
        D["oww"] = nc.dram_tensor("oww", [NCH, C, 128], F32, kind="ExternalInput")
        D["cos1"] = nc.dram_tensor("cos1", [128, TL], BF16, kind="ExternalInput")
        D["sin1"] = nc.dram_tensor("sin1", [128, TL], BF16, kind="ExternalInput")
        D["l1wd"] = nc.dram_tensor("l1wd", [128, 16], F32, kind="ExternalInput")
        D["hi"] = nc.dram_tensor("hi", [C, TL], F32)
        D["shg"] = nc.dram_tensor("shg", [NIT, C, 128], F32, kind="ExternalInput")
        D["shu"] = nc.dram_tensor("shu", [NIT, C, 128], F32, kind="ExternalInput")
        D["shd"] = nc.dram_tensor("shd", [NCH, IEXP, 128], F32, kind="ExternalInput")
        D["rg"] = nc.dram_tensor("rg", [7, NIT, C, 128], F32, kind="ExternalInput")
        D["ru"] = nc.dram_tensor("ru", [7, NIT, C, 128], F32, kind="ExternalInput")
        D["rd"] = nc.dram_tensor("rd", [7, NCH, IEXP, 128], F32, kind="ExternalInput")
        D["rwT"] = nc.dram_tensor("rwT", [C, 7], F32, kind="ExternalInput")
        D["rbd"] = nc.dram_tensor("rbd", [7, 1], F32, kind="ExternalInput")
        D["l2wd"] = nc.dram_tensor("l2wd", [128, 16], F32, kind="ExternalInput")
        D["x4"] = nc.dram_tensor("x4", [C, TL], F32, kind="ExternalOutput")
        D["sder"] = nc.dram_tensor("sder", [1, 16], F32)
        D["drfg"] = nc.dram_tensor("drfg", [128, T], BF16)
        D["e2xg"] = nc.dram_tensor("e2xg", [C, T], BF16)
        D["e3b"] = nc.dram_tensor("e3b", [C, T], BF16)
        D["bcg"] = nc.dram_tensor("bcg", [32, T], BF16)
        D["wbn"] = nc.dram_tensor("wbn", [8, TL], F32)
        D["_octx"] = octx
        D["abscol"] = octx.enter_context(nc.sbuf_tensor("abscol_g", [128, 4], F32))
        D["abstmp"] = octx.enter_context(nc.sbuf_tensor("abstmp_g", [128, 1], F32))
        S = Sched(nc, octx)
        groups = ([emit_prelude, emit_phase_a], [emit_phase_b],
                  [emit_phase_c], [emit_phase_d], [emit_phase_e])
        for grp in groups:
            with ExitStack() as pctx:
                for em in grp:
                    em(nc, S, pctx, D)
            S.barrier()
        S.emit_all()
    return nc


def _stage_fused(inputs):
    ia = _stage_phase_a(inputs)
    ib = _stage_phase_b(inputs)
    ic = _stage_phase_c(inputs)
    id_ = _stage_phase_d(inputs)
    ie = _stage_phase_e(inputs)
    in_maps = []
    for c in range(8):
        m = {}
        for part in (ia[c], ib[c], ic[c], id_[c], ie[c]):
            m.update(part)
        in_maps.append(m)
    return in_maps


def kernel(**inputs):
    nc = build_fused()
    in_maps = _stage_fused(inputs)
    res = run_bass_kernel_spmd(nc, in_maps, core_ids=list(range(8)))
    out = np.zeros((B, T, C), np.float32)
    for c in range(8):
        b, j = divmod(c, 2)
        out[b, _tok_idx(j), :] = res.results[c]["x4"].T
    return out


# version 28
# speedup vs baseline: 1.0261x; 1.0113x over previous
import numpy as np
import ml_dtypes
from contextlib import ExitStack

import concourse.bass as bass
import concourse.mybir as mybir
from concourse.bass_utils import run_bass_kernel_spmd

F32 = mybir.dt.float32
BF16 = mybir.dt.bfloat16
AF = mybir.ActivationFunctionType
ALU = mybir.AluOpType
BF = ml_dtypes.bfloat16

B, T, C = 4, 2048, 2048
NH, DH = 16, 128
TL = T // 2
NCH = C // 128
NT = TL // 512
NKT = T // 128
LN_EPS = 1e-5
RMS_EPS0 = 1.1920929e-07
NEG = -1.0e30
ISQ = 1.0 / np.sqrt(128.0)

PAIRS = [[0, 1], [2, 3], [4, 5], [6, 7]]


CAPZ = [16 - 2 * l for l in range(8)]


def tok_blocks(p):
    return [(15 - p) - 2 * l for l in range(8)]


def _wid(kt):
    return 128 * ((16 - kt + 1) // 2)


MSEGS = []
for _l in range(8):
    _cap = CAPZ[_l]
    MSEGS.append((_cap - 2, _l, (2 * _l) * 128))
    MSEGS.append((_cap - 1, _l, (2 * _l + 1) * 128))
MASK_COLS = 16 * 128


QUEUES = ("pe", "act", "dve", "pool", "sp")


NDMA_SEMS = 4


class Sched:

    def __init__(self, nc, ctx):
        self.nc = nc
        self.sems = {}
        for q in QUEUES:
            self.sems[f"c_{q}"] = ctx.enter_context(nc.semaphore(f"c_{q}"))
            for k in range(NDMA_SEMS):
                self.sems[f"d_{q}{k}"] = ctx.enter_context(nc.semaphore(f"d_{q}{k}"))
        self.cval = {q: 0 for q in QUEUES}
        self.didx = {q: 0 for q in QUEUES}
        self.ops = {q: [] for q in QUEUES}
        self.seen = {q: {} for q in QUEUES}
        self.semval = {name: 0 for name in self.sems}
        self.res = {}

    def op(self, q, emit, reads=(), writes=(), dma=False, dma_n=1, tag=""):
        toks = []
        for k in reads:
            e = self.res.get(k)
            if e and e["w"]:
                toks.append(e["w"])
        for k in writes:
            e = self.res.get(k)
            if e:
                if e["w"]:
                    toks.append(e["w"])
                toks.extend(e["r"])
        need = {}
        for (dq, sem, dv) in toks:
            if dq == q and q == "pe" and sem == "c_pe":
                continue
            if dv > need.get(sem, -1):
                need[sem] = dv
        if dma:
            i = self.didx[q]
            self.didx[q] = i + 1
            sem = f"d_{q}{i % NDMA_SEMS}"
            prev = self.semval.get(sem, 0)
            if prev > 0 and prev > need.get(sem, -1):
                need[sem] = prev
            tok = (q, sem, prev + 16 * dma_n)
            inc_sem, inc = sem, 16
        else:
            self.cval[q] += 1
            tok = (q, f"c_{q}", self.cval[q])
            inc_sem, inc = f"c_{q}", 1
        waits = []
        for sem in sorted(need):
            dv = need[sem]
            if dv <= self.seen[q].get(sem, 0):
                continue
            waits.append((sem, dv))
            self.seen[q][sem] = dv
        self.ops[q].append((waits, emit, inc_sem, inc, tag))
        self.semval[inc_sem] = (self.semval.get(inc_sem, 0) + inc) if not dma else tok[2]
        for k in reads:
            self.res.setdefault(k, {"w": None, "r": []})["r"].append(tok)
        for k in writes:
            self.res[k] = {"w": tok, "r": []}
        return tok

    def barrier(self):
        vals = dict(self.semval)
        for q in QUEUES:
            waits = []
            for sem in sorted(vals):
                v = vals[sem]
                if v <= 0 or sem == f"c_{q}":
                    continue
                if v <= self.seen[q].get(sem, 0):
                    continue
                waits.append((sem, v))
                self.seen[q][sem] = v
            self.cval[q] += 1
            self.semval[f"c_{q}"] = self.cval[q]
            self.ops[q].append((waits, (lambda e: e.nop()), f"c_{q}", 1, "barrier"))

    def emit_all(self):
        nc = self.nc
        with nc.Block() as blk:
            def mk(q):
                def body(eng):
                    for oi, (waits, emit, inc_sem, inc, tag) in enumerate(self.ops[q]):
                        for sem, dv in waits:
                            eng.wait_ge(self.sems[sem], dv)
                        try:
                            inst = emit(eng)
                        except Exception:
                            print(f"EMIT FAIL q={q} idx={oi} tag={tag} waits={waits}")
                            raise
                        inst.then_inc(self.sems[inc_sem], inc)
                return body
            blk.tensor(mk("pe"))
            blk.scalar(mk("act"))
            blk.vector(mk("dve"))
            blk.gpsimd(mk("pool"))
            blk.sync(mk("sp"))


_NAMECTR = [0]


def _uniq(name):
    _NAMECTR[0] += 1
    return f"{name}_{_NAMECTR[0]}"


def sbt(ctx, nc, name, shape, dtype):
    return ctx.enter_context(nc.sbuf_tensor(_uniq(name), shape, dtype))


def pst(ctx, nc, name, shape, dtype=F32):
    return ctx.enter_context(nc.psum_tensor(_uniq(name), shape, dtype))



def mm_accum(S, n, psum_key, emit_i, reads_i, tag=""):
    for i in range(n):
        w = [psum_key] if (i == 0 or i == n - 1) else []
        S.op("pe", (lambda e, i=i: emit_i(e, i)), reads=reads_i(i), writes=w, tag=tag)


def emit_rstd_row(S, E, row, key, inv_n, eps):
    S.op("dve", lambda e: e.tensor_scalar(row[:, :], row[:, :], inv_n, eps,
                                          ALU.mult, ALU.add),
         reads=[key], writes=[key])
    S.op("act", lambda e: e.activation(row[:, :], row[:, :], AF.Ln),
         reads=[key], writes=[key])
    S.op("act", lambda e: e.activation(row[:, :], row[:, :], AF.Exp, scale=-0.5),
         reads=[key], writes=[key])


def emit_layernorm(S, E, xkeys, xtiles, lnb_row, gain_col=None, rms_only=False,
                   eps=LN_EPS, n_feat=C, apply=True):
    nch = len(xtiles)
    ps0, ps1, pr0, pr1 = E["ps0"], E["ps1"], E["pr0"], E["pr1"]
    rows, lnb = E["rows"], E["lnb"]
    for dc in range(nch):
        if not rms_only:
            for tc in range(NT):
                prt, prk = (pr0, "pr0") if tc == 0 else (pr1, "pr1")
                w = [prk] if (dc == 0 or dc == nch - 1) else []
                S.op("pe", (lambda e, dc=dc, tc=tc, prt=prt: e.matmul(
                    prt[0:1, :], E["ones_bf"][:, 0:1],
                    xtiles[dc][:, tc * 512:(tc + 1) * 512],
                    start=(dc == 0), stop=(dc == nch - 1))),
                    reads=[xkeys[dc], "ones"], writes=w, tag="lnS")
        S.op("act", (lambda e, dc=dc: e.activation(E["sqt"][:, 0:TL],
                                                   xtiles[dc][:, :], AF.Square)),
             reads=[xkeys[dc]], writes=[E.get("sqt_key", "sqt")], tag="lnSq")
        for tc in range(NT):
            pqt, pqk = (ps0, "ps0") if tc == 0 else (ps1, "ps1")
            w = [pqk] if (dc == 0 or dc == nch - 1) else []
            S.op("pe", (lambda e, dc=dc, tc=tc, pqt=pqt: e.matmul(
                pqt[0:1, :], E["ones_bf"][:, 0:1],
                E["sqt"][:, tc * 512:(tc + 1) * 512],
                start=(dc == 0), stop=(dc == nch - 1))),
                reads=[E.get("sqt_key", "sqt"), "ones"], writes=w, tag="lnQ")
    inv_n = 1.0 / n_feat
    for tc in range(NT):
        pqk = "ps0" if tc == 0 else "ps1"
        pqt = E[pqk]
        S.op("dve", (lambda e, tc=tc, pqt=pqt: e.tensor_copy(
            rows["qrow"][0:1, tc * 512:(tc + 1) * 512], pqt[0:1, :])),
            reads=[pqk], writes=["qrow"])
    if not rms_only:
        for tc in range(NT):
            prk = "pr0" if tc == 0 else "pr1"
            prt = E[prk]
            S.op("dve", (lambda e, tc=tc, prt=prt: e.tensor_copy(
                rows["srow"][0:1, tc * 512:(tc + 1) * 512], prt[0:1, :])),
                reads=[prk], writes=["srow"])
        S.op("dve", lambda e: e.tensor_scalar_mul(rows["srow"][:, :], rows["srow"][:, :], inv_n),
             reads=["srow"], writes=["srow"])
        S.op("dve", lambda e: e.tensor_scalar_mul(rows["qrow"][:, :], rows["qrow"][:, :], inv_n),
             reads=["qrow"], writes=["qrow"])
        S.op("dve", lambda e: e.tensor_tensor(rows["brow"][:, :], rows["srow"][:, :], rows["srow"][:, :], ALU.mult),
             reads=["srow"], writes=["brow"])
        S.op("dve", lambda e: e.tensor_tensor(rows["qrow"][:, :], rows["qrow"][:, :], rows["brow"][:, :], ALU.subtract),
             reads=["qrow", "brow"], writes=["qrow"])
        emit_rstd_row(S, E, rows["qrow"], "qrow", 1.0, eps)
    else:
        emit_rstd_row(S, E, rows["qrow"], "qrow", inv_n, eps)
    if not rms_only:
        S.op("dve", lambda e: e.tensor_tensor(rows["brow"][:, :], rows["srow"][:, :], rows["qrow"][:, :], ALU.mult),
             reads=["srow", "qrow"], writes=["brow"])
        S.op("dve", lambda e: e.tensor_scalar_mul(rows["brow"][:, :], rows["brow"][:, :], -1.0),
             reads=["brow"], writes=["brow"])
    S.op("sp", (lambda e: e.dma_start(
        bass.AP(lnb, lnb_row * TL, [[TL, 1], [1, TL]]), rows["qrow"][:, :])),
        reads=["qrow"], writes=[f"lnb{lnb_row}"], dma=True)
    if not rms_only:
        S.op("sp", (lambda e: e.dma_start(
            bass.AP(lnb, (lnb_row + 1) * TL, [[TL, 1], [1, TL]]), rows["brow"][:, :])),
            reads=["brow"], writes=[f"lnb{lnb_row + 1}"], dma=True)
    if not apply:
        return
    S.op("sp", (lambda e: e.dma_start(
        E["arep"][:, :], bass.AP(lnb, lnb_row * TL, [[0, 128], [1, TL]]))),
        reads=[f"lnb{lnb_row}"], writes=["arep"], dma=True)
    if not rms_only:
        S.op("sp", (lambda e: e.dma_start(
            E["brep"][:, :], bass.AP(lnb, (lnb_row + 1) * TL, [[0, 128], [1, TL]]))),
            reads=[f"lnb{lnb_row + 1}"], writes=["brep"], dma=True)
    for dc in range(nch):
        S.op("dve", (lambda e, dc=dc: e.tensor_tensor(
            xtiles[dc][:, :], xtiles[dc][:, :], E["arep"][:, :], ALU.mult)),
            reads=[xkeys[dc], "arep"], writes=[xkeys[dc]])
        if not rms_only:
            S.op("dve", (lambda e, dc=dc: e.tensor_tensor(
                xtiles[dc][:, :], xtiles[dc][:, :], E["brep"][:, :], ALU.add)),
                reads=[xkeys[dc], "brep"], writes=[xkeys[dc]])




def _rope_tables_l0():
    inv = 1.0 / 10000.0 ** (np.arange(0, DH, 2, dtype=np.float64) / DH)
    t = np.arange(T, dtype=np.float64)
    fr = np.outer(t, inv)
    sin = np.repeat(np.sin(fr), 2, axis=1)
    cos = np.repeat(np.cos(fr), 2, axis=1)
    sgn = np.where(np.arange(DH) % 2 == 0, -1.0, 1.0)
    return cos.T.astype(np.float32), (sin * sgn[None, :]).T.astype(np.float32)


def _mask_l0(is_odd):
    p = 1 if is_odd else 0
    m = np.zeros((128, MASK_COLS), np.float32)
    for (kt, l, off) in MSEGS:
        g = (15 - p) - 2 * l
        if kt > g:
            m[:, off:off + 128] = NEG
        elif kt == g:
            for c in range(128):
                m[c + 1:, off + c] = NEG
    return m


def _tok_idx(p):
    return np.concatenate([np.arange(g * 128, (g + 1) * 128)
                           for g in tok_blocks(p)])


def _stage_phase_a(inputs):
    x = np.asarray(inputs["L0_attn_x"])
    wqkv = np.asarray(inputs["L0_attn_W_qkv"])
    wo = np.asarray(inputs["L0_attn_W_o"])
    x_fm = np.ascontiguousarray(x.transpose(0, 2, 1))
    wqkvT = np.ascontiguousarray(wqkv.T)
    qk = np.ascontiguousarray(wqkvT[:, 0:4096].reshape(C, 32, 128).transpose(1, 0, 2))
    v = np.ascontiguousarray(wqkvT[:, 4096:6144].reshape(C, 16, 128).transpose(1, 0, 2))
    woT = np.ascontiguousarray(wo.T.reshape(C, 16, 128).transpose(1, 0, 2))
    cosf, sinf = _rope_tables_l0()
    ones = np.ones((128, 1), np.float32).astype(BF)
    ident = np.eye(128, dtype=np.float32).astype(BF)
    masks = [_mask_l0(False).astype(BF), _mask_l0(True).astype(BF)]
    in_maps = []
    for c in range(8):
        b, j = divmod(c, 2)
        idx = _tok_idx(j)
        in_maps.append({
            "x": np.ascontiguousarray(x_fm[b][:, idx]),
            "wqkT": qk, "wvT": v, "woT": woT,
            "cos0": np.ascontiguousarray(cosf[:, idx]).astype(BF),
            "sin0": np.ascontiguousarray(sinf[:, idx]).astype(BF),
            "mask0": masks[j],
            "onesd": ones, "identd": ident,
        })
    return in_maps




HID = 5461
HIDP = 5504
NHT = HIDP // 128


def _stage_phase_b(inputs, x1_fm=None, x1_pc=None):
    w1 = np.asarray(inputs["L0_ffn_w1"])
    w3 = np.asarray(inputs["L0_ffn_w3"])
    w2 = np.asarray(inputs["L0_ffn_w2"])

    def pad_slab_T(w):
        wT = np.zeros((C, HIDP), np.float32)
        wT[:, 0:HID] = w.T
        return np.ascontiguousarray(wT.reshape(C, NHT, 128).transpose(1, 0, 2))

    w1sl = pad_slab_T(w1)
    w3sl = pad_slab_T(w3)
    w2T = np.zeros((HIDP, C), np.float32)
    w2T[0:HID, :] = w2.T
    w2sl = np.ascontiguousarray(w2T.reshape(HIDP, NCH, 128).transpose(1, 0, 2))
    ones = np.ones((128, 1), np.float32)
    in_maps = []
    for c in range(8):
        b, j = divmod(c, 2)
        m = {"w1s": w1sl, "w3s": w3sl, "w2s": w2sl,
             "onesd": ones.astype(BF), "onesf": ones}
        if x1_pc is not None:
            m["x1d"] = x1_pc[c]
        elif x1_fm is not None:
            m["x1d"] = np.ascontiguousarray(x1_fm[b, :, j * TL:(j + 1) * TL])
        in_maps.append(m)
    return in_maps


NSSM = 16
TCH = 512
NCHK = T // TCH
E2ROWS = 160 + C


def _stage_phase_c(inputs, x2_fm=None, x2_pc=None):
    wx = np.asarray(inputs["L1_attn_W_xproj"])
    wdtf = np.asarray(inputs["L1_attn_W_dt"])
    bdt = np.asarray(inputs["L1_attn_b_dt"])
    alog = np.asarray(inputs["L1_attn_A_log"])
    dpar = np.asarray(inputs["L1_attn_D_param"])
    wxTs = np.ascontiguousarray(wx.T)
    dcols = np.ascontiguousarray(dpar.reshape(16, 128).T)
    ones = np.ones((128, 1), np.float32).astype(BF)
    in_maps = []
    for c in range(8):
        b, half = divmod(c, 2)
        dsl = slice(half * 1024, (half + 1) * 1024)
        wdtT_h = np.ascontiguousarray(wdtf[dsl, :].T)
        alog_h = np.ascontiguousarray(
            alog[dsl].reshape(8, 128, 16).transpose(1, 0, 2).reshape(128, 128))
        bdt_h = np.ascontiguousarray(bdt[dsl].reshape(8, 128).T)
        hoffv = np.array([[half * 1024 * T, half * TL, 0, 0]], np.int32)
        m = {"wxT": wxTs, "wdtT": wdtT_h, "alogh": alog_h, "bdth": bdt_h,
             "dfull": dcols, "onesd": ones, "hoffd": hoffv}
        if x2_pc is not None:
            m["x2d"] = x2_pc[c]
        elif x2_fm is not None:
            m["x2d"] = np.ascontiguousarray(x2_fm[b, :, half * TL:(half + 1) * TL])
        in_maps.append(m)
    return in_maps


LAT = 512
NLAT = LAT // 128


def _rope_tables_l1():
    RD = 64
    inv = 1.0 / 10000.0 ** (np.arange(0, RD, 2, dtype=np.float64) / RD)
    t = np.arange(T, dtype=np.float64)
    ang = np.outer(t, inv)
    cosb = np.repeat(np.cos(ang), 2, axis=1)
    sinb = np.repeat(np.sin(ang), 2, axis=1)
    sgn = np.where(np.arange(64) % 2 == 0, -1.0, 1.0)
    sinb = sinb * sgn[None, :]
    cos128 = np.concatenate([cosb, cosb], axis=1).T
    sin128 = np.concatenate([sinb, sinb], axis=1).T
    return cos128.astype(np.float32), sin128.astype(np.float32)


ROPE_PERM = np.empty(64, np.int64)
ROPE_PERM[0::2] = np.arange(32)
ROPE_PERM[1::2] = np.arange(32) + 32


def _perm_rope_cols(w):
    w2 = w.reshape(w.shape[0], 16, 64)[:, :, ROPE_PERM]
    return np.ascontiguousarray(w2.reshape(w.shape[0], 1024))


def _slab(w, kdim, nf):
    return np.ascontiguousarray(w.reshape(kdim, nf, 128).transpose(1, 0, 2))


def _stage_phase_d(inputs, x3_fm=None, x3_pc=None):
    g = lambda k: np.asarray(inputs[f"L1_ffn_{k}"])
    kvdw = _slab(g("kv_d_w"), C, NLAT)
    qdw = _slab(g("q_d_w"), C, NLAT)
    kuw = _slab(g("k_u_w"), LAT, 8)
    quw = _slab(g("q_u_w"), LAT, 8)
    vuw = np.ascontiguousarray(g("v_u_w"))
    rkw = _slab(_perm_rope_cols(g("rope_k_w")), C, 8)
    rqw = _slab(_perm_rope_cols(g("rope_q_w")), LAT, 8)
    oww = _slab(g("o_w"), C, NCH)
    l1w = np.ascontiguousarray(g("ln1_w").reshape(16, 128).T)
    cosf, sinf = _rope_tables_l1()
    ones = np.ones((128, 1), np.float32).astype(BF)
    masks = [_mask_l0(False).astype(BF), _mask_l0(True).astype(BF)]
    in_maps = []
    for c in range(8):
        b, j = divmod(c, 2)
        idx = _tok_idx(j)
        m = {"kvdw": kvdw, "qdw": qdw, "kuw": kuw, "quw": quw, "vuw": vuw,
             "rkw": rkw, "rqw": rqw, "oww": oww, "l1wd": l1w,
             "cos1": np.ascontiguousarray(cosf[:, idx]).astype(BF),
             "sin1": np.ascontiguousarray(sinf[:, idx]).astype(BF),
             "mask0": masks[j], "onesd": ones,
             "identd": np.eye(128, dtype=np.float32).astype(BF)}
        if x3_pc is not None:
            m["x3d"] = x3_pc[c]
        elif x3_fm is not None:
            m["x3d"] = np.ascontiguousarray(x3_fm[b, :, cols])
        in_maps.append(m)
    return in_maps


IEXP = 1024
NIT = IEXP // 128
NEXP = 8


def _stage_phase_e(inputs, h_fm=None, x3_fm=None, h_pc=None, x3_pc=None):
    g = lambda k: np.asarray(inputs[f"L1_ffn_{k}"])
    shg = _slab(g("sh_gate"), C, NIT)
    shu = _slab(g("sh_up"), C, NIT)
    shd = _slab(g("sh_down"), IEXP, NCH)
    rgs = np.stack([_slab(g("r_gate")[e], C, NIT) for e in range(7)])
    rus = np.stack([_slab(g("r_up")[e], C, NIT) for e in range(7)])
    rds = np.stack([_slab(g("r_down")[e], IEXP, NCH) for e in range(7)])
    rwTs = np.ascontiguousarray(g("router_w"))
    rbs = np.ascontiguousarray(g("router_b").reshape(7, 1))
    l2w = np.ascontiguousarray(g("ln2_w").reshape(16, 128).T)
    ones = np.ones((128, 1), np.float32).astype(BF)
    in_maps = []
    for c in range(8):
        b, j = divmod(c, 2)
        cols = slice(j * TL, (j + 1) * TL)
        m = {"shg": shg, "shu": shu, "shd": shd, "rg": rgs, "ru": rus, "rd": rds,
             "rwT": rwTs, "rbd": rbs, "l2wd": l2w, "onesd": ones}
        if h_pc is not None:
            m["hdd"] = h_pc[c]
            m["x3d"] = x3_pc[c]
        elif h_fm is not None:
            m["hdd"] = np.ascontiguousarray(h_fm[b, :, cols])
            m["x3d"] = np.ascontiguousarray(x3_fm[b, :, cols])
        in_maps.append(m)
    return in_maps


def emit_prelude(nc, S, ctx, D):
    w1s, w3s, w2s = D["w1s"], D["w3s"], D["w2s"]
    abscol, abstmp = D["abscol"], D["abstmp"]
    pwf = [sbt(ctx, nc, f"pwf{i}", [128, C], F32) for i in range(2)]
    pab = sbt(ctx, nc, "pab", [128, C], BF16)
    S.op("dve", lambda e: e.memset(abscol[:, :], 0.0), writes=["abscol"])
    ti = 0
    for (srcd, gi, nel) in ((w1s, 0, NHT * C * 128), (w3s, 1, NHT * C * 128),
                            (w2s, 2, NCH * HIDP * 128)):
        ntile = nel // (128 * C)
        for t_ in range(ntile):
            wk = f"pwf{ti % 2}"
            S.op("sp" if ti % 2 == 0 else "pool",
                 (lambda e, srcd=srcd, t_=t_, ti=ti: e.dma_start(
                     pwf[ti % 2][:, :],
                     bass.AP(srcd, t_ * 128 * C, [[C, 128], [1, C]]))),
                 writes=[wk], dma=True, tag="pwl")
            S.op("act", (lambda e, ti=ti: e.activation(
                pab[:, :], pwf[ti % 2][:, :], AF.Abs,
                accum_out=abstmp[:, 0:1])),
                reads=[wk], writes=["pab", "abstmp"], tag="pabs")
            S.op("dve", (lambda e, gi=gi: e.tensor_tensor(
                abscol[:, gi:gi + 1], abscol[:, gi:gi + 1], abstmp[:, 0:1], ALU.add)),
                reads=["abstmp", "abscol"], writes=["abscol"], tag="pacc")
            ti += 1



def emit_phase_a(nc, S, ctx, D):
    x = D["x"]
    wqkT = D["wqkT"]
    wvT = D["wvT"]
    woT = D["woT"]
    cos0 = D["cos0"]
    sin0 = D["sin0"]
    mask0 = D["mask0"]
    onesd = D["onesd"]
    identd = D["identd"]
    x1 = D["x1i"]
    k_in = D["k_in"]
    k_out = D["k_out"]
    v_in = D["v_in"]
    v_out = D["v_out"]
    lnb = D["lnb"]
    xb = [sbt(ctx, nc, f"xb{i}", [128, TL], BF16) for i in range(NCH)]
    qt = [sbt(ctx, nc, f"qt{i}", [128, TL], BF16) for i in range(NH)]
    ot = [sbt(ctx, nc, f"ot{i}", [128, TL], BF16) for i in range(NH)]
    sqt2 = [sbt(ctx, nc, f"sqt2_{i}", [128, TL], BF16) for i in range(2)]
    sqt = sqt2[0]
    wslab = [sbt(ctx, nc, f"wslab{i}", [128, C], BF16) for i in range(2)]
    ktile = [sbt(ctx, nc, f"ktile{i}", [128, TL], BF16) for i in range(2)]
    shuf2 = [sbt(ctx, nc, f"shuf2_{i}", [128, TL], BF16) for i in range(2)]
    cosT = sbt(ctx, nc, "cosT", [128, TL], BF16)
    sinT = sbt(ctx, nc, "sinT", [128, TL], BF16)
    maskT = sbt(ctx, nc, "maskT", [128, MASK_COLS], BF16)
    ones_bf = sbt(ctx, nc, "ones_bf", [128, 1], BF16)
    ident = sbt(ctx, nc, "ident", [128, 128], BF16)
    khead = [sbt(ctx, nc, f"khead{i}", [128, T], BF16) for i in range(2)]
    vhead = [sbt(ctx, nc, f"vhead{i}", [128, T], BF16) for i in range(2)]
    ptile2 = [sbt(ctx, nc, f"ptile{i}", [128, TL], BF16) for i in range(2)]
    vexp = sbt(ctx, nc, "vexp", [128, TL], BF16)

    arep = sbt(ctx, nc, "arep", [128, TL], F32)
    brep = sbt(ctx, nc, "brep", [128, TL], F32)
    rrep2 = [arep, brep]
    xres = [sbt(ctx, nc, "xres0", [128, TL], F32)]
    x1b = [sbt(ctx, nc, f"x1b{i}", [128, TL], F32) for i in range(2)]
    rows = {k: sbt(ctx, nc, k, [1, TL], F32) for k in
            ("srow", "qrow", "brow")}
    rrow2 = [sbt(ctx, nc, f"rrow2_{i}", [1, TL], F32) for i in range(2)]

    ps4 = pst(ctx, nc, "ps4", [128, 512])
    ps0 = pst(ctx, nc, "ps0", [128, 512])
    ps1 = pst(ctx, nc, "ps1", [128, 512])
    ps2 = pst(ctx, nc, "ps2", [128, 512])
    ps3 = pst(ctx, nc, "ps3", [128, 512])
    pr0 = pst(ctx, nc, "pr0", [128, 512])
    pr1 = pst(ctx, nc, "pr1", [128, 512])
    psb = pst(ctx, nc, "psb", [128, 1024], BF16)
    PS = {0: ps0, 1: ps1, 2: ps2, 3: ps3}

    E = {"ones_bf": ones_bf, "sqt": sqt, "sqt_key": "sqtp0", "rows": rows,
         "lnb": lnb, "arep": arep, "brep": brep,
         "ps0": ps0, "ps1": ps1, "pr0": pr0, "pr1": pr1}

    for dc in range(NCH):
        S.op("pool", (lambda e, dc=dc: e.dma_start(
            xb[dc][:, :], x[dc * 128:(dc + 1) * 128, :])),
            writes=[f"xb{dc}"], dma=True)
    S.op("sp", lambda e: e.dma_start(cosT[:, :], cos0[:, :]), writes=["cosT"], dma=True)
    S.op("sp", lambda e: e.dma_start(sinT[:, :], sin0[:, :]), writes=["sinT"], dma=True)
    S.op("sp", lambda e: e.dma_start(maskT[:, :], mask0[:, :]), writes=["maskT"], dma=True)
    S.op("sp", lambda e: e.dma_start(ones_bf[:, :], onesd[:, :]), writes=["ones"], dma=True)
    S.op("sp", lambda e: e.dma_start(ident[:, :], identd[:, :]), writes=["ident"], dma=True)

    xkeys = [f"xb{i}" for i in range(NCH)]
    emit_layernorm(S, E, xkeys, xb, 0)

    def proj_head(src_dram, f_idx, dest_tile, dest_key, do_rope):
        par = f_idx % 2
        wk = f"wslab{par}"
        shf, shfk = shuf2[par], f"shuf{par}"
        sq, sqk = sqt2[par], f"sqtp{par}"
        rro, rrok = rrow2[par], f"rrow{par}"
        rrp, rrpk = rrep2[par], f"rrep{par}"
        S.op("pool", (lambda e: e.dma_start(
            wslab[par][:, :],
            bass.AP(src_dram, f_idx * C * 128,
                    [[128, 128], [128 * 128, NCH], [1, 128]]))),
            writes=[wk], dma=True)
        for tc in range(NT):
            pk = ("ps2", "ps3", "ps4")[(f_idx * NT + tc) % 3]
            pt = E_PS[pk]
            for dc in range(NCH):
                w = [pk] if (dc == 0 or dc == NCH - 1) else []
                S.op("pe", (lambda e, tc=tc, dc=dc, pt=pt: e.matmul(
                    pt[:, :], wslab[par][:, dc * 128:(dc + 1) * 128],
                    xb[dc][:, tc * 512:(tc + 1) * 512],
                    start=(dc == 0), stop=(dc == NCH - 1))),
                    reads=[wk, f"xb{dc}"], writes=w, tag=f"pj{f_idx}")
            if par == 0:
                S.op("dve", (lambda e, tc=tc, pt=pt: e.tensor_copy(
                    dest_tile[:, tc * 512:(tc + 1) * 512], pt[:, :])),
                    reads=[pk], writes=[dest_key], tag="pjev")
            else:
                S.op("act", (lambda e, tc=tc, pt=pt: e.copy(
                    dest_tile[:, tc * 512:(tc + 1) * 512], pt[:, :])),
                    reads=[pk], writes=[dest_key], tag="pjev")
        if not do_rope:
            return
        mask32 = [i ^ 1 for i in range(32)]
        S.op("dve", lambda e: e.stream_shuffle(shf[:, :], dest_tile[:, :], mask32),
             reads=[dest_key], writes=[shfk])
        S.op("dve", lambda e: e.tensor_tensor(dest_tile[:, :], dest_tile[:, :], cosT[:, :], ALU.mult),
             reads=[dest_key, "cosT"], writes=[dest_key])
        S.op("dve", lambda e: e.tensor_tensor(shf[:, :], shf[:, :], sinT[:, :], ALU.mult),
             reads=[shfk, "sinT"], writes=[shfk])
        S.op("dve", lambda e: e.tensor_tensor(dest_tile[:, :], dest_tile[:, :], shf[:, :], ALU.add),
             reads=[dest_key, shfk], writes=[dest_key])
        S.op("act", lambda e: e.activation(sq[:, :], dest_tile[:, :], AF.Square),
             reads=[dest_key], writes=[sqk])
        for tc in range(NT):
            prk = "pr0" if tc == 0 else "pr1"
            prt = E[prk]
            S.op("pe", (lambda e, tc=tc, prt=prt: e.matmul(
                prt[0:1, :], ones_bf[:, 0:1], sq[:, tc * 512:(tc + 1) * 512],
                start=True, stop=True)),
                reads=[sqk, "ones"], writes=[prk])
            S.op("dve", (lambda e, tc=tc, prt=prt: e.tensor_copy(
                rro[0:1, tc * 512:(tc + 1) * 512], prt[0:1, :])),
                reads=[prk], writes=[rrok])
        emit_rstd_row(S, E, rro, rrok, 1.0 / 128, RMS_EPS0)
        S.op("sp", (lambda e: e.dma_start(
            bass.AP(lnb, (2 + par) * TL, [[TL, 1], [1, TL]]), rro[:, :])),
            reads=[rrok], writes=[f"lnb{2 + par}"], dma=True)
        S.op("sp", (lambda e: e.dma_start(
            rrp[:, :], bass.AP(lnb, (2 + par) * TL, [[0, 128], [1, TL]]))),
            reads=[f"lnb{2 + par}"], writes=[rrpk], dma=True)
        S.op("dve", lambda e: e.tensor_tensor(dest_tile[:, :], dest_tile[:, :], rrp[:, :], ALU.mult),
             reads=[dest_key, rrpk], writes=[dest_key])

    E_PS = {"ps2": ps2, "ps3": ps3, "ps4": ps4}
    for h in range(NH):
        proj_head(wqkT, h, qt[h], f"qt{h}", True)
    for h in range(NH):
        kb = ktile[h % 2]
        proj_head(wqkT, 16 + h, kb, f"ktile{h % 2}", True)
        S.op("sp", (lambda e, h=h, kb=kb: e.dma_start(
            k_in[h * 128:(h + 1) * 128, :], kb[:, :])),
            reads=[f"ktile{h % 2}"], writes=[f"k_in{h}"], dma=True)

    S.op("pool", (lambda e: e.collective_compute(
        "AllGather", ALU.bypass, replica_groups=PAIRS,
        ins=[k_in.ap().opt()], outs=[k_out.ap().opt()])),
        reads=[f"k_in{h}" for h in range(NH)], writes=["k_out"], tag="agK")

    for h in range(NH):
        proj_head(wvT, h, ktile[0], "ktile0", False)
        for tt in range(8):
            S.op("pe", (lambda e, tt=tt: e.transpose(
                psb[:, tt * 128:(tt + 1) * 128],
                ktile[0][:, tt * 128:(tt + 1) * 128], ident[:, :])),
                reads=["ktile0", "ident"], writes=["psb"], tag="vT")
        S.op("dve", lambda e: e.tensor_copy(vexp[:, :], psb[:, :]),
             reads=["psb"], writes=["vexp"])
        S.op("sp", (lambda e, h=h: e.dma_start(
            bass.AP(v_in, h * 128, [[C, 128], [128 * C, 8], [1, 128]]),
            vexp[:, :])),
            reads=["vexp"], writes=[f"v_in{h}"], dma=True)

    S.op("pool", (lambda e: e.collective_compute(
        "AllGather", ALU.bypass, replica_groups=PAIRS,
        ins=[v_in.ap().opt()], outs=[v_out.ap().opt()])),
        reads=[f"v_in{h}" for h in range(NH)], writes=["v_out"], tag="agV")

    AV_LAST = {0: 15, 1: 7}
    for h in range(NH):
        kb, kk = khead[h % 2], f"khead{h % 2}"
        vb, vk = vhead[h % 2], f"vhead{h % 2}"
        for r_ in range(2):
            S.op("sp", (lambda e, h=h, kb=kb, r_=r_: e.dma_start(
                bass.AP(kb, (1 - r_) * 128, [[T, 128], [256, 8], [1, 128]]),
                bass.AP(k_out, r_ * C * TL + h * 128 * TL + 7 * 128,
                        [[TL, 128], [-128, 8], [1, 128]]))),
                reads=["k_out", "v_out"], writes=[kk], dma=True)
            S.op("sp", (lambda e, h=h, vb=vb, r_=r_: e.dma_start(
                bass.AP(vb, (1 - r_) * 128, [[T, 128], [256, 8], [1, 128]]),
                bass.AP(v_out, r_ * TL * C + 7 * 128 * C + h * 128,
                        [[C, 128], [-128 * C, 8], [1, 128]]))),
                reads=["v_out", "k_out"], writes=[vk], dma=True)
        first = [True, True]
        for kt in range(NKT):
            wd = _wid(kt)
            msegs_kt = [(l_, off_) for (mk_, l_, off_) in MSEGS if mk_ == kt]
            chunks = []
            c0 = 0
            while c0 < wd:
                chunks.append((c0, min(c0 + 512, wd)))
                c0 = min(c0 + 512, wd)
            for ci, (c0, c1) in enumerate(chunks):
                spk = ("ps2", "ps3", "ps4")[(kt + ci) % 3]
                sps = E_PS[spk]
                S.op("pe", (lambda e, kt=kt, c0=c0, c1=c1, sps=sps, kb=kb, h=h: e.matmul(
                    sps[:, 0:c1 - c0], kb[:, kt * 128:(kt + 1) * 128],
                    qt[h][:, c0:c1], start=True, stop=True)),
                    reads=[kk, f"qt{h}"], writes=[spk], tag=f"sc{h}")
                for (ml_, moff_) in msegs_kt:
                    a0 = ml_ * 128
                    if a0 >= c0 and a0 + 128 <= c1:
                        S.op("dve", (lambda e, a0=a0, c0=c0, sps=sps, moff_=moff_: e.tensor_tensor(
                            sps[:, a0 - c0:a0 - c0 + 128], sps[:, a0 - c0:a0 - c0 + 128],
                            maskT[:, moff_:moff_ + 128], ALU.add)),
                            reads=[spk, "maskT"], writes=[spk], tag="msk")
                S.op("act", (lambda e, c0=c0, c1=c1, sps=sps, kt=kt: e.activation(
                    ptile2[kt % 2][:, c0:c1], sps[:, 0:c1 - c0], AF.Exp, scale=ISQ)),
                    reads=[spk], writes=[f"ptile{kt % 2}"], tag="exp")
            b0 = 0
            while b0 < wd:
                blk_i = b0 // 512
                b1 = min((blk_i + 1) * 512, wd)
                opk = "ps0" if blk_i == 0 else "ps1"
                opt_ = PS[blk_i]
                rpk = "pr0" if blk_i == 0 else "pr1"
                rpt = E[rpk]
                wo_ = [opk] if (first[blk_i] or kt == AV_LAST[blk_i]) else []
                wr_ = [rpk] if (first[blk_i] or kt == AV_LAST[blk_i]) else []
                S.op("pe", (lambda e, kt=kt, b0=b0, b1=b1, blk_i=blk_i, vb=vb,
                            opt_=opt_, f=first[blk_i]: e.matmul(
                    opt_[:, b0 - blk_i * 512:b1 - blk_i * 512],
                    vb[:, kt * 128:(kt + 1) * 128], ptile2[kt % 2][:, b0:b1],
                    start=f, stop=(kt == AV_LAST[blk_i]))),
                    reads=[vk, f"ptile{kt % 2}"], writes=wo_, tag=f"av{h}")
                S.op("pe", (lambda e, kt=kt, b0=b0, b1=b1, blk_i=blk_i,
                            rpt=rpt, f=first[blk_i]: e.matmul(
                    rpt[0:1, b0 - blk_i * 512:b1 - blk_i * 512],
                    ones_bf[:, 0:1], ptile2[kt % 2][:, b0:b1],
                    start=f, stop=(kt == AV_LAST[blk_i]))),
                    reads=[f"ptile{kt % 2}", "ones"], writes=wr_, tag=f"rs{h}")
                first[blk_i] = False
                b0 = b1
        par = h % 2
        rro, rrok = rrow2[par], f"rrow{par}"
        rrp, rrpk = rrep2[par], f"rrep{par}"
        for blk_i in range(2):
            rpk = "pr0" if blk_i == 0 else "pr1"
            rpt = E[rpk]
            S.op("dve", (lambda e, blk_i=blk_i, rpt=rpt, rro=rro: e.tensor_copy(
                rro[0:1, blk_i * 512:(blk_i + 1) * 512], rpt[0:1, :])),
                reads=[rpk], writes=[rrok])
        S.op("dve", (lambda e, rro=rro: e.reciprocal(rro[:, :], rro[:, :])),
             reads=[rrok], writes=[rrok])
        S.op("sp", (lambda e, rro=rro: e.dma_start(
            bass.AP(lnb, (4 + par) * TL, [[TL, 1], [1, TL]]), rro[:, :])),
            reads=[rrok], writes=[f"lnb{4 + par}"], dma=True)
        S.op("sp", (lambda e, rrp=rrp: e.dma_start(
            rrp[:, :], bass.AP(lnb, (4 + par) * TL, [[0, 128], [1, TL]]))),
            reads=[f"lnb{4 + par}"], writes=[rrpk], dma=True)
        for blk_i in range(2):
            opk = "ps0" if blk_i == 0 else "ps1"
            opt_ = PS[blk_i]
            q_ = "dve" if blk_i == 0 else "act"
            if q_ == "dve":
                S.op("dve", (lambda e, blk_i=blk_i, h=h, opt_=opt_: e.tensor_copy(
                    ot[h][:, blk_i * 512:(blk_i + 1) * 512], opt_[:, :])),
                    reads=[opk], writes=[f"ot{h}"], tag="oev")
            else:
                S.op("act", (lambda e, blk_i=blk_i, h=h, opt_=opt_: e.copy(
                    ot[h][:, blk_i * 512:(blk_i + 1) * 512], opt_[:, :])),
                    reads=[opk], writes=[f"ot{h}"], tag="oev")
        S.op("dve", (lambda e, h=h, rrp=rrp: e.tensor_tensor(
            ot[h][:, :], ot[h][:, :], rrp[:, :], ALU.mult)),
            reads=[f"ot{h}", rrpk], writes=[f"ot{h}"], tag="onrm")

    for o in range(NCH):
        wk = f"wslab{o % 2}"
        S.op("pool", (lambda e, o=o: e.dma_start(
            wslab[o % 2][:, :],
            bass.AP(woT, o * C * 128, [[128, 128], [128 * 128, NCH], [1, 128]]))),
            writes=[wk], dma=True)
        S.op("sp", (lambda e, o=o: e.dma_start(
            xres[0][:, :], x[o * 128:(o + 1) * 128, :])),
            writes=["xres0"], dma=True)
        for tc in range(NT):
            pk = "ps2" if tc == 0 else "ps3"
            pt = E_PS[pk]
            for h in range(NH):
                w = [pk] if (h == 0 or h == NH - 1) else []
                S.op("pe", (lambda e, o=o, tc=tc, h=h, pt=pt: e.matmul(
                    pt[:, :], wslab[o % 2][:, h * 128:(h + 1) * 128],
                    ot[h][:, tc * 512:(tc + 1) * 512],
                    start=(h == 0), stop=(h == NH - 1))),
                    reads=[wk, f"ot{h}"], writes=w, tag=f"wo{o}")
            S.op("dve", (lambda e, o=o, tc=tc, pt=pt: e.tensor_tensor(
                x1b[o % 2][:, tc * 512:(tc + 1) * 512], pt[:, :],
                xres[0][:, tc * 512:(tc + 1) * 512], ALU.add)),
                reads=[pk, "xres0"], writes=[f"x1b{o % 2}"], tag="res")
        S.op("sp", (lambda e, o=o: e.dma_start(
            x1[o * 128:(o + 1) * 128, :], x1b[o % 2][:, :])),
            reads=[f"x1b{o % 2}"], writes=[f"x1st{o % 2}"], dma=True)

    S.op("sp", lambda e: e.nop(), reads=["x1st0", "x1st1"])


def emit_phase_b(nc, S, ctx, D):
    sder = D["sder"]
    x1d = D["x1i"]
    w1s = D["w1s"]
    w3s = D["w3s"]
    w2s = D["w2s"]
    onesd = D["onesd"]
    onesf = D["onesf"]
    x2 = D["x2i"]
    lnb = D["lnb"]

    xb = [sbt(ctx, nc, f"xb{i}", [128, TL], BF16) for i in range(NCH)]
    gt = [sbt(ctx, nc, f"gt{i}", [128, TL], BF16) for i in range(NHT)]
    wf32 = [sbt(ctx, nc, f"wf32_{i}", [128, C], F32) for i in range(2)]
    w2f32 = [sbt(ctx, nc, f"w2f32_{i}", [128, 1408], F32) for i in range(2)]
    qsl = [sbt(ctx, nc, f"qsl{i}", [128, C], BF16) for i in range(2)]
    q2sl = [sbt(ctx, nc, "q2sl0", [128, HIDP], BF16)]
    qtmp = sbt(ctx, nc, "qtmp", [128, 2048], BF16)
    sgt = sbt(ctx, nc, "sgt", [128, 2048], BF16)
    h1t = sbt(ctx, nc, "h1t", [128, TL], BF16)
    sqt = h1t
    onesbf = sbt(ctx, nc, "onesbf", [128, 1], BF16)
    onesf32 = sbt(ctx, nc, "onesf32", [128, 1], F32)
    srow = sbt(ctx, nc, "srow_s", [1, 16], F32)
    scols = sbt(ctx, nc, "scols", [128, 16], F32)
    arep = sbt(ctx, nc, "arep", [128, TL], F32)
    brep = sbt(ctx, nc, "brep", [128, TL], F32)
    x2b = [sbt(ctx, nc, "x2b0", [128, TL], F32)] * 2
    rows = {k: sbt(ctx, nc, k, [1, TL], F32) for k in
            ("srow", "qrow", "brow")}

    ps0 = pst(ctx, nc, "ps0", [128, 512])
    ps1 = pst(ctx, nc, "ps1", [128, 512])
    ps2 = pst(ctx, nc, "ps2", [128, 512])
    ps3 = pst(ctx, nc, "ps3", [128, 512])
    pr0 = pst(ctx, nc, "pr0", [128, 512])
    pr1 = pst(ctx, nc, "pr1", [128, 512])

    E = {"ones_bf": onesbf, "sqt": sqt, "rows": rows, "lnb": lnb,
         "arep": arep, "brep": brep,
         "ps0": ps0, "ps1": ps1, "pr0": pr0, "pr1": pr1}
    E_PS = {"ps0": ps0, "ps1": ps1, "ps2": ps2, "ps3": ps3}

    S.op("sp", lambda e: e.dma_start(onesbf[:, :], onesd[:, :]), writes=["ones"], dma=True)
    S.op("sp", lambda e: e.dma_start(onesf32[:, :], onesf[:, :]), writes=["onesf"], dma=True)
    for dc in range(NCH):
        S.op("pool", (lambda e, dc=dc: e.dma_start(
            xb[dc][:, :], x1d[dc * 128:(dc + 1) * 128, :])),
            writes=[f"xb{dc}"], dma=True)

    abscol = D["abscol"]
    S.op("pe", lambda e: e.matmul(pr0[0:1, 0:3], onesf32[:, 0:1], abscol[:, 0:3],
                                  start=True, stop=True),
         reads=["abscol", "onesf"], writes=["pr0"], tag="absred")
    CNT = {0: HID * C, 1: HID * C, 2: HID * C}
    S.op("dve", lambda e: e.memset(srow[:, :], 0.0), writes=["srow_s"])
    S.op("dve", lambda e: e.tensor_copy(srow[0:1, 0:3], pr0[0:1, 0:3]),
         reads=["pr0", "srow_s"], writes=["srow_s"])
    for gi in range(3):
        S.op("dve", (lambda e, gi=gi: e.tensor_scalar_mul(
            srow[0:1, 4 + gi * 4:5 + gi * 4], srow[0:1, gi:gi + 1],
            0.5 / CNT[gi])), reads=["srow_s"], writes=["srow_s"])
        S.op("dve", (lambda e, gi=gi: e.tensor_scalar_mul(
            srow[0:1, 5 + gi * 4:6 + gi * 4], srow[0:1, gi:gi + 1],
            -0.5 / CNT[gi])), reads=["srow_s"], writes=["srow_s"])
        S.op("dve", (lambda e, gi=gi: e.tensor_scalar_mul(
            srow[0:1, 6 + gi * 4:7 + gi * 4], srow[0:1, gi:gi + 1],
            1.0 / CNT[gi])), reads=["srow_s"], writes=["srow_s"])
        S.op("dve", (lambda e, gi=gi: e.tensor_scalar_mul(
            srow[0:1, 7 + gi * 4:8 + gi * 4], srow[0:1, gi:gi + 1],
            -1.0 / CNT[gi])), reads=["srow_s"], writes=["srow_s"])
    S.op("sp", (lambda e: e.dma_start(
        bass.AP(sder, 0, [[16, 1], [1, 16]]), srow[0:1, 0:16])),
        reads=["srow_s"], writes=["sderd"], dma=True)
    S.op("sp", (lambda e: e.dma_start(
        scols[:, :], bass.AP(sder, 0, [[0, 128], [1, 16]]))),
        reads=["sderd"], writes=["scols"], dma=True)

    xkeys = [f"xb{i}" for i in range(NCH)]
    emit_layernorm(S, E, xkeys, xb, 0)

    def quant_slab(f32buf, f32key, bfbuf, bfkey, gi, width, off=0):
        for h0 in range(0, width, 2048):
            h1 = min(h0 + 2048, width)
            wdt_ = h1 - h0
            S.op("act", (lambda e, h0=h0, wdt_=wdt_: e.activation(
                sgt[:, 0:wdt_], f32buf[:, h0:h0 + wdt_], AF.Sign,
                bias=scols[:, 5 + gi * 4:6 + gi * 4])),
                reads=[f32key, "scols"], writes=["sgt"], tag="qs1")
            S.op("act", (lambda e, h0=h0, wdt_=wdt_: e.activation(
                qtmp[:, 0:wdt_], f32buf[:, h0:h0 + wdt_], AF.Sign,
                bias=scols[:, 4 + gi * 4:5 + gi * 4])),
                reads=[f32key, "scols"], writes=["qtmp"], tag="qs2")
            S.op("dve", (lambda e, wdt_=wdt_: e.tensor_tensor(
                qtmp[:, 0:wdt_], qtmp[:, 0:wdt_], sgt[:, 0:wdt_], ALU.add)),
                reads=["qtmp", "sgt"], writes=["qtmp"], tag="qadd")
            S.op("dve", (lambda e, h0=h0, wdt_=wdt_: e.tensor_scalar_mul(
                bfbuf[:, off + h0:off + h0 + wdt_], qtmp[:, 0:wdt_],
                scols[:, 4 + gi * 4:5 + gi * 4])),
                reads=["qtmp", "scols"], writes=[bfkey], tag="qmul")

    for f in range(NHT):
        for wi, (src, gi) in enumerate([(w1s, 0), (w3s, 1)]):
            wk = f"wf{wi}"
            qk = f"qsl{wi}"
            S.op("sp", (lambda e, src=src, f=f, buf=wf32[wi]: e.dma_start(
                buf[:, :], bass.AP(src, f * C * 128,
                                   [[128, 128], [128 * 128, NCH], [1, 128]]))),
                writes=[wk], dma=True)
            quant_slab(wf32[wi], wk, qsl[wi], qk, gi, C)
            for tc in range(NT):
                pk = ["ps0", "ps1"][tc] if wi == 0 else ["ps2", "ps3"][tc]
                pt = E_PS[pk]
                for dc in range(NCH):
                    w = [pk] if (dc == 0 or dc == NCH - 1) else []
                    S.op("pe", (lambda e, tc=tc, dc=dc, pt=pt, wi=wi: e.matmul(
                        pt[:, :], qsl[wi][:, dc * 128:(dc + 1) * 128],
                        xb[dc][:, tc * 512:(tc + 1) * 512],
                        start=(dc == 0), stop=(dc == NCH - 1))),
                        reads=[qk, f"xb{dc}"], writes=w, tag=f"h{wi}")
            if wi == 0:
                for tc in range(NT):
                    pk = ["ps0", "ps1"][tc]
                    S.op("act", (lambda e, tc=tc, pt=E_PS[pk]: e.activation(
                        h1t[:, tc * 512:(tc + 1) * 512], pt[:, :], AF.Silu)),
                        reads=[pk], writes=["h1t"], tag="silu")
        for tc in range(NT):
            pk = ["ps2", "ps3"][tc]
            S.op("dve", (lambda e, tc=tc, f=f, pt=E_PS[pk]: e.tensor_tensor(
                gt[f][:, tc * 512:(tc + 1) * 512],
                h1t[:, tc * 512:(tc + 1) * 512], pt[:, :], ALU.mult)),
                reads=["h1t", pk], writes=[f"gt{f}"], tag="gmul")

    W2Q = ((0, 11), (11, 11), (22, 11), (33, 10))
    for o in range(NCH):
        for hh, (c0, ncc) in enumerate(W2Q):
            wk = f"w2f{hh % 2}"
            qk = f"q2h{hh}"
            S.op("sp", (lambda e, o=o, buf=w2f32[hh % 2], c0=c0, ncc=ncc: e.dma_start(
                buf[:, 0:ncc * 128],
                bass.AP(w2s, o * HIDP * 128 + c0 * 128 * 128,
                        [[128, 128], [128 * 128, ncc], [1, 128]]))),
                writes=[wk], dma=True)
            quant_slab(w2f32[hh % 2], wk, q2sl[0], qk, 2, ncc * 128, off=c0 * 128)
        S.op("sp", (lambda e, o=o: e.dma_start(
            x2b[0][:, :], x1d[o * 128:(o + 1) * 128, :])),
            writes=["x2b0"], dma=True)
        for tc in range(NT):
            pk = ["ps0", "ps1"][tc]
            pt = E_PS[pk]
            for hc in range(NHT):
                w = [pk] if (hc == 0 or hc == NHT - 1) else []
                qkh = f"q2h{min(hc // 11, 3)}"
                S.op("pe", (lambda e, o=o, tc=tc, hc=hc, pt=pt: e.matmul(
                    pt[:, :], q2sl[0][:, hc * 128:(hc + 1) * 128],
                    gt[hc][:, tc * 512:(tc + 1) * 512],
                    start=(hc == 0), stop=(hc == NHT - 1))),
                    reads=[qkh, f"gt{hc}"], writes=w, tag=f"w2_{o}")
            S.op("dve", (lambda e, o=o, tc=tc, pt=pt: e.tensor_tensor(
                x2b[0][:, tc * 512:(tc + 1) * 512], pt[:, :],
                x2b[0][:, tc * 512:(tc + 1) * 512], ALU.add)),
                reads=[pk, "x2b0"], writes=["x2b0"], tag="res")
        S.op("sp", (lambda e, o=o: e.dma_start(
            x2[o * 128:(o + 1) * 128, :], x2b[0][:, :])),
            reads=["x2b0"], writes=[f"x2st{o % 2}"], dma=True)

    S.op("sp", lambda e: e.nop(), reads=["x2st0", "x2st1"])


def emit_phase_c(nc, S, ctx, D):
    drfg = D["drfg"]
    bcg = D["bcg"]
    e2xg = D["e2xg"]
    e3b = D["e3b"]
    x2d = D["x2i"]
    wxT = D["wxT"]
    wdtT = D["wdtT"]
    alogh = D["alogh"]
    bdth = D["bdth"]
    dfull = D["dfull"]
    onesd = D["onesd"]
    hoffd = D["hoffd"]
    x3 = D["x3i"]
    e2in = D["e2in"]
    e2out = D["e2out"]
    e2inB = D["e2inB"]
    e2outB = D["e2outB"]
    e3in = D["e3in"]
    e3out = D["e3out"]
    e3inB = D["e3inB"]
    e3outB = D["e3outB"]
    lnb = D["lnb"]

    xb = [sbt(ctx, nc, f"xb{i}", [128, TL], BF16) for i in range(NCH)]
    dlt = [sbt(ctx, nc, f"dlt{i}", [128, T], BF16) for i in range(8)]
    wxt = sbt(ctx, nc, "wxt", [128, NCH * 160], BF16)
    wdt = sbt(ctx, nc, "wdt", [128, 1024], BF16)
    at = sbt(ctx, nc, "at", [128, 128], F32)
    bcol = sbt(ctx, nc, "bcol", [128, 8], F32)
    dcol = sbt(ctx, nc, "dcol", [128, 16], F32)
    hoff = sbt(ctx, nc, "hoff", [1, 4], mybir.dt.int32)
    drf = sbt(ctx, nc, "drf", [128, T], BF16)
    bcexp = sbt(ctx, nc, "bcexp", [32, TL], BF16)
    dAs2 = [sbt(ctx, nc, f"dAs{i}", [128, NSSM * TCH], BF16) for i in range(2)]
    sqt = dAs2[0]
    zt2 = dAs2[0]
    bus = sbt(ctx, nc, "bus", [128, NSSM * TCH], BF16)
    brep = sbt(ctx, nc, "brep_c", [128, NSSM * TCH], BF16)
    crep = sbt(ctx, nc, "crep_c", [128, NSSM * TCH], BF16)
    xmall = sbt(ctx, nc, "xmall", [128, 8 * TCH], BF16)
    dxt2 = [sbt(ctx, nc, f"dxt{i}", [128, TCH], BF16) for i in range(2)]
    stateb = [sbt(ctx, nc, f"state{i}", [128, NSSM], BF16) for i in range(8)]
    c0t = sbt(ctx, nc, "c0t", [128, NSSM], BF16)
    ych2 = [sbt(ctx, nc, f"ych{i}", [128, TCH], F32) for i in range(2)]
    yba = sbt(ctx, nc, "yba", [128, 4 * TL], BF16)
    zt = sbt(ctx, nc, "zt", [128, 512], F32)
    onesbf = sbt(ctx, nc, "onesbf", [128, 1], BF16)
    arep = sbt(ctx, nc, "arep", [128, TL], F32)
    brepl = sbt(ctx, nc, "brepl", [128, TL], F32)
    x3b = [sbt(ctx, nc, f"x3b{i}", [128, 512], F32) for i in range(2)]
    rows = {k: sbt(ctx, nc, k, [1, TL], F32) for k in ("srow", "qrow", "brow")}

    ps0 = pst(ctx, nc, "ps0", [128, 512])
    ps1 = pst(ctx, nc, "ps1", [128, 512])
    ps2 = pst(ctx, nc, "ps2", [128, 512])
    ps3 = pst(ctx, nc, "ps3", [128, 512])
    pr0 = pst(ctx, nc, "pr0", [128, 512])
    pr1 = pst(ctx, nc, "pr1", [128, 512])
    E = {"ones_bf": onesbf, "sqt": sqt, "sqt_key": "dAs0", "rows": rows,
         "lnb": lnb, "arep": arep, "brep": brepl,
         "ps0": ps0, "ps1": ps1, "pr0": pr0, "pr1": pr1}
    PSL = [ps0, ps1, ps2, ps3]

    _rctx = D.get("_octx", ctx)
    rgA = _rctx.enter_context(nc.gpsimd.register(_uniq("rgA")))
    rgB = _rctx.enter_context(nc.gpsimd.register(_uniq("rgB")))
    rgT = _rctx.enter_context(nc.gpsimd.register(_uniq("rgT")))


    S.op("sp", lambda e: e.dma_start(onesbf[:, :], onesd[:, :]), writes=["ones"], dma=True)
    S.op("sp", lambda e: e.dma_start(hoff[:, :], hoffd[:, :]), writes=["hoff"], dma=True)
    S.op("pool", lambda e: e.dma_start(
        wxt[:, :], bass.AP(wxT, 0, [[160, 128], [128 * 160, NCH], [1, 160]])),
        writes=["wxt"], dma=True)
    S.op("pool", lambda e: e.dma_start(wdt[:, :], wdtT[:, :]), writes=["wdt"], dma=True)
    S.op("sp", lambda e: e.dma_start(at[:, :], alogh[:, :]), writes=["at"], dma=True)
    S.op("sp", lambda e: e.dma_start(bcol[:, :], bdth[:, :]), writes=["bcol"], dma=True)
    S.op("sp", lambda e: e.dma_start(dcol[:, :], dfull[:, :]), writes=["dcol"], dma=True)
    for dc in range(NCH):
        S.op("pool", (lambda e, dc=dc: e.dma_start(
            xb[dc][:, :], x2d[dc * 128:(dc + 1) * 128, :])),
            writes=[f"xb{dc}"], dma=True)
    S.op("act", lambda e: e.activation(at[:, :], at[:, :], AF.Exp),
         reads=["at"], writes=["at"])
    S.op("dve", lambda e: e.tensor_scalar_mul(at[:, :], at[:, :], -1.0),
         reads=["at"], writes=["at"])
    S.op("pool", lambda e: (e.reg_load(rgA, hoff[0:1, 0:1]),
                            e.reg_load(rgB, hoff[0:1, 1:2]),
                            e.nop())[-1],
         reads=["hoff"], writes=["regs"], tag="regload")

    xkeys = [f"xb{i}" for i in range(NCH)]
    emit_layernorm(S, E, xkeys, xb, 0)

    for tc in range(NT):
        for grp, (m0, mw, pk_) in enumerate([(0, 128, ("ps0", "ps1")),
                                             (128, 32, ("ps2", "ps3"))]):
            pk = pk_[tc]
            pt = E[pk] if pk in E else PSL[int(pk[2])]
            for dc in range(NCH):
                w = [pk] if (dc == 0 or dc == NCH - 1) else []
                S.op("pe", (lambda e, tc=tc, dc=dc, m0=m0, mw=mw, pt=pt: e.matmul(
                    pt[0:mw, :], wxt[:, dc * 160 + m0:dc * 160 + m0 + mw],
                    xb[dc][:, tc * 512:(tc + 1) * 512],
                    start=(dc == 0), stop=(dc == NCH - 1))),
                    reads=["wxt", f"xb{dc}"], writes=w, tag=f"xz{grp}")
        S.op("dve", (lambda e, tc=tc, pt=PSL[tc]: e.tensor_copy(
            drf[:, tc * 512:(tc + 1) * 512], pt[:, :])),
            reads=[("ps0", "ps1")[tc]], writes=["drf"], tag="drev")
        S.op("dve", (lambda e, tc=tc, pt=PSL[2 + tc]: e.tensor_copy(
            bcexp[0:32, tc * 512:(tc + 1) * 512], pt[0:32, :])),
            reads=[("ps2", "ps3")[tc]], writes=["bcexp"], tag="bcev")

    S.op("sp", lambda e: e.dma_start(e2in[0:128, :], drf[:, 0:TL]),
         reads=["drf"], writes=["e2_dr"], dma=True)
    S.op("sp", lambda e: e.dma_start(e2in[128:160, :], bcexp[0:32, :]),
         reads=["bcexp"], writes=["e2_bc"], dma=True)
    S.op("pool", (lambda e: e.collective_compute(
        "AllGather", ALU.bypass, replica_groups=PAIRS,
        ins=[e2in.ap().opt()], outs=[e2out.ap().opt()])),
        reads=["e2_dr", "e2_bc"], writes=["e2out"], tag="agE2")
    for dc in range(NCH):
        S.op("sp", (lambda e, dc=dc: e.dma_start(
            e2inB[dc * 128:(dc + 1) * 128, :], xb[dc][:, :])),
            reads=[f"xb{dc}"], writes=[f"e2_x{dc}"], dma=True)
    S.op("pool", (lambda e: e.collective_compute(
        "AllGather", ALU.bypass, replica_groups=PAIRS,
        ins=[e2inB.ap().opt()], outs=[e2outB.ap().opt()])),
        reads=[f"e2_x{dc}" for dc in range(NCH)],
        writes=["e2outB"], tag="agE2b")

    for r_ in range(2):
        S.op("sp", (lambda e, r_=r_: e.dma_start(
            bass.AP(drfg, (1 - r_) * 128, [[T, 128], [256, 8], [1, 128]]),
            bass.AP(e2out, r_ * 160 * TL + 7 * 128,
                    [[TL, 128], [-128, 8], [1, 128]]))),
            reads=["e2out"], writes=["drfg"], dma=True)
        S.op("sp", (lambda e, r_=r_: e.dma_start(
            bass.AP(bcg, (1 - r_) * 128, [[T, 32], [256, 8], [1, 128]]),
            bass.AP(e2out, (r_ * 160 + 128) * TL + 7 * 128,
                    [[TL, 32], [-128, 8], [1, 128]]))),
            reads=["e2out"], writes=["bcg"], dma=True)
    S.op("sp", lambda e: e.dma_start(drf[:, :], drfg[:, :]),
         reads=["drfg"], writes=["drf"], dma=True)
    for r_ in range(2):
        for rc in range(16):
            S.op("sp" if rc % 2 == 0 else "pool",
                 (lambda e, r_=r_, rc=rc: e.dma_start(
                     bass.AP(e2xg, rc * 128 * T + (1 - r_) * 128,
                             [[T, 128], [256, 8], [1, 128]]),
                     bass.AP(e2outB, (r_ * 2048 + rc * 128) * TL + 7 * 128,
                             [[TL, 128], [-128, 8], [1, 128]]))),
                 reads=["e2outB"], writes=["e2xg"], dma=True, tag="e2xg")
    for sl in range(8):
        for ch in range(NCHK):
            pk = ["ps0", "ps1"][ch % 2]
            pt = PSL[ch % 2]
            S.op("pe", (lambda e, sl=sl, ch=ch, pt=pt: e.matmul(
                pt[:, :], wdt[:, sl * 128:(sl + 1) * 128],
                drf[:, ch * 512:(ch + 1) * 512], start=True, stop=True)),
                reads=["wdt", "drf"], writes=[pk], tag="dmm")
            S.op("dve", (lambda e, sl=sl, pt=pt: e.tensor_scalar_add(
                zt[:, :], pt[:, :], bcol[:, sl:sl + 1])),
                reads=[pk, "bcol"], writes=["zt"], tag="spz")
            S.op("act", lambda e: e.activation(zt2[:, 0:512], zt[:, :], AF.Abs),
                 reads=["zt"], writes=["dAs0"])
            S.op("act", lambda e: e.activation(zt2[:, 0:512], zt2[:, 0:512], AF.Exp, scale=-1.0),
                 reads=["dAs0"], writes=["dAs0"])
            S.op("act", lambda e: e.activation(zt2[:, 0:512], zt2[:, 0:512], AF.Ln, bias=1.0),
                 reads=["dAs0"], writes=["dAs0"])
            S.op("dve", lambda e: e.tensor_scalar_max(zt[:, :], zt[:, :], 0.0),
                 reads=["zt"], writes=["zt"])
            S.op("dve", (lambda e, sl=sl, ch=ch: e.tensor_tensor(
                dlt[sl][:, ch * 512:(ch + 1) * 512], zt[:, :], zt2[:, 0:512], ALU.add)),
                reads=["zt", "dAs0"], writes=[f"dlt{sl}"], tag="spadd")

    for dt in range(8):
        S.op("dve", (lambda e, dt=dt: e.memset(stateb[dt][:, :], 0.0)),
             writes=[f"state{dt}"], tag="st0")
    for ch in range(NCHK):
        S.op("sp", (lambda e, ch=ch: e.dma_start(
            brep[:, :], bass.AP(bcg, ch * TCH,
                                [[0, 128], [T, NSSM], [1, TCH]]))),
            reads=["bcg"], writes=["brep_c"], dma=True)
        S.op("sp", (lambda e, ch=ch: e.dma_start(
            crep[:, :], bass.AP(bcg, 16 * T + ch * TCH,
                                [[0, 128], [T, NSSM], [1, TCH]]))),
            reads=["bcg"], writes=["crep_c"], dma=True)
        base = ch * TCH
        S.op("pool", (lambda e, base=base: (
            e.reg_mov(rgT, base),
            e.reg_add(rgT, rgT, rgA),
            e.dma_start(xmall[:, :],
                        bass.AP(e2xg, rgT,
                                [[T, 128], [128 * T, 8], [1, TCH]])))[-1]),
            reads=["e2xg", "regs"], writes=["xmall"], dma=True, tag="xm")
        for dt in range(8):
            par = (ch * 8 + dt) % 2
            dAs, dak = dAs2[par], f"dAs{par}"
            dxt, dxk = dxt2[par], f"dxt{par}"
            ych, yck = ych2[par], f"ych{par}"
            S.op("pool", (lambda e, dt=dt, ch=ch, dxt=dxt: e.tensor_tensor(
                dxt[:, :], dlt[dt][:, ch * 512:(ch + 1) * 512],
                xmall[:, dt * TCH:(dt + 1) * TCH], ALU.mult)),
                reads=[f"dlt{dt}", "xmall"], writes=[dxk], tag="dx")
            for n in range(NSSM):
                S.op("act", (lambda e, dt=dt, ch=ch, n=n, dAs=dAs: e.activation(
                    dAs[:, n * TCH:(n + 1) * TCH],
                    dlt[dt][:, ch * 512:(ch + 1) * 512],
                    AF.Exp, scale=at[:, dt * 16 + n:dt * 16 + n + 1])),
                    reads=[f"dlt{dt}", "at"], writes=[dak], tag="dA")
            S.op("dve", (lambda e, dxt=dxt: e.tensor_tensor(
                bus[:, :], bass.AP(dxt, 0, [[TCH, 128], [0, NSSM], [1, TCH]]),
                brep[:, :], ALU.mult)),
                reads=[dxk, "brep_c"], writes=["bus"], tag="bu")
            S.op("dve", (lambda e, dt=dt, dAs=dAs: e.tensor_tensor(
                c0t[:, :], bass.AP(dAs, 0, [[NSSM * TCH, 128], [TCH, NSSM]]),
                stateb[dt][:, :], ALU.mult)),
                reads=[dak, f"state{dt}"], writes=["c0t"], tag="fold")
            S.op("dve", (lambda e: e.tensor_tensor(
                bass.AP(bus, 0, [[NSSM * TCH, 128], [TCH, NSSM]]),
                bass.AP(bus, 0, [[NSSM * TCH, 128], [TCH, NSSM]]),
                c0t[:, :], ALU.add)),
                reads=["bus", "c0t"], writes=["bus"], tag="fold")
            S.op("dve", (lambda e, dAs=dAs: e.memset(
                bass.AP(dAs, 0, [[NSSM * TCH, 128], [TCH, NSSM]]), 0.0)),
                reads=[dak], writes=[dak], tag="fold")
            S.op("dve", (lambda e, dAs=dAs: e.tensor_tensor_scan(
                bus[:, :], dAs[:, :], bus[:, :], 0.0, ALU.mult, ALU.add)),
                reads=[dak, "bus"], writes=["bus"], tag="scan")
            S.op("dve", (lambda e, dt=dt: e.tensor_copy(
                stateb[dt][:, :], bass.AP(bus, TCH - 1, [[NSSM * TCH, 128], [TCH, NSSM]]))),
                reads=["bus"], writes=[f"state{dt}"], tag="tail")
            S.op("dve", lambda e: e.tensor_tensor(bus[:, :], bus[:, :], crep[:, :], ALU.mult),
                 reads=["bus", "crep_c"], writes=["bus"], tag="hc")
            for lvl in range(3):
                w_ = NSSM * TCH >> (lvl + 1)
                S.op("dve", (lambda e, w_=w_: e.tensor_tensor(
                    bus[:, 0:w_], bus[:, 0:w_], bus[:, w_:2 * w_], ALU.add)),
                    reads=["bus"], writes=["bus"], tag="yt")
            S.op("dve", (lambda e, ych=ych: e.tensor_tensor(
                ych[:, :], bus[:, 0:TCH], bus[:, TCH:2 * TCH], ALU.add)),
                reads=["bus"], writes=[yck], tag="yt4")
            edst = e3in if dt < 4 else e3inB
            S.op("pool", (lambda e, dt=dt, ch=ch, ych=ych, edst=edst: e.dma_start(
                edst[(dt % 4) * 128:(dt % 4 + 1) * 128,
                     ch * TCH:(ch + 1) * TCH], ych[:, :])),
                reads=[yck], writes=[f"e3x{ch}{dt}"], dma=True)

    S.op("pool", (lambda e: e.collective_compute(
        "AllGather", ALU.bypass, replica_groups=PAIRS,
        ins=[e3in.ap().opt()], outs=[e3out.ap().opt()])),
        reads=[f"e3x{ch}{dt}" for ch in range(NCHK) for dt in range(4)],
        writes=["e3out"], tag="agE3")
    S.op("pool", (lambda e: e.collective_compute(
        "AllGather", ALU.bypass, replica_groups=PAIRS,
        ins=[e3inB.ap().opt()], outs=[e3outB.ap().opt()])),
        reads=[f"e3x{ch}{dt}" for ch in range(NCHK) for dt in range(4, 8)],
        writes=["e3outB"], tag="agE3b")

    for rc in (0, 1, 2, 3, 8, 9, 10, 11, 4, 5, 6, 7, 12, 13, 14, 15):
        if rc < 4:
            esrc, ek, srow = e3out, "e3out", rc * 128
        elif 8 <= rc < 12:
            esrc, ek, srow = e3out, "e3out", 512 + (rc - 8) * 128
        elif 4 <= rc < 8:
            esrc, ek, srow = e3outB, "e3outB", (rc - 4) * 128
        else:
            esrc, ek, srow = e3outB, "e3outB", 512 + (rc - 12) * 128
        for p_ in range(2):
            S.op("sp" if rc % 2 == 0 else "pool",
                 (lambda e, p_=p_, rc=rc, esrc=esrc, srow=srow: e.dma_start(
                     bass.AP(e3b, rc * 128 * T + p_ * TL,
                             [[T, 128], [1, TL]]),
                     bass.AP(esrc, srow * T + (15 - p_) * 128,
                             [[T, 128], [-256, 8], [1, 128]]))),
                 reads=[ek], writes=[f"e3bg{rc // 4}"], dma=True, tag="e3b")
    for g4 in (0, 2, 1, 3):
        base = g4 * 4 * 128 * T
        S.op("pool", (lambda e, base=base: (
            e.reg_mov(rgT, base),
            e.reg_add(rgT, rgT, rgB),
            e.dma_start(yba[:, :],
                        bass.AP(e3b, rgT,
                                [[T, 128], [128 * T, 4], [1, TL]])))[-1]),
            reads=[f"e3bg{g4}", "regs"], writes=["yba"], dma=True, tag="yimp")
        for oo in range(4):
            o = g4 * 4 + oo
            for hh in range(2):
                cs0, cs1 = hh * 512, (hh + 1) * 512
                S.op("sp", (lambda e, o=o, cs0=cs0, cs1=cs1, hh=hh: e.dma_start(
                    x3b[hh][:, :], x2d[o * 128:(o + 1) * 128, cs0:cs1])),
                    writes=[f"x3b{hh}"], dma=True)
                S.op("dve", (lambda e, oo=oo, cs0=cs0, hh=hh: e.tensor_tensor(
                    x3b[hh][:, :], x3b[hh][:, :],
                    yba[:, oo * TL + cs0:oo * TL + cs0 + 512], ALU.add)),
                    reads=[f"x3b{hh}", "yba"], writes=[f"x3b{hh}"], tag="resy")
                S.op("dve", (lambda e, o=o, cs0=cs0, cs1=cs1, hh=hh: e.scalar_tensor_tensor(
                    x3b[hh][:, :], xb[o][:, cs0:cs1], dcol[:, o:o + 1],
                    x3b[hh][:, :], ALU.mult, ALU.add)),
                    reads=[f"xb{o}", "dcol", f"x3b{hh}"], writes=[f"x3b{hh}"], tag="resd")
                S.op("sp", (lambda e, o=o, cs0=cs0, cs1=cs1, hh=hh: e.dma_start(
                    x3[o * 128:(o + 1) * 128, cs0:cs1], x3b[hh][:, :])),
                    reads=[f"x3b{hh}"], writes=[f"x3st{hh}"], dma=True)

    S.op("sp", lambda e: e.nop(), reads=["x3st0", "x3st1"])


def emit_phase_d(nc, S, ctx, D):
    x3d = D["x3i"]
    kvdw = D["kvdw"]
    qdw = D["qdw"]
    kuw = D["kuw"]
    quw = D["quw"]
    vuw = D["vuw"]
    rkw = D["rkw"]
    rqw = D["rqw"]
    oww = D["oww"]
    cos1 = D["cos1"]
    sin1 = D["sin1"]
    l1wd = D["l1wd"]
    mask0 = D["mask0"]
    onesd = D["onesd"]
    identd = D["identd"]
    hd = D["hi"]
    k_in = D["k_in"]
    k_out = D["k_out"]
    v_in = D["v_in"]
    v_out = D["v_out"]
    lnb = D["lnb"]

    xb = [sbt(ctx, nc, f"xb{i}", [128, TL], BF16) for i in range(NCH)]
    qasm = [sbt(ctx, nc, f"qasm{i}", [128, TL], BF16) for i in range(NH)]
    ot = [sbt(ctx, nc, f"ot{i}", [128, TL], BF16) for i in range(NH)]
    kvd = [sbt(ctx, nc, f"kvd{i}", [128, TL], BF16) for i in range(NLAT)]
    qd = [sbt(ctx, nc, f"qd{i}", [128, TL], BF16) for i in range(NLAT)]
    kasm = [sbt(ctx, nc, f"kasm{i}", [128, TL], BF16) for i in range(2)]
    rsc = [sbt(ctx, nc, f"rsc{i}", [128, TL], BF16) for i in range(2)]
    shuf = sbt(ctx, nc, "shuf", [128, TL], BF16)
    sqt = shuf
    wslab = [sbt(ctx, nc, f"wslab{i}", [128, C], BF16) for i in range(2)]
    cosT = sbt(ctx, nc, "cosT", [128, TL], BF16)
    sinT = sbt(ctx, nc, "sinT", [128, TL], BF16)
    maskT = sbt(ctx, nc, "maskT", [128, MASK_COLS], BF16)
    ones_bf = sbt(ctx, nc, "ones_bf", [128, 1], BF16)
    l1w = sbt(ctx, nc, "l1w", [128, 16], F32)
    khead = [sbt(ctx, nc, f"khead{i}", [128, T], BF16) for i in range(2)]
    vhead = [sbt(ctx, nc, f"vhead{i}", [128, T], BF16) for i in range(2)]
    ptile2 = [sbt(ctx, nc, f"ptile{i}", [128, TL], BF16) for i in range(2)]
    rrep = sbt(ctx, nc, "rrep", [128, TL], F32)
    arep = sbt(ctx, nc, "arep", [128, TL], F32)
    brep = sbt(ctx, nc, "brep", [128, TL], F32)
    rmsrep = rrep
    ident = sbt(ctx, nc, "ident", [128, 128], BF16)
    hb = [sbt(ctx, nc, f"hb{i}", [128, TL], F32) for i in range(2)]
    rows = {k: sbt(ctx, nc, k, [1, TL], F32) for k in ("srow", "qrow", "brow")}

    ps0 = pst(ctx, nc, "ps0", [128, 512])
    ps1 = pst(ctx, nc, "ps1", [128, 512])
    ps2 = pst(ctx, nc, "ps2", [128, 512])
    ps3 = pst(ctx, nc, "ps3", [128, 512])
    pr0 = pst(ctx, nc, "pr0", [128, 512])
    pr1 = pst(ctx, nc, "pr1", [128, 512])
    PS = {0: ps0, 1: ps1}
    E_PS = {"ps0": ps0, "ps1": ps1, "ps2": ps2, "ps3": ps3}
    E = {"ones_bf": ones_bf, "sqt": sqt, "sqt_key": "shuf", "rows": rows,
         "lnb": lnb, "arep": arep, "brep": brep,
         "ps0": ps0, "ps1": ps1, "pr0": pr0, "pr1": pr1}

    for dc in range(NCH):
        S.op("pool", (lambda e, dc=dc: e.dma_start(
            xb[dc][:, :], x3d[dc * 128:(dc + 1) * 128, :])),
            writes=[f"xb{dc}"], dma=True)
    S.op("sp", lambda e: e.dma_start(cosT[:, :], cos1[:, :]), writes=["cosT"], dma=True)
    S.op("sp", lambda e: e.dma_start(sinT[:, :], sin1[:, :]), writes=["sinT"], dma=True)
    S.op("sp", lambda e: e.dma_start(maskT[:, :], mask0[:, :]), writes=["maskT"], dma=True)
    S.op("sp", lambda e: e.dma_start(ones_bf[:, :], onesd[:, :]), writes=["ones"], dma=True)
    S.op("sp", lambda e: e.dma_start(l1w[:, :], l1wd[:, :]), writes=["l1w"], dma=True)
    S.op("sp", lambda e: e.dma_start(ident[:, :], identd[:, :]), writes=["ident"], dma=True)

    xkeys = [f"xb{i}" for i in range(NCH)]
    emit_layernorm(S, E, xkeys, xb, 0)
    emit_layernorm(S, E, xkeys, xb, 2, rms_only=True, apply=False)
    S.op("sp", (lambda e: e.dma_start(
        rmsrep[:, :], bass.AP(lnb, 2 * TL, [[0, 128], [1, TL]]))),
        reads=["lnb2"], writes=["rrep"], dma=True)
    for dc in range(NCH):
        S.op("dve", (lambda e, dc=dc: e.scalar_tensor_tensor(
            xb[dc][:, :], xb[dc][:, :], l1w[:, dc:dc + 1], rmsrep[:, :],
            ALU.mult, ALU.mult)),
            reads=[f"xb{dc}", "l1w", "rrep"], writes=[f"xb{dc}"], tag="rmsap")

    def slab_load(dram, f, kchunks, wk):
        S.op("pool", (lambda e, f=f, kchunks=kchunks: e.dma_start(
            wslab[f % 2][:, 0:kchunks * 128],
            bass.AP(dram, f * kchunks * 128 * 128,
                    [[128, 128], [128 * 128, kchunks], [1, 128]]))),
            writes=[wk], dma=True)

    def mm_ft(src_tiles, src_keys, kchunks, f, pk, extra=None):
        wk = f"wslab{f % 2}"
        for tc in range(NT):
            pkt = ["ps2", "ps3"][tc] if pk == "auto" else pk
            ptt = E_PS[pkt]
            for dc in range(kchunks):
                w = [pkt] if (dc == 0 or dc == kchunks - 1) else []
                S.op("pe", (lambda e, tc=tc, dc=dc, ptt=ptt, f=f: e.matmul(
                    ptt[:, :], wslab[f % 2][:, dc * 128:(dc + 1) * 128],
                    src_tiles[dc][:, tc * 512:(tc + 1) * 512],
                    start=(dc == 0), stop=(dc == kchunks - 1))),
                    reads=[wk, src_keys[dc]], writes=w, tag="mmft")
            if extra:
                extra(tc, pkt, E_PS[pkt])

    kvdk = [f"kvd{i}" for i in range(NLAT)]
    qdk = [f"qd{i}" for i in range(NLAT)]

    for f in range(NLAT):
        slab_load(kvdw, f, NCH, f"wslab{f % 2}")
        def ev(tc, pk, pt, f=f):
            S.op("dve", (lambda e, tc=tc, pt=pt, f=f: e.tensor_copy(
                kvd[f][:, tc * 512:(tc + 1) * 512], pt[:, :])),
                reads=[pk], writes=[kvdk[f]], tag="kvdev")
        mm_ft(xb, xkeys, NCH, f, "auto", ev)
    for f in range(NLAT):
        slab_load(qdw, f, NCH, f"wslab{f % 2}")
        def ev(tc, pk, pt, f=f):
            S.op("dve", (lambda e, tc=tc, pt=pt, f=f: e.tensor_copy(
                qd[f][:, tc * 512:(tc + 1) * 512], pt[:, :])),
                reads=[pk], writes=[qdk[f]], tag="qdev")
        mm_ft(qd if False else xb, xkeys, NCH, f, "auto", ev)

    mask32 = [i ^ 1 for i in range(32)]

    def rope_tile(tile, key):
        S.op("dve", (lambda e, tile=tile: e.stream_shuffle(shuf[:, :], tile[:, :], mask32)),
             reads=[key], writes=["shuf"])
        S.op("dve", (lambda e, tile=tile: e.tensor_tensor(tile[:, :], tile[:, :], cosT[:, :], ALU.mult)),
             reads=[key, "cosT"], writes=[key])
        S.op("dve", (lambda e: e.tensor_tensor(shuf[:, :], shuf[:, :], sinT[:, :], ALU.mult)),
             reads=["shuf", "sinT"], writes=["shuf"])
        S.op("dve", (lambda e, tile=tile: e.tensor_tensor(tile[:, :], tile[:, :], shuf[:, :], ALU.add)),
             reads=[key, "shuf"], writes=[key])

    for f in range(8):
        slab_load(quw, f, NLAT, f"wslab{f % 2}")
        def evqp(tc, pk, pt, f=f):
            for half in range(2):
                h = 2 * f + half
                S.op("dve", (lambda e, tc=tc, pt=pt, half=half, h=h: e.tensor_copy(
                    qasm[h][0:64, tc * 512:(tc + 1) * 512],
                    pt[64 * half:64 * half + 64, :])),
                    reads=[pk], writes=[f"qasm{h}"], tag="qpev")
        mm_ft(qd, qdk, NLAT, f, "auto", evqp)
        slab_load(rqw, f, NLAT, f"wslab{f % 2}")
        rk = f"rsc{f % 2}"
        def evqr(tc, pk, pt, f=f, rk=rk):
            S.op("dve", (lambda e, tc=tc, pt=pt, f=f: e.tensor_copy(
                rsc[f % 2][:, tc * 512:(tc + 1) * 512], pt[:, :])),
                reads=[pk], writes=[rk], tag="qrev")
        mm_ft(qd, qdk, NLAT, f, "auto", evqr)
        rope_tile(rsc[f % 2], rk)
        for half in range(2):
            h = 2 * f + half
            S.op("dve", (lambda e, half=half, h=h, f=f: e.tensor_copy(
                qasm[h][64:128, :], rsc[f % 2][64 * half:64 * half + 64, :])),
                reads=[rk], writes=[f"qasm{h}"], tag="qrasm")

    for f in range(8):
        ak0, ak1 = "kasm0x", "kasm1x"
        slab_load(kuw, f, NLAT, f"wslab{f % 2}")
        def evkp(tc, pk, pt, f=f):
            for half in range(2):
                S.op("dve", (lambda e, tc=tc, pt=pt, half=half: e.tensor_copy(
                    kasm[half][0:64, tc * 512:(tc + 1) * 512],
                    pt[64 * half:64 * half + 64, :])),
                    reads=[pk], writes=[("kasm0x", "kasm1x")[half]], tag="kpev")
        mm_ft(kvd, kvdk, NLAT, f, "auto", evkp)
        slab_load(rkw, f, NCH, f"wslab{f % 2}")
        rk = f"rsc{f % 2}"
        def evkr(tc, pk, pt, f=f, rk=rk):
            S.op("dve", (lambda e, tc=tc, pt=pt, f=f: e.tensor_copy(
                rsc[f % 2][:, tc * 512:(tc + 1) * 512], pt[:, :])),
                reads=[pk], writes=[rk], tag="krev")
        mm_ft(xb, xkeys, NCH, f, "auto", evkr)
        rope_tile(rsc[f % 2], rk)
        for half in range(2):
            S.op("dve", (lambda e, half=half, f=f: e.tensor_copy(
                kasm[half][64:128, :], rsc[f % 2][64 * half:64 * half + 64, :])),
                reads=[rk], writes=[("kasm0x", "kasm1x")[half]], tag="krasm")
        for half in range(2):
            h = 2 * f + half
            S.op("sp", (lambda e, h=h, half=half: e.dma_start(
                k_in[h * 128:(h + 1) * 128, :], kasm[half][:, :])),
                reads=[("kasm0x", "kasm1x")[half]], writes=[f"k_in{h}"], dma=True)

    S.op("pool", (lambda e: e.collective_compute(
        "AllGather", ALU.bypass, replica_groups=PAIRS,
        ins=[k_in.ap().opt()], outs=[k_out.ap().opt()])),
        reads=[f"k_in{h}" for h in range(NH)], writes=["k_out"], tag="agK")

    vexp = sbt(ctx, nc, "vexpT", [128, TL], BF16)
    psb = pst(ctx, nc, "psb", [128, 1024], BF16)
    for f in range(NH):
        wk = f"wslab{f % 2}"
        S.op("pool", (lambda e, f=f: e.dma_start(
            wslab[f % 2][:, 0:LAT],
            bass.AP(vuw, f * 128, [[C, 128], [128 * C, NLAT] if False else [128 * C, NLAT], [1, 128]])
            if False else
            bass.AP(vuw, f * 128, [[C, 128], [128 * C, NLAT], [1, 128]]))),
            writes=[wk], dma=True)
        for tc in range(NT):
            pk = ["ps2", "ps3"][tc]
            pt = E_PS[pk]
            for dc in range(NLAT):
                w = [pk] if (dc == 0 or dc == NLAT - 1) else []
                S.op("pe", (lambda e, tc=tc, dc=dc, pt=pt, f=f: e.matmul(
                    pt[:, :], wslab[f % 2][:, dc * 128:(dc + 1) * 128],
                    kvd[dc][:, tc * 512:(tc + 1) * 512],
                    start=(dc == 0), stop=(dc == NLAT - 1))),
                    reads=[wk, kvdk[dc]], writes=w, tag=f"vf{f}")
            S.op("dve", (lambda e, tc=tc, pt=pt: e.tensor_copy(
                rsc[0][:, tc * 512:(tc + 1) * 512], pt[:, :])),
                reads=[pk], writes=["rsc0"], tag="vevf")
        for tt in range(8):
            S.op("pe", (lambda e, tt=tt: e.transpose(
                psb[:, tt * 128:(tt + 1) * 128],
                rsc[0][:, tt * 128:(tt + 1) * 128], ident[:, :])),
                reads=["rsc0", "ident"], writes=["psb"], tag="vT")
        S.op("dve", lambda e: e.tensor_copy(vexp[:, :], psb[:, :]),
             reads=["psb"], writes=["vexpT"])
        S.op("sp", (lambda e, f=f: e.dma_start(
            bass.AP(v_in, f * 128, [[C, 128], [128 * C, 8], [1, 128]]),
            vexp[:, :])),
            reads=["vexpT"], writes=[f"v_inx{f}"], dma=True)

    vinkeys = [f"v_inx{f}" for f in range(NH)]
    S.op("pool", (lambda e: e.collective_compute(
        "AllGather", ALU.bypass, replica_groups=PAIRS,
        ins=[v_in.ap().opt()], outs=[v_out.ap().opt()])),
        reads=vinkeys, writes=["v_out"], tag="agV")

    AV_LAST = {0: 15, 1: 7}
    for h in range(NH):
        kb, kk = khead[h % 2], f"khead{h % 2}"
        vb, vk = vhead[h % 2], f"vhead{h % 2}"
        for r_ in range(2):
            S.op("sp", (lambda e, h=h, kb=kb, r_=r_: e.dma_start(
                bass.AP(kb, (1 - r_) * 128, [[T, 128], [256, 8], [1, 128]]),
                bass.AP(k_out, r_ * C * TL + h * 128 * TL + 7 * 128,
                        [[TL, 128], [-128, 8], [1, 128]]))),
                reads=["k_out", "v_out"], writes=[kk], dma=True)
            S.op("sp", (lambda e, h=h, vb=vb, r_=r_: e.dma_start(
                bass.AP(vb, (1 - r_) * 128, [[T, 128], [256, 8], [1, 128]]),
                bass.AP(v_out, r_ * TL * C + 7 * 128 * C + h * 128,
                        [[C, 128], [-128 * C, 8], [1, 128]]))),
                reads=["v_out", "k_out"], writes=[vk], dma=True)
        first = [True, True]
        for kt in range(NKT):
            wd = _wid(kt)
            msegs_kt = [(l_, off_) for (mk_, l_, off_) in MSEGS if mk_ == kt]
            chunks = []
            c0 = 0
            while c0 < wd:
                chunks.append((c0, min(c0 + 512, wd)))
                c0 = min(c0 + 512, wd)
            for ci, (c0, c1) in enumerate(chunks):
                spk = "ps2" if (kt + ci) % 2 == 0 else "ps3"
                sps = E_PS[spk]
                S.op("pe", (lambda e, kt=kt, c0=c0, c1=c1, sps=sps, kb=kb, h=h: e.matmul(
                    sps[:, 0:c1 - c0], kb[:, kt * 128:(kt + 1) * 128],
                    qasm[h][:, c0:c1], start=True, stop=True)),
                    reads=[kk, f"qasm{h}"], writes=[spk], tag=f"sc{h}")
                for (ml_, moff_) in msegs_kt:
                    a0 = ml_ * 128
                    if a0 >= c0 and a0 + 128 <= c1:
                        S.op("dve", (lambda e, a0=a0, c0=c0, sps=sps, moff_=moff_: e.tensor_tensor(
                            sps[:, a0 - c0:a0 - c0 + 128], sps[:, a0 - c0:a0 - c0 + 128],
                            maskT[:, moff_:moff_ + 128], ALU.add)),
                            reads=[spk, "maskT"], writes=[spk], tag="msk")
                S.op("act", (lambda e, c0=c0, c1=c1, sps=sps, kt=kt: e.activation(
                    ptile2[kt % 2][:, c0:c1], sps[:, 0:c1 - c0], AF.Exp, scale=ISQ)),
                    reads=[spk], writes=[f"ptile{kt % 2}"], tag="exp")
            b0 = 0
            while b0 < wd:
                blk_i = b0 // 512
                b1 = min((blk_i + 1) * 512, wd)
                opk = "ps0" if blk_i == 0 else "ps1"
                opt_ = PS[blk_i]
                rpk = "pr0" if blk_i == 0 else "pr1"
                rpt = E[rpk]
                wo_ = [opk] if (first[blk_i] or kt == AV_LAST[blk_i]) else []
                wr_ = [rpk] if (first[blk_i] or kt == AV_LAST[blk_i]) else []
                S.op("pe", (lambda e, kt=kt, b0=b0, b1=b1, blk_i=blk_i, vb=vb,
                            opt_=opt_, f=first[blk_i]: e.matmul(
                    opt_[:, b0 - blk_i * 512:b1 - blk_i * 512],
                    vb[:, kt * 128:(kt + 1) * 128], ptile2[kt % 2][:, b0:b1],
                    start=f, stop=(kt == AV_LAST[blk_i]))),
                    reads=[vk, f"ptile{kt % 2}"], writes=wo_, tag=f"av{h}")
                S.op("pe", (lambda e, kt=kt, b0=b0, b1=b1, blk_i=blk_i,
                            rpt=rpt, f=first[blk_i]: e.matmul(
                    rpt[0:1, b0 - blk_i * 512:b1 - blk_i * 512],
                    ones_bf[:, 0:1], ptile2[kt % 2][:, b0:b1],
                    start=f, stop=(kt == AV_LAST[blk_i]))),
                    reads=[f"ptile{kt % 2}", "ones"], writes=wr_, tag=f"rs{h}")
                first[blk_i] = False
                b0 = b1
        for blk_i in range(2):
            rpk = "pr0" if blk_i == 0 else "pr1"
            rpt = E[rpk]
            S.op("dve", (lambda e, blk_i=blk_i, rpt=rpt: e.tensor_copy(
                rows["qrow"][0:1, blk_i * 512:(blk_i + 1) * 512], rpt[0:1, :])),
                reads=[rpk], writes=["qrow"])
        S.op("dve", lambda e: e.reciprocal(rows["qrow"][:, :], rows["qrow"][:, :]),
             reads=["qrow"], writes=["qrow"])
        S.op("sp", (lambda e: e.dma_start(
            bass.AP(lnb, 3 * TL, [[TL, 1], [1, TL]]), rows["qrow"][:, :])),
            reads=["qrow"], writes=["lnb3"], dma=True)
        S.op("sp", (lambda e: e.dma_start(
            rrep[:, :], bass.AP(lnb, 3 * TL, [[0, 128], [1, TL]]))),
            reads=["lnb3"], writes=["rrep"], dma=True)
        for blk_i in range(2):
            opk = "ps0" if blk_i == 0 else "ps1"
            opt_ = PS[blk_i]
            if blk_i == 0:
                S.op("dve", (lambda e, blk_i=blk_i, h=h, opt_=opt_: e.tensor_copy(
                    ot[h][:, blk_i * 512:(blk_i + 1) * 512], opt_[:, :])),
                    reads=[opk], writes=[f"ot{h}"], tag="oev")
            else:
                S.op("act", (lambda e, blk_i=blk_i, h=h, opt_=opt_: e.copy(
                    ot[h][:, blk_i * 512:(blk_i + 1) * 512], opt_[:, :])),
                    reads=[opk], writes=[f"ot{h}"], tag="oev")
        S.op("dve", (lambda e, h=h: e.tensor_tensor(
            ot[h][:, :], ot[h][:, :], rrep[:, :], ALU.mult)),
            reads=[f"ot{h}", "rrep"], writes=[f"ot{h}"], tag="onrm")

    for o in range(NCH):
        wk = f"wslab{o % 2}"
        slab_load(oww, o, NCH, wk)
        S.op("sp", (lambda e, o=o: e.dma_start(
            hb[o % 2][:, :], x3d[o * 128:(o + 1) * 128, :])),
            writes=[f"hb{o % 2}"], dma=True)
        S.op("dve", (lambda e, o=o: e.tensor_tensor(
            hb[o % 2][:, :], hb[o % 2][:, :], arep[:, :], ALU.mult)),
            reads=[f"hb{o % 2}", "arep"], writes=[f"hb{o % 2}"], tag="urec")
        S.op("dve", (lambda e, o=o: e.tensor_tensor(
            hb[o % 2][:, :], hb[o % 2][:, :], brep[:, :], ALU.add)),
            reads=[f"hb{o % 2}", "brep"], writes=[f"hb{o % 2}"], tag="urec2")
        for tc in range(NT):
            pk = ["ps2", "ps3"][tc]
            pt = E_PS[pk]
            for hh in range(NH):
                w = [pk] if (hh == 0 or hh == NH - 1) else []
                S.op("pe", (lambda e, o=o, tc=tc, hh=hh, pt=pt: e.matmul(
                    pt[:, :], wslab[o % 2][:, hh * 128:(hh + 1) * 128],
                    ot[hh][:, tc * 512:(tc + 1) * 512],
                    start=(hh == 0), stop=(hh == NH - 1))),
                    reads=[wk, f"ot{hh}"], writes=w, tag=f"ow{o}")
            S.op("dve", (lambda e, o=o, tc=tc, pt=pt: e.tensor_tensor(
                hb[o % 2][:, tc * 512:(tc + 1) * 512], pt[:, :],
                hb[o % 2][:, tc * 512:(tc + 1) * 512], ALU.add)),
                reads=[pk, f"hb{o % 2}"], writes=[f"hb{o % 2}"], tag="hres")
        S.op("sp", (lambda e, o=o: e.dma_start(
            hd[o * 128:(o + 1) * 128, :], hb[o % 2][:, :])),
            reads=[f"hb{o % 2}"], writes=[f"hst{o % 2}"], dma=True)

    S.op("sp", lambda e: e.nop(), reads=["hst0", "hst1"])


def emit_phase_e(nc, S, ctx, D):
    wbn = D["wbn"]
    hdd = D["hi"]
    x3d = D["x3i"]
    shg = D["shg"]
    shu = D["shu"]
    shd = D["shd"]
    rg = D["rg"]
    ru = D["ru"]
    rd = D["rd"]
    rwT = D["rwT"]
    rbd = D["rbd"]
    l2wd = D["l2wd"]
    onesd = D["onesd"]
    x4 = D["x4"]
    lnb = D["lnb"]

    hb = [sbt(ctx, nc, f"hb{i}", [128, TL], BF16) for i in range(NCH)]
    ge = [[sbt(ctx, nc, f"ge{gi}_{i}", [128, TL], BF16) for i in range(NIT)]
          for gi in range(4)]
    macc = [sbt(ctx, nc, f"macc{i}", [128, TL], BF16) for i in range(NCH)]
    sqt = sbt(ctx, nc, "sqt", [128, TL], BF16)
    wslab = [sbt(ctx, nc, f"wslab{i}", [128, C], BF16) for i in range(2)]
    sil = sbt(ctx, nc, "sil", [128, 512], BF16)
    rwt = sbt(ctx, nc, "rwt", [128, 112], BF16)
    rbt = sbt(ctx, nc, "rbt", [7, 1], F32)
    l2w = sbt(ctx, nc, "l2w", [128, 16], F32)
    ones_bf = sbt(ctx, nc, "ones_bf", [128, 1], BF16)
    pt32 = sbt(ctx, nc, "pt32", [32, TL], F32)
    m1t = sbt(ctx, nc, "m1t", [32, TL], F32)
    m2t = sbt(ctx, nc, "m2t", [32, TL], F32)
    pro = sbt(ctx, nc, "pro", [32, TL], F32)
    wrep = sbt(ctx, nc, "wrep", [128, TL], F32)
    arep = sbt(ctx, nc, "arep", [128, TL], F32)
    brep = sbt(ctx, nc, "brep", [128, TL], F32)
    x3h = [sbt(ctx, nc, f"x3h{i}", [128, TL], F32) for i in range(2)]
    x4b = [sbt(ctx, nc, f"x4b{i}", [128, TL], F32) for i in range(2)]
    rows = {k: sbt(ctx, nc, k, [1, TL], F32) for k in ("srow", "qrow", "brow")}

    ps0 = pst(ctx, nc, "ps0", [128, 512])
    ps1 = pst(ctx, nc, "ps1", [128, 512])
    ps2 = pst(ctx, nc, "ps2", [128, 512])
    ps3 = pst(ctx, nc, "ps3", [128, 512])
    pr0 = pst(ctx, nc, "pr0", [128, 512])
    pr1 = pst(ctx, nc, "pr1", [128, 512])
    E_PS = {"ps0": ps0, "ps1": ps1, "ps2": ps2, "ps3": ps3}
    E = {"ones_bf": ones_bf, "sqt": sqt, "rows": rows, "lnb": lnb,
         "arep": arep, "brep": brep,
         "ps0": ps0, "ps1": ps1, "pr0": pr0, "pr1": pr1}

    for dc in range(NCH):
        S.op("pool", (lambda e, dc=dc: e.dma_start(
            hb[dc][:, :], hdd[dc * 128:(dc + 1) * 128, :])),
            writes=[f"hb{dc}"], dma=True)
    S.op("sp", lambda e: e.dma_start(ones_bf[:, :], onesd[:, :]), writes=["ones"], dma=True)
    S.op("sp", lambda e: e.dma_start(l2w[:, :], l2wd[:, :]), writes=["l2w"], dma=True)
    S.op("sp", lambda e: e.dma_start(rbt[:, :], rbd[:, :]), writes=["rbt"], dma=True)
    S.op("pool", lambda e: e.dma_start(
        rwt[:, 0:112], bass.AP(rwT, 0, [[7, 128], [128 * 7, NCH], [1, 7]])),
        writes=["rwt"], dma=True)

    hkeys = [f"hb{i}" for i in range(NCH)]
    emit_layernorm(S, E, hkeys, hb, 2, rms_only=True, apply=False)
    S.op("sp", (lambda e: e.dma_start(
        arep[:, :], bass.AP(lnb, 2 * TL, [[0, 128], [1, TL]]))),
        reads=["lnb2"], writes=["arep"], dma=True)
    for dc in range(NCH):
        S.op("dve", (lambda e, dc=dc: e.scalar_tensor_tensor(
            hb[dc][:, :], hb[dc][:, :], l2w[:, dc:dc + 1], arep[:, :],
            ALU.mult, ALU.mult)),
            reads=[f"hb{dc}", "l2w", "arep"], writes=[f"hb{dc}"], tag="rmsap")

    for tc in range(NT):
        prk = ["pr0", "pr1"][tc]
        prt = E[prk]
        for dc in range(NCH):
            w = [prk] if (dc == 0 or dc == NCH - 1) else []
            S.op("pe", (lambda e, tc=tc, dc=dc, prt=prt: e.matmul(
                prt[0:7, :], rwt[:, dc * 7:(dc + 1) * 7],
                hb[dc][:, tc * 512:(tc + 1) * 512],
                start=(dc == 0), stop=(dc == NCH - 1))),
                reads=["rwt", f"hb{dc}"], writes=w, tag="rlog")
    S.op("dve", lambda e: e.memset(pt32[:, :], NEG), writes=["pt32"])
    for tc in range(NT):
        prk = ["pr0", "pr1"][tc]
        prt = E[prk]
        S.op("dve", (lambda e, tc=tc, prt=prt: e.tensor_scalar_add(
            pt32[0:7, tc * 512:(tc + 1) * 512], prt[0:7, :], rbt[0:7, 0:1])),
            reads=[prk, "rbt", "pt32"], writes=["pt32"], tag="rbias")
    S.op("act", lambda e: e.activation(pt32[0:7, :], pt32[0:7, :], AF.Sigmoid),
         reads=["pt32"], writes=["pt32"])
    S.op("dve", lambda e: e.tensor_copy(pro[:, :], pt32[:, :]),
         reads=["pt32"], writes=["pro"])

    def logmax(dst, dkey, srckey):
        for k in (1, 2, 4):
            mask = [(r & ~7) | ((r + k) & 7) for r in range(32)]
            S.op("dve", (lambda e, mask=mask, dst=dst: e.stream_shuffle(
                m2t[:, :] if dst is m1t else m1t[:, :], dst[:, :], mask)),
                reads=[dkey], writes=["m2t" if dst is m1t else "m1t"], tag="shf")
            S.op("dve", (lambda e, dst=dst: e.tensor_tensor(
                dst[:, :], dst[:, :],
                m2t[:, :] if dst is m1t else m1t[:, :], ALU.max)),
                reads=[dkey, "m2t" if dst is m1t else "m1t"], writes=[dkey], tag="lmx")

    S.op("dve", lambda e: e.tensor_copy(m1t[:, :], pt32[:, :]),
         reads=["pt32"], writes=["m1t"])
    logmax(m1t, "m1t", "pt32")
    S.op("dve", lambda e: e.tensor_tensor(m2t[:, :], pro[:, :], m1t[:, :], ALU.is_equal),
         reads=["pro", "m1t"], writes=["m2t"])
    S.op("dve", lambda e: e.tensor_scalar_mul(m2t[:, :], m2t[:, :], NEG),
         reads=["m2t"], writes=["m2t"])
    S.op("dve", lambda e: e.tensor_tensor(pt32[:, :], pro[:, :], m2t[:, :], ALU.add),
         reads=["pro", "m2t", "pt32"], writes=["pt32"])
    logmax(pt32, "pt32", None)
    S.op("dve", lambda e: e.tensor_tensor(m2t[:, :], pro[:, :], pt32[:, :], ALU.is_ge),
         reads=["pro", "pt32"], writes=["m2t"])
    S.op("dve", lambda e: e.tensor_tensor(m2t[:, :], m2t[:, :], pro[:, :], ALU.mult),
         reads=["m2t", "pro"], writes=["m2t"])
    S.op("dve", lambda e: e.tensor_tensor(m1t[:, :], m1t[:, :], pt32[:, :], ALU.add),
         reads=["m1t", "pt32"], writes=["m1t"])
    S.op("dve", lambda e: e.reciprocal(m1t[:, :], m1t[:, :]),
         reads=["m1t"], writes=["m1t"])
    S.op("dve", lambda e: e.tensor_tensor(m2t[:, :], m2t[:, :], m1t[:, :], ALU.mult),
         reads=["m2t", "m1t"], writes=["m2t"])
    S.op("dve", lambda e: e.tensor_scalar_mul(m2t[:, :], m2t[:, :], 0.125),
         reads=["m2t"], writes=["m2t"])
    S.op("sp", (lambda e: e.dma_start(
        wbn[0:7, :], m2t[0:7, :])), reads=["m2t"], writes=["wbn"], dma=True)

    def expert_mm(e_idx, gslot):
        for f in range(NIT):
            for (wsrc, which) in ((shg if e_idx == 0 else rg, 0),
                                  (shu if e_idx == 0 else ru, 1)):
                wk = f"wslab{which}"
                off = (f if e_idx == 0 else ((e_idx - 1) * NIT + f)) * C * 128
                S.op("pool", (lambda e, off=off, wsrc=wsrc, which=which: e.dma_start(
                    wslab[which][:, :],
                    bass.AP(wsrc, off, [[128, 128], [128 * 128, NCH], [1, 128]]))),
                    writes=[wk], dma=True)
            for tc in range(NT):
                pgk, puk = "ps2", "ps3"
                pgt, put = ps2, ps3
                for dc in range(NCH):
                    w = [pgk] if (dc == 0 or dc == NCH - 1) else []
                    S.op("pe", (lambda e, tc=tc, dc=dc, pgt=pgt: e.matmul(
                        pgt[:, :], wslab[0][:, dc * 128:(dc + 1) * 128],
                        hb[dc][:, tc * 512:(tc + 1) * 512],
                        start=(dc == 0), stop=(dc == NCH - 1))),
                        reads=["wslab0", f"hb{dc}"], writes=w, tag="gmm")
                for dc in range(NCH):
                    w = [puk] if (dc == 0 or dc == NCH - 1) else []
                    S.op("pe", (lambda e, tc=tc, dc=dc, put=put: e.matmul(
                        put[:, :], wslab[1][:, dc * 128:(dc + 1) * 128],
                        hb[dc][:, tc * 512:(tc + 1) * 512],
                        start=(dc == 0), stop=(dc == NCH - 1))),
                        reads=["wslab1", f"hb{dc}"], writes=w, tag="umm")
                S.op("act", (lambda e, pgt=pgt: e.activation(
                    sil[:, :], pgt[:, :], AF.Silu)),
                    reads=[pgk], writes=["sil"], tag="silu")
                gk = f"ge{gslot}_{f}"
                S.op("dve", (lambda e, tc=tc, f=f, put=put, gslot=gslot: e.tensor_tensor(
                    ge[gslot][f][:, tc * 512:(tc + 1) * 512], sil[:, :],
                    put[:, :], ALU.mult)),
                    reads=["sil", puk], writes=[gk], tag="gmul")
        if e_idx == 0:
            for f in range(NIT):
                gk = f"ge{gslot}_{f}"
                S.op("dve", (lambda e, f=f, gslot=gslot: e.tensor_scalar_mul(
                    ge[gslot][f][:, :], ge[gslot][f][:, :], 0.125)),
                    reads=[gk], writes=[gk], tag="gsc")
        else:
            S.op("sp", (lambda e, e_idx=e_idx: e.dma_start(
                wrep[:, :], bass.AP(wbn, (e_idx - 1) * TL, [[0, 128], [1, TL]]))),
                reads=["wbn"], writes=["wrep"], dma=True)
            for f in range(NIT):
                gk = f"ge{gslot}_{f}"
                S.op("dve", (lambda e, f=f, gslot=gslot: e.tensor_tensor(
                    ge[gslot][f][:, :], ge[gslot][f][:, :], wrep[:, :], ALU.mult)),
                    reads=[gk, "wrep"], writes=[gk], tag="gscw")

    for grp in range(2):
        for gi in range(4):
            expert_mm(grp * 4 + gi, gi)
        for o in range(NCH):
            for tc in range(NT):
                pk = ["ps0", "ps1"][(o * NT + tc) % 2]
                pt = E_PS[pk]
                nacc = 4 * NIT
                idx = 0
                for gi in range(4):
                    e_idx = grp * 4 + gi
                    wk2 = "w2slab"
                    for hc in range(NIT):
                        if tc == 0 and hc == 0:
                            pass
                        idx += 1
        for o in range(NCH):
            for gi in range(4):
                e_idx = grp * 4 + gi
                wk = f"wslab{gi % 2}"
                src = shd if e_idx == 0 else rd
                off = (o if e_idx == 0 else ((e_idx - 1) * NCH + o)) * IEXP * 128
                S.op("pool", (lambda e, off=off, src=src, gi=gi: e.dma_start(
                    wslab[gi % 2][:, 0:IEXP],
                    bass.AP(src, off, [[128, 128], [128 * 128, NIT], [1, 128]]))),
                    writes=[wk], dma=True)
                for tc in range(NT):
                    pk = ["ps0", "ps1"][tc]
                    pt = E_PS[pk]
                    for hc in range(NIT):
                        first_mm = (gi == 0 and hc == 0)
                        last_mm = (gi == 3 and hc == NIT - 1)
                        w = [pk] if (first_mm or last_mm) else []
                        S.op("pe", (lambda e, tc=tc, hc=hc, pt=pt, gi=gi: e.matmul(
                            pt[:, :], wslab[gi % 2][:, hc * 128:(hc + 1) * 128],
                            ge[gi][hc][:, tc * 512:(tc + 1) * 512],
                            start=(gi == 0 and hc == 0),
                            stop=(gi == 3 and hc == NIT - 1))),
                            reads=[wk, f"ge{gi}_{hc}"], writes=w, tag=f"dmm{o}")
            if grp == 0:
                for tc in range(NT):
                    pk = ["ps0", "ps1"][tc]
                    pt = E_PS[pk]
                    S.op("dve", (lambda e, o=o, tc=tc, pt=pt: e.tensor_copy(
                        macc[o][:, tc * 512:(tc + 1) * 512], pt[:, :])),
                        reads=[pk], writes=[f"macc{o}"], tag="mev")
            else:
                S.op("sp", (lambda e, o=o: e.dma_start(
                    x3h[o % 2][:, :], x3d[o * 128:(o + 1) * 128, :])),
                    writes=[f"x3h{o % 2}"], dma=True)
                S.op("sp", (lambda e, o=o: e.dma_start(
                    x4b[o % 2][:, :], hdd[o * 128:(o + 1) * 128, :])),
                    writes=[f"x4b{o % 2}"], dma=True)
                S.op("dve", (lambda e, o=o: e.tensor_tensor(
                    x4b[o % 2][:, :], x4b[o % 2][:, :], x3h[o % 2][:, :], ALU.add)),
                    reads=[f"x4b{o % 2}", f"x3h{o % 2}"], writes=[f"x4b{o % 2}"], tag="xh")
                for tc in range(NT):
                    pk = ["ps0", "ps1"][tc]
                    pt = E_PS[pk]
                    S.op("dve", (lambda e, o=o, tc=tc, pt=pt: e.tensor_tensor(
                        x3h[o % 2][:, tc * 512:(tc + 1) * 512], pt[:, :],
                        macc[o][:, tc * 512:(tc + 1) * 512], ALU.add)),
                        reads=[pk, f"macc{o}"], writes=[f"x3h{o % 2}"], tag="m2a")
                    S.op("dve", (lambda e, o=o, tc=tc: e.tensor_tensor(
                        x4b[o % 2][:, tc * 512:(tc + 1) * 512],
                        x4b[o % 2][:, tc * 512:(tc + 1) * 512],
                        x3h[o % 2][:, tc * 512:(tc + 1) * 512], ALU.add)),
                        reads=[f"x4b{o % 2}", f"x3h{o % 2}"], writes=[f"x4b{o % 2}"], tag="x4a")
                S.op("sp", (lambda e, o=o: e.dma_start(
                    x4[o * 128:(o + 1) * 128, :], x4b[o % 2][:, :])),
                    reads=[f"x4b{o % 2}"], writes=[f"x4st{o % 2}"], dma=True)

    S.op("sp", lambda e: e.nop(), reads=["x4st0", "x4st1"])



def build_fused():
    import os as _os0
    nc = bass.Bass("TRN2", target_bir_lowering=False)
    with ExitStack() as octx:
        D = {}
        D["x"] = nc.dram_tensor("x", [C, TL], F32, kind="ExternalInput")
        D["wqkT"] = nc.dram_tensor("wqkT", [32, C, 128], F32, kind="ExternalInput")
        D["wvT"] = nc.dram_tensor("wvT", [16, C, 128], F32, kind="ExternalInput")
        D["woT"] = nc.dram_tensor("woT", [16, C, 128], F32, kind="ExternalInput")
        D["cos0"] = nc.dram_tensor("cos0", [128, TL], BF16, kind="ExternalInput")
        D["sin0"] = nc.dram_tensor("sin0", [128, TL], BF16, kind="ExternalInput")
        D["mask0"] = nc.dram_tensor("mask0", [128, MASK_COLS], BF16, kind="ExternalInput")
        D["onesd"] = nc.dram_tensor("onesd", [128, 1], BF16, kind="ExternalInput")
        D["identd"] = nc.dram_tensor("identd", [128, 128], BF16, kind="ExternalInput")
        D["x1i"] = nc.dram_tensor("x1i", [C, TL], F32)
        D["k_in"] = nc.dram_tensor("k_in", [C, TL], BF16)
        D["k_out"] = nc.dram_tensor("k_out", [2 * C, TL], BF16)
        D["v_in"] = nc.dram_tensor("v_in", [TL, C], BF16)
        D["v_out"] = nc.dram_tensor("v_out", [T, C], BF16)
        D["lnb"] = nc.dram_tensor("lnb", [8, TL], F32)
        D["w1s"] = nc.dram_tensor("w1s", [NHT, C, 128], F32, kind="ExternalInput")
        D["w3s"] = nc.dram_tensor("w3s", [NHT, C, 128], F32, kind="ExternalInput")
        D["w2s"] = nc.dram_tensor("w2s", [NCH, HIDP, 128], F32, kind="ExternalInput")
        D["onesf"] = nc.dram_tensor("onesf", [128, 1], F32, kind="ExternalInput")
        if "x2i" not in D:
            if "x2i" not in D:
                D["x2i"] = nc.dram_tensor("x2i", [C, TL], F32)
        D["wxT"] = nc.dram_tensor("wxT", [C, 160], F32, kind="ExternalInput")
        D["wdtT"] = nc.dram_tensor("wdtT", [128, 1024], F32, kind="ExternalInput")
        D["alogh"] = nc.dram_tensor("alogh", [128, 128], F32, kind="ExternalInput")
        D["bdth"] = nc.dram_tensor("bdth", [128, 8], F32, kind="ExternalInput")
        D["dfull"] = nc.dram_tensor("dfull", [128, 16], F32, kind="ExternalInput")
        D["hoffd"] = nc.dram_tensor("hoffd", [1, 4], mybir.dt.int32, kind="ExternalInput")
        D["x3i"] = nc.dram_tensor("x3i", [C, TL], F32)
        D["e2in"] = nc.dram_tensor("e2in", [160, TL], BF16)
        D["e2out"] = nc.dram_tensor("e2out", [320, TL], BF16)
        D["e2inB"] = nc.dram_tensor("e2inB", [2048, TL], BF16)
        D["e2outB"] = nc.dram_tensor("e2outB", [4096, TL], BF16)
        D["e3in"] = nc.dram_tensor("e3in", [512, T], BF16)
        D["e3out"] = nc.dram_tensor("e3out", [1024, T], BF16)
        D["e3inB"] = nc.dram_tensor("e3inB", [512, T], BF16)
        D["e3outB"] = nc.dram_tensor("e3outB", [1024, T], BF16)
        D["kvdw"] = nc.dram_tensor("kvdw", [NLAT, C, 128], F32, kind="ExternalInput")
        D["qdw"] = nc.dram_tensor("qdw", [NLAT, C, 128], F32, kind="ExternalInput")
        D["kuw"] = nc.dram_tensor("kuw", [8, LAT, 128], F32, kind="ExternalInput")
        D["quw"] = nc.dram_tensor("quw", [8, LAT, 128], F32, kind="ExternalInput")
        D["vuw"] = nc.dram_tensor("vuw", [LAT, C], F32, kind="ExternalInput")
        D["rkw"] = nc.dram_tensor("rkw", [8, C, 128], F32, kind="ExternalInput")
        D["rqw"] = nc.dram_tensor("rqw", [8, LAT, 128], F32, kind="ExternalInput")
        D["oww"] = nc.dram_tensor("oww", [NCH, C, 128], F32, kind="ExternalInput")
        D["cos1"] = nc.dram_tensor("cos1", [128, TL], BF16, kind="ExternalInput")
        D["sin1"] = nc.dram_tensor("sin1", [128, TL], BF16, kind="ExternalInput")
        D["l1wd"] = nc.dram_tensor("l1wd", [128, 16], F32, kind="ExternalInput")
        D["hi"] = nc.dram_tensor("hi", [C, TL], F32)
        D["shg"] = nc.dram_tensor("shg", [NIT, C, 128], F32, kind="ExternalInput")
        D["shu"] = nc.dram_tensor("shu", [NIT, C, 128], F32, kind="ExternalInput")
        D["shd"] = nc.dram_tensor("shd", [NCH, IEXP, 128], F32, kind="ExternalInput")
        D["rg"] = nc.dram_tensor("rg", [7, NIT, C, 128], F32, kind="ExternalInput")
        D["ru"] = nc.dram_tensor("ru", [7, NIT, C, 128], F32, kind="ExternalInput")
        D["rd"] = nc.dram_tensor("rd", [7, NCH, IEXP, 128], F32, kind="ExternalInput")
        D["rwT"] = nc.dram_tensor("rwT", [C, 7], F32, kind="ExternalInput")
        D["rbd"] = nc.dram_tensor("rbd", [7, 1], F32, kind="ExternalInput")
        D["l2wd"] = nc.dram_tensor("l2wd", [128, 16], F32, kind="ExternalInput")
        D["x4"] = nc.dram_tensor("x4", [C, TL], F32, kind="ExternalOutput")
        D["sder"] = nc.dram_tensor("sder", [1, 16], F32)
        D["drfg"] = nc.dram_tensor("drfg", [128, T], BF16)
        D["e2xg"] = nc.dram_tensor("e2xg", [C, T], BF16)
        D["e3b"] = nc.dram_tensor("e3b", [C, T], BF16)
        D["bcg"] = nc.dram_tensor("bcg", [32, T], BF16)
        D["wbn"] = nc.dram_tensor("wbn", [8, TL], F32)
        D["_octx"] = octx
        D["abscol"] = octx.enter_context(nc.sbuf_tensor("abscol_g", [128, 4], F32))
        D["abstmp"] = octx.enter_context(nc.sbuf_tensor("abstmp_g", [128, 1], F32))
        S = Sched(nc, octx)
        groups = ([emit_prelude, emit_phase_a], [emit_phase_b],
                  [emit_phase_c], [emit_phase_d], [emit_phase_e])
        for grp in groups:
            with ExitStack() as pctx:
                for em in grp:
                    em(nc, S, pctx, D)
            S.barrier()
        S.emit_all()
    return nc


def _stage_fused(inputs):
    ia = _stage_phase_a(inputs)
    ib = _stage_phase_b(inputs)
    ic = _stage_phase_c(inputs)
    id_ = _stage_phase_d(inputs)
    ie = _stage_phase_e(inputs)
    in_maps = []
    for c in range(8):
        m = {}
        for part in (ia[c], ib[c], ic[c], id_[c], ie[c]):
            m.update(part)
        in_maps.append(m)
    return in_maps


def kernel(**inputs):
    nc = build_fused()
    in_maps = _stage_fused(inputs)
    res = run_bass_kernel_spmd(nc, in_maps, core_ids=list(range(8)))
    out = np.zeros((B, T, C), np.float32)
    for c in range(8):
        b, j = divmod(c, 2)
        out[b, _tok_idx(j), :] = res.results[c]["x4"].T
    return out


# version 29 (final)
# speedup vs baseline: 1.0296x; 1.0034x over previous
import numpy as np
import ml_dtypes
from contextlib import ExitStack

import concourse.bass as bass
import concourse.mybir as mybir
from concourse.bass_utils import run_bass_kernel_spmd

F32 = mybir.dt.float32
BF16 = mybir.dt.bfloat16
AF = mybir.ActivationFunctionType
ALU = mybir.AluOpType
BF = ml_dtypes.bfloat16

B, T, C = 4, 2048, 2048
NH, DH = 16, 128
TL = T // 2
NCH = C // 128
NT = TL // 512
NKT = T // 128
LN_EPS = 1e-5
RMS_EPS0 = 1.1920929e-07
NEG = -1.0e30
ISQ = 1.0 / np.sqrt(128.0)

PAIRS = [[0, 1], [2, 3], [4, 5], [6, 7]]


CAPZ = [16 - 2 * l for l in range(8)]


def tok_blocks(p):
    return [(15 - p) - 2 * l for l in range(8)]


def _wid(kt):
    return 128 * ((16 - kt + 1) // 2)


MSEGS = []
for _l in range(8):
    _cap = CAPZ[_l]
    MSEGS.append((_cap - 2, _l, (2 * _l) * 128))
    MSEGS.append((_cap - 1, _l, (2 * _l + 1) * 128))
MASK_COLS = 16 * 128


QUEUES = ("pe", "act", "dve", "pool", "sp")


NDMA_SEMS = 4


class Sched:

    def __init__(self, nc, ctx):
        self.nc = nc
        self.sems = {}
        for q in QUEUES:
            self.sems[f"c_{q}"] = ctx.enter_context(nc.semaphore(f"c_{q}"))
            for k in range(NDMA_SEMS):
                self.sems[f"d_{q}{k}"] = ctx.enter_context(nc.semaphore(f"d_{q}{k}"))
        self.cval = {q: 0 for q in QUEUES}
        self.didx = {q: 0 for q in QUEUES}
        self.ops = {q: [] for q in QUEUES}
        self.seen = {q: {} for q in QUEUES}
        self.semval = {name: 0 for name in self.sems}
        self.res = {}

    def op(self, q, emit, reads=(), writes=(), dma=False, dma_n=1, tag=""):
        toks = []
        for k in reads:
            e = self.res.get(k)
            if e and e["w"]:
                toks.append(e["w"])
        for k in writes:
            e = self.res.get(k)
            if e:
                if e["w"]:
                    toks.append(e["w"])
                toks.extend(e["r"])
        need = {}
        for (dq, sem, dv) in toks:
            if dq == q and q == "pe" and sem == "c_pe":
                continue
            if dv > need.get(sem, -1):
                need[sem] = dv
        if dma:
            i = self.didx[q]
            self.didx[q] = i + 1
            sem = f"d_{q}{i % NDMA_SEMS}"
            prev = self.semval.get(sem, 0)
            if prev > 0 and prev > need.get(sem, -1):
                need[sem] = prev
            tok = (q, sem, prev + 16 * dma_n)
            inc_sem, inc = sem, 16
        else:
            self.cval[q] += 1
            tok = (q, f"c_{q}", self.cval[q])
            inc_sem, inc = f"c_{q}", 1
        waits = []
        for sem in sorted(need):
            dv = need[sem]
            if dv <= self.seen[q].get(sem, 0):
                continue
            waits.append((sem, dv))
            self.seen[q][sem] = dv
        self.ops[q].append((waits, emit, inc_sem, inc, tag))
        self.semval[inc_sem] = (self.semval.get(inc_sem, 0) + inc) if not dma else tok[2]
        for k in reads:
            self.res.setdefault(k, {"w": None, "r": []})["r"].append(tok)
        for k in writes:
            self.res[k] = {"w": tok, "r": []}
        return tok

    def barrier(self):
        vals = dict(self.semval)
        for q in QUEUES:
            waits = []
            for sem in sorted(vals):
                v = vals[sem]
                if v <= 0 or sem == f"c_{q}":
                    continue
                if v <= self.seen[q].get(sem, 0):
                    continue
                waits.append((sem, v))
                self.seen[q][sem] = v
            self.cval[q] += 1
            self.semval[f"c_{q}"] = self.cval[q]
            self.ops[q].append((waits, (lambda e: e.nop()), f"c_{q}", 1, "barrier"))

    def emit_all(self):
        nc = self.nc
        with nc.Block() as blk:
            def mk(q):
                def body(eng):
                    for oi, (waits, emit, inc_sem, inc, tag) in enumerate(self.ops[q]):
                        for sem, dv in waits:
                            eng.wait_ge(self.sems[sem], dv)
                        try:
                            inst = emit(eng)
                        except Exception:
                            print(f"EMIT FAIL q={q} idx={oi} tag={tag} waits={waits}")
                            raise
                        inst.then_inc(self.sems[inc_sem], inc)
                return body
            blk.tensor(mk("pe"))
            blk.scalar(mk("act"))
            blk.vector(mk("dve"))
            blk.gpsimd(mk("pool"))
            blk.sync(mk("sp"))


_NAMECTR = [0]


def _uniq(name):
    _NAMECTR[0] += 1
    return f"{name}_{_NAMECTR[0]}"


def sbt(ctx, nc, name, shape, dtype):
    return ctx.enter_context(nc.sbuf_tensor(_uniq(name), shape, dtype))


def pst(ctx, nc, name, shape, dtype=F32):
    return ctx.enter_context(nc.psum_tensor(_uniq(name), shape, dtype))



def mm_accum(S, n, psum_key, emit_i, reads_i, tag=""):
    for i in range(n):
        w = [psum_key] if (i == 0 or i == n - 1) else []
        S.op("pe", (lambda e, i=i: emit_i(e, i)), reads=reads_i(i), writes=w, tag=tag)


def emit_rstd_row(S, E, row, key, inv_n, eps):
    S.op("dve", lambda e: e.tensor_scalar(row[:, :], row[:, :], inv_n, eps,
                                          ALU.mult, ALU.add),
         reads=[key], writes=[key])
    S.op("act", lambda e: e.activation(row[:, :], row[:, :], AF.Ln),
         reads=[key], writes=[key])
    S.op("act", lambda e: e.activation(row[:, :], row[:, :], AF.Exp, scale=-0.5),
         reads=[key], writes=[key])


def emit_layernorm(S, E, xkeys, xtiles, lnb_row, gain_col=None, rms_only=False,
                   eps=LN_EPS, n_feat=C, apply=True):
    nch = len(xtiles)
    ps0, ps1, pr0, pr1 = E["ps0"], E["ps1"], E["pr0"], E["pr1"]
    rows, lnb = E["rows"], E["lnb"]
    for dc in range(nch):
        if not rms_only:
            for tc in range(NT):
                prt, prk = (pr0, "pr0") if tc == 0 else (pr1, "pr1")
                w = [prk] if (dc == 0 or dc == nch - 1) else []
                S.op("pe", (lambda e, dc=dc, tc=tc, prt=prt: e.matmul(
                    prt[0:1, :], E["ones_bf"][:, 0:1],
                    xtiles[dc][:, tc * 512:(tc + 1) * 512],
                    start=(dc == 0), stop=(dc == nch - 1))),
                    reads=[xkeys[dc], "ones"], writes=w, tag="lnS")
        S.op("act", (lambda e, dc=dc: e.activation(E["sqt"][:, 0:TL],
                                                   xtiles[dc][:, :], AF.Square)),
             reads=[xkeys[dc]], writes=[E.get("sqt_key", "sqt")], tag="lnSq")
        for tc in range(NT):
            pqt, pqk = (ps0, "ps0") if tc == 0 else (ps1, "ps1")
            w = [pqk] if (dc == 0 or dc == nch - 1) else []
            S.op("pe", (lambda e, dc=dc, tc=tc, pqt=pqt: e.matmul(
                pqt[0:1, :], E["ones_bf"][:, 0:1],
                E["sqt"][:, tc * 512:(tc + 1) * 512],
                start=(dc == 0), stop=(dc == nch - 1))),
                reads=[E.get("sqt_key", "sqt"), "ones"], writes=w, tag="lnQ")
    inv_n = 1.0 / n_feat
    for tc in range(NT):
        pqk = "ps0" if tc == 0 else "ps1"
        pqt = E[pqk]
        S.op("dve", (lambda e, tc=tc, pqt=pqt: e.tensor_copy(
            rows["qrow"][0:1, tc * 512:(tc + 1) * 512], pqt[0:1, :])),
            reads=[pqk], writes=["qrow"])
    if not rms_only:
        for tc in range(NT):
            prk = "pr0" if tc == 0 else "pr1"
            prt = E[prk]
            S.op("dve", (lambda e, tc=tc, prt=prt: e.tensor_copy(
                rows["srow"][0:1, tc * 512:(tc + 1) * 512], prt[0:1, :])),
                reads=[prk], writes=["srow"])
        S.op("dve", lambda e: e.tensor_scalar_mul(rows["srow"][:, :], rows["srow"][:, :], inv_n),
             reads=["srow"], writes=["srow"])
        S.op("dve", lambda e: e.tensor_scalar_mul(rows["qrow"][:, :], rows["qrow"][:, :], inv_n),
             reads=["qrow"], writes=["qrow"])
        S.op("dve", lambda e: e.tensor_tensor(rows["brow"][:, :], rows["srow"][:, :], rows["srow"][:, :], ALU.mult),
             reads=["srow"], writes=["brow"])
        S.op("dve", lambda e: e.tensor_tensor(rows["qrow"][:, :], rows["qrow"][:, :], rows["brow"][:, :], ALU.subtract),
             reads=["qrow", "brow"], writes=["qrow"])
        emit_rstd_row(S, E, rows["qrow"], "qrow", 1.0, eps)
    else:
        emit_rstd_row(S, E, rows["qrow"], "qrow", inv_n, eps)
    if not rms_only:
        S.op("dve", lambda e: e.tensor_tensor(rows["brow"][:, :], rows["srow"][:, :], rows["qrow"][:, :], ALU.mult),
             reads=["srow", "qrow"], writes=["brow"])
        S.op("dve", lambda e: e.tensor_scalar_mul(rows["brow"][:, :], rows["brow"][:, :], -1.0),
             reads=["brow"], writes=["brow"])
    S.op("sp", (lambda e: e.dma_start(
        bass.AP(lnb, lnb_row * TL, [[TL, 1], [1, TL]]), rows["qrow"][:, :])),
        reads=["qrow"], writes=[f"lnb{lnb_row}"], dma=True)
    if not rms_only:
        S.op("sp", (lambda e: e.dma_start(
            bass.AP(lnb, (lnb_row + 1) * TL, [[TL, 1], [1, TL]]), rows["brow"][:, :])),
            reads=["brow"], writes=[f"lnb{lnb_row + 1}"], dma=True)
    if not apply:
        return
    S.op("sp", (lambda e: e.dma_start(
        E["arep"][:, :], bass.AP(lnb, lnb_row * TL, [[0, 128], [1, TL]]))),
        reads=[f"lnb{lnb_row}"], writes=["arep"], dma=True)
    if not rms_only:
        S.op("sp", (lambda e: e.dma_start(
            E["brep"][:, :], bass.AP(lnb, (lnb_row + 1) * TL, [[0, 128], [1, TL]]))),
            reads=[f"lnb{lnb_row + 1}"], writes=["brep"], dma=True)
    for dc in range(nch):
        S.op("dve", (lambda e, dc=dc: e.tensor_tensor(
            xtiles[dc][:, :], xtiles[dc][:, :], E["arep"][:, :], ALU.mult)),
            reads=[xkeys[dc], "arep"], writes=[xkeys[dc]])
        if not rms_only:
            S.op("dve", (lambda e, dc=dc: e.tensor_tensor(
                xtiles[dc][:, :], xtiles[dc][:, :], E["brep"][:, :], ALU.add)),
                reads=[xkeys[dc], "brep"], writes=[xkeys[dc]])




def _rope_tables_l0():
    inv = 1.0 / 10000.0 ** (np.arange(0, DH, 2, dtype=np.float64) / DH)
    t = np.arange(T, dtype=np.float64)
    fr = np.outer(t, inv)
    sin = np.repeat(np.sin(fr), 2, axis=1)
    cos = np.repeat(np.cos(fr), 2, axis=1)
    sgn = np.where(np.arange(DH) % 2 == 0, -1.0, 1.0)
    return cos.T.astype(np.float32), (sin * sgn[None, :]).T.astype(np.float32)


def _mask_l0(is_odd):
    p = 1 if is_odd else 0
    m = np.zeros((128, MASK_COLS), np.float32)
    for (kt, l, off) in MSEGS:
        g = (15 - p) - 2 * l
        if kt > g:
            m[:, off:off + 128] = NEG
        elif kt == g:
            for c in range(128):
                m[c + 1:, off + c] = NEG
    return m


def _tok_idx(p):
    return np.concatenate([np.arange(g * 128, (g + 1) * 128)
                           for g in tok_blocks(p)])


def _stage_phase_a(inputs):
    x = np.asarray(inputs["L0_attn_x"])
    wqkv = np.asarray(inputs["L0_attn_W_qkv"])
    wo = np.asarray(inputs["L0_attn_W_o"])
    x_fm = np.ascontiguousarray(x.transpose(0, 2, 1))
    wqkvT = np.ascontiguousarray(wqkv.T)
    qk = np.ascontiguousarray(wqkvT[:, 0:4096].reshape(C, 32, 128).transpose(1, 0, 2))
    v = np.ascontiguousarray(wqkvT[:, 4096:6144].reshape(C, 16, 128).transpose(1, 0, 2))
    woT = np.ascontiguousarray(wo.T.reshape(C, 16, 128).transpose(1, 0, 2))
    cosf, sinf = _rope_tables_l0()
    ones = np.ones((128, 1), np.float32).astype(BF)
    ident = np.eye(128, dtype=np.float32).astype(BF)
    masks = [_mask_l0(False).astype(BF), _mask_l0(True).astype(BF)]
    in_maps = []
    for c in range(8):
        b, j = divmod(c, 2)
        idx = _tok_idx(j)
        in_maps.append({
            "x": np.ascontiguousarray(x_fm[b][:, idx]),
            "wqkT": qk, "wvT": v, "woT": woT,
            "cos0": np.ascontiguousarray(cosf[:, idx]).astype(BF),
            "sin0": np.ascontiguousarray(sinf[:, idx]).astype(BF),
            "mask0": masks[j],
            "onesd": ones, "identd": ident,
        })
    return in_maps




HID = 5461
HIDP = 5504
NHT = HIDP // 128


def _stage_phase_b(inputs, x1_fm=None, x1_pc=None):
    w1 = np.asarray(inputs["L0_ffn_w1"])
    w3 = np.asarray(inputs["L0_ffn_w3"])
    w2 = np.asarray(inputs["L0_ffn_w2"])

    def pad_slab_T(w):
        wT = np.zeros((C, HIDP), np.float32)
        wT[:, 0:HID] = w.T
        return np.ascontiguousarray(wT.reshape(C, NHT, 128).transpose(1, 0, 2))

    w1sl = pad_slab_T(w1)
    w3sl = pad_slab_T(w3)
    w2T = np.zeros((HIDP, C), np.float32)
    w2T[0:HID, :] = w2.T
    w2sl = np.ascontiguousarray(w2T.reshape(HIDP, NCH, 128).transpose(1, 0, 2))
    ones = np.ones((128, 1), np.float32)
    in_maps = []
    for c in range(8):
        b, j = divmod(c, 2)
        m = {"w1s": w1sl, "w3s": w3sl, "w2s": w2sl,
             "onesd": ones.astype(BF), "onesf": ones}
        if x1_pc is not None:
            m["x1d"] = x1_pc[c]
        elif x1_fm is not None:
            m["x1d"] = np.ascontiguousarray(x1_fm[b, :, j * TL:(j + 1) * TL])
        in_maps.append(m)
    return in_maps


NSSM = 16
TCH = 512
NCHK = T // TCH
E2ROWS = 160 + C


def _stage_phase_c(inputs, x2_fm=None, x2_pc=None):
    wx = np.asarray(inputs["L1_attn_W_xproj"])
    wdtf = np.asarray(inputs["L1_attn_W_dt"])
    bdt = np.asarray(inputs["L1_attn_b_dt"])
    alog = np.asarray(inputs["L1_attn_A_log"])
    dpar = np.asarray(inputs["L1_attn_D_param"])
    wxTs = np.ascontiguousarray(wx.T)
    dcols = np.ascontiguousarray(dpar.reshape(16, 128).T)
    ones = np.ones((128, 1), np.float32).astype(BF)
    in_maps = []
    for c in range(8):
        b, half = divmod(c, 2)
        dsl = slice(half * 1024, (half + 1) * 1024)
        wdtT_h = np.ascontiguousarray(wdtf[dsl, :].T)
        alog_h = np.ascontiguousarray(
            alog[dsl].reshape(8, 128, 16).transpose(1, 0, 2).reshape(128, 128))
        bdt_h = np.ascontiguousarray(bdt[dsl].reshape(8, 128).T)
        hoffv = np.array([[half * 1024 * T, half * TL, 0, 0]], np.int32)
        m = {"wxT": wxTs, "wdtT": wdtT_h, "alogh": alog_h, "bdth": bdt_h,
             "dfull": dcols, "onesd": ones, "hoffd": hoffv}
        if x2_pc is not None:
            m["x2d"] = x2_pc[c]
        elif x2_fm is not None:
            m["x2d"] = np.ascontiguousarray(x2_fm[b, :, half * TL:(half + 1) * TL])
        in_maps.append(m)
    return in_maps


LAT = 512
NLAT = LAT // 128


def _rope_tables_l1():
    RD = 64
    inv = 1.0 / 10000.0 ** (np.arange(0, RD, 2, dtype=np.float64) / RD)
    t = np.arange(T, dtype=np.float64)
    ang = np.outer(t, inv)
    cosb = np.repeat(np.cos(ang), 2, axis=1)
    sinb = np.repeat(np.sin(ang), 2, axis=1)
    sgn = np.where(np.arange(64) % 2 == 0, -1.0, 1.0)
    sinb = sinb * sgn[None, :]
    cos128 = np.concatenate([cosb, cosb], axis=1).T
    sin128 = np.concatenate([sinb, sinb], axis=1).T
    return cos128.astype(np.float32), sin128.astype(np.float32)


ROPE_PERM = np.empty(64, np.int64)
ROPE_PERM[0::2] = np.arange(32)
ROPE_PERM[1::2] = np.arange(32) + 32


def _perm_rope_cols(w):
    w2 = w.reshape(w.shape[0], 16, 64)[:, :, ROPE_PERM]
    return np.ascontiguousarray(w2.reshape(w.shape[0], 1024))


def _slab(w, kdim, nf):
    return np.ascontiguousarray(w.reshape(kdim, nf, 128).transpose(1, 0, 2))


def _stage_phase_d(inputs, x3_fm=None, x3_pc=None):
    g = lambda k: np.asarray(inputs[f"L1_ffn_{k}"])
    kvdw = _slab(g("kv_d_w"), C, NLAT)
    qdw = _slab(g("q_d_w"), C, NLAT)
    kuw = _slab(g("k_u_w"), LAT, 8)
    quw = _slab(g("q_u_w"), LAT, 8)
    vuw = np.ascontiguousarray(g("v_u_w"))
    rkw = _slab(_perm_rope_cols(g("rope_k_w")), C, 8)
    rqw = _slab(_perm_rope_cols(g("rope_q_w")), LAT, 8)
    oww = _slab(g("o_w"), C, NCH)
    l1w = np.ascontiguousarray(g("ln1_w").reshape(16, 128).T)
    cosf, sinf = _rope_tables_l1()
    ones = np.ones((128, 1), np.float32).astype(BF)
    masks = [_mask_l0(False).astype(BF), _mask_l0(True).astype(BF)]
    in_maps = []
    for c in range(8):
        b, j = divmod(c, 2)
        idx = _tok_idx(j)
        m = {"kvdw": kvdw, "qdw": qdw, "kuw": kuw, "quw": quw, "vuw": vuw,
             "rkw": rkw, "rqw": rqw, "oww": oww, "l1wd": l1w,
             "cos1": np.ascontiguousarray(cosf[:, idx]).astype(BF),
             "sin1": np.ascontiguousarray(sinf[:, idx]).astype(BF),
             "mask0": masks[j], "onesd": ones,
             "identd": np.eye(128, dtype=np.float32).astype(BF)}
        if x3_pc is not None:
            m["x3d"] = x3_pc[c]
        elif x3_fm is not None:
            m["x3d"] = np.ascontiguousarray(x3_fm[b, :, cols])
        in_maps.append(m)
    return in_maps


IEXP = 1024
NIT = IEXP // 128
NEXP = 8


def _stage_phase_e(inputs, h_fm=None, x3_fm=None, h_pc=None, x3_pc=None):
    g = lambda k: np.asarray(inputs[f"L1_ffn_{k}"])
    shg = _slab(g("sh_gate"), C, NIT)
    shu = _slab(g("sh_up"), C, NIT)
    shd = _slab(g("sh_down"), IEXP, NCH)
    rgs = np.stack([_slab(g("r_gate")[e], C, NIT) for e in range(7)])
    rus = np.stack([_slab(g("r_up")[e], C, NIT) for e in range(7)])
    rds = np.stack([_slab(g("r_down")[e], IEXP, NCH) for e in range(7)])
    rwTs = np.ascontiguousarray(g("router_w"))
    rbs = np.ascontiguousarray(g("router_b").reshape(7, 1))
    l2w = np.ascontiguousarray(g("ln2_w").reshape(16, 128).T)
    ones = np.ones((128, 1), np.float32).astype(BF)
    in_maps = []
    for c in range(8):
        b, j = divmod(c, 2)
        cols = slice(j * TL, (j + 1) * TL)
        m = {"shg": shg, "shu": shu, "shd": shd, "rg": rgs, "ru": rus, "rd": rds,
             "rwT": rwTs, "rbd": rbs, "l2wd": l2w, "onesd": ones}
        if h_pc is not None:
            m["hdd"] = h_pc[c]
            m["x3d"] = x3_pc[c]
        elif h_fm is not None:
            m["hdd"] = np.ascontiguousarray(h_fm[b, :, cols])
            m["x3d"] = np.ascontiguousarray(x3_fm[b, :, cols])
        in_maps.append(m)
    return in_maps


def emit_prelude(nc, S, ctx, D):
    w1s, w3s, w2s = D["w1s"], D["w3s"], D["w2s"]
    abscol, abstmp = D["abscol"], D["abstmp"]
    pwf = [sbt(ctx, nc, f"pwf{i}", [128, C], F32) for i in range(2)]
    pab = sbt(ctx, nc, "pab", [128, C], BF16)
    S.op("dve", lambda e: e.memset(abscol[:, :], 0.0), writes=["abscol"])
    ti = 0
    for (srcd, gi, nel) in ((w1s, 0, NHT * C * 128), (w3s, 1, NHT * C * 128),
                            (w2s, 2, NCH * HIDP * 128)):
        ntile = nel // (128 * C)
        for t_ in range(ntile):
            wk = f"pwf{ti % 2}"
            S.op("sp" if ti % 2 == 0 else "pool",
                 (lambda e, srcd=srcd, t_=t_, ti=ti: e.dma_start(
                     pwf[ti % 2][:, :],
                     bass.AP(srcd, t_ * 128 * C, [[C, 128], [1, C]]))),
                 writes=[wk], dma=True, tag="pwl")
            S.op("act", (lambda e, ti=ti: e.activation(
                pab[:, :], pwf[ti % 2][:, :], AF.Abs,
                accum_out=abstmp[:, 0:1])),
                reads=[wk], writes=["pab", "abstmp"], tag="pabs")
            S.op("dve", (lambda e, gi=gi: e.tensor_tensor(
                abscol[:, gi:gi + 1], abscol[:, gi:gi + 1], abstmp[:, 0:1], ALU.add)),
                reads=["abstmp", "abscol"], writes=["abscol"], tag="pacc")
            ti += 1



def emit_phase_a(nc, S, ctx, D):
    x = D["x"]
    wqkT = D["wqkT"]
    wvT = D["wvT"]
    woT = D["woT"]
    cos0 = D["cos0"]
    sin0 = D["sin0"]
    mask0 = D["mask0"]
    onesd = D["onesd"]
    identd = D["identd"]
    x1 = D["x1i"]
    k_in = D["k_in"]
    k_out = D["k_out"]
    v_in = D["v_in"]
    v_out = D["v_out"]
    lnb = D["lnb"]
    xb = [sbt(ctx, nc, f"xb{i}", [128, TL], BF16) for i in range(NCH)]
    qt = [sbt(ctx, nc, f"qt{i}", [128, TL], BF16) for i in range(NH)]
    ot = [sbt(ctx, nc, f"ot{i}", [128, TL], BF16) for i in range(NH)]
    sqt2 = [sbt(ctx, nc, f"sqt2_{i}", [128, TL], BF16) for i in range(2)]
    sqt = sqt2[0]
    wslab = [sbt(ctx, nc, f"wslab{i}", [128, C], BF16) for i in range(2)]
    ktile = [sbt(ctx, nc, f"ktile{i}", [128, TL], BF16) for i in range(2)]
    shuf2 = [sbt(ctx, nc, f"shuf2_{i}", [128, TL], BF16) for i in range(2)]
    cosT = sbt(ctx, nc, "cosT", [128, TL], BF16)
    sinT = sbt(ctx, nc, "sinT", [128, TL], BF16)
    maskT = sbt(ctx, nc, "maskT", [128, MASK_COLS], BF16)
    ones_bf = sbt(ctx, nc, "ones_bf", [128, 1], BF16)
    ident = sbt(ctx, nc, "ident", [128, 128], BF16)
    khead = [sbt(ctx, nc, f"khead{i}", [128, T], BF16) for i in range(2)]
    vhead = [sbt(ctx, nc, f"vhead{i}", [128, T], BF16) for i in range(2)]
    ptile2 = [sbt(ctx, nc, f"ptile{i}", [128, TL], BF16) for i in range(2)]
    vexp = sbt(ctx, nc, "vexp", [128, TL], BF16)

    arep = sbt(ctx, nc, "arep", [128, TL], F32)
    brep = sbt(ctx, nc, "brep", [128, TL], F32)
    rrep2 = [arep, brep]
    xres = [sbt(ctx, nc, "xres0", [128, TL], F32)]
    x1b = [sbt(ctx, nc, f"x1b{i}", [128, TL], F32) for i in range(2)]
    rows = {k: sbt(ctx, nc, k, [1, TL], F32) for k in
            ("srow", "qrow", "brow")}
    rrow2 = [sbt(ctx, nc, f"rrow2_{i}", [1, TL], F32) for i in range(2)]

    ps4 = pst(ctx, nc, "ps4", [128, 512])
    ps0 = pst(ctx, nc, "ps0", [128, 512])
    ps1 = pst(ctx, nc, "ps1", [128, 512])
    ps2 = pst(ctx, nc, "ps2", [128, 512])
    ps3 = pst(ctx, nc, "ps3", [128, 512])
    pr0 = pst(ctx, nc, "pr0", [128, 512])
    pr1 = pst(ctx, nc, "pr1", [128, 512])
    psb = pst(ctx, nc, "psb", [128, 1024], BF16)
    PS = {0: ps0, 1: ps1, 2: ps2, 3: ps3}

    E = {"ones_bf": ones_bf, "sqt": sqt, "sqt_key": "sqtp0", "rows": rows,
         "lnb": lnb, "arep": arep, "brep": brep,
         "ps0": ps0, "ps1": ps1, "pr0": pr0, "pr1": pr1}

    for dc in range(NCH):
        S.op("pool", (lambda e, dc=dc: e.dma_start(
            xb[dc][:, :], x[dc * 128:(dc + 1) * 128, :])),
            writes=[f"xb{dc}"], dma=True)
    S.op("sp", lambda e: e.dma_start(cosT[:, :], cos0[:, :]), writes=["cosT"], dma=True)
    S.op("sp", lambda e: e.dma_start(sinT[:, :], sin0[:, :]), writes=["sinT"], dma=True)
    S.op("sp", lambda e: e.dma_start(maskT[:, :], mask0[:, :]), writes=["maskT"], dma=True)
    S.op("sp", lambda e: e.dma_start(ones_bf[:, :], onesd[:, :]), writes=["ones"], dma=True)
    S.op("sp", lambda e: e.dma_start(ident[:, :], identd[:, :]), writes=["ident"], dma=True)

    xkeys = [f"xb{i}" for i in range(NCH)]
    emit_layernorm(S, E, xkeys, xb, 0)

    def proj_head(src_dram, f_idx, dest_tile, dest_key, do_rope):
        par = f_idx % 2
        wk = f"wslab{par}"
        shf, shfk = shuf2[par], f"shuf{par}"
        sq, sqk = sqt2[par], f"sqtp{par}"
        rro, rrok = rrow2[par], f"rrow{par}"
        rrp, rrpk = rrep2[par], f"rrep{par}"
        S.op("pool", (lambda e: e.dma_start(
            wslab[par][:, :],
            bass.AP(src_dram, f_idx * C * 128,
                    [[128, 128], [128 * 128, NCH], [1, 128]]))),
            writes=[wk], dma=True)
        for tc in range(NT):
            pk = ("ps2", "ps3", "ps4")[(f_idx * NT + tc) % 3]
            pt = E_PS[pk]
            for dc in range(NCH):
                w = [pk] if (dc == 0 or dc == NCH - 1) else []
                S.op("pe", (lambda e, tc=tc, dc=dc, pt=pt: e.matmul(
                    pt[:, :], wslab[par][:, dc * 128:(dc + 1) * 128],
                    xb[dc][:, tc * 512:(tc + 1) * 512],
                    start=(dc == 0), stop=(dc == NCH - 1))),
                    reads=[wk, f"xb{dc}"], writes=w, tag=f"pj{f_idx}")
            if par == 0:
                S.op("dve", (lambda e, tc=tc, pt=pt: e.tensor_copy(
                    dest_tile[:, tc * 512:(tc + 1) * 512], pt[:, :])),
                    reads=[pk], writes=[dest_key], tag="pjev")
            else:
                S.op("act", (lambda e, tc=tc, pt=pt: e.copy(
                    dest_tile[:, tc * 512:(tc + 1) * 512], pt[:, :])),
                    reads=[pk], writes=[dest_key], tag="pjev")
        if not do_rope:
            return
        mask32 = [i ^ 1 for i in range(32)]
        S.op("dve", lambda e: e.stream_shuffle(shf[:, :], dest_tile[:, :], mask32),
             reads=[dest_key], writes=[shfk])
        S.op("dve", lambda e: e.tensor_tensor(dest_tile[:, :], dest_tile[:, :], cosT[:, :], ALU.mult),
             reads=[dest_key, "cosT"], writes=[dest_key])
        S.op("dve", lambda e: e.tensor_tensor(shf[:, :], shf[:, :], sinT[:, :], ALU.mult),
             reads=[shfk, "sinT"], writes=[shfk])
        S.op("dve", lambda e: e.tensor_tensor(dest_tile[:, :], dest_tile[:, :], shf[:, :], ALU.add),
             reads=[dest_key, shfk], writes=[dest_key])
        S.op("act", lambda e: e.activation(sq[:, :], dest_tile[:, :], AF.Square),
             reads=[dest_key], writes=[sqk])
        for tc in range(NT):
            prk = "pr0" if tc == 0 else "pr1"
            prt = E[prk]
            S.op("pe", (lambda e, tc=tc, prt=prt: e.matmul(
                prt[0:1, :], ones_bf[:, 0:1], sq[:, tc * 512:(tc + 1) * 512],
                start=True, stop=True)),
                reads=[sqk, "ones"], writes=[prk])
            S.op("dve", (lambda e, tc=tc, prt=prt: e.tensor_copy(
                rro[0:1, tc * 512:(tc + 1) * 512], prt[0:1, :])),
                reads=[prk], writes=[rrok])
        emit_rstd_row(S, E, rro, rrok, 1.0 / 128, RMS_EPS0)
        S.op("sp", (lambda e: e.dma_start(
            bass.AP(lnb, (2 + par) * TL, [[TL, 1], [1, TL]]), rro[:, :])),
            reads=[rrok], writes=[f"lnb{2 + par}"], dma=True)
        S.op("sp", (lambda e: e.dma_start(
            rrp[:, :], bass.AP(lnb, (2 + par) * TL, [[0, 128], [1, TL]]))),
            reads=[f"lnb{2 + par}"], writes=[rrpk], dma=True)
        S.op("dve", lambda e: e.tensor_tensor(dest_tile[:, :], dest_tile[:, :], rrp[:, :], ALU.mult),
             reads=[dest_key, rrpk], writes=[dest_key])

    E_PS = {"ps2": ps2, "ps3": ps3, "ps4": ps4}
    for h in range(NH):
        proj_head(wqkT, h, qt[h], f"qt{h}", True)
    for h in range(NH):
        kb = ktile[h % 2]
        proj_head(wqkT, 16 + h, kb, f"ktile{h % 2}", True)
        S.op("sp", (lambda e, h=h, kb=kb: e.dma_start(
            k_in[h * 128:(h + 1) * 128, :], kb[:, :])),
            reads=[f"ktile{h % 2}"], writes=[f"k_in{h}"], dma=True)

    S.op("pool", (lambda e: e.collective_compute(
        "AllGather", ALU.bypass, replica_groups=PAIRS,
        ins=[k_in.ap().opt()], outs=[k_out.ap().opt()])),
        reads=[f"k_in{h}" for h in range(NH)], writes=["k_out"], tag="agK")

    for h in range(NH):
        proj_head(wvT, h, ktile[0], "ktile0", False)
        for tt in range(8):
            S.op("pe", (lambda e, tt=tt: e.transpose(
                psb[:, tt * 128:(tt + 1) * 128],
                ktile[0][:, tt * 128:(tt + 1) * 128], ident[:, :])),
                reads=["ktile0", "ident"], writes=["psb"], tag="vT")
        S.op("dve", lambda e: e.tensor_copy(vexp[:, :], psb[:, :]),
             reads=["psb"], writes=["vexp"])
        S.op("sp", (lambda e, h=h: e.dma_start(
            bass.AP(v_in, h * 128, [[C, 128], [128 * C, 8], [1, 128]]),
            vexp[:, :])),
            reads=["vexp"], writes=[f"v_in{h}"], dma=True)

    S.op("pool", (lambda e: e.collective_compute(
        "AllGather", ALU.bypass, replica_groups=PAIRS,
        ins=[v_in.ap().opt()], outs=[v_out.ap().opt()])),
        reads=[f"v_in{h}" for h in range(NH)], writes=["v_out"], tag="agV")

    AV_LAST = {0: 15, 1: 7}
    for h in range(NH):
        kb, kk = khead[h % 2], f"khead{h % 2}"
        vb, vk = vhead[h % 2], f"vhead{h % 2}"
        for r_ in range(2):
            S.op("sp", (lambda e, h=h, kb=kb, r_=r_: e.dma_start(
                bass.AP(kb, (1 - r_) * 128, [[T, 128], [256, 8], [1, 128]]),
                bass.AP(k_out, r_ * C * TL + h * 128 * TL + 7 * 128,
                        [[TL, 128], [-128, 8], [1, 128]]))),
                reads=["k_out"], writes=[kk], dma=True)
            S.op("sp", (lambda e, h=h, vb=vb, r_=r_: e.dma_start(
                bass.AP(vb, (1 - r_) * 128, [[T, 128], [256, 8], [1, 128]]),
                bass.AP(v_out, r_ * TL * C + 7 * 128 * C + h * 128,
                        [[C, 128], [-128 * C, 8], [1, 128]]))),
                reads=["v_out", "k_out"], writes=[vk], dma=True)
        first = [True, True]
        for kt in range(NKT):
            wd = _wid(kt)
            msegs_kt = [(l_, off_) for (mk_, l_, off_) in MSEGS if mk_ == kt]
            chunks = []
            c0 = 0
            while c0 < wd:
                chunks.append((c0, min(c0 + 512, wd)))
                c0 = min(c0 + 512, wd)
            for ci, (c0, c1) in enumerate(chunks):
                spk = ("ps2", "ps3", "ps4")[(kt + ci) % 3]
                sps = E_PS[spk]
                S.op("pe", (lambda e, kt=kt, c0=c0, c1=c1, sps=sps, kb=kb, h=h: e.matmul(
                    sps[:, 0:c1 - c0], kb[:, kt * 128:(kt + 1) * 128],
                    qt[h][:, c0:c1], start=True, stop=True)),
                    reads=[kk, f"qt{h}"], writes=[spk], tag=f"sc{h}")
                for (ml_, moff_) in msegs_kt:
                    a0 = ml_ * 128
                    if a0 >= c0 and a0 + 128 <= c1:
                        S.op("dve", (lambda e, a0=a0, c0=c0, sps=sps, moff_=moff_: e.tensor_tensor(
                            sps[:, a0 - c0:a0 - c0 + 128], sps[:, a0 - c0:a0 - c0 + 128],
                            maskT[:, moff_:moff_ + 128], ALU.add)),
                            reads=[spk, "maskT"], writes=[spk], tag="msk")
                S.op("act", (lambda e, c0=c0, c1=c1, sps=sps, kt=kt: e.activation(
                    ptile2[kt % 2][:, c0:c1], sps[:, 0:c1 - c0], AF.Exp, scale=ISQ)),
                    reads=[spk], writes=[f"ptile{kt % 2}"], tag="exp")
            b0 = 0
            while b0 < wd:
                blk_i = b0 // 512
                b1 = min((blk_i + 1) * 512, wd)
                opk = "ps0" if blk_i == 0 else "ps1"
                opt_ = PS[blk_i]
                rpk = "pr0" if blk_i == 0 else "pr1"
                rpt = E[rpk]
                wo_ = [opk] if (first[blk_i] or kt == AV_LAST[blk_i]) else []
                wr_ = [rpk] if (first[blk_i] or kt == AV_LAST[blk_i]) else []
                S.op("pe", (lambda e, kt=kt, b0=b0, b1=b1, blk_i=blk_i, vb=vb,
                            opt_=opt_, f=first[blk_i]: e.matmul(
                    opt_[:, b0 - blk_i * 512:b1 - blk_i * 512],
                    vb[:, kt * 128:(kt + 1) * 128], ptile2[kt % 2][:, b0:b1],
                    start=f, stop=(kt == AV_LAST[blk_i]))),
                    reads=[vk, f"ptile{kt % 2}"], writes=wo_, tag=f"av{h}")
                S.op("pe", (lambda e, kt=kt, b0=b0, b1=b1, blk_i=blk_i,
                            rpt=rpt, f=first[blk_i]: e.matmul(
                    rpt[0:1, b0 - blk_i * 512:b1 - blk_i * 512],
                    ones_bf[:, 0:1], ptile2[kt % 2][:, b0:b1],
                    start=f, stop=(kt == AV_LAST[blk_i]))),
                    reads=[f"ptile{kt % 2}", "ones"], writes=wr_, tag=f"rs{h}")
                first[blk_i] = False
                b0 = b1
        par = h % 2
        rro, rrok = rrow2[par], f"rrow{par}"
        rrp, rrpk = rrep2[par], f"rrep{par}"
        for blk_i in range(2):
            rpk = "pr0" if blk_i == 0 else "pr1"
            rpt = E[rpk]
            S.op("dve", (lambda e, blk_i=blk_i, rpt=rpt, rro=rro: e.tensor_copy(
                rro[0:1, blk_i * 512:(blk_i + 1) * 512], rpt[0:1, :])),
                reads=[rpk], writes=[rrok])
        S.op("dve", (lambda e, rro=rro: e.reciprocal(rro[:, :], rro[:, :])),
             reads=[rrok], writes=[rrok])
        S.op("sp", (lambda e, rro=rro: e.dma_start(
            bass.AP(lnb, (4 + par) * TL, [[TL, 1], [1, TL]]), rro[:, :])),
            reads=[rrok], writes=[f"lnb{4 + par}"], dma=True)
        S.op("sp", (lambda e, rrp=rrp: e.dma_start(
            rrp[:, :], bass.AP(lnb, (4 + par) * TL, [[0, 128], [1, TL]]))),
            reads=[f"lnb{4 + par}"], writes=[rrpk], dma=True)
        for blk_i in range(2):
            opk = "ps0" if blk_i == 0 else "ps1"
            opt_ = PS[blk_i]
            q_ = "dve" if blk_i == 0 else "act"
            if q_ == "dve":
                S.op("dve", (lambda e, blk_i=blk_i, h=h, opt_=opt_: e.tensor_copy(
                    ot[h][:, blk_i * 512:(blk_i + 1) * 512], opt_[:, :])),
                    reads=[opk], writes=[f"ot{h}"], tag="oev")
            else:
                S.op("act", (lambda e, blk_i=blk_i, h=h, opt_=opt_: e.copy(
                    ot[h][:, blk_i * 512:(blk_i + 1) * 512], opt_[:, :])),
                    reads=[opk], writes=[f"ot{h}"], tag="oev")
        S.op("dve", (lambda e, h=h, rrp=rrp: e.tensor_tensor(
            ot[h][:, :], ot[h][:, :], rrp[:, :], ALU.mult)),
            reads=[f"ot{h}", rrpk], writes=[f"ot{h}"], tag="onrm")

    for o in range(NCH):
        wk = f"wslab{o % 2}"
        S.op("pool", (lambda e, o=o: e.dma_start(
            wslab[o % 2][:, :],
            bass.AP(woT, o * C * 128, [[128, 128], [128 * 128, NCH], [1, 128]]))),
            writes=[wk], dma=True)
        S.op("sp", (lambda e, o=o: e.dma_start(
            xres[0][:, :], x[o * 128:(o + 1) * 128, :])),
            writes=["xres0"], dma=True)
        for tc in range(NT):
            pk = "ps2" if tc == 0 else "ps3"
            pt = E_PS[pk]
            for h in range(NH):
                w = [pk] if (h == 0 or h == NH - 1) else []
                S.op("pe", (lambda e, o=o, tc=tc, h=h, pt=pt: e.matmul(
                    pt[:, :], wslab[o % 2][:, h * 128:(h + 1) * 128],
                    ot[h][:, tc * 512:(tc + 1) * 512],
                    start=(h == 0), stop=(h == NH - 1))),
                    reads=[wk, f"ot{h}"], writes=w, tag=f"wo{o}")
            S.op("dve", (lambda e, o=o, tc=tc, pt=pt: e.tensor_tensor(
                x1b[o % 2][:, tc * 512:(tc + 1) * 512], pt[:, :],
                xres[0][:, tc * 512:(tc + 1) * 512], ALU.add)),
                reads=[pk, "xres0"], writes=[f"x1b{o % 2}"], tag="res")
        S.op("sp", (lambda e, o=o: e.dma_start(
            x1[o * 128:(o + 1) * 128, :], x1b[o % 2][:, :])),
            reads=[f"x1b{o % 2}"], writes=[f"x1st{o % 2}"], dma=True)

    S.op("sp", lambda e: e.nop(), reads=["x1st0", "x1st1"])


def emit_phase_b(nc, S, ctx, D):
    sder = D["sder"]
    x1d = D["x1i"]
    w1s = D["w1s"]
    w3s = D["w3s"]
    w2s = D["w2s"]
    onesd = D["onesd"]
    onesf = D["onesf"]
    x2 = D["x2i"]
    lnb = D["lnb"]

    xb = [sbt(ctx, nc, f"xb{i}", [128, TL], BF16) for i in range(NCH)]
    gt = [sbt(ctx, nc, f"gt{i}", [128, TL], BF16) for i in range(NHT)]
    wf32 = [sbt(ctx, nc, f"wf32_{i}", [128, C], F32) for i in range(2)]
    w2f32 = [sbt(ctx, nc, f"w2f32_{i}", [128, 1408], F32) for i in range(2)]
    qsl = [sbt(ctx, nc, f"qsl{i}", [128, C], BF16) for i in range(2)]
    q2sl = [sbt(ctx, nc, "q2sl0", [128, HIDP], BF16)]
    qtmp = sbt(ctx, nc, "qtmp", [128, 2048], BF16)
    sgt = sbt(ctx, nc, "sgt", [128, 2048], BF16)
    h1t = sbt(ctx, nc, "h1t", [128, TL], BF16)
    sqt = h1t
    onesbf = sbt(ctx, nc, "onesbf", [128, 1], BF16)
    onesf32 = sbt(ctx, nc, "onesf32", [128, 1], F32)
    srow = sbt(ctx, nc, "srow_s", [1, 16], F32)
    scols = sbt(ctx, nc, "scols", [128, 16], F32)
    arep = sbt(ctx, nc, "arep", [128, TL], F32)
    brep = sbt(ctx, nc, "brep", [128, TL], F32)
    x2b = [sbt(ctx, nc, "x2b0", [128, TL], F32)] * 2
    rows = {k: sbt(ctx, nc, k, [1, TL], F32) for k in
            ("srow", "qrow", "brow")}

    ps0 = pst(ctx, nc, "ps0", [128, 512])
    ps1 = pst(ctx, nc, "ps1", [128, 512])
    ps2 = pst(ctx, nc, "ps2", [128, 512])
    ps3 = pst(ctx, nc, "ps3", [128, 512])
    pr0 = pst(ctx, nc, "pr0", [128, 512])
    pr1 = pst(ctx, nc, "pr1", [128, 512])

    E = {"ones_bf": onesbf, "sqt": sqt, "rows": rows, "lnb": lnb,
         "arep": arep, "brep": brep,
         "ps0": ps0, "ps1": ps1, "pr0": pr0, "pr1": pr1}
    E_PS = {"ps0": ps0, "ps1": ps1, "ps2": ps2, "ps3": ps3}

    S.op("sp", lambda e: e.dma_start(onesbf[:, :], onesd[:, :]), writes=["ones"], dma=True)
    S.op("sp", lambda e: e.dma_start(onesf32[:, :], onesf[:, :]), writes=["onesf"], dma=True)
    for dc in range(NCH):
        S.op("pool", (lambda e, dc=dc: e.dma_start(
            xb[dc][:, :], x1d[dc * 128:(dc + 1) * 128, :])),
            writes=[f"xb{dc}"], dma=True)

    abscol = D["abscol"]
    S.op("pe", lambda e: e.matmul(pr0[0:1, 0:3], onesf32[:, 0:1], abscol[:, 0:3],
                                  start=True, stop=True),
         reads=["abscol", "onesf"], writes=["pr0"], tag="absred")
    CNT = {0: HID * C, 1: HID * C, 2: HID * C}
    S.op("dve", lambda e: e.memset(srow[:, :], 0.0), writes=["srow_s"])
    S.op("dve", lambda e: e.tensor_copy(srow[0:1, 0:3], pr0[0:1, 0:3]),
         reads=["pr0", "srow_s"], writes=["srow_s"])
    for gi in range(3):
        S.op("dve", (lambda e, gi=gi: e.tensor_scalar_mul(
            srow[0:1, 4 + gi * 4:5 + gi * 4], srow[0:1, gi:gi + 1],
            0.5 / CNT[gi])), reads=["srow_s"], writes=["srow_s"])
        S.op("dve", (lambda e, gi=gi: e.tensor_scalar_mul(
            srow[0:1, 5 + gi * 4:6 + gi * 4], srow[0:1, gi:gi + 1],
            -0.5 / CNT[gi])), reads=["srow_s"], writes=["srow_s"])
        S.op("dve", (lambda e, gi=gi: e.tensor_scalar_mul(
            srow[0:1, 6 + gi * 4:7 + gi * 4], srow[0:1, gi:gi + 1],
            1.0 / CNT[gi])), reads=["srow_s"], writes=["srow_s"])
        S.op("dve", (lambda e, gi=gi: e.tensor_scalar_mul(
            srow[0:1, 7 + gi * 4:8 + gi * 4], srow[0:1, gi:gi + 1],
            -1.0 / CNT[gi])), reads=["srow_s"], writes=["srow_s"])
    S.op("sp", (lambda e: e.dma_start(
        bass.AP(sder, 0, [[16, 1], [1, 16]]), srow[0:1, 0:16])),
        reads=["srow_s"], writes=["sderd"], dma=True)
    S.op("sp", (lambda e: e.dma_start(
        scols[:, :], bass.AP(sder, 0, [[0, 128], [1, 16]]))),
        reads=["sderd"], writes=["scols"], dma=True)

    xkeys = [f"xb{i}" for i in range(NCH)]
    emit_layernorm(S, E, xkeys, xb, 0)

    def quant_slab(f32buf, f32key, bfbuf, bfkey, gi, width, off=0):
        for h0 in range(0, width, 2048):
            h1 = min(h0 + 2048, width)
            wdt_ = h1 - h0
            S.op("act", (lambda e, h0=h0, wdt_=wdt_: e.activation(
                sgt[:, 0:wdt_], f32buf[:, h0:h0 + wdt_], AF.Sign,
                bias=scols[:, 5 + gi * 4:6 + gi * 4])),
                reads=[f32key, "scols"], writes=["sgt"], tag="qs1")
            S.op("act", (lambda e, h0=h0, wdt_=wdt_: e.activation(
                qtmp[:, 0:wdt_], f32buf[:, h0:h0 + wdt_], AF.Sign,
                bias=scols[:, 4 + gi * 4:5 + gi * 4])),
                reads=[f32key, "scols"], writes=["qtmp"], tag="qs2")
            S.op("dve", (lambda e, wdt_=wdt_: e.tensor_tensor(
                qtmp[:, 0:wdt_], qtmp[:, 0:wdt_], sgt[:, 0:wdt_], ALU.add)),
                reads=["qtmp", "sgt"], writes=["qtmp"], tag="qadd")
            S.op("dve", (lambda e, h0=h0, wdt_=wdt_: e.tensor_scalar_mul(
                bfbuf[:, off + h0:off + h0 + wdt_], qtmp[:, 0:wdt_],
                scols[:, 4 + gi * 4:5 + gi * 4])),
                reads=["qtmp", "scols"], writes=[bfkey], tag="qmul")

    for f in range(NHT):
        for wi, (src, gi) in enumerate([(w1s, 0), (w3s, 1)]):
            wk = f"wf{wi}"
            qk = f"qsl{wi}"
            S.op("sp", (lambda e, src=src, f=f, buf=wf32[wi]: e.dma_start(
                buf[:, :], bass.AP(src, f * C * 128,
                                   [[128, 128], [128 * 128, NCH], [1, 128]]))),
                writes=[wk], dma=True)
            quant_slab(wf32[wi], wk, qsl[wi], qk, gi, C)
            for tc in range(NT):
                pk = ["ps0", "ps1"][tc] if wi == 0 else ["ps2", "ps3"][tc]
                pt = E_PS[pk]
                for dc in range(NCH):
                    w = [pk] if (dc == 0 or dc == NCH - 1) else []
                    S.op("pe", (lambda e, tc=tc, dc=dc, pt=pt, wi=wi: e.matmul(
                        pt[:, :], qsl[wi][:, dc * 128:(dc + 1) * 128],
                        xb[dc][:, tc * 512:(tc + 1) * 512],
                        start=(dc == 0), stop=(dc == NCH - 1))),
                        reads=[qk, f"xb{dc}"], writes=w, tag=f"h{wi}")
            if wi == 0:
                for tc in range(NT):
                    pk = ["ps0", "ps1"][tc]
                    S.op("act", (lambda e, tc=tc, pt=E_PS[pk]: e.activation(
                        h1t[:, tc * 512:(tc + 1) * 512], pt[:, :], AF.Silu)),
                        reads=[pk], writes=["h1t"], tag="silu")
        for tc in range(NT):
            pk = ["ps2", "ps3"][tc]
            S.op("dve", (lambda e, tc=tc, f=f, pt=E_PS[pk]: e.tensor_tensor(
                gt[f][:, tc * 512:(tc + 1) * 512],
                h1t[:, tc * 512:(tc + 1) * 512], pt[:, :], ALU.mult)),
                reads=["h1t", pk], writes=[f"gt{f}"], tag="gmul")

    W2Q = ((0, 11), (11, 11), (22, 11), (33, 10))
    for o in range(NCH):
        for hh, (c0, ncc) in enumerate(W2Q):
            wk = f"w2f{hh % 2}"
            qk = f"q2h{hh}"
            S.op("sp", (lambda e, o=o, buf=w2f32[hh % 2], c0=c0, ncc=ncc: e.dma_start(
                buf[:, 0:ncc * 128],
                bass.AP(w2s, o * HIDP * 128 + c0 * 128 * 128,
                        [[128, 128], [128 * 128, ncc], [1, 128]]))),
                writes=[wk], dma=True)
            quant_slab(w2f32[hh % 2], wk, q2sl[0], qk, 2, ncc * 128, off=c0 * 128)
        S.op("sp", (lambda e, o=o: e.dma_start(
            x2b[0][:, :], x1d[o * 128:(o + 1) * 128, :])),
            writes=["x2b0"], dma=True)
        for tc in range(NT):
            pk = ["ps0", "ps1"][tc]
            pt = E_PS[pk]
            for hc in range(NHT):
                w = [pk] if (hc == 0 or hc == NHT - 1) else []
                qkh = f"q2h{min(hc // 11, 3)}"
                S.op("pe", (lambda e, o=o, tc=tc, hc=hc, pt=pt: e.matmul(
                    pt[:, :], q2sl[0][:, hc * 128:(hc + 1) * 128],
                    gt[hc][:, tc * 512:(tc + 1) * 512],
                    start=(hc == 0), stop=(hc == NHT - 1))),
                    reads=[qkh, f"gt{hc}"], writes=w, tag=f"w2_{o}")
            S.op("dve", (lambda e, o=o, tc=tc, pt=pt: e.tensor_tensor(
                x2b[0][:, tc * 512:(tc + 1) * 512], pt[:, :],
                x2b[0][:, tc * 512:(tc + 1) * 512], ALU.add)),
                reads=[pk, "x2b0"], writes=["x2b0"], tag="res")
        S.op("sp", (lambda e, o=o: e.dma_start(
            x2[o * 128:(o + 1) * 128, :], x2b[0][:, :])),
            reads=["x2b0"], writes=[f"x2st{o % 2}"], dma=True)

    S.op("sp", lambda e: e.nop(), reads=["x2st0", "x2st1"])


def emit_phase_c(nc, S, ctx, D):
    drfg = D["drfg"]
    bcg = D["bcg"]
    e2xg = D["e2xg"]
    e3b = D["e3b"]
    x2d = D["x2i"]
    wxT = D["wxT"]
    wdtT = D["wdtT"]
    alogh = D["alogh"]
    bdth = D["bdth"]
    dfull = D["dfull"]
    onesd = D["onesd"]
    hoffd = D["hoffd"]
    x3 = D["x3i"]
    e2in = D["e2in"]
    e2out = D["e2out"]
    e2inB = D["e2inB"]
    e2outB = D["e2outB"]
    e3in = D["e3in"]
    e3out = D["e3out"]
    e3inB = D["e3inB"]
    e3outB = D["e3outB"]
    lnb = D["lnb"]

    xb = [sbt(ctx, nc, f"xb{i}", [128, TL], BF16) for i in range(NCH)]
    dlt = [sbt(ctx, nc, f"dlt{i}", [128, T], BF16) for i in range(8)]
    wxt = sbt(ctx, nc, "wxt", [128, NCH * 160], BF16)
    wdt = sbt(ctx, nc, "wdt", [128, 1024], BF16)
    at = sbt(ctx, nc, "at", [128, 128], F32)
    bcol = sbt(ctx, nc, "bcol", [128, 8], F32)
    dcol = sbt(ctx, nc, "dcol", [128, 16], F32)
    hoff = sbt(ctx, nc, "hoff", [1, 4], mybir.dt.int32)
    drf = sbt(ctx, nc, "drf", [128, T], BF16)
    bcexp = sbt(ctx, nc, "bcexp", [32, TL], BF16)
    dAs2 = [sbt(ctx, nc, f"dAs{i}", [128, NSSM * TCH], BF16) for i in range(2)]
    sqt = dAs2[0]
    zt2 = dAs2[0]
    bus = sbt(ctx, nc, "bus", [128, NSSM * TCH], BF16)
    brep = sbt(ctx, nc, "brep_c", [128, NSSM * TCH], BF16)
    crep = sbt(ctx, nc, "crep_c", [128, NSSM * TCH], BF16)
    xmall = sbt(ctx, nc, "xmall", [128, 8 * TCH], BF16)
    dxt2 = [sbt(ctx, nc, f"dxt{i}", [128, TCH], BF16) for i in range(2)]
    stateb = [sbt(ctx, nc, f"state{i}", [128, NSSM], BF16) for i in range(8)]
    c0t = sbt(ctx, nc, "c0t", [128, NSSM], BF16)
    ych2 = [sbt(ctx, nc, f"ych{i}", [128, TCH], F32) for i in range(2)]
    yba = sbt(ctx, nc, "yba", [128, 4 * TL], BF16)
    zt = sbt(ctx, nc, "zt", [128, 512], F32)
    onesbf = sbt(ctx, nc, "onesbf", [128, 1], BF16)
    arep = sbt(ctx, nc, "arep", [128, TL], F32)
    brepl = sbt(ctx, nc, "brepl", [128, TL], F32)
    x3b = [sbt(ctx, nc, f"x3b{i}", [128, 512], F32) for i in range(2)]
    rows = {k: sbt(ctx, nc, k, [1, TL], F32) for k in ("srow", "qrow", "brow")}

    ps0 = pst(ctx, nc, "ps0", [128, 512])
    ps1 = pst(ctx, nc, "ps1", [128, 512])
    ps2 = pst(ctx, nc, "ps2", [128, 512])
    ps3 = pst(ctx, nc, "ps3", [128, 512])
    pr0 = pst(ctx, nc, "pr0", [128, 512])
    pr1 = pst(ctx, nc, "pr1", [128, 512])
    E = {"ones_bf": onesbf, "sqt": sqt, "sqt_key": "dAs0", "rows": rows,
         "lnb": lnb, "arep": arep, "brep": brepl,
         "ps0": ps0, "ps1": ps1, "pr0": pr0, "pr1": pr1}
    PSL = [ps0, ps1, ps2, ps3]

    _rctx = D.get("_octx", ctx)
    rgA = _rctx.enter_context(nc.gpsimd.register(_uniq("rgA")))
    rgB = _rctx.enter_context(nc.gpsimd.register(_uniq("rgB")))
    rgT = _rctx.enter_context(nc.gpsimd.register(_uniq("rgT")))


    S.op("sp", lambda e: e.dma_start(onesbf[:, :], onesd[:, :]), writes=["ones"], dma=True)
    S.op("sp", lambda e: e.dma_start(hoff[:, :], hoffd[:, :]), writes=["hoff"], dma=True)
    S.op("pool", lambda e: e.dma_start(
        wxt[:, :], bass.AP(wxT, 0, [[160, 128], [128 * 160, NCH], [1, 160]])),
        writes=["wxt"], dma=True)
    S.op("pool", lambda e: e.dma_start(wdt[:, :], wdtT[:, :]), writes=["wdt"], dma=True)
    S.op("sp", lambda e: e.dma_start(at[:, :], alogh[:, :]), writes=["at"], dma=True)
    S.op("sp", lambda e: e.dma_start(bcol[:, :], bdth[:, :]), writes=["bcol"], dma=True)
    S.op("sp", lambda e: e.dma_start(dcol[:, :], dfull[:, :]), writes=["dcol"], dma=True)
    for dc in range(NCH):
        S.op("pool", (lambda e, dc=dc: e.dma_start(
            xb[dc][:, :], x2d[dc * 128:(dc + 1) * 128, :])),
            writes=[f"xb{dc}"], dma=True)
    S.op("act", lambda e: e.activation(at[:, :], at[:, :], AF.Exp),
         reads=["at"], writes=["at"])
    S.op("dve", lambda e: e.tensor_scalar_mul(at[:, :], at[:, :], -1.0),
         reads=["at"], writes=["at"])
    S.op("pool", lambda e: (e.reg_load(rgA, hoff[0:1, 0:1]),
                            e.reg_load(rgB, hoff[0:1, 1:2]),
                            e.nop())[-1],
         reads=["hoff"], writes=["regs"], tag="regload")

    xkeys = [f"xb{i}" for i in range(NCH)]
    emit_layernorm(S, E, xkeys, xb, 0)

    for tc in range(NT):
        for grp, (m0, mw, pk_) in enumerate([(0, 128, ("ps0", "ps1")),
                                             (128, 32, ("ps2", "ps3"))]):
            pk = pk_[tc]
            pt = E[pk] if pk in E else PSL[int(pk[2])]
            for dc in range(NCH):
                w = [pk] if (dc == 0 or dc == NCH - 1) else []
                S.op("pe", (lambda e, tc=tc, dc=dc, m0=m0, mw=mw, pt=pt: e.matmul(
                    pt[0:mw, :], wxt[:, dc * 160 + m0:dc * 160 + m0 + mw],
                    xb[dc][:, tc * 512:(tc + 1) * 512],
                    start=(dc == 0), stop=(dc == NCH - 1))),
                    reads=["wxt", f"xb{dc}"], writes=w, tag=f"xz{grp}")
        S.op("dve", (lambda e, tc=tc, pt=PSL[tc]: e.tensor_copy(
            drf[:, tc * 512:(tc + 1) * 512], pt[:, :])),
            reads=[("ps0", "ps1")[tc]], writes=["drf"], tag="drev")
        S.op("dve", (lambda e, tc=tc, pt=PSL[2 + tc]: e.tensor_copy(
            bcexp[0:32, tc * 512:(tc + 1) * 512], pt[0:32, :])),
            reads=[("ps2", "ps3")[tc]], writes=["bcexp"], tag="bcev")

    S.op("sp", lambda e: e.dma_start(e2in[0:128, :], drf[:, 0:TL]),
         reads=["drf"], writes=["e2_dr"], dma=True)
    S.op("sp", lambda e: e.dma_start(e2in[128:160, :], bcexp[0:32, :]),
         reads=["bcexp"], writes=["e2_bc"], dma=True)
    S.op("pool", (lambda e: e.collective_compute(
        "AllGather", ALU.bypass, replica_groups=PAIRS,
        ins=[e2in.ap().opt()], outs=[e2out.ap().opt()])),
        reads=["e2_dr", "e2_bc"], writes=["e2out"], tag="agE2")
    for dc in range(NCH):
        S.op("sp", (lambda e, dc=dc: e.dma_start(
            e2inB[dc * 128:(dc + 1) * 128, :], xb[dc][:, :])),
            reads=[f"xb{dc}"], writes=[f"e2_x{dc}"], dma=True)
    S.op("pool", (lambda e: e.collective_compute(
        "AllGather", ALU.bypass, replica_groups=PAIRS,
        ins=[e2inB.ap().opt()], outs=[e2outB.ap().opt()])),
        reads=[f"e2_x{dc}" for dc in range(NCH)],
        writes=["e2outB"], tag="agE2b")

    for r_ in range(2):
        S.op("sp", (lambda e, r_=r_: e.dma_start(
            bass.AP(drfg, (1 - r_) * 128, [[T, 128], [256, 8], [1, 128]]),
            bass.AP(e2out, r_ * 160 * TL + 7 * 128,
                    [[TL, 128], [-128, 8], [1, 128]]))),
            reads=["e2out"], writes=["drfg"], dma=True)
        S.op("sp", (lambda e, r_=r_: e.dma_start(
            bass.AP(bcg, (1 - r_) * 128, [[T, 32], [256, 8], [1, 128]]),
            bass.AP(e2out, (r_ * 160 + 128) * TL + 7 * 128,
                    [[TL, 32], [-128, 8], [1, 128]]))),
            reads=["e2out"], writes=["bcg"], dma=True)
    S.op("sp", lambda e: e.dma_start(drf[:, :], drfg[:, :]),
         reads=["drfg"], writes=["drf"], dma=True)
    for r_ in range(2):
        for rc in range(16):
            S.op("sp" if rc % 2 == 0 else "pool",
                 (lambda e, r_=r_, rc=rc: e.dma_start(
                     bass.AP(e2xg, rc * 128 * T + (1 - r_) * 128,
                             [[T, 128], [256, 8], [1, 128]]),
                     bass.AP(e2outB, (r_ * 2048 + rc * 128) * TL + 7 * 128,
                             [[TL, 128], [-128, 8], [1, 128]]))),
                 reads=["e2outB"], writes=["e2xg"], dma=True, tag="e2xg")
    for sl in range(8):
        for ch in range(NCHK):
            pk = ["ps0", "ps1"][ch % 2]
            pt = PSL[ch % 2]
            S.op("pe", (lambda e, sl=sl, ch=ch, pt=pt: e.matmul(
                pt[:, :], wdt[:, sl * 128:(sl + 1) * 128],
                drf[:, ch * 512:(ch + 1) * 512], start=True, stop=True)),
                reads=["wdt", "drf"], writes=[pk], tag="dmm")
            S.op("dve", (lambda e, sl=sl, pt=pt: e.tensor_scalar_add(
                zt[:, :], pt[:, :], bcol[:, sl:sl + 1])),
                reads=[pk, "bcol"], writes=["zt"], tag="spz")
            S.op("act", lambda e: e.activation(zt2[:, 0:512], zt[:, :], AF.Abs),
                 reads=["zt"], writes=["dAs0"])
            S.op("act", lambda e: e.activation(zt2[:, 0:512], zt2[:, 0:512], AF.Exp, scale=-1.0),
                 reads=["dAs0"], writes=["dAs0"])
            S.op("act", lambda e: e.activation(zt2[:, 0:512], zt2[:, 0:512], AF.Ln, bias=1.0),
                 reads=["dAs0"], writes=["dAs0"])
            S.op("dve", lambda e: e.tensor_scalar_max(zt[:, :], zt[:, :], 0.0),
                 reads=["zt"], writes=["zt"])
            S.op("dve", (lambda e, sl=sl, ch=ch: e.tensor_tensor(
                dlt[sl][:, ch * 512:(ch + 1) * 512], zt[:, :], zt2[:, 0:512], ALU.add)),
                reads=["zt", "dAs0"], writes=[f"dlt{sl}"], tag="spadd")

    for dt in range(8):
        S.op("dve", (lambda e, dt=dt: e.memset(stateb[dt][:, :], 0.0)),
             writes=[f"state{dt}"], tag="st0")
    for ch in range(NCHK):
        S.op("sp", (lambda e, ch=ch: e.dma_start(
            brep[:, :], bass.AP(bcg, ch * TCH,
                                [[0, 128], [T, NSSM], [1, TCH]]))),
            reads=["bcg"], writes=["brep_c"], dma=True)
        S.op("sp", (lambda e, ch=ch: e.dma_start(
            crep[:, :], bass.AP(bcg, 16 * T + ch * TCH,
                                [[0, 128], [T, NSSM], [1, TCH]]))),
            reads=["bcg"], writes=["crep_c"], dma=True)
        base = ch * TCH
        S.op("pool", (lambda e, base=base: (
            e.reg_mov(rgT, base),
            e.reg_add(rgT, rgT, rgA),
            e.dma_start(xmall[:, :],
                        bass.AP(e2xg, rgT,
                                [[T, 128], [128 * T, 8], [1, TCH]])))[-1]),
            reads=["e2xg", "regs"], writes=["xmall"], dma=True, tag="xm")
        for dt in range(8):
            par = (ch * 8 + dt) % 2
            dAs, dak = dAs2[par], f"dAs{par}"
            dxt, dxk = dxt2[par], f"dxt{par}"
            ych, yck = ych2[par], f"ych{par}"
            S.op("pool", (lambda e, dt=dt, ch=ch, dxt=dxt: e.tensor_tensor(
                dxt[:, :], dlt[dt][:, ch * 512:(ch + 1) * 512],
                xmall[:, dt * TCH:(dt + 1) * TCH], ALU.mult)),
                reads=[f"dlt{dt}", "xmall"], writes=[dxk], tag="dx")
            for n in range(NSSM):
                S.op("act", (lambda e, dt=dt, ch=ch, n=n, dAs=dAs: e.activation(
                    dAs[:, n * TCH:(n + 1) * TCH],
                    dlt[dt][:, ch * 512:(ch + 1) * 512],
                    AF.Exp, scale=at[:, dt * 16 + n:dt * 16 + n + 1])),
                    reads=[f"dlt{dt}", "at"], writes=[dak], tag="dA")
            S.op("dve", (lambda e, dxt=dxt: e.tensor_tensor(
                bus[:, :], bass.AP(dxt, 0, [[TCH, 128], [0, NSSM], [1, TCH]]),
                brep[:, :], ALU.mult)),
                reads=[dxk, "brep_c"], writes=["bus"], tag="bu")
            S.op("dve", (lambda e, dt=dt, dAs=dAs: e.tensor_tensor(
                c0t[:, :], bass.AP(dAs, 0, [[NSSM * TCH, 128], [TCH, NSSM]]),
                stateb[dt][:, :], ALU.mult)),
                reads=[dak, f"state{dt}"], writes=["c0t"], tag="fold")
            S.op("dve", (lambda e: e.tensor_tensor(
                bass.AP(bus, 0, [[NSSM * TCH, 128], [TCH, NSSM]]),
                bass.AP(bus, 0, [[NSSM * TCH, 128], [TCH, NSSM]]),
                c0t[:, :], ALU.add)),
                reads=["bus", "c0t"], writes=["bus"], tag="fold")
            S.op("dve", (lambda e, dAs=dAs: e.memset(
                bass.AP(dAs, 0, [[NSSM * TCH, 128], [TCH, NSSM]]), 0.0)),
                reads=[dak], writes=[dak], tag="fold")
            S.op("dve", (lambda e, dAs=dAs: e.tensor_tensor_scan(
                bus[:, :], dAs[:, :], bus[:, :], 0.0, ALU.mult, ALU.add)),
                reads=[dak, "bus"], writes=["bus"], tag="scan")
            S.op("dve", (lambda e, dt=dt: e.tensor_copy(
                stateb[dt][:, :], bass.AP(bus, TCH - 1, [[NSSM * TCH, 128], [TCH, NSSM]]))),
                reads=["bus"], writes=[f"state{dt}"], tag="tail")
            S.op("dve", lambda e: e.tensor_tensor(bus[:, :], bus[:, :], crep[:, :], ALU.mult),
                 reads=["bus", "crep_c"], writes=["bus"], tag="hc")
            for lvl in range(3):
                w_ = NSSM * TCH >> (lvl + 1)
                S.op("dve", (lambda e, w_=w_: e.tensor_tensor(
                    bus[:, 0:w_], bus[:, 0:w_], bus[:, w_:2 * w_], ALU.add)),
                    reads=["bus"], writes=["bus"], tag="yt")
            S.op("dve", (lambda e, ych=ych: e.tensor_tensor(
                ych[:, :], bus[:, 0:TCH], bus[:, TCH:2 * TCH], ALU.add)),
                reads=["bus"], writes=[yck], tag="yt4")
            edst = e3in if dt < 4 else e3inB
            S.op("pool", (lambda e, dt=dt, ch=ch, ych=ych, edst=edst: e.dma_start(
                edst[(dt % 4) * 128:(dt % 4 + 1) * 128,
                     ch * TCH:(ch + 1) * TCH], ych[:, :])),
                reads=[yck], writes=[f"e3x{ch}{dt}"], dma=True)

    S.op("pool", (lambda e: e.collective_compute(
        "AllGather", ALU.bypass, replica_groups=PAIRS,
        ins=[e3in.ap().opt()], outs=[e3out.ap().opt()])),
        reads=[f"e3x{ch}{dt}" for ch in range(NCHK) for dt in range(4)],
        writes=["e3out"], tag="agE3")
    S.op("pool", (lambda e: e.collective_compute(
        "AllGather", ALU.bypass, replica_groups=PAIRS,
        ins=[e3inB.ap().opt()], outs=[e3outB.ap().opt()])),
        reads=[f"e3x{ch}{dt}" for ch in range(NCHK) for dt in range(4, 8)],
        writes=["e3outB"], tag="agE3b")

    for rc in (0, 1, 2, 3, 8, 9, 10, 11, 4, 5, 6, 7, 12, 13, 14, 15):
        if rc < 4:
            esrc, ek, srow = e3out, "e3out", rc * 128
        elif 8 <= rc < 12:
            esrc, ek, srow = e3out, "e3out", 512 + (rc - 8) * 128
        elif 4 <= rc < 8:
            esrc, ek, srow = e3outB, "e3outB", (rc - 4) * 128
        else:
            esrc, ek, srow = e3outB, "e3outB", 512 + (rc - 12) * 128
        for p_ in range(2):
            S.op("sp" if rc % 2 == 0 else "pool",
                 (lambda e, p_=p_, rc=rc, esrc=esrc, srow=srow: e.dma_start(
                     bass.AP(e3b, rc * 128 * T + p_ * TL,
                             [[T, 128], [1, TL]]),
                     bass.AP(esrc, srow * T + (15 - p_) * 128,
                             [[T, 128], [-256, 8], [1, 128]]))),
                 reads=[ek], writes=[f"e3bg{rc // 4}"], dma=True, tag="e3b")
    for g4 in (0, 2, 1, 3):
        base = g4 * 4 * 128 * T
        S.op("pool", (lambda e, base=base: (
            e.reg_mov(rgT, base),
            e.reg_add(rgT, rgT, rgB),
            e.dma_start(yba[:, :],
                        bass.AP(e3b, rgT,
                                [[T, 128], [128 * T, 4], [1, TL]])))[-1]),
            reads=[f"e3bg{g4}", "regs"], writes=["yba"], dma=True, tag="yimp")
        for oo in range(4):
            o = g4 * 4 + oo
            for hh in range(2):
                cs0, cs1 = hh * 512, (hh + 1) * 512
                S.op("sp", (lambda e, o=o, cs0=cs0, cs1=cs1, hh=hh: e.dma_start(
                    x3b[hh][:, :], x2d[o * 128:(o + 1) * 128, cs0:cs1])),
                    writes=[f"x3b{hh}"], dma=True)
                S.op("dve", (lambda e, oo=oo, cs0=cs0, hh=hh: e.tensor_tensor(
                    x3b[hh][:, :], x3b[hh][:, :],
                    yba[:, oo * TL + cs0:oo * TL + cs0 + 512], ALU.add)),
                    reads=[f"x3b{hh}", "yba"], writes=[f"x3b{hh}"], tag="resy")
                S.op("dve", (lambda e, o=o, cs0=cs0, cs1=cs1, hh=hh: e.scalar_tensor_tensor(
                    x3b[hh][:, :], xb[o][:, cs0:cs1], dcol[:, o:o + 1],
                    x3b[hh][:, :], ALU.mult, ALU.add)),
                    reads=[f"xb{o}", "dcol", f"x3b{hh}"], writes=[f"x3b{hh}"], tag="resd")
                S.op("sp", (lambda e, o=o, cs0=cs0, cs1=cs1, hh=hh: e.dma_start(
                    x3[o * 128:(o + 1) * 128, cs0:cs1], x3b[hh][:, :])),
                    reads=[f"x3b{hh}"], writes=[f"x3st{hh}"], dma=True)

    S.op("sp", lambda e: e.nop(), reads=["x3st0", "x3st1"])


def emit_phase_d(nc, S, ctx, D):
    x3d = D["x3i"]
    kvdw = D["kvdw"]
    qdw = D["qdw"]
    kuw = D["kuw"]
    quw = D["quw"]
    vuw = D["vuw"]
    rkw = D["rkw"]
    rqw = D["rqw"]
    oww = D["oww"]
    cos1 = D["cos1"]
    sin1 = D["sin1"]
    l1wd = D["l1wd"]
    mask0 = D["mask0"]
    onesd = D["onesd"]
    identd = D["identd"]
    hd = D["hi"]
    k_in = D["k_in"]
    k_out = D["k_out"]
    v_in = D["v_in"]
    v_out = D["v_out"]
    lnb = D["lnb"]

    xb = [sbt(ctx, nc, f"xb{i}", [128, TL], BF16) for i in range(NCH)]
    qasm = [sbt(ctx, nc, f"qasm{i}", [128, TL], BF16) for i in range(NH)]
    ot = [sbt(ctx, nc, f"ot{i}", [128, TL], BF16) for i in range(NH)]
    kvd = [sbt(ctx, nc, f"kvd{i}", [128, TL], BF16) for i in range(NLAT)]
    qd = [sbt(ctx, nc, f"qd{i}", [128, TL], BF16) for i in range(NLAT)]
    kasm = [sbt(ctx, nc, f"kasm{i}", [128, TL], BF16) for i in range(2)]
    rsc = [sbt(ctx, nc, f"rsc{i}", [128, TL], BF16) for i in range(2)]
    shuf = sbt(ctx, nc, "shuf", [128, TL], BF16)
    sqt = shuf
    wslab = [sbt(ctx, nc, f"wslab{i}", [128, C], BF16) for i in range(2)]
    cosT = sbt(ctx, nc, "cosT", [128, TL], BF16)
    sinT = sbt(ctx, nc, "sinT", [128, TL], BF16)
    maskT = sbt(ctx, nc, "maskT", [128, MASK_COLS], BF16)
    ones_bf = sbt(ctx, nc, "ones_bf", [128, 1], BF16)
    l1w = sbt(ctx, nc, "l1w", [128, 16], F32)
    khead = [sbt(ctx, nc, f"khead{i}", [128, T], BF16) for i in range(2)]
    vhead = [sbt(ctx, nc, f"vhead{i}", [128, T], BF16) for i in range(2)]
    ptile2 = [sbt(ctx, nc, f"ptile{i}", [128, TL], BF16) for i in range(2)]
    rrep = sbt(ctx, nc, "rrep", [128, TL], F32)
    arep = sbt(ctx, nc, "arep", [128, TL], F32)
    brep = sbt(ctx, nc, "brep", [128, TL], F32)
    rmsrep = rrep
    ident = sbt(ctx, nc, "ident", [128, 128], BF16)
    hb = [sbt(ctx, nc, f"hb{i}", [128, TL], F32) for i in range(2)]
    rows = {k: sbt(ctx, nc, k, [1, TL], F32) for k in ("srow", "qrow", "brow")}

    ps0 = pst(ctx, nc, "ps0", [128, 512])
    ps1 = pst(ctx, nc, "ps1", [128, 512])
    ps2 = pst(ctx, nc, "ps2", [128, 512])
    ps3 = pst(ctx, nc, "ps3", [128, 512])
    pr0 = pst(ctx, nc, "pr0", [128, 512])
    pr1 = pst(ctx, nc, "pr1", [128, 512])
    PS = {0: ps0, 1: ps1}
    E_PS = {"ps0": ps0, "ps1": ps1, "ps2": ps2, "ps3": ps3}
    E = {"ones_bf": ones_bf, "sqt": sqt, "sqt_key": "shuf", "rows": rows,
         "lnb": lnb, "arep": arep, "brep": brep,
         "ps0": ps0, "ps1": ps1, "pr0": pr0, "pr1": pr1}

    for dc in range(NCH):
        S.op("pool", (lambda e, dc=dc: e.dma_start(
            xb[dc][:, :], x3d[dc * 128:(dc + 1) * 128, :])),
            writes=[f"xb{dc}"], dma=True)
    S.op("sp", lambda e: e.dma_start(cosT[:, :], cos1[:, :]), writes=["cosT"], dma=True)
    S.op("sp", lambda e: e.dma_start(sinT[:, :], sin1[:, :]), writes=["sinT"], dma=True)
    S.op("sp", lambda e: e.dma_start(maskT[:, :], mask0[:, :]), writes=["maskT"], dma=True)
    S.op("sp", lambda e: e.dma_start(ones_bf[:, :], onesd[:, :]), writes=["ones"], dma=True)
    S.op("sp", lambda e: e.dma_start(l1w[:, :], l1wd[:, :]), writes=["l1w"], dma=True)
    S.op("sp", lambda e: e.dma_start(ident[:, :], identd[:, :]), writes=["ident"], dma=True)

    xkeys = [f"xb{i}" for i in range(NCH)]
    emit_layernorm(S, E, xkeys, xb, 0)
    emit_layernorm(S, E, xkeys, xb, 2, rms_only=True, apply=False)
    S.op("sp", (lambda e: e.dma_start(
        rmsrep[:, :], bass.AP(lnb, 2 * TL, [[0, 128], [1, TL]]))),
        reads=["lnb2"], writes=["rrep"], dma=True)
    for dc in range(NCH):
        S.op("dve", (lambda e, dc=dc: e.scalar_tensor_tensor(
            xb[dc][:, :], xb[dc][:, :], l1w[:, dc:dc + 1], rmsrep[:, :],
            ALU.mult, ALU.mult)),
            reads=[f"xb{dc}", "l1w", "rrep"], writes=[f"xb{dc}"], tag="rmsap")

    def slab_load(dram, f, kchunks, wk):
        S.op("pool", (lambda e, f=f, kchunks=kchunks: e.dma_start(
            wslab[f % 2][:, 0:kchunks * 128],
            bass.AP(dram, f * kchunks * 128 * 128,
                    [[128, 128], [128 * 128, kchunks], [1, 128]]))),
            writes=[wk], dma=True)

    def mm_ft(src_tiles, src_keys, kchunks, f, pk, extra=None):
        wk = f"wslab{f % 2}"
        for tc in range(NT):
            pkt = ["ps2", "ps3"][tc] if pk == "auto" else pk
            ptt = E_PS[pkt]
            for dc in range(kchunks):
                w = [pkt] if (dc == 0 or dc == kchunks - 1) else []
                S.op("pe", (lambda e, tc=tc, dc=dc, ptt=ptt, f=f: e.matmul(
                    ptt[:, :], wslab[f % 2][:, dc * 128:(dc + 1) * 128],
                    src_tiles[dc][:, tc * 512:(tc + 1) * 512],
                    start=(dc == 0), stop=(dc == kchunks - 1))),
                    reads=[wk, src_keys[dc]], writes=w, tag="mmft")
            if extra:
                extra(tc, pkt, E_PS[pkt])

    kvdk = [f"kvd{i}" for i in range(NLAT)]
    qdk = [f"qd{i}" for i in range(NLAT)]

    for f in range(NLAT):
        slab_load(kvdw, f, NCH, f"wslab{f % 2}")
        def ev(tc, pk, pt, f=f):
            S.op("dve", (lambda e, tc=tc, pt=pt, f=f: e.tensor_copy(
                kvd[f][:, tc * 512:(tc + 1) * 512], pt[:, :])),
                reads=[pk], writes=[kvdk[f]], tag="kvdev")
        mm_ft(xb, xkeys, NCH, f, "auto", ev)
    for f in range(NLAT):
        slab_load(qdw, f, NCH, f"wslab{f % 2}")
        def ev(tc, pk, pt, f=f):
            S.op("dve", (lambda e, tc=tc, pt=pt, f=f: e.tensor_copy(
                qd[f][:, tc * 512:(tc + 1) * 512], pt[:, :])),
                reads=[pk], writes=[qdk[f]], tag="qdev")
        mm_ft(qd if False else xb, xkeys, NCH, f, "auto", ev)

    mask32 = [i ^ 1 for i in range(32)]

    def rope_tile(tile, key):
        S.op("dve", (lambda e, tile=tile: e.stream_shuffle(shuf[:, :], tile[:, :], mask32)),
             reads=[key], writes=["shuf"])
        S.op("dve", (lambda e, tile=tile: e.tensor_tensor(tile[:, :], tile[:, :], cosT[:, :], ALU.mult)),
             reads=[key, "cosT"], writes=[key])
        S.op("dve", (lambda e: e.tensor_tensor(shuf[:, :], shuf[:, :], sinT[:, :], ALU.mult)),
             reads=["shuf", "sinT"], writes=["shuf"])
        S.op("dve", (lambda e, tile=tile: e.tensor_tensor(tile[:, :], tile[:, :], shuf[:, :], ALU.add)),
             reads=[key, "shuf"], writes=[key])

    for f in range(8):
        slab_load(quw, f, NLAT, f"wslab{f % 2}")
        def evqp(tc, pk, pt, f=f):
            for half in range(2):
                h = 2 * f + half
                S.op("dve", (lambda e, tc=tc, pt=pt, half=half, h=h: e.tensor_copy(
                    qasm[h][0:64, tc * 512:(tc + 1) * 512],
                    pt[64 * half:64 * half + 64, :])),
                    reads=[pk], writes=[f"qasm{h}"], tag="qpev")
        mm_ft(qd, qdk, NLAT, f, "auto", evqp)
        slab_load(rqw, f, NLAT, f"wslab{f % 2}")
        rk = f"rsc{f % 2}"
        def evqr(tc, pk, pt, f=f, rk=rk):
            S.op("dve", (lambda e, tc=tc, pt=pt, f=f: e.tensor_copy(
                rsc[f % 2][:, tc * 512:(tc + 1) * 512], pt[:, :])),
                reads=[pk], writes=[rk], tag="qrev")
        mm_ft(qd, qdk, NLAT, f, "auto", evqr)
        rope_tile(rsc[f % 2], rk)
        for half in range(2):
            h = 2 * f + half
            S.op("dve", (lambda e, half=half, h=h, f=f: e.tensor_copy(
                qasm[h][64:128, :], rsc[f % 2][64 * half:64 * half + 64, :])),
                reads=[rk], writes=[f"qasm{h}"], tag="qrasm")

    for f in range(8):
        ak0, ak1 = "kasm0x", "kasm1x"
        slab_load(kuw, f, NLAT, f"wslab{f % 2}")
        def evkp(tc, pk, pt, f=f):
            for half in range(2):
                S.op("dve", (lambda e, tc=tc, pt=pt, half=half: e.tensor_copy(
                    kasm[half][0:64, tc * 512:(tc + 1) * 512],
                    pt[64 * half:64 * half + 64, :])),
                    reads=[pk], writes=[("kasm0x", "kasm1x")[half]], tag="kpev")
        mm_ft(kvd, kvdk, NLAT, f, "auto", evkp)
        slab_load(rkw, f, NCH, f"wslab{f % 2}")
        rk = f"rsc{f % 2}"
        def evkr(tc, pk, pt, f=f, rk=rk):
            S.op("dve", (lambda e, tc=tc, pt=pt, f=f: e.tensor_copy(
                rsc[f % 2][:, tc * 512:(tc + 1) * 512], pt[:, :])),
                reads=[pk], writes=[rk], tag="krev")
        mm_ft(xb, xkeys, NCH, f, "auto", evkr)
        rope_tile(rsc[f % 2], rk)
        for half in range(2):
            S.op("dve", (lambda e, half=half, f=f: e.tensor_copy(
                kasm[half][64:128, :], rsc[f % 2][64 * half:64 * half + 64, :])),
                reads=[rk], writes=[("kasm0x", "kasm1x")[half]], tag="krasm")
        for half in range(2):
            h = 2 * f + half
            S.op("sp", (lambda e, h=h, half=half: e.dma_start(
                k_in[h * 128:(h + 1) * 128, :], kasm[half][:, :])),
                reads=[("kasm0x", "kasm1x")[half]], writes=[f"k_in{h}"], dma=True)

    S.op("pool", (lambda e: e.collective_compute(
        "AllGather", ALU.bypass, replica_groups=PAIRS,
        ins=[k_in.ap().opt()], outs=[k_out.ap().opt()])),
        reads=[f"k_in{h}" for h in range(NH)], writes=["k_out"], tag="agK")

    vexp = sbt(ctx, nc, "vexpT", [128, TL], BF16)
    psb = pst(ctx, nc, "psb", [128, 1024], BF16)
    for f in range(NH):
        wk = f"wslab{f % 2}"
        S.op("pool", (lambda e, f=f: e.dma_start(
            wslab[f % 2][:, 0:LAT],
            bass.AP(vuw, f * 128, [[C, 128], [128 * C, NLAT] if False else [128 * C, NLAT], [1, 128]])
            if False else
            bass.AP(vuw, f * 128, [[C, 128], [128 * C, NLAT], [1, 128]]))),
            writes=[wk], dma=True)
        for tc in range(NT):
            pk = ["ps2", "ps3"][tc]
            pt = E_PS[pk]
            for dc in range(NLAT):
                w = [pk] if (dc == 0 or dc == NLAT - 1) else []
                S.op("pe", (lambda e, tc=tc, dc=dc, pt=pt, f=f: e.matmul(
                    pt[:, :], wslab[f % 2][:, dc * 128:(dc + 1) * 128],
                    kvd[dc][:, tc * 512:(tc + 1) * 512],
                    start=(dc == 0), stop=(dc == NLAT - 1))),
                    reads=[wk, kvdk[dc]], writes=w, tag=f"vf{f}")
            S.op("dve", (lambda e, tc=tc, pt=pt: e.tensor_copy(
                rsc[0][:, tc * 512:(tc + 1) * 512], pt[:, :])),
                reads=[pk], writes=["rsc0"], tag="vevf")
        for tt in range(8):
            S.op("pe", (lambda e, tt=tt: e.transpose(
                psb[:, tt * 128:(tt + 1) * 128],
                rsc[0][:, tt * 128:(tt + 1) * 128], ident[:, :])),
                reads=["rsc0", "ident"], writes=["psb"], tag="vT")
        S.op("dve", lambda e: e.tensor_copy(vexp[:, :], psb[:, :]),
             reads=["psb"], writes=["vexpT"])
        S.op("sp", (lambda e, f=f: e.dma_start(
            bass.AP(v_in, f * 128, [[C, 128], [128 * C, 8], [1, 128]]),
            vexp[:, :])),
            reads=["vexpT"], writes=[f"v_inx{f}"], dma=True)

    vinkeys = [f"v_inx{f}" for f in range(NH)]
    S.op("pool", (lambda e: e.collective_compute(
        "AllGather", ALU.bypass, replica_groups=PAIRS,
        ins=[v_in.ap().opt()], outs=[v_out.ap().opt()])),
        reads=vinkeys, writes=["v_out"], tag="agV")

    AV_LAST = {0: 15, 1: 7}
    for h in range(NH):
        kb, kk = khead[h % 2], f"khead{h % 2}"
        vb, vk = vhead[h % 2], f"vhead{h % 2}"
        for r_ in range(2):
            S.op("sp", (lambda e, h=h, kb=kb, r_=r_: e.dma_start(
                bass.AP(kb, (1 - r_) * 128, [[T, 128], [256, 8], [1, 128]]),
                bass.AP(k_out, r_ * C * TL + h * 128 * TL + 7 * 128,
                        [[TL, 128], [-128, 8], [1, 128]]))),
                reads=["k_out"], writes=[kk], dma=True)
            S.op("sp", (lambda e, h=h, vb=vb, r_=r_: e.dma_start(
                bass.AP(vb, (1 - r_) * 128, [[T, 128], [256, 8], [1, 128]]),
                bass.AP(v_out, r_ * TL * C + 7 * 128 * C + h * 128,
                        [[C, 128], [-128 * C, 8], [1, 128]]))),
                reads=["v_out", "k_out"], writes=[vk], dma=True)
        first = [True, True]
        for kt in range(NKT):
            wd = _wid(kt)
            msegs_kt = [(l_, off_) for (mk_, l_, off_) in MSEGS if mk_ == kt]
            chunks = []
            c0 = 0
            while c0 < wd:
                chunks.append((c0, min(c0 + 512, wd)))
                c0 = min(c0 + 512, wd)
            for ci, (c0, c1) in enumerate(chunks):
                spk = "ps2" if (kt + ci) % 2 == 0 else "ps3"
                sps = E_PS[spk]
                S.op("pe", (lambda e, kt=kt, c0=c0, c1=c1, sps=sps, kb=kb, h=h: e.matmul(
                    sps[:, 0:c1 - c0], kb[:, kt * 128:(kt + 1) * 128],
                    qasm[h][:, c0:c1], start=True, stop=True)),
                    reads=[kk, f"qasm{h}"], writes=[spk], tag=f"sc{h}")
                for (ml_, moff_) in msegs_kt:
                    a0 = ml_ * 128
                    if a0 >= c0 and a0 + 128 <= c1:
                        S.op("dve", (lambda e, a0=a0, c0=c0, sps=sps, moff_=moff_: e.tensor_tensor(
                            sps[:, a0 - c0:a0 - c0 + 128], sps[:, a0 - c0:a0 - c0 + 128],
                            maskT[:, moff_:moff_ + 128], ALU.add)),
                            reads=[spk, "maskT"], writes=[spk], tag="msk")
                S.op("act", (lambda e, c0=c0, c1=c1, sps=sps, kt=kt: e.activation(
                    ptile2[kt % 2][:, c0:c1], sps[:, 0:c1 - c0], AF.Exp, scale=ISQ)),
                    reads=[spk], writes=[f"ptile{kt % 2}"], tag="exp")
            b0 = 0
            while b0 < wd:
                blk_i = b0 // 512
                b1 = min((blk_i + 1) * 512, wd)
                opk = "ps0" if blk_i == 0 else "ps1"
                opt_ = PS[blk_i]
                rpk = "pr0" if blk_i == 0 else "pr1"
                rpt = E[rpk]
                wo_ = [opk] if (first[blk_i] or kt == AV_LAST[blk_i]) else []
                wr_ = [rpk] if (first[blk_i] or kt == AV_LAST[blk_i]) else []
                S.op("pe", (lambda e, kt=kt, b0=b0, b1=b1, blk_i=blk_i, vb=vb,
                            opt_=opt_, f=first[blk_i]: e.matmul(
                    opt_[:, b0 - blk_i * 512:b1 - blk_i * 512],
                    vb[:, kt * 128:(kt + 1) * 128], ptile2[kt % 2][:, b0:b1],
                    start=f, stop=(kt == AV_LAST[blk_i]))),
                    reads=[vk, f"ptile{kt % 2}"], writes=wo_, tag=f"av{h}")
                S.op("pe", (lambda e, kt=kt, b0=b0, b1=b1, blk_i=blk_i,
                            rpt=rpt, f=first[blk_i]: e.matmul(
                    rpt[0:1, b0 - blk_i * 512:b1 - blk_i * 512],
                    ones_bf[:, 0:1], ptile2[kt % 2][:, b0:b1],
                    start=f, stop=(kt == AV_LAST[blk_i]))),
                    reads=[f"ptile{kt % 2}", "ones"], writes=wr_, tag=f"rs{h}")
                first[blk_i] = False
                b0 = b1
        for blk_i in range(2):
            rpk = "pr0" if blk_i == 0 else "pr1"
            rpt = E[rpk]
            S.op("dve", (lambda e, blk_i=blk_i, rpt=rpt: e.tensor_copy(
                rows["qrow"][0:1, blk_i * 512:(blk_i + 1) * 512], rpt[0:1, :])),
                reads=[rpk], writes=["qrow"])
        S.op("dve", lambda e: e.reciprocal(rows["qrow"][:, :], rows["qrow"][:, :]),
             reads=["qrow"], writes=["qrow"])
        S.op("sp", (lambda e: e.dma_start(
            bass.AP(lnb, 3 * TL, [[TL, 1], [1, TL]]), rows["qrow"][:, :])),
            reads=["qrow"], writes=["lnb3"], dma=True)
        S.op("sp", (lambda e: e.dma_start(
            rrep[:, :], bass.AP(lnb, 3 * TL, [[0, 128], [1, TL]]))),
            reads=["lnb3"], writes=["rrep"], dma=True)
        for blk_i in range(2):
            opk = "ps0" if blk_i == 0 else "ps1"
            opt_ = PS[blk_i]
            if blk_i == 0:
                S.op("dve", (lambda e, blk_i=blk_i, h=h, opt_=opt_: e.tensor_copy(
                    ot[h][:, blk_i * 512:(blk_i + 1) * 512], opt_[:, :])),
                    reads=[opk], writes=[f"ot{h}"], tag="oev")
            else:
                S.op("act", (lambda e, blk_i=blk_i, h=h, opt_=opt_: e.copy(
                    ot[h][:, blk_i * 512:(blk_i + 1) * 512], opt_[:, :])),
                    reads=[opk], writes=[f"ot{h}"], tag="oev")
        S.op("dve", (lambda e, h=h: e.tensor_tensor(
            ot[h][:, :], ot[h][:, :], rrep[:, :], ALU.mult)),
            reads=[f"ot{h}", "rrep"], writes=[f"ot{h}"], tag="onrm")

    for o in range(NCH):
        wk = f"wslab{o % 2}"
        slab_load(oww, o, NCH, wk)
        S.op("sp", (lambda e, o=o: e.dma_start(
            hb[o % 2][:, :], x3d[o * 128:(o + 1) * 128, :])),
            writes=[f"hb{o % 2}"], dma=True)
        S.op("dve", (lambda e, o=o: e.tensor_tensor(
            hb[o % 2][:, :], hb[o % 2][:, :], arep[:, :], ALU.mult)),
            reads=[f"hb{o % 2}", "arep"], writes=[f"hb{o % 2}"], tag="urec")
        S.op("dve", (lambda e, o=o: e.tensor_tensor(
            hb[o % 2][:, :], hb[o % 2][:, :], brep[:, :], ALU.add)),
            reads=[f"hb{o % 2}", "brep"], writes=[f"hb{o % 2}"], tag="urec2")
        for tc in range(NT):
            pk = ["ps2", "ps3"][tc]
            pt = E_PS[pk]
            for hh in range(NH):
                w = [pk] if (hh == 0 or hh == NH - 1) else []
                S.op("pe", (lambda e, o=o, tc=tc, hh=hh, pt=pt: e.matmul(
                    pt[:, :], wslab[o % 2][:, hh * 128:(hh + 1) * 128],
                    ot[hh][:, tc * 512:(tc + 1) * 512],
                    start=(hh == 0), stop=(hh == NH - 1))),
                    reads=[wk, f"ot{hh}"], writes=w, tag=f"ow{o}")
            S.op("dve", (lambda e, o=o, tc=tc, pt=pt: e.tensor_tensor(
                hb[o % 2][:, tc * 512:(tc + 1) * 512], pt[:, :],
                hb[o % 2][:, tc * 512:(tc + 1) * 512], ALU.add)),
                reads=[pk, f"hb{o % 2}"], writes=[f"hb{o % 2}"], tag="hres")
        S.op("sp", (lambda e, o=o: e.dma_start(
            hd[o * 128:(o + 1) * 128, :], hb[o % 2][:, :])),
            reads=[f"hb{o % 2}"], writes=[f"hst{o % 2}"], dma=True)

    S.op("sp", lambda e: e.nop(), reads=["hst0", "hst1"])


def emit_phase_e(nc, S, ctx, D):
    wbn = D["wbn"]
    hdd = D["hi"]
    x3d = D["x3i"]
    shg = D["shg"]
    shu = D["shu"]
    shd = D["shd"]
    rg = D["rg"]
    ru = D["ru"]
    rd = D["rd"]
    rwT = D["rwT"]
    rbd = D["rbd"]
    l2wd = D["l2wd"]
    onesd = D["onesd"]
    x4 = D["x4"]
    lnb = D["lnb"]

    hb = [sbt(ctx, nc, f"hb{i}", [128, TL], BF16) for i in range(NCH)]
    ge = [[sbt(ctx, nc, f"ge{gi}_{i}", [128, TL], BF16) for i in range(NIT)]
          for gi in range(4)]
    macc = [sbt(ctx, nc, f"macc{i}", [128, TL], BF16) for i in range(NCH)]
    sqt = sbt(ctx, nc, "sqt", [128, TL], BF16)
    wslab = [sbt(ctx, nc, f"wslab{i}", [128, C], BF16) for i in range(2)]
    sil = sbt(ctx, nc, "sil", [128, 512], BF16)
    rwt = sbt(ctx, nc, "rwt", [128, 112], BF16)
    rbt = sbt(ctx, nc, "rbt", [7, 1], F32)
    l2w = sbt(ctx, nc, "l2w", [128, 16], F32)
    ones_bf = sbt(ctx, nc, "ones_bf", [128, 1], BF16)
    pt32 = sbt(ctx, nc, "pt32", [32, TL], F32)
    m1t = sbt(ctx, nc, "m1t", [32, TL], F32)
    m2t = sbt(ctx, nc, "m2t", [32, TL], F32)
    pro = sbt(ctx, nc, "pro", [32, TL], F32)
    wrep = sbt(ctx, nc, "wrep", [128, TL], F32)
    arep = sbt(ctx, nc, "arep", [128, TL], F32)
    brep = sbt(ctx, nc, "brep", [128, TL], F32)
    x3h = [sbt(ctx, nc, f"x3h{i}", [128, TL], F32) for i in range(2)]
    x4b = [sbt(ctx, nc, f"x4b{i}", [128, TL], F32) for i in range(2)]
    rows = {k: sbt(ctx, nc, k, [1, TL], F32) for k in ("srow", "qrow", "brow")}

    ps0 = pst(ctx, nc, "ps0", [128, 512])
    ps1 = pst(ctx, nc, "ps1", [128, 512])
    ps2 = pst(ctx, nc, "ps2", [128, 512])
    ps3 = pst(ctx, nc, "ps3", [128, 512])
    pr0 = pst(ctx, nc, "pr0", [128, 512])
    pr1 = pst(ctx, nc, "pr1", [128, 512])
    E_PS = {"ps0": ps0, "ps1": ps1, "ps2": ps2, "ps3": ps3}
    E = {"ones_bf": ones_bf, "sqt": sqt, "rows": rows, "lnb": lnb,
         "arep": arep, "brep": brep,
         "ps0": ps0, "ps1": ps1, "pr0": pr0, "pr1": pr1}

    for dc in range(NCH):
        S.op("pool", (lambda e, dc=dc: e.dma_start(
            hb[dc][:, :], hdd[dc * 128:(dc + 1) * 128, :])),
            writes=[f"hb{dc}"], dma=True)
    S.op("sp", lambda e: e.dma_start(ones_bf[:, :], onesd[:, :]), writes=["ones"], dma=True)
    S.op("sp", lambda e: e.dma_start(l2w[:, :], l2wd[:, :]), writes=["l2w"], dma=True)
    S.op("sp", lambda e: e.dma_start(rbt[:, :], rbd[:, :]), writes=["rbt"], dma=True)
    S.op("pool", lambda e: e.dma_start(
        rwt[:, 0:112], bass.AP(rwT, 0, [[7, 128], [128 * 7, NCH], [1, 7]])),
        writes=["rwt"], dma=True)

    hkeys = [f"hb{i}" for i in range(NCH)]
    emit_layernorm(S, E, hkeys, hb, 2, rms_only=True, apply=False)
    S.op("sp", (lambda e: e.dma_start(
        arep[:, :], bass.AP(lnb, 2 * TL, [[0, 128], [1, TL]]))),
        reads=["lnb2"], writes=["arep"], dma=True)
    for dc in range(NCH):
        S.op("dve", (lambda e, dc=dc: e.scalar_tensor_tensor(
            hb[dc][:, :], hb[dc][:, :], l2w[:, dc:dc + 1], arep[:, :],
            ALU.mult, ALU.mult)),
            reads=[f"hb{dc}", "l2w", "arep"], writes=[f"hb{dc}"], tag="rmsap")

    for tc in range(NT):
        prk = ["pr0", "pr1"][tc]
        prt = E[prk]
        for dc in range(NCH):
            w = [prk] if (dc == 0 or dc == NCH - 1) else []
            S.op("pe", (lambda e, tc=tc, dc=dc, prt=prt: e.matmul(
                prt[0:7, :], rwt[:, dc * 7:(dc + 1) * 7],
                hb[dc][:, tc * 512:(tc + 1) * 512],
                start=(dc == 0), stop=(dc == NCH - 1))),
                reads=["rwt", f"hb{dc}"], writes=w, tag="rlog")
    S.op("dve", lambda e: e.memset(pt32[:, :], NEG), writes=["pt32"])
    for tc in range(NT):
        prk = ["pr0", "pr1"][tc]
        prt = E[prk]
        S.op("dve", (lambda e, tc=tc, prt=prt: e.tensor_scalar_add(
            pt32[0:7, tc * 512:(tc + 1) * 512], prt[0:7, :], rbt[0:7, 0:1])),
            reads=[prk, "rbt", "pt32"], writes=["pt32"], tag="rbias")
    S.op("act", lambda e: e.activation(pt32[0:7, :], pt32[0:7, :], AF.Sigmoid),
         reads=["pt32"], writes=["pt32"])
    S.op("dve", lambda e: e.tensor_copy(pro[:, :], pt32[:, :]),
         reads=["pt32"], writes=["pro"])

    def logmax(dst, dkey, srckey):
        for k in (1, 2, 4):
            mask = [(r & ~7) | ((r + k) & 7) for r in range(32)]
            S.op("dve", (lambda e, mask=mask, dst=dst: e.stream_shuffle(
                m2t[:, :] if dst is m1t else m1t[:, :], dst[:, :], mask)),
                reads=[dkey], writes=["m2t" if dst is m1t else "m1t"], tag="shf")
            S.op("dve", (lambda e, dst=dst: e.tensor_tensor(
                dst[:, :], dst[:, :],
                m2t[:, :] if dst is m1t else m1t[:, :], ALU.max)),
                reads=[dkey, "m2t" if dst is m1t else "m1t"], writes=[dkey], tag="lmx")

    S.op("dve", lambda e: e.tensor_copy(m1t[:, :], pt32[:, :]),
         reads=["pt32"], writes=["m1t"])
    logmax(m1t, "m1t", "pt32")
    S.op("dve", lambda e: e.tensor_tensor(m2t[:, :], pro[:, :], m1t[:, :], ALU.is_equal),
         reads=["pro", "m1t"], writes=["m2t"])
    S.op("dve", lambda e: e.tensor_scalar_mul(m2t[:, :], m2t[:, :], NEG),
         reads=["m2t"], writes=["m2t"])
    S.op("dve", lambda e: e.tensor_tensor(pt32[:, :], pro[:, :], m2t[:, :], ALU.add),
         reads=["pro", "m2t", "pt32"], writes=["pt32"])
    logmax(pt32, "pt32", None)
    S.op("dve", lambda e: e.tensor_tensor(m2t[:, :], pro[:, :], pt32[:, :], ALU.is_ge),
         reads=["pro", "pt32"], writes=["m2t"])
    S.op("dve", lambda e: e.tensor_tensor(m2t[:, :], m2t[:, :], pro[:, :], ALU.mult),
         reads=["m2t", "pro"], writes=["m2t"])
    S.op("dve", lambda e: e.tensor_tensor(m1t[:, :], m1t[:, :], pt32[:, :], ALU.add),
         reads=["m1t", "pt32"], writes=["m1t"])
    S.op("dve", lambda e: e.reciprocal(m1t[:, :], m1t[:, :]),
         reads=["m1t"], writes=["m1t"])
    S.op("dve", lambda e: e.tensor_tensor(m2t[:, :], m2t[:, :], m1t[:, :], ALU.mult),
         reads=["m2t", "m1t"], writes=["m2t"])
    S.op("dve", lambda e: e.tensor_scalar_mul(m2t[:, :], m2t[:, :], 0.125),
         reads=["m2t"], writes=["m2t"])
    S.op("sp", (lambda e: e.dma_start(
        wbn[0:7, :], m2t[0:7, :])), reads=["m2t"], writes=["wbn"], dma=True)

    def expert_mm(e_idx, gslot):
        for f in range(NIT):
            for (wsrc, which) in ((shg if e_idx == 0 else rg, 0),
                                  (shu if e_idx == 0 else ru, 1)):
                wk = f"wslab{which}"
                off = (f if e_idx == 0 else ((e_idx - 1) * NIT + f)) * C * 128
                S.op("pool", (lambda e, off=off, wsrc=wsrc, which=which: e.dma_start(
                    wslab[which][:, :],
                    bass.AP(wsrc, off, [[128, 128], [128 * 128, NCH], [1, 128]]))),
                    writes=[wk], dma=True)
            for tc in range(NT):
                pgk, puk = "ps2", "ps3"
                pgt, put = ps2, ps3
                for dc in range(NCH):
                    w = [pgk] if (dc == 0 or dc == NCH - 1) else []
                    S.op("pe", (lambda e, tc=tc, dc=dc, pgt=pgt: e.matmul(
                        pgt[:, :], wslab[0][:, dc * 128:(dc + 1) * 128],
                        hb[dc][:, tc * 512:(tc + 1) * 512],
                        start=(dc == 0), stop=(dc == NCH - 1))),
                        reads=["wslab0", f"hb{dc}"], writes=w, tag="gmm")
                for dc in range(NCH):
                    w = [puk] if (dc == 0 or dc == NCH - 1) else []
                    S.op("pe", (lambda e, tc=tc, dc=dc, put=put: e.matmul(
                        put[:, :], wslab[1][:, dc * 128:(dc + 1) * 128],
                        hb[dc][:, tc * 512:(tc + 1) * 512],
                        start=(dc == 0), stop=(dc == NCH - 1))),
                        reads=["wslab1", f"hb{dc}"], writes=w, tag="umm")
                S.op("act", (lambda e, pgt=pgt: e.activation(
                    sil[:, :], pgt[:, :], AF.Silu)),
                    reads=[pgk], writes=["sil"], tag="silu")
                gk = f"ge{gslot}_{f}"
                S.op("dve", (lambda e, tc=tc, f=f, put=put, gslot=gslot: e.tensor_tensor(
                    ge[gslot][f][:, tc * 512:(tc + 1) * 512], sil[:, :],
                    put[:, :], ALU.mult)),
                    reads=["sil", puk], writes=[gk], tag="gmul")
        if e_idx == 0:
            for f in range(NIT):
                gk = f"ge{gslot}_{f}"
                S.op("dve", (lambda e, f=f, gslot=gslot: e.tensor_scalar_mul(
                    ge[gslot][f][:, :], ge[gslot][f][:, :], 0.125)),
                    reads=[gk], writes=[gk], tag="gsc")
        else:
            S.op("sp", (lambda e, e_idx=e_idx: e.dma_start(
                wrep[:, :], bass.AP(wbn, (e_idx - 1) * TL, [[0, 128], [1, TL]]))),
                reads=["wbn"], writes=["wrep"], dma=True)
            for f in range(NIT):
                gk = f"ge{gslot}_{f}"
                S.op("dve", (lambda e, f=f, gslot=gslot: e.tensor_tensor(
                    ge[gslot][f][:, :], ge[gslot][f][:, :], wrep[:, :], ALU.mult)),
                    reads=[gk, "wrep"], writes=[gk], tag="gscw")

    for grp in range(2):
        for gi in range(4):
            expert_mm(grp * 4 + gi, gi)
        for o in range(NCH):
            for tc in range(NT):
                pk = ["ps0", "ps1"][(o * NT + tc) % 2]
                pt = E_PS[pk]
                nacc = 4 * NIT
                idx = 0
                for gi in range(4):
                    e_idx = grp * 4 + gi
                    wk2 = "w2slab"
                    for hc in range(NIT):
                        if tc == 0 and hc == 0:
                            pass
                        idx += 1
        for o in range(NCH):
            for gi in range(4):
                e_idx = grp * 4 + gi
                wk = f"wslab{gi % 2}"
                src = shd if e_idx == 0 else rd
                off = (o if e_idx == 0 else ((e_idx - 1) * NCH + o)) * IEXP * 128
                S.op("pool", (lambda e, off=off, src=src, gi=gi: e.dma_start(
                    wslab[gi % 2][:, 0:IEXP],
                    bass.AP(src, off, [[128, 128], [128 * 128, NIT], [1, 128]]))),
                    writes=[wk], dma=True)
                for tc in range(NT):
                    pk = ["ps0", "ps1"][tc]
                    pt = E_PS[pk]
                    for hc in range(NIT):
                        first_mm = (gi == 0 and hc == 0)
                        last_mm = (gi == 3 and hc == NIT - 1)
                        w = [pk] if (first_mm or last_mm) else []
                        S.op("pe", (lambda e, tc=tc, hc=hc, pt=pt, gi=gi: e.matmul(
                            pt[:, :], wslab[gi % 2][:, hc * 128:(hc + 1) * 128],
                            ge[gi][hc][:, tc * 512:(tc + 1) * 512],
                            start=(gi == 0 and hc == 0),
                            stop=(gi == 3 and hc == NIT - 1))),
                            reads=[wk, f"ge{gi}_{hc}"], writes=w, tag=f"dmm{o}")
            if grp == 0:
                for tc in range(NT):
                    pk = ["ps0", "ps1"][tc]
                    pt = E_PS[pk]
                    S.op("dve", (lambda e, o=o, tc=tc, pt=pt: e.tensor_copy(
                        macc[o][:, tc * 512:(tc + 1) * 512], pt[:, :])),
                        reads=[pk], writes=[f"macc{o}"], tag="mev")
            else:
                S.op("sp", (lambda e, o=o: e.dma_start(
                    x3h[o % 2][:, :], x3d[o * 128:(o + 1) * 128, :])),
                    writes=[f"x3h{o % 2}"], dma=True)
                S.op("sp", (lambda e, o=o: e.dma_start(
                    x4b[o % 2][:, :], hdd[o * 128:(o + 1) * 128, :])),
                    writes=[f"x4b{o % 2}"], dma=True)
                S.op("dve", (lambda e, o=o: e.tensor_tensor(
                    x4b[o % 2][:, :], x4b[o % 2][:, :], x3h[o % 2][:, :], ALU.add)),
                    reads=[f"x4b{o % 2}", f"x3h{o % 2}"], writes=[f"x4b{o % 2}"], tag="xh")
                for tc in range(NT):
                    pk = ["ps0", "ps1"][tc]
                    pt = E_PS[pk]
                    S.op("dve", (lambda e, o=o, tc=tc, pt=pt: e.tensor_tensor(
                        x3h[o % 2][:, tc * 512:(tc + 1) * 512], pt[:, :],
                        macc[o][:, tc * 512:(tc + 1) * 512], ALU.add)),
                        reads=[pk, f"macc{o}"], writes=[f"x3h{o % 2}"], tag="m2a")
                    S.op("dve", (lambda e, o=o, tc=tc: e.tensor_tensor(
                        x4b[o % 2][:, tc * 512:(tc + 1) * 512],
                        x4b[o % 2][:, tc * 512:(tc + 1) * 512],
                        x3h[o % 2][:, tc * 512:(tc + 1) * 512], ALU.add)),
                        reads=[f"x4b{o % 2}", f"x3h{o % 2}"], writes=[f"x4b{o % 2}"], tag="x4a")
                S.op("sp", (lambda e, o=o: e.dma_start(
                    x4[o * 128:(o + 1) * 128, :], x4b[o % 2][:, :])),
                    reads=[f"x4b{o % 2}"], writes=[f"x4st{o % 2}"], dma=True)

    S.op("sp", lambda e: e.nop(), reads=["x4st0", "x4st1"])



def build_fused():
    import os as _os0
    nc = bass.Bass("TRN2", target_bir_lowering=False)
    with ExitStack() as octx:
        D = {}
        D["x"] = nc.dram_tensor("x", [C, TL], F32, kind="ExternalInput")
        D["wqkT"] = nc.dram_tensor("wqkT", [32, C, 128], F32, kind="ExternalInput")
        D["wvT"] = nc.dram_tensor("wvT", [16, C, 128], F32, kind="ExternalInput")
        D["woT"] = nc.dram_tensor("woT", [16, C, 128], F32, kind="ExternalInput")
        D["cos0"] = nc.dram_tensor("cos0", [128, TL], BF16, kind="ExternalInput")
        D["sin0"] = nc.dram_tensor("sin0", [128, TL], BF16, kind="ExternalInput")
        D["mask0"] = nc.dram_tensor("mask0", [128, MASK_COLS], BF16, kind="ExternalInput")
        D["onesd"] = nc.dram_tensor("onesd", [128, 1], BF16, kind="ExternalInput")
        D["identd"] = nc.dram_tensor("identd", [128, 128], BF16, kind="ExternalInput")
        D["x1i"] = nc.dram_tensor("x1i", [C, TL], F32)
        D["k_in"] = nc.dram_tensor("k_in", [C, TL], BF16)
        D["k_out"] = nc.dram_tensor("k_out", [2 * C, TL], BF16)
        D["v_in"] = nc.dram_tensor("v_in", [TL, C], BF16)
        D["v_out"] = nc.dram_tensor("v_out", [T, C], BF16)
        D["lnb"] = nc.dram_tensor("lnb", [8, TL], F32)
        D["w1s"] = nc.dram_tensor("w1s", [NHT, C, 128], F32, kind="ExternalInput")
        D["w3s"] = nc.dram_tensor("w3s", [NHT, C, 128], F32, kind="ExternalInput")
        D["w2s"] = nc.dram_tensor("w2s", [NCH, HIDP, 128], F32, kind="ExternalInput")
        D["onesf"] = nc.dram_tensor("onesf", [128, 1], F32, kind="ExternalInput")
        if "x2i" not in D:
            if "x2i" not in D:
                D["x2i"] = nc.dram_tensor("x2i", [C, TL], F32)
        D["wxT"] = nc.dram_tensor("wxT", [C, 160], F32, kind="ExternalInput")
        D["wdtT"] = nc.dram_tensor("wdtT", [128, 1024], F32, kind="ExternalInput")
        D["alogh"] = nc.dram_tensor("alogh", [128, 128], F32, kind="ExternalInput")
        D["bdth"] = nc.dram_tensor("bdth", [128, 8], F32, kind="ExternalInput")
        D["dfull"] = nc.dram_tensor("dfull", [128, 16], F32, kind="ExternalInput")
        D["hoffd"] = nc.dram_tensor("hoffd", [1, 4], mybir.dt.int32, kind="ExternalInput")
        D["x3i"] = nc.dram_tensor("x3i", [C, TL], F32)
        D["e2in"] = nc.dram_tensor("e2in", [160, TL], BF16)
        D["e2out"] = nc.dram_tensor("e2out", [320, TL], BF16)
        D["e2inB"] = nc.dram_tensor("e2inB", [2048, TL], BF16)
        D["e2outB"] = nc.dram_tensor("e2outB", [4096, TL], BF16)
        D["e3in"] = nc.dram_tensor("e3in", [512, T], BF16)
        D["e3out"] = nc.dram_tensor("e3out", [1024, T], BF16)
        D["e3inB"] = nc.dram_tensor("e3inB", [512, T], BF16)
        D["e3outB"] = nc.dram_tensor("e3outB", [1024, T], BF16)
        D["kvdw"] = nc.dram_tensor("kvdw", [NLAT, C, 128], F32, kind="ExternalInput")
        D["qdw"] = nc.dram_tensor("qdw", [NLAT, C, 128], F32, kind="ExternalInput")
        D["kuw"] = nc.dram_tensor("kuw", [8, LAT, 128], F32, kind="ExternalInput")
        D["quw"] = nc.dram_tensor("quw", [8, LAT, 128], F32, kind="ExternalInput")
        D["vuw"] = nc.dram_tensor("vuw", [LAT, C], F32, kind="ExternalInput")
        D["rkw"] = nc.dram_tensor("rkw", [8, C, 128], F32, kind="ExternalInput")
        D["rqw"] = nc.dram_tensor("rqw", [8, LAT, 128], F32, kind="ExternalInput")
        D["oww"] = nc.dram_tensor("oww", [NCH, C, 128], F32, kind="ExternalInput")
        D["cos1"] = nc.dram_tensor("cos1", [128, TL], BF16, kind="ExternalInput")
        D["sin1"] = nc.dram_tensor("sin1", [128, TL], BF16, kind="ExternalInput")
        D["l1wd"] = nc.dram_tensor("l1wd", [128, 16], F32, kind="ExternalInput")
        D["hi"] = nc.dram_tensor("hi", [C, TL], F32)
        D["shg"] = nc.dram_tensor("shg", [NIT, C, 128], F32, kind="ExternalInput")
        D["shu"] = nc.dram_tensor("shu", [NIT, C, 128], F32, kind="ExternalInput")
        D["shd"] = nc.dram_tensor("shd", [NCH, IEXP, 128], F32, kind="ExternalInput")
        D["rg"] = nc.dram_tensor("rg", [7, NIT, C, 128], F32, kind="ExternalInput")
        D["ru"] = nc.dram_tensor("ru", [7, NIT, C, 128], F32, kind="ExternalInput")
        D["rd"] = nc.dram_tensor("rd", [7, NCH, IEXP, 128], F32, kind="ExternalInput")
        D["rwT"] = nc.dram_tensor("rwT", [C, 7], F32, kind="ExternalInput")
        D["rbd"] = nc.dram_tensor("rbd", [7, 1], F32, kind="ExternalInput")
        D["l2wd"] = nc.dram_tensor("l2wd", [128, 16], F32, kind="ExternalInput")
        D["x4"] = nc.dram_tensor("x4", [C, TL], F32, kind="ExternalOutput")
        D["sder"] = nc.dram_tensor("sder", [1, 16], F32)
        D["drfg"] = nc.dram_tensor("drfg", [128, T], BF16)
        D["e2xg"] = nc.dram_tensor("e2xg", [C, T], BF16)
        D["e3b"] = nc.dram_tensor("e3b", [C, T], BF16)
        D["bcg"] = nc.dram_tensor("bcg", [32, T], BF16)
        D["wbn"] = nc.dram_tensor("wbn", [8, TL], F32)
        D["_octx"] = octx
        D["abscol"] = octx.enter_context(nc.sbuf_tensor("abscol_g", [128, 4], F32))
        D["abstmp"] = octx.enter_context(nc.sbuf_tensor("abstmp_g", [128, 1], F32))
        S = Sched(nc, octx)
        groups = ([emit_prelude, emit_phase_a], [emit_phase_b],
                  [emit_phase_c], [emit_phase_d], [emit_phase_e])
        for grp in groups:
            with ExitStack() as pctx:
                for em in grp:
                    em(nc, S, pctx, D)
            S.barrier()
        S.emit_all()
    return nc


def _stage_fused(inputs):
    ia = _stage_phase_a(inputs)
    ib = _stage_phase_b(inputs)
    ic = _stage_phase_c(inputs)
    id_ = _stage_phase_d(inputs)
    ie = _stage_phase_e(inputs)
    in_maps = []
    for c in range(8):
        m = {}
        for part in (ia[c], ib[c], ic[c], id_[c], ie[c]):
            m.update(part)
        in_maps.append(m)
    return in_maps


def kernel(**inputs):
    nc = build_fused()
    in_maps = _stage_fused(inputs)
    res = run_bass_kernel_spmd(nc, in_maps, core_ids=list(range(8)))
    out = np.zeros((B, T, C), np.float32)
    for c in range(8):
        b, j = divmod(c, 2)
        out[b, _tok_idx(j), :] = res.results[c]["x4"].T
    return out
